# version 09
# baseline (speedup 1.0000x reference)
import contextlib
import numpy as np
import ml_dtypes
import concourse.bass as bass
import concourse.mybir as mybir
from concourse.bass import broadcast_tensor_aps
from concourse.bass_utils import run_bass_kernel_spmd

F32 = mybir.dt.float32
BF16 = mybir.dt.bfloat16
AF = mybir.ActivationFunctionType
ALU = mybir.AluOpType
AX = mybir.AxisListType
bfdt = ml_dtypes.bfloat16

NCORES = 8
SEQ_PER_CORE = 2
S = 2048
D = 1024
I = 2816
TS = 128
NT = S // TS
NCH = D // 128
EPS_LN = 1e-5

ENGS = ("sync", "gpsimd", "scalar", "vector", "tensor")
OWN_WAITS = True
DRAIN_FUSABLE = True
END_CLEAR = True
BATCH_INCS = ("tensor",)


class Sched:

    def __init__(self, nc, ctx, sem_names=("t", "v", "a")):
        self.nc = nc
        self.ctx = ctx
        self.pfx = ""
        self.swdge_sems = set()
        self.items = {e: [] for e in ENGS}
        self.sems = {}
        self.cnt = {}
        for n in sem_names:
            self.sem(n)

    def ev(self, name):
        if name.startswith("d_"):
            name = self.pfx + name
        return (name, self.cnt[name])

    def barrier(self):
        snap = tuple((n, c) for n, c in self.cnt.items() if c > 0)
        for e in ENGS:
            self.op(e, None, waits=snap)

    def end_clear(self):
        self.barrier()
        self.sem("fin")
        for e in ENGS:
            if e != "gpsimd":
                self.op(e, lambda en: en.nop(), inc="fin")
        fin_target = self.cnt["fin"]
        sems = [h for n, h in self.sems.items() if n not in self.swdge_sems]

        def fn(e, sems=sems):
            ins = None
            for h in sems:
                ins = e.sem_clear(h)
            return ins
        self.op("gpsimd", fn, waits=(("fin", fin_target),))

    def sem(self, name):
        if name not in self.sems:
            self.sems[name] = self.ctx.enter_context(
                self.nc.semaphore("sem_" + name))
            self.cnt[name] = 0
        return name

    def op(self, eng, fn, waits=(), inc=None, amt=1):
        own = {"vector": "v", "scalar": "a"}.get(eng)
        own_wait = None
        if own is not None and fn is not None:
            if inc is None:
                inc = own
            own_wait = (own, self.cnt[own])
        ev = None
        waits = tuple(w for w in waits if w is not None)
        abs_cnt = None
        if inc is not None:
            self.cnt[inc] += amt
            abs_cnt = self.cnt[inc]
            ev = (inc, abs_cnt)
        self.items[eng].append((fn, waits, inc, amt, own_wait, abs_cnt))
        return ev

    def dma(self, eng, out, in_, sem, waits=()):
        sem = self.pfx + sem
        self.sem(sem)
        if eng == "gpsimd":
            self.swdge_sems.add(sem)
        return self.op(eng, lambda e, o=out, i=in_: e.dma_start(out=o, in_=i),
                       waits=waits, inc=sem, amt=16)

    def build(self, block):
        sems = self.sems
        waited_vals = {}
        for e in ENGS:
            for fn, waits, inc, amt, own_wait, abs_cnt in self.items[e]:
                for (s_, v_) in waits:
                    waited_vals.setdefault(s_, set()).add(v_)
        for e in ENGS:
            items = self.items[e]
            if not items:
                continue
            use_drain = e in ("vector", "scalar")
            batch = e in BATCH_INCS
            last_idx = {}
            for i, (fn, waits, inc, amt, own_wait, abs_cnt) in enumerate(items):
                if inc is not None and fn is not None:
                    last_idx[inc] = i
            plan = []
            pending = {}
            for i, (fn, waits, inc, amt, own_wait, abs_cnt) in enumerate(items):
                em = None
                if inc is not None and fn is not None:
                    if amt == 16:
                        em = (inc, 16)
                    elif not batch:
                        em = (inc, amt)
                    else:
                        pending[inc] = pending.get(inc, 0) + amt
                        if (abs_cnt in waited_vals.get(inc, ()) or
                                i == last_idx[inc]):
                            em = (inc, pending[inc])
                            pending[inc] = 0
                plan.append(em)

            def make_emit(items=items, plan=plan, use_drain=use_drain):
                def emit(engobj):
                    waited = {}
                    for (fn, waits, inc, amt, own_wait, abs_cnt), em in zip(items, plan):
                        if OWN_WAITS and own_wait is not None:
                            waits = (own_wait,) + waits
                        for (s_, v_) in waits:
                            if v_ > 0 and waited.get(s_, -1) < v_:
                                engobj.wait_ge(sems[s_], v_)
                                waited[s_] = v_
                        if fn is None:
                            continue
                        instr = fn(engobj)
                        if em is not None:
                            sname, a2 = em
                            if a2 == 16 or not use_drain:
                                instr.then_inc(sems[sname], a2)
                            else:
                                engobj.drain(fusable=DRAIN_FUSABLE).then_inc(
                                    sems[sname], a2)
                return emit
            getattr(block, e)(make_emit())
        self.items = {e: [] for e in ENGS}


def ev_max(*evs):
    best = {}
    for ev in evs:
        if ev is None or (isinstance(ev, tuple) and len(ev) == 0):
            continue
        if isinstance(ev, tuple) and isinstance(ev[0], tuple):
            for (s_, v_) in ev:
                if best.get(s_, -1) < v_:
                    best[s_] = v_
        else:
            s_, v_ = ev
            if best.get(s_, -1) < v_:
                best[s_] = v_
    return tuple(best.items())



def emit_quant(sch, q, src, out_bf16, C, wv, wa):
    NB = C // 16
    absx, sgn, a1, c5 = q["absx"], q["sgn"], q["a1"], q["c5"]
    bm, sc, rs, hs = q["bm"], q["sc"], q["rs"], q["hs"]
    war = q.get("_last")
    wa = ev_max(wa, war)
    wv = ev_max(wv)
    e_absx = sch.op("scalar", lambda e: e.activation(
        out=absx[:, 0:C], in_=src, func=AF.Abs), waits=wa, inc="a")
    e_sgn = sch.op("scalar", lambda e: e.activation(
        out=sgn[:, 0:C], in_=src, func=AF.Sign), waits=wa, inc="a")
    e_bm = sch.op("vector", lambda e: e.tensor_reduce(
        out=bm[:, 0:NB].rearrange("p (b o) -> p b o", o=1),
        in_=src.rearrange("p (b s) -> p b s", s=16),
        axis=AX.X, op=ALU.max, apply_absolute_value=True), waits=wv, inc="v")
    e_sc = sch.op("vector", lambda e: e.tensor_scalar(
        out=sc[:, 0:NB], in0=bm[:, 0:NB], scalar1=1.0 / 6.0, scalar2=1e-12,
        op0=ALU.mult, op1=ALU.max), waits=(e_bm,), inc="v")
    e_rs = sch.op("vector", lambda e: e.reciprocal(
        out=rs[:, 0:NB], in_=sc[:, 0:NB]), waits=(e_sc,), inc="v")
    sch.op("vector", lambda e: e.tensor_scalar(
        out=hs[:, 0:NB], in0=sc[:, 0:NB], scalar1=0.5, scalar2=None, op0=ALU.mult))
    av = absx[:, 0:C].rearrange("p (b s) -> p b s", s=16)
    rv = rs[:, 0:NB].rearrange("p (b o) -> p b o", o=1)
    avb, rvb = broadcast_tensor_aps(av, rv)
    sch.op("vector", lambda e: e.tensor_tensor(out=av, in0=avb, in1=rvb, op=ALU.mult),
           waits=ev_max(e_absx, e_rs))
    u = absx[:, 0:C]
    sch.op("vector", lambda e: e.tensor_scalar(
        out=c5[:, 0:C], in0=u, scalar1=5.0, scalar2=2.0, op0=ALU.is_gt, op1=ALU.mult))
    for t_ in (3.5, 2.5):
        sch.op("vector", lambda e, t_=t_: e.scalar_tensor_tensor(
            out=c5[:, 0:C], in0=u, scalar=t_, in1=c5[:, 0:C],
            op0=ALU.is_gt, op1=ALU.add))
    sch.op("vector", lambda e: e.tensor_scalar(
        out=a1[:, 0:C], in0=u, scalar1=0.25, scalar2=None, op0=ALU.is_gt))
    for t_ in (0.75, 1.25, 1.75):
        sch.op("vector", lambda e, t_=t_: e.scalar_tensor_tensor(
            out=a1[:, 0:C], in0=u, scalar=t_, in1=a1[:, 0:C],
            op0=ALU.is_gt, op1=ALU.add))
    sch.op("vector", lambda e: e.scalar_tensor_tensor(
        out=a1[:, 0:C], in0=c5[:, 0:C], scalar=2.0, in1=a1[:, 0:C],
        op0=ALU.mult, op1=ALU.add))
    gv = sgn[:, 0:C].rearrange("p (b s) -> p b s", s=16)
    hv = hs[:, 0:NB].rearrange("p (b o) -> p b o", o=1)
    gvb, hvb = broadcast_tensor_aps(gv, hv)
    sch.op("vector", lambda e: e.tensor_tensor(out=gv, in0=gvb, in1=hvb, op=ALU.mult),
           waits=ev_max(e_sgn))
    ev = sch.op("vector", lambda e: e.tensor_tensor(
        out=out_bf16, in0=a1[:, 0:C], in1=sgn[:, 0:C], op=ALU.mult), inc="v")
    q["_last"] = ev
    return ev


def emit_ln(sch, b, x_src, out_xn, e_x):
    war = b.get("_last")
    wx = ev_max(e_x)
    wxa = ev_max(e_x, war)
    e_s1 = sch.op("vector", lambda e: e.reduce_sum(
        out=b["s1"][:], in_=x_src, axis=AX.X), waits=wx, inc="v")
    e_sq = sch.op("scalar", lambda e: e.activation(
        out=b["sqs"][:], in_=x_src, func=AF.Square), waits=wxa, inc="a")
    e_ssq = sch.op("vector", lambda e: e.reduce_sum(
        out=b["ssq"][:], in_=b["sqs"][:], axis=AX.X), waits=(e_sq,), inc="v")
    e_mean = sch.op("vector", lambda e: e.tensor_scalar(
        out=b["mean"][:], in0=b["s1"][:], scalar1=1.0 / D, scalar2=None,
        op0=ALU.mult), waits=ev_max(e_s1, e_ssq), inc="v")
    sch.op("vector", lambda e: e.tensor_scalar(
        out=b["ex2"][:], in0=b["ssq"][:], scalar1=1.0 / D, scalar2=EPS_LN,
        op0=ALU.mult, op1=ALU.add))
    sch.op("vector", lambda e: e.tensor_tensor(
        out=b["msq"][:], in0=b["mean"][:], in1=b["mean"][:], op=ALU.mult),
        waits=(e_mean,))
    e_var = sch.op("vector", lambda e: e.tensor_tensor(
        out=b["var"][:], in0=b["ex2"][:], in1=b["msq"][:], op=ALU.subtract), inc="v")
    e_sd = sch.op("scalar", lambda e: e.activation(
        out=b["sd"][:], in_=b["var"][:], func=AF.Sqrt),
        waits=ev_max(e_var, war), inc="a")
    e_rstd = sch.op("vector", lambda e: e.reciprocal(
        out=b["rstd"][:], in_=b["sd"][:]), waits=(e_sd,), inc="v")
    e_xn = sch.op("vector", lambda e: e.tensor_scalar(
        out=out_xn, in0=x_src, scalar1=b["mean"][:][:, 0:1],
        scalar2=b["rstd"][:][:, 0:1], op0=ALU.subtract, op1=ALU.mult),
        waits=(e_rstd,), inc="v")
    b["_last"] = e_xn
    return e_xn


def emit_transpose_evac(sch, psum, ident, srcs, dsts, w_tensor, w_evac,
                        evac_eng="scalar", dst_groups=None):
    evac_events = []
    n = len(srcs)
    last = None
    for cb in range(0, n, 4):
        nslot = min(4, n - cb)
        e_tps = None
        for s_i in range(nslot):
            src = srcs[cb + s_i]
            e_tps = sch.op("tensor", lambda e, src=src, s_i=s_i: e.transpose(
                psum[:, s_i, 0:128], src, ident), waits=w_tensor if s_i == 0 else (),
                inc="t" if s_i == nslot - 1 else None)
        if dst_groups is not None:
            dstg = dst_groups(cb, nslot)
            last = sch.op(evac_eng, lambda e, dstg=dstg, nslot=nslot: e.copy(
                out=dstg, in_=psum[:, 0:nslot, 0:128]),
                waits=ev_max(e_tps, w_evac), inc="a")
            evac_events.append(last)
        else:
            evs = []
            for s_i in range(nslot):
                dst = dsts[cb + s_i]
                evs.append(sch.op(evac_eng, lambda e, dst=dst, s_i=s_i: e.copy(
                    out=dst, in_=psum[:, s_i, 0:128]),
                    waits=ev_max(e_tps, w_evac) if s_i == 0 else (), inc="a"))
            last = evs[-1]
            evac_events.extend(evs)
        w_tensor = ev_max(w_tensor, last)
    return last, evac_events



def emit_l0a(nc, sch, E, dr, pfx=""):
    sch.pfx = pfx
    x_d = dr["x"]
    wg_d = dr["w_gate"]
    wu_d = dr["w_up"]
    wd_d = dr["w_down"]
    ident_d = dr["ident"]
    y_d = dr["y_out"]
    xnfm_d = dr["xnfm_out"]

    NIC = 6
    IC_OFF = [0, 512, 1024, 1536, 2048, 2560]
    IC_SZ = [512] * 5 + [256]
    NRT_G = I // 128
    NRT_D = D // 128

    wgqT = E(nc.sbuf_tensor(pfx + "wgqT", [128, NCH, I], BF16))
    wuqT = E(nc.sbuf_tensor(pfx + "wuqT", [128, NCH, I], BF16))
    wdqT = E(nc.sbuf_tensor(pfx + "wdqT", [128, NRT_G, D], BF16))

    q = {n: E(nc.sbuf_tensor(pfx + "q_" + n, [128, 1408], F32))
         for n in ("absx", "sgn", "a1", "c5")}
    q.update({n: E(nc.sbuf_tensor(pfx + "q_" + n, [128, 88], F32))
              for n in ("bm", "sc", "rs", "hs")})
    lnb = {n: E(nc.sbuf_tensor(pfx + "ln_" + n, [128, 1], F32))
           for n in ("s1", "ssq", "mean", "ex2", "msq", "var", "sd", "rstd")}
    xtile = E(nc.sbuf_tensor(pfx + "xtile", [128, 2, D], F32))
    xn_tm = E(nc.sbuf_tensor(pfx + "xn_tm", [128, D], F32))
    xn_stage = E(nc.sbuf_tensor(pfx + "xn_stage", [128, NCH, 128], BF16))
    xq_tm = E(nc.sbuf_tensor(pfx + "xq_tm", [128, D], BF16))
    xq_fm = E(nc.sbuf_tensor(pfx + "xq_fm", [128, NCH, 128], BF16))
    g_sb = E(nc.sbuf_tensor(pfx + "g_sb", [128, 512], F32))
    h_sb = E(nc.sbuf_tensor(pfx + "h_sb", [128, I], F32))
    hq_b = E(nc.sbuf_tensor(pfx + "hq_b", [128, I], BF16))
    hq_fm = E(nc.sbuf_tensor(pfx + "hq_fm", [128, NRT_G, 128], BF16))
    y_sb = E(nc.sbuf_tensor(pfx + "y_sb", [128, D], F32))
    lnb["sqs"] = y_sb
    wt_in = h_sb
    wq_b = hq_b
    idt = E(nc.sbuf_tensor(pfx + "idt", [128, 128], F32))
    idtb = E(nc.sbuf_tensor(pfx + "idtb", [128, 128], BF16))
    tp32 = E(nc.psum_tensor(pfx + "tp32", [128, 4, 128], F32))
    tpb16 = E(nc.psum_tensor(pfx + "tpb16", [128, 4, 256], BF16))
    g_ps = E(nc.psum_tensor(pfx + "g_ps", [128, 512], F32))
    u_ps = E(nc.psum_tensor(pfx + "u_ps", [128, 512], F32))
    d_ps = E(nc.psum_tensor(pfx + "d_ps", [128, 2, 512], F32))


    e_id = sch.dma("sync", idt[:], ident_d, "d_id")
    e_idb = sch.dma("gpsimd", idtb[:], ident_d, "d_idb")

    e_tpb_war = None
    wt_release = [None, None]
    slot = 0
    for (w_src, n_rt, row_c, dstT) in ((wg_d, NRT_G, D, wgqT),
                                       (wu_d, NRT_G, D, wuqT),
                                       (wd_d, NRT_D, I, wdqT)):
        n_chT = row_c // 128
        for r in range(n_rt):
            par = 0
            slot += 1
            e_q = None
            for c0 in range(0, row_c, 1024):
                c1 = min(row_c, c0 + 1024)
                e_ld = sch.dma("sync", wt_in[:, c0:c1],
                               w_src[r * 128:(r + 1) * 128, c0:c1],
                               "d_w%d" % par,
                               waits=ev_max(wt_release[par], e_q))
                e_q = emit_quant(sch, q, wt_in[:, c0:c1],
                                 wq_b[:, c0:c1], c1 - c0, e_ld, e_ld)
            wt_release[par] = e_q
            srcs = [wq_b[:, c * 128:(c + 1) * 128] for c in range(n_chT)]
            dsts = [dstT[:, c, r * 128:(r + 1) * 128] for c in range(n_chT)]
            last, _ = emit_transpose_evac(
                sch, tpb16, idtb[:], srcs, dsts,
                w_tensor=ev_max(e_q, e_idb, e_tpb_war), w_evac=())
            e_tpb_war = last
            q["_last"] = ev_max(q["_last"], ("t", sch.cnt["t"]))
    e_wts = ("a", sch.cnt["a"])

    xtile_release = [None, None]
    e_y_store = None
    e_xnstage_free = None
    e_tp32_war = None
    e_g0_free = None
    e_g1_free = None
    e_d_free = None
    e_hqfm_free = None
    e_xqfm_free = None
    e_hq_prev = None
    tiles = [(s_, t) for s_ in range(SEQ_PER_CORE) for t in range(NT)]
    e_load = {}

    def sched_load(idx):
        s_, t = tiles[idx]
        par = idx % 2
        e_load[idx] = sch.dma("sync", xtile[:, par],
                              x_d[s_, t * TS:(t + 1) * TS, :],
                              "d_x%d" % par,
                              waits=ev_max(xtile_release[par]))

    sched_load(0)
    for idx, (s_, t) in enumerate(tiles):
        par = idx % 2
        t0 = t * TS
        if idx + 1 < len(tiles):
            sched_load(idx + 1)
        e_x = e_load[idx]
        e_xn = emit_ln(sch, lnb, xtile[:, par], xn_tm[:], ev_max(e_x))
        srcs = [xn_tm[:, c * 128:(c + 1) * 128] for c in range(NCH)]
        dsts = [xn_stage[:, c, :] for c in range(NCH)]
        e_stg, _ = emit_transpose_evac(
            sch, tp32, idt[:], srcs, dsts,
            w_tensor=ev_max(e_xn, e_id, e_tp32_war),
            w_evac=ev_max(e_xnstage_free))
        e_tp32_war = e_stg
        e_xnstage_free = sch.dma(
            "sync", xnfm_d[s_].rearrange("c p t -> p c t")[:, :, t0:t0 + TS],
            xn_stage[:], "d_stg", waits=(e_stg,))
        e_xq = emit_quant(sch, q, xn_tm[:], xq_tm[:], D, e_xn, e_xn)
        srcs = [xq_tm[:, c * 128:(c + 1) * 128] for c in range(NCH)]
        dsts = [xq_fm[:, c, :] for c in range(NCH)]
        e_xqfm, _ = emit_transpose_evac(
            sch, tpb16, idtb[:], srcs, dsts,
            w_tensor=ev_max(e_xq, e_idb, e_tpb_war, e_wts, e_xqfm_free),
            w_evac=())
        e_tpb_war = e_xqfm
        e_up = None
        e_h = None
        for ic in range(NIC):
            io0, isz = IC_OFF[ic], IC_SZ[ic]
            for (ps, wT, e_free) in ((g_ps, wgqT, e_g0_free),
                                     (u_ps, wuqT, e_g1_free)):
                for c in range(NCH):
                    e_mm = sch.op(
                        "tensor", lambda e, c=c, ps=ps, wT=wT, io0=io0, isz=isz:
                        e.matmul(ps[:, 0:isz], xq_fm[:, c, :],
                                 wT[:, c, io0:io0 + isz],
                                 start=(c == 0), stop=(c == NCH - 1)),
                        waits=ev_max(e_xqfm, e_wts, e_free) if c == 0 else (),
                        inc="t" if c == NCH - 1 else None)
                if ps is g_ps:
                    e_gate = e_mm
                else:
                    e_up = e_mm
            e_silu = sch.op("scalar", lambda e, isz=isz: e.activation(
                out=g_sb[:, 0:isz], in_=g_ps[:, 0:isz], func=AF.Silu),
                waits=ev_max(e_gate, e_g0_free), inc="a")
            e_g0_free = e_silu
            e_h = sch.op("vector", lambda e, isz=isz, io0=io0: e.tensor_tensor(
                out=h_sb[:, io0:io0 + isz], in0=g_sb[:, 0:isz],
                in1=u_ps[:, 0:isz], op=ALU.mult),
                waits=ev_max(e_silu, e_up, e_hq_prev), inc="v")
            e_g1_free = e_h
        e_hq = None
        for half in range(2):
            e_hq = emit_quant(sch, q, h_sb[:, half * 1408:(half + 1) * 1408],
                              hq_b[:, half * 1408:(half + 1) * 1408], 1408,
                              e_h, e_h)
        e_hq_prev = e_hq
        srcs = [hq_b[:, c * 128:(c + 1) * 128] for c in range(NRT_G)]
        e_hqf, _ = emit_transpose_evac(
            sch, tpb16, idtb[:], srcs, None,
            w_tensor=ev_max(e_hq, e_tpb_war, e_hqfm_free), w_evac=(),
            dst_groups=lambda cb, ns: hq_fm[:, cb:cb + ns, :])
        e_tpb_war = e_hqf
        e_hq_done = [e_hqf]
        e_dn0 = None
        for half in range(2):
            for c in range(NRT_G):
                e_dn = sch.op(
                    "tensor", lambda e, c=c, half=half: e.matmul(
                        d_ps[:, half], hq_fm[:, c, :],
                        wdqT[:, c, half * 512:half * 512 + 512],
                        start=(c == 0), stop=(c == NRT_G - 1)),
                    waits=ev_max(*(e_hq_done + [e_d_free])) if c == 0 else (),
                    inc="t" if c == NRT_G - 1 else None)
            if half == 0:
                e_dn0 = e_dn
        e_hqfm_free = e_dn
        sch.op("vector", lambda e, par=par: e.tensor_tensor(
            out=y_sb[:, 0:512], in0=xtile[:, par, 0:512], in1=d_ps[:, 0],
            op=ALU.add), waits=ev_max(e_dn0, e_x, e_y_store), inc="v")
        e_yv = sch.op("vector", lambda e, par=par: e.tensor_tensor(
            out=y_sb[:, 512:1024], in0=xtile[:, par, 512:1024],
            in1=d_ps[:, 1], op=ALU.add), waits=ev_max(e_dn), inc="v")
        e_d_free = e_yv
        xtile_release[par] = e_yv
        e_y_store = sch.dma("sync", y_d[s_, t0:t0 + TS, :], y_sb[:],
                            "d_y", waits=(e_yv,))
        lnb["_last"] = ev_max(lnb.get("_last"), e_y_store)
    sch.op("sync", None, waits=ev_max(sch.ev("d_y"),
                                      sch.ev("d_stg")))




def build_l0a(nc):
    x_d = nc.dram_tensor("x", [SEQ_PER_CORE, S, D], F32, kind="ExternalInput").ap()
    wg_d = nc.dram_tensor("w_gate", [I, D], F32, kind="ExternalInput").ap()
    wu_d = nc.dram_tensor("w_up", [I, D], F32, kind="ExternalInput").ap()
    wd_d = nc.dram_tensor("w_down", [D, I], F32, kind="ExternalInput").ap()
    ident_d = nc.dram_tensor("ident", [128, 128], F32, kind="ExternalInput").ap()
    y_d = nc.dram_tensor("y_out", [SEQ_PER_CORE, S, D], F32, kind="ExternalOutput").ap()
    xnfm_d = nc.dram_tensor("xnfm_out", [SEQ_PER_CORE, NCH, 128, S], BF16,
                            kind="ExternalOutput").ap()
    dr = {"x": x_d, "w_gate": wg_d, "w_up": wu_d, "w_down": wd_d,
          "ident": ident_d, "y_out": y_d, "xnfm_out": xnfm_d}
    with contextlib.ExitStack() as ctx:
        sch = Sched(nc, ctx)
        block = ctx.enter_context(nc.Block())
        with contextlib.ExitStack() as pctx:
            emit_l0a(nc, sch, pctx.enter_context, dr)
            sch.build(block)
    return nc


def emit_l0b(nc, sch, E, dr, pfx=""):
    sch.pfx = pfx
    y_d = dr["y_in"]
    qkvwT_d = dr["qkv_wT"]
    qkvb_d = dr["qkv_b_pc"]
    projwT_d = dr["proj_wT"]
    projbr_d = dr["proj_b_row"]
    NH = 16
    NP = 8
    xnfm_d = dr["xnfm"]
    x1_d = dr["x1_out"]
    nrm_scr = nc.dram_tensor("nrm_scr", [SEQ_PER_CORE, NP, 8 * 512], F32,
                             kind="Internal").ap()
    onf_scr = nc.dram_tensor("onf_scr", [SEQ_PER_CORE, NP, 2, 64, S], BF16,
                             kind="Internal").ap()

    qkv_sb = E(nc.sbuf_tensor(pfx + "qkv_sb", [128, NCH, 3 * D], BF16))
    projw_sb = E(nc.sbuf_tensor(pfx + "projw_sb", [128, NCH, D], BF16))
    qkvb_sb = E(nc.sbuf_tensor(pfx + "qkvb_sb", [128, 24], F32))

    bvb = E(nc.sbuf_tensor(pfx + "bvb", [128, 8], BF16))
    ones_r = E(nc.sbuf_tensor(pfx + "ones_r", [1, 128], BF16))
    ones_f = E(nc.sbuf_tensor(pfx + "ones_f", [1, 64], F32))
    effrow_b = E(nc.sbuf_tensor(pfx + "effrow_b", [1, D], BF16))
    xnfm_sb = E(nc.sbuf_tensor(pfx + "xnfm_sb", [128, NCH, S], BF16))
    qf2 = E(nc.sbuf_tensor(pfx + "qf", [128, 2, S], BF16))
    kf2 = E(nc.sbuf_tensor(pfx + "kf", [128, 2, S], BF16))
    v_aug = E(nc.sbuf_tensor(pfx + "v_aug", [128, NT, NH * 65], BF16))
    p_sb = E(nc.sbuf_tensor(pfx + "p_sb", [128, 2, 2, 512], BF16))
    o_raw = E(nc.sbuf_tensor(pfx + "o_raw", [64, 2, 4, 512], BF16))
    nrm_sb = E(nc.sbuf_tensor(pfx + "nrm_sb", [128, 8 * 512], F32))
    rin_sb = E(nc.sbuf_tensor(pfx + "rin_sb", [128, 32], F32))
    o_nfm = E(nc.sbuf_tensor(pfx + "o_nfm", [128, NCH, S], BF16))
    projbr_b = o_nfm[0:1, 0, 0:D]
    y_tile = nrm_sb[:, 0:D]
    sT2 = E(nc.psum_tensor(pfx + "sT2", [128, 2, 512], F32))
    o_psA = E(nc.psum_tensor(pfx + "o_psA", [65, 512], F32))
    o_psB = E(nc.psum_tensor(pfx + "o_psB", [65, 512], F32))
    qk_ps = E(nc.psum_tensor(pfx + "qk_ps", [128, 512], F32))
    nbc_ps = E(nc.psum_tensor(pfx + "nbc_ps", [64, 512], F32))
    proj_ps = E(nc.psum_tensor(pfx + "proj_ps", [128, 2, 512], F32))


    e_qkvw = sch.dma("gpsimd", qkv_sb[:],
                     qkvwT_d.rearrange("(c p) o -> p c o", p=128), "d_qkvw")
    e_projw = sch.dma("gpsimd", projw_sb[:],
                      projwT_d.rearrange("(c p) o -> p c o", p=128), "d_projw")
    e_projb = sch.dma("gpsimd", projbr_b, projbr_d, "d_projb")
    e_qkvb = sch.dma("sync", qkvb_sb[:], qkvb_d, "d_qkvb")
    e_ones = sch.op("vector", lambda e: e.memset(ones_r[:], 1.0))
    e_ones_f = sch.op("vector", lambda e: e.memset(ones_f[:], 1.0))
    e_bvb = sch.op("vector", lambda e: e.tensor_copy(
        bvb[:], qkvb_sb[:, 16:24]), waits=(e_qkvb,))
    va_ones = v_aug[:].rearrange("p t (h c) -> p t h c", c=65)[:, :, :, 64:65]
    e_vones = sch.op("vector", lambda e: e.memset(va_ones, 1.0))

    e_eff = None
    for half in range(2):
        for c in range(NCH):
            sch.op("tensor", lambda e, c=c, half=half: e.matmul(
                proj_ps[0:1, half], bvb[:, c:c + 1],
                projw_sb[:, c, half * 512:half * 512 + 512],
                start=(c == 0), stop=False),
                waits=ev_max(e_bvb, e_projw) if (c == 0 and half == 0) else ())
        e_eff = sch.op("tensor", lambda e, half=half: e.matmul(
            proj_ps[0:1, half], ones_r[0:1, 0:1],
            projbr_b[0:1, half * 512:half * 512 + 512],
            start=False, stop=True),
            waits=ev_max(e_ones, e_projb) if half == 0 else (), inc="t")
    e_effrow = sch.op("scalar", lambda e: e.copy(
        out=effrow_b[:], in_=proj_ps[0:1, :].rearrange("p h j -> p (h j)")),
        waits=(e_eff,))

    e_xn_users = None
    e_onfm_users = None
    e_vaug_users = None
    e_x1_store = None
    e_onh_store = None
    e_nbc_free = None
    e_nbc_sb_free = {}
    for s_ in range(SEQ_PER_CORE):
        e_xn = sch.dma("sync", xnfm_sb[:],
                       xnfm_d[s_].rearrange("c p t -> p c t"), "d_xn",
                       waits=ev_max(e_xn_users))
        e_vevac = None
        e_vps_free = None
        e_vmm = None
        for tt in range(NT):
            for half in range(2):
                for c in range(NCH):
                    e_vmm = sch.op(
                        "tensor", lambda e, tt=tt, half=half, c=c: e.matmul(
                            qk_ps[:, 0:512],
                            xnfm_sb[:, c, tt * TS:(tt + 1) * TS],
                            qkv_sb[:, c, 2 * D + half * 512:2 * D + (half + 1) * 512],
                            start=(c == 0), stop=(c == NCH - 1)),
                        waits=ev_max(e_xn, e_qkvw, e_vps_free, e_vones)
                        if c == 0 else (),
                        inc="t" if c == NCH - 1 else None)
                vslice = v_aug[:].rearrange(
                    "p t (h c) -> p t h c", c=65)[:, tt, half * 8:(half + 1) * 8, 0:64]
                e_vevac = sch.op(
                    "scalar", lambda e, vslice=vslice: e.copy(
                        out=vslice,
                        in_=qk_ps[:, 0:512].rearrange("p (h c) -> p h c", c=64)),
                    waits=ev_max(e_vmm, e_vaug_users))
                e_vps_free = e_vevac
        e_vdone = e_vevac

        e_qkps_free = e_vps_free
        e_qf_user = {0: None, 1: None}
        e_kf_user = {0: None, 1: None}
        e_praw_free = None
        e_sT_free = {0: None, 1: None}
        e_p_free = {}
        e_onh_user = None
        e_nrm_user = None
        e_mm = None
        for hp in range(NP):
            hpar = hp % 2
            qf = qf2[:, hpar]
            kf = kf2[:, hpar]
            o0 = hp * 128
            e_qev = e_kev = None
            for which, (dst, bias_col, col0, e_user) in enumerate((
                    (qf, hp, o0, e_qf_user[hpar]),
                    (kf, 8 + hp, D + o0, e_kf_user[hpar]))):
                for qc in range(4):
                    for c in range(NCH):
                        e_mm = sch.op(
                            "tensor",
                            lambda e, c=c, qc=qc, col0=col0: e.matmul(
                                qk_ps[:, 0:512],
                                qkv_sb[:, c, col0:col0 + 128],
                                xnfm_sb[:, c, qc * 512:(qc + 1) * 512],
                                start=(c == 0), stop=(c == NCH - 1)),
                            waits=ev_max(e_xn, e_qkvw, e_qkps_free, e_user)
                            if c == 0 else (),
                            inc="t" if c == NCH - 1 else None)
                    e_ev = sch.op(
                        "scalar",
                        lambda e, dst=dst, qc=qc, bias_col=bias_col: e.activation(
                            out=dst[:, qc * 512:(qc + 1) * 512],
                            in_=qk_ps[:, 0:512], func=AF.Identity,
                            bias=qkvb_sb[:, bias_col:bias_col + 1]),
                        waits=ev_max(e_mm, e_qkvb))
                    e_qkps_free = e_ev
                if which == 0:
                    e_qev = e_ev
                else:
                    e_kev = e_ev
            e_att = {}
            for qc in range(4):
                for kt in range(NT):
                    par = kt % 2
                    for hi, (sT, o_ps) in enumerate(((sT2[:, 0], o_psA),
                                                     (sT2[:, 1], o_psB))):
                        b0 = hi * 64
                        e_sc = sch.op(
                            "tensor",
                            lambda e, sT=sT, b0=b0, kt=kt, qc=qc, qf=qf, kf=kf:
                            e.matmul(
                                sT[:, 0:512],
                                kf[b0:b0 + 64, kt * TS:(kt + 1) * TS],
                                qf[b0:b0 + 64, qc * 512:(qc + 1) * 512],
                                start=True, stop=True,
                                tile_position=(b0, 0)),
                            waits=ev_max(e_qev, e_kev, e_sT_free[hi]),
                            inc="t")
                        e_exp = sch.op(
                            "scalar",
                            lambda e, sT=sT, par=par, hi=hi: e.activation(
                                out=p_sb[:, par, hi, :], in_=sT[:, 0:512],
                                func=AF.Exp, scale=0.125),
                            waits=ev_max(e_sc, e_p_free.get((par, hi))))
                        e_sT_free[hi] = e_exp
                        h = 2 * hp + hi
                        e_att[(qc, hi)] = sch.op(
                            "tensor",
                            lambda e, o_ps=o_ps, kt=kt, par=par, hi=hi, h=h:
                            e.matmul(
                                o_ps[:, 0:512],
                                v_aug[:].rearrange(
                                    "p t (hh c) -> p t hh c", c=65)[:, kt, h, :],
                                p_sb[:, par, hi, :],
                                start=(kt == 0), stop=(kt == NT - 1)),
                            waits=ev_max(e_exp, e_vdone,
                                         (e_praw_free if kt == 0 else None)),
                            inc="t")
                        e_p_free[(par, hi)] = e_att[(qc, hi)]
                evs = []
                for hi, o_ps in enumerate((o_psA, o_psB)):
                    evs.append(sch.op(
                        "scalar", lambda e, o_ps=o_ps, hi=hi, qc=qc: e.copy(
                            out=o_raw[:, hi, qc, :], in_=o_ps[0:64, 0:512]),
                        waits=ev_max(e_att[(qc, hi)], e_onh_user)))
                    evs.append(sch.op(
                        "scalar", lambda e, o_ps=o_ps, hi=hi, qc=qc: e.copy(
                            out=nrm_sb[64:65, (qc * 2 + hi) * 512:
                                       (qc * 2 + hi + 1) * 512],
                            in_=o_ps[64:65, 0:512]),
                        waits=ev_max(e_nrm_user)))
                e_praw_free = ev_max(*evs)
            e_qf_user[hpar] = e_kf_user[hpar] = ("t", sch.cnt["t"])
            e_h1 = sch.dma("sync", nrm_scr[s_, hp], nrm_sb[64:65, :], "d_nbo",
                           waits=ev_max(e_praw_free))
            e_nrm_user = e_h1
            e_h2 = sch.dma("sync", rin_sb[:],
                           nrm_scr[s_, hp].rearrange("(r j) -> r j", r=128),
                           "d_nbi", waits=(e_h1,))
            e_rcp = sch.op("vector", lambda e: e.reciprocal(
                out=rin_sb[:], in_=rin_sb[:]), waits=(e_h2,))
            e_h3 = sch.dma("sync", nrm_scr[s_, hp].rearrange("(r j) -> r j", r=128),
                           rin_sb[:], "d_nbo", waits=(e_rcp,))
            e_h4 = sch.dma("sync", nrm_sb[0:1, :],
                           nrm_scr[s_, hp].rearrange("j -> () j"),
                           "d_nbi", waits=(e_h3,))
            e_div_last = None
            for qc in range(4):
                for hi in range(2):
                    slot = qc * 2 + hi
                    e_nbc = sch.op(
                        "tensor", lambda e, slot=slot: e.matmul(
                            nbc_ps[:, 0:512], ones_f[0:1, :],
                            nrm_sb[0:1, slot * 512:(slot + 1) * 512],
                            start=True, stop=True),
                        waits=ev_max(e_h4, e_nbc_free, e_ones_f), inc="t")
                    e_div = sch.op(
                        "vector", lambda e, hi=hi, qc=qc: e.tensor_tensor(
                            out=o_raw[:, hi, qc, :], in0=o_raw[:, hi, qc, :],
                            in1=nbc_ps[0:64, 0:512], op=ALU.mult),
                        waits=ev_max(e_nbc, e_onh_store))
                    e_nbc_free = e_div
                    e_div_last = e_div
            e_onh_store = sch.dma(
                "sync",
                onf_scr[s_, hp].rearrange("h d (qc j) -> d h qc j", j=512),
                o_raw[:], "d_onh", waits=ev_max(e_div_last))
            e_onh_user = e_onh_store
        e_onfm = sch.dma(
            "sync", o_nfm[:],
            onf_scr[s_].rearrange("hp h d t -> (h d) hp t"), "d_onfm",
            waits=ev_max(e_onh_store, e_onfm_users, e_effrow))
        e_projps_free = {0: None, 1: None}
        e_x1v = None
        e_pj = None
        for tt in range(NT):
            e_yld = sch.dma("sync", y_tile, y_d[s_, tt * TS:(tt + 1) * TS, :],
                            "d_yl", waits=ev_max(e_x1_store, e_onfm))
            for half in range(2):
                for c in range(NCH):
                    sch.op("tensor", lambda e, c=c, half=half, tt=tt: e.matmul(
                        proj_ps[:, half], o_nfm[:, c, tt * TS:(tt + 1) * TS],
                        projw_sb[:, c, half * 512:(half + 1) * 512],
                        start=(c == 0), stop=False),
                        waits=ev_max(e_onfm, e_effrow, e_ones,
                                     e_projps_free[half]) if c == 0 else ())
                e_pj = sch.op("tensor", lambda e, half=half: e.matmul(
                    proj_ps[:, half], ones_r[0:1, :],
                    effrow_b[0:1, half * 512:(half + 1) * 512],
                    start=False, stop=True), inc="t")
                e_add = sch.op(
                    "vector", lambda e, half=half: e.tensor_tensor(
                        out=y_tile[:, half * 512:(half + 1) * 512],
                        in0=y_tile[:, half * 512:(half + 1) * 512],
                        in1=proj_ps[:, half], op=ALU.add),
                    waits=ev_max(e_pj, e_yld))
                e_projps_free[half] = e_add
                e_x1v = e_add
            e_x1_store = sch.dma("sync", x1_d[s_, tt * TS:(tt + 1) * TS, :],
                                 y_tile, "d_x1", waits=(e_x1v,))
        e_xn_users = ev_max(e_vmm, e_mm)
        e_onfm_users = e_pj
        e_vaug_users = ("t", sch.cnt["t"])
    sch.op("sync", None, waits=ev_max(sch.ev("d_x1")))




def build_l0b(nc):
    xnfm_d = nc.dram_tensor("xnfm", [SEQ_PER_CORE, NCH, 128, S], BF16,
                            kind="ExternalInput").ap()
    y_d = nc.dram_tensor("y_in", [SEQ_PER_CORE, S, D], F32, kind="ExternalInput").ap()
    qkvwT_d = nc.dram_tensor("qkv_wT", [D, 3 * D], F32, kind="ExternalInput").ap()
    qkvb_d = nc.dram_tensor("qkv_b_pc", [128, 24], F32, kind="ExternalInput").ap()
    projwT_d = nc.dram_tensor("proj_wT", [D, D], F32, kind="ExternalInput").ap()
    projbr_d = nc.dram_tensor("proj_b_row", [1, D], F32, kind="ExternalInput").ap()
    x1_d = nc.dram_tensor("x1_out", [SEQ_PER_CORE, S, D], F32,
                          kind="ExternalOutput").ap()
    dr = {"xnfm": xnfm_d, "y_in": y_d, "qkv_wT": qkvwT_d, "qkv_b_pc": qkvb_d,
          "proj_wT": projwT_d, "proj_b_row": projbr_d, "x1_out": x1_d}
    with contextlib.ExitStack() as ctx:
        sch = Sched(nc, ctx)
        block = ctx.enter_context(nc.Block())
        with contextlib.ExitStack() as pctx:
            emit_l0b(nc, sch, pctx.enter_context, dr)
            sch.build(block)
    return nc


def emit_l1a(nc, sch, E, dr, pfx=""):
    sch.pfx = pfx
    x1_d = dr["x1"]
    wka_d = dr["Wk_a"]
    wva_d = dr["Wv_a"]
    wktf_d = dr["WkT_f"]
    bkr_d = dr["bk_row"]
    mur_d = dr["mu_r_pc"]
    muk_d = dr["mu_k_pc"]
    ident_d = dr["ident"]
    NH = 16
    xn1fm_d = dr["xn1fm_out"]
    xrfm_d = dr["xrfm_out"]
    kafm_d = dr["kafm_out"]
    kv_d = dr["kv_out"]

    wka_sb = E(nc.sbuf_tensor(pfx + "wka_sb", [128, NCH, D], BF16))
    wva_sb = E(nc.sbuf_tensor(pfx + "wva_sb", [128, NCH, D], BF16))
    wktf_sb = E(nc.sbuf_tensor(pfx + "wktf_sb", [128, NCH, D], BF16))
    bkr_b = E(nc.sbuf_tensor(pfx + "bkr_b", [1, D], BF16))
    mur_sb = E(nc.sbuf_tensor(pfx + "mur_sb", [128, NCH], F32))
    muk_sb = E(nc.sbuf_tensor(pfx + "muk_sb", [128, NCH], F32))
    ones_r = E(nc.sbuf_tensor(pfx + "ones_r", [1, 128], BF16))
    idt = E(nc.sbuf_tensor(pfx + "idt", [128, 128], F32))
    lnb = {n: E(nc.sbuf_tensor(pfx + "ln_" + n, [128, 1], F32))
           for n in ("s1", "ssq", "mean", "ex2", "msq", "var", "sd", "rstd")}
    lnb["sqs"] = E(nc.sbuf_tensor(pfx + "ln_sqs", [128, D], F32))
    x1t = E(nc.sbuf_tensor(pfx + "x1t", [128, 2, D], F32))
    xn1_tm = E(nc.sbuf_tensor(pfx + "xn1_tm", [128, D], F32))
    win = E(nc.sbuf_tensor(pfx + "win", [128, NCH, 129], F32))
    stg = E(nc.sbuf_tensor(pfx + "stg", [128, NCH, 128], BF16))
    xr_stg = E(nc.sbuf_tensor(pfx + "xr_stg", [128, NCH, 128], BF16))
    dmix = E(nc.sbuf_tensor(pfx + "dmix", [128, NCH, 128], F32))
    tmix = E(nc.sbuf_tensor(pfx + "tmix", [128, NCH, 128], F32))
    xk_grp = E(nc.sbuf_tensor(pfx + "xk_grp", [128, NCH, 512], BF16))
    rl_sb = E(nc.sbuf_tensor(pfx + "rl_sb", [128, 512], F32))
    ka_stg = E(nc.sbuf_tensor(pfx + "ka_stg", [128, NCH, 512], BF16))
    kfeat = E(nc.sbuf_tensor(pfx + "kfeat", [128, NT, D], BF16))
    v_aug = E(nc.sbuf_tensor(pfx + "v_aug", [128, NT, NH * 65], BF16))
    e_sb = E(nc.sbuf_tensor(pfx + "e_sb", [128, D], F32))
    r_sb = E(nc.sbuf_tensor(pfx + "r_sb", [128, D], F32))
    kv_stg = E(nc.sbuf_tensor(pfx + "kv_stg", [64, NH, 65], BF16))
    tp32 = E(nc.psum_tensor(pfx + "tp32", [128, 4, 128], F32))
    k_ps = E(nc.psum_tensor(pfx + "k_ps", [128, 512], F32))
    v_ps = E(nc.psum_tensor(pfx + "v_ps", [128, 512], F32))
    kf_ps = E(nc.psum_tensor(pfx + "kf_ps", [128, 2, 512], F32))
    kv_ps = E(nc.psum_tensor(pfx + "kv_ps", [64, 2, 512], F32))


    e_wka = sch.dma("gpsimd", wka_sb[:],
                    wka_d.rearrange("(c p) o -> p c o", p=128), "d_wka")
    e_wva = sch.dma("gpsimd", wva_sb[:],
                    wva_d.rearrange("(c p) o -> p c o", p=128), "d_wva")
    e_wktf = sch.dma("gpsimd", wktf_sb[:],
                     wktf_d.rearrange("(c p) o -> p c o", p=128), "d_wktf")
    e_bkr = sch.dma("gpsimd", bkr_b[:], bkr_d, "d_bkr")
    e_mur = sch.dma("sync", mur_sb[:], mur_d, "d_mur")
    e_muk = sch.dma("sync", muk_sb[:], muk_d, "d_muk")
    e_id = sch.dma("sync", idt[:], ident_d, "d_id")
    e_ones = sch.op("vector", lambda e: e.memset(ones_r[:], 1.0))
    va_ones = v_aug[:].rearrange("p t (h c) -> p t h c", c=65)[:, :, :, 64:65]
    e_vones = sch.op("vector", lambda e: e.memset(va_ones, 1.0))

    x1_release = [None, None]
    e_stg_user = None
    e_xr_user = None
    e_ka_user = None
    e_tp32_war = None
    e_kps_free = None
    e_vps_free = None
    e_kf_free = None
    e_dmix_user = None
    e_win_user = None
    e_kfeat_users = None
    e_vaug_users = None
    e_kvstg_free = None
    e_er_user = None
    e_load = {}
    tiles = [(s_, t) for s_ in range(SEQ_PER_CORE) for t in range(NT)]

    def sched_load(idx):
        s_, t = tiles[idx]
        par = idx % 2
        e_load[idx] = sch.dma("sync", x1t[:, par],
                              x1_d[s_, t * TS:(t + 1) * TS, :],
                              "d_x%d" % par, waits=ev_max(x1_release[par]))

    sched_load(0)
    for idx, (s_, t) in enumerate(tiles):
        par = idx % 2
        t0 = t * TS
        g = t % 4
        if idx + 1 < len(tiles):
            sched_load(idx + 1)
        e_x = e_load[idx]
        if t == 0:
            e_z = sch.op("vector", lambda e: e.memset(win[:, :, 0:1], 0.0),
                         waits=ev_max(e_win_user))
        else:
            e_z = sch.op("vector", lambda e: e.tensor_copy(
                win[:, :, 0:1], win[:, :, 128:129]), waits=ev_max(e_win_user))
        e_xn = emit_ln(sch, lnb, x1t[:, par], xn1_tm[:], e_x)
        x1_release[par] = e_xn
        evs = []
        for cb in (0, 4):
            e_tps = None
            for s_i in range(4):
                c = cb + s_i
                e_tps = sch.op("tensor", lambda e, c=c, s_i=s_i: e.transpose(
                    tp32[:, s_i, 0:128], xn1_tm[:, c * 128:(c + 1) * 128], idt[:]),
                    waits=ev_max(e_xn, e_id, e_tp32_war) if s_i == 0 else (),
                    inc="t" if s_i == 3 else None)
            for s_i in range(4):
                c = cb + s_i
                sch.op("scalar", lambda e, c=c, s_i=s_i: e.copy(
                    out=win[:, c, 1:129], in_=tp32[:, s_i, 0:128]),
                    waits=ev_max(e_tps, e_z, e_dmix_user) if s_i == 0 else ())
                evs.append(sch.op("scalar", lambda e, c=c, s_i=s_i: e.copy(
                    out=stg[:, c, :], in_=tp32[:, s_i, 0:128]),
                    waits=ev_max(e_stg_user)))
            e_tp32_war = evs[-1]
        e_evac = evs[-1]
        e_win_user = None
        e_stg_user = sch.dma(
            "sync", xn1fm_d[s_].rearrange("c p t -> p c t")[:, :, t0:t0 + TS],
            stg[:], "d_stg", waits=(e_evac,))
        e_d = sch.op("vector", lambda e: e.tensor_tensor(
            out=dmix[:], in0=win[:, :, 1:129], in1=win[:, :, 0:128],
            op=ALU.subtract), waits=ev_max(e_evac))
        mu_r_bc = mur_sb[:].rearrange("p (c o) -> p c o", o=1)
        mu_k_bc = muk_sb[:].rearrange("p (c o) -> p c o", o=1)
        dv = dmix[:]
        dvb, murb = broadcast_tensor_aps(dv, mu_r_bc)
        _, mukb = broadcast_tensor_aps(dv, mu_k_bc)
        e_t1 = sch.op("vector", lambda e: e.tensor_tensor(
            out=tmix[:], in0=dvb, in1=murb, op=ALU.mult),
            waits=ev_max(e_d, e_mur))
        e_xr = sch.op("vector", lambda e: e.tensor_tensor(
            out=xr_stg[:], in0=tmix[:], in1=win[:, :, 0:128], op=ALU.add),
            waits=ev_max(e_xr_user))
        e_xr_user = sch.dma(
            "sync", xrfm_d[s_].rearrange("c p t -> p c t")[:, :, t0:t0 + TS],
            xr_stg[:], "d_xr", waits=(e_xr,))
        e_t2 = sch.op("vector", lambda e: e.tensor_tensor(
            out=tmix[:], in0=dvb, in1=mukb, op=ALU.mult), waits=ev_max(e_muk))
        e_xk = sch.op("vector", lambda e, g=g: e.tensor_tensor(
            out=xk_grp[:, :, g * 128:(g + 1) * 128], in0=tmix[:],
            in1=win[:, :, 0:128], op=ALU.add), waits=ev_max(e_kf_free))
        e_win_user = e_xk
        e_dmix_user = e_t2
        kf_evs = []
        for half in range(2):
            for c in range(NCH):
                e_kmm = sch.op("tensor", lambda e, c=c, half=half: e.matmul(
                    k_ps[:, 0:512], stg[:, c, :],
                    wka_sb[:, c, half * 512:(half + 1) * 512],
                    start=(c == 0), stop=False),
                    waits=ev_max(e_evac, e_wka, e_kps_free) if c == 0 else ())
            e_kmm = sch.op("tensor", lambda e, half=half: e.matmul(
                k_ps[:, 0:512], ones_r[0:1, :],
                bkr_b[0:1, half * 512:(half + 1) * 512],
                start=False, stop=True),
                waits=ev_max(e_ones, e_bkr), inc="t")
            e_e = sch.op("scalar", lambda e, half=half: e.activation(
                out=e_sb[:, half * 512:(half + 1) * 512], in_=k_ps[:, 0:512],
                func=AF.Exp), waits=ev_max(e_kmm, e_er_user))
            e_r = sch.op("scalar", lambda e, half=half: e.activation(
                out=r_sb[:, half * 512:(half + 1) * 512], in_=k_ps[:, 0:512],
                func=AF.Relu), waits=())
            e_kf = sch.op("vector", lambda e, half=half, t=t: e.scalar_tensor_tensor(
                out=kfeat[:, t, half * 512:(half + 1) * 512],
                in0=e_sb[:, half * 512:(half + 1) * 512], scalar=1.0,
                in1=r_sb[:, half * 512:(half + 1) * 512],
                op0=ALU.min, op1=ALU.add),
                waits=ev_max(e_r, e_kfeat_users))
            e_kps_free = e_r
            kf_evs.append(e_kf)
        e_kf_done = kf_evs[-1]
        e_er_user = e_kf_done
        v_evs = []
        for half in range(2):
            for c in range(NCH):
                e_vmm = sch.op("tensor", lambda e, c=c, half=half: e.matmul(
                    v_ps[:, 0:512], stg[:, c, :],
                    wva_sb[:, c, half * 512:(half + 1) * 512],
                    start=(c == 0), stop=(c == NCH - 1)),
                    waits=ev_max(e_evac, e_wva, e_vps_free, e_vones)
                    if c == 0 else (),
                    inc="t" if c == NCH - 1 else None)
            vslice = v_aug[:].rearrange(
                "p t (h c) -> p t h c", c=65)[:, t, half * 8:(half + 1) * 8, 0:64]
            e_vev = sch.op("scalar", lambda e, vslice=vslice: e.copy(
                out=vslice,
                in_=v_ps[:, 0:512].rearrange("p (h c) -> p h c", c=64)),
                waits=ev_max(e_vmm, e_vaug_users))
            e_vps_free = e_vev
            v_evs.append(e_vev)
        e_v_done = v_evs[-1]
        e_stg_user = ev_max(e_stg_user, ("t", sch.cnt["t"]))
        if g == 3:
            gi = t // 4
            ka_evs = []
            for dc in range(NCH):
                for cc in range(NCH):
                    e_km = sch.op("tensor", lambda e, dc=dc, cc=cc: e.matmul(
                        kf_ps[:, dc % 2], wktf_sb[:, cc, dc * 128:(dc + 1) * 128],
                        xk_grp[:, cc, :], start=(cc == 0), stop=(cc == NCH - 1)),
                        waits=ev_max(e_xk, e_wktf,
                                     ka_evs[-2] if len(ka_evs) >= 2 else None)
                        if cc == 0 else (),
                        inc="t" if cc == NCH - 1 else None)
                e_rl = sch.op("scalar", lambda e, dc=dc: e.activation(
                    out=rl_sb[:], in_=kf_ps[:, dc % 2], func=AF.Relu),
                    waits=ev_max(e_km))
                e_ka = sch.op("scalar", lambda e, dc=dc: e.activation(
                    out=ka_stg[:, dc, :], in_=rl_sb[:], func=AF.Square),
                    waits=ev_max(e_ka_user))
                ka_evs.append(e_ka)
            e_ka_user = sch.dma(
                "sync",
                kafm_d[s_].rearrange("c p t -> p c t")[:, :, gi * 512:(gi + 1) * 512],
                ka_stg[:], "d_ka", waits=(ka_evs[-1],))
            e_kf_free = ("a", sch.cnt["a"])
        if t == NT - 1:
            e_kv_evs = []
            e_kvps_free = {}
            for h in range(NH):
                pb = h % 2
                for tt in range(NT):
                    e_kvm = sch.op(
                        "tensor", lambda e, h=h, tt=tt, pb=pb: e.matmul(
                            kv_ps[:, pb, 0:65],
                            kfeat[:, tt, (h // 2) * 128 + (h % 2) * 64:
                                  (h // 2) * 128 + (h % 2) * 64 + 64],
                            v_aug[:].rearrange(
                                "p t (hh c) -> p t hh c", c=65)[:, tt, h, :],
                            start=(tt == 0), stop=(tt == NT - 1)),
                        waits=ev_max(e_kf_done, e_v_done, e_kvps_free.get(pb))
                        if tt == 0 else (),
                        inc="t" if tt == NT - 1 else None)
                e_kve = sch.op("scalar", lambda e, h=h, pb=pb: e.copy(
                    out=kv_stg[:, h, :], in_=kv_ps[0:64, pb, 0:65]),
                    waits=ev_max(e_kvm, e_kvstg_free))
                e_kvps_free[pb] = e_kve
                e_kv_evs.append(e_kve)
            e_kvstg_free = sch.dma(
                "sync", kv_d[s_].rearrange("h d c -> d h c"), kv_stg[:],
                "d_kv", waits=(e_kv_evs[-1],))
            e_kfeat_users = ("t", sch.cnt["t"])
            e_vaug_users = ("t", sch.cnt["t"])
    sch.op("sync", None, waits=ev_max(
        sch.ev("d_kv"), sch.ev("d_stg"),
        sch.ev("d_xr"), sch.ev("d_ka")))




def build_l1a(nc):
    x1_d = nc.dram_tensor("x1", [SEQ_PER_CORE, S, D], F32, kind="ExternalInput").ap()
    wka_d = nc.dram_tensor("Wk_a", [D, D], F32, kind="ExternalInput").ap()
    wva_d = nc.dram_tensor("Wv_a", [D, D], F32, kind="ExternalInput").ap()
    wktf_d = nc.dram_tensor("WkT_f", [D, D], F32, kind="ExternalInput").ap()
    bkr_d = nc.dram_tensor("bk_row", [1, D], F32, kind="ExternalInput").ap()
    mur_d = nc.dram_tensor("mu_r_pc", [128, NCH], F32, kind="ExternalInput").ap()
    muk_d = nc.dram_tensor("mu_k_pc", [128, NCH], F32, kind="ExternalInput").ap()
    ident_d = nc.dram_tensor("ident", [128, 128], F32, kind="ExternalInput").ap()
    xn1fm_d = nc.dram_tensor("xn1fm_out", [SEQ_PER_CORE, NCH, 128, S], BF16,
                             kind="ExternalOutput").ap()
    xrfm_d = nc.dram_tensor("xrfm_out", [SEQ_PER_CORE, NCH, 128, S], BF16,
                            kind="ExternalOutput").ap()
    kafm_d = nc.dram_tensor("kafm_out", [SEQ_PER_CORE, NCH, 128, S], BF16,
                            kind="ExternalOutput").ap()
    kv_d = nc.dram_tensor("kv_out", [SEQ_PER_CORE, 16, 64, 65], BF16,
                          kind="ExternalOutput").ap()
    dr = {"x1": x1_d, "Wk_a": wka_d, "Wv_a": wva_d, "WkT_f": wktf_d,
          "bk_row": bkr_d, "mu_r_pc": mur_d, "mu_k_pc": muk_d, "ident": ident_d,
          "xn1fm_out": xn1fm_d, "xrfm_out": xrfm_d, "kafm_out": kafm_d,
          "kv_out": kv_d}
    with contextlib.ExitStack() as ctx:
        sch = Sched(nc, ctx)
        block = ctx.enter_context(nc.Block())
        with contextlib.ExitStack() as pctx:
            emit_l1a(nc, sch, pctx.enter_context, dr)
            sch.build(block)
    return nc


def emit_l1b(nc, sch, E, dr, pfx=""):
    sch.pfx = pfx
    x1_d = dr["x1"]
    wqa_d = dr["Wq_a"]
    bq_d = dr["bq_pc"]
    wo_d = dr["Wo_a"]
    bor_d = dr["bo_row"]
    bv_d = dr["bv_pc"]
    wrt_d = dr["WrT_f"]
    wvt_d = dr["WvT_f"]
    NH = 16
    NP = 8
    xn1fm_d = dr["xn1fm"]
    xrfm_d = dr["xrfm"]
    kafm_d = dr["kafm"]
    kv_d = dr["kv"]
    out_d = dr["out_final"]
    nrm_scr = nc.dram_tensor("nrm2_scr", [SEQ_PER_CORE, 4, 2, 8 * 512], F32,
                             kind="Internal").ap()
    o1_scr = nc.dram_tensor("o1_scr", [SEQ_PER_CORE, 4, NP, 2, 64, 512], BF16,
                            kind="Internal").ap()

    wqa_sb = E(nc.sbuf_tensor(pfx + "wqa_sb", [128, NCH, D], BF16))
    wo_sb = E(nc.sbuf_tensor(pfx + "wo_sb", [128, NCH, D], BF16))
    wrt_sb = E(nc.sbuf_tensor(pfx + "wrt_sb", [128, NCH, D], BF16))
    wvt_sb = E(nc.sbuf_tensor(pfx + "wvt_sb", [128, NCH, D], BF16))
    bq_sb = E(nc.sbuf_tensor(pfx + "bq_sb", [128, NCH], F32))
    bv_sb = E(nc.sbuf_tensor(pfx + "bv_sb", [128, NCH], F32))
    bor_b = E(nc.sbuf_tensor(pfx + "bor_b", [1, D], BF16))
    bvb = E(nc.sbuf_tensor(pfx + "bvb", [128, NCH], BF16))
    ones_r = E(nc.sbuf_tensor(pfx + "ones_r", [1, 128], BF16))
    ones_f = E(nc.sbuf_tensor(pfx + "ones_f", [1, 64], F32))
    effrow_b = E(nc.sbuf_tensor(pfx + "effrow_b", [1, D], BF16))
    kv_sb = E(nc.sbuf_tensor(pfx + "kv_sb", [128, NP, 65], BF16))
    xn1_qc = E(nc.sbuf_tensor(pfx + "xn1_qc", [128, NCH, 512], BF16))
    qfeat = E(nc.sbuf_tensor(pfx + "qfeat", [128, NCH, 512], BF16))
    e_sb = E(nc.sbuf_tensor(pfx + "e_sb", [128, 512], F32))
    r_sb = E(nc.sbuf_tensor(pfx + "r_sb", [128, 512], F32))
    o_raw = E(nc.sbuf_tensor(pfx + "o_raw", [64, NH, 512], BF16))
    nrm_sb = E(nc.sbuf_tensor(pfx + "nrm_sb", [128, 8 * 512], F32))
    rin_sb = E(nc.sbuf_tensor(pfx + "rin_sb", [128, 32], F32))
    o1_nfm = E(nc.sbuf_tensor(pfx + "o1_nfm", [128, NCH, 512], BF16))
    x1t = E(nc.sbuf_tensor(pfx + "x1t", [128, D], F32))
    xr_t = E(nc.sbuf_tensor(pfx + "xr_t", [128, 2, NCH, 128], BF16))
    ka_t = E(nc.sbuf_tensor(pfx + "ka_t", [128, 2, NCH, 128], BF16))
    rsg_sb = E(nc.sbuf_tensor(pfx + "rsg_sb", [128, D], F32))
    fv_sb = E(nc.sbuf_tensor(pfx + "fv_sb", [128, D], F32))
    q_ps = E(nc.psum_tensor(pfx + "q_ps", [128, 512], F32))
    o1A = E(nc.psum_tensor(pfx + "o1A", [65, 512], F32))
    o1B = E(nc.psum_tensor(pfx + "o1B", [65, 512], F32))
    nbc_ps = E(nc.psum_tensor(pfx + "nbc_ps", [64, 512], F32))
    pr_ps = E(nc.psum_tensor(pfx + "pr_ps", [128, 2, 512], F32))
    rv_ps = E(nc.psum_tensor(pfx + "rv_ps", [128, 2, 512], F32))


    e_wqa = sch.dma("gpsimd", wqa_sb[:],
                    wqa_d.rearrange("(c p) o -> p c o", p=128), "d_wqa")
    e_wo = sch.dma("gpsimd", wo_sb[:],
                   wo_d.rearrange("(c p) o -> p c o", p=128), "d_wo")
    e_wrt = sch.dma("gpsimd", wrt_sb[:],
                    wrt_d.rearrange("(c p) o -> p c o", p=128), "d_wrt")
    e_wvt = sch.dma("gpsimd", wvt_sb[:],
                    wvt_d.rearrange("(c p) o -> p c o", p=128), "d_wvt")
    e_bor = sch.dma("gpsimd", bor_b[:], bor_d, "d_bor")
    e_bq = sch.dma("sync", bq_sb[:], bq_d, "d_bq")
    e_bv = sch.dma("sync", bv_sb[:], bv_d, "d_bv")
    e_ones = sch.op("vector", lambda e: e.memset(ones_r[:], 1.0))
    e_ones_f = sch.op("vector", lambda e: e.memset(ones_f[:], 1.0))
    e_bvb = sch.op("vector", lambda e: e.tensor_copy(bvb[:], bv_sb[:]),
                   waits=(e_bv,))

    e_eff = None
    for half in range(2):
        for c in range(NCH):
            sch.op("tensor", lambda e, c=c, half=half: e.matmul(
                pr_ps[0:1, half], bvb[:, c:c + 1],
                wo_sb[:, c, half * 512:half * 512 + 512],
                start=(c == 0), stop=False),
                waits=ev_max(e_bvb, e_wo) if (c == 0 and half == 0) else ())
        e_eff = sch.op("tensor", lambda e, half=half: e.matmul(
            pr_ps[0:1, half], ones_r[0:1, 0:1],
            bor_b[0:1, half * 512:half * 512 + 512],
            start=False, stop=True),
            waits=ev_max(e_ones, e_bor) if half == 0 else (), inc="t")
    e_effrow = sch.op("scalar", lambda e: e.copy(
        out=effrow_b[:], in_=pr_ps[0:1, :].rearrange("p h j -> p (h j)")),
        waits=(e_eff,))

    e_kv_user = None
    e_xn_user = None
    e_qf_user = None
    e_er_user = None
    e_oraw_evac_war = None
    e_nrm_user = None
    e_nbc_free = None
    e_o1_store = None
    e_onfm_user = None
    e_x1_user = None
    e_xrka_rel = [None, None]
    e_out_store = None
    e_qps_free = None
    e_prps_free = {0: None, 1: None}
    e_rvps_sig = {0: None, 1: None}
    e_rvps_ff = {0: None, 1: None}
    for s_ in range(SEQ_PER_CORE):
        e_kv = sch.dma("sync", kv_sb[:],
                       kv_d[s_].rearrange("(hp h2) d c -> (h2 d) hp c", h2=2),
                       "d_kv", waits=ev_max(e_kv_user))
        for qc in range(4):
            e_xn = sch.dma("sync", xn1_qc[:],
                           xn1fm_d[s_].rearrange(
                               "c p t -> p c t")[:, :, qc * 512:(qc + 1) * 512],
                           "d_xn", waits=ev_max(e_xn_user))
            for dc in range(NCH):
                for c in range(NCH):
                    e_qmm = sch.op("tensor", lambda e, dc=dc, c=c: e.matmul(
                        q_ps[:, 0:512], wqa_sb[:, c, dc * 128:(dc + 1) * 128],
                        xn1_qc[:, c, :], start=(c == 0), stop=(c == NCH - 1)),
                        waits=ev_max(e_xn, e_wqa, e_qps_free) if c == 0 else (),
                        inc="t" if c == NCH - 1 else None)
                e_e = sch.op("scalar", lambda e, dc=dc: e.activation(
                    out=e_sb[:], in_=q_ps[:, 0:512], func=AF.Exp,
                    bias=bq_sb[:, dc:dc + 1]),
                    waits=ev_max(e_qmm, e_bq, e_er_user))
                e_r = sch.op("scalar", lambda e, dc=dc: e.activation(
                    out=r_sb[:], in_=q_ps[:, 0:512], func=AF.Relu,
                    bias=bq_sb[:, dc:dc + 1]), waits=())
                e_qf = sch.op("vector", lambda e, dc=dc: e.scalar_tensor_tensor(
                    out=qfeat[:, dc, :], in0=e_sb[:], scalar=1.0,
                    in1=r_sb[:], op0=ALU.min, op1=ALU.add),
                    waits=ev_max(e_r, e_qf_user))
                e_er_user = e_qf
                e_qps_free = e_r
            e_qf_done = e_qf
            e_div_all = []
            for grp in range(2):
                evs = []
                for hpl in range(4):
                    hp = grp * 4 + hpl
                    for hi, o_ps in enumerate((o1A, o1B)):
                        b0 = hi * 64
                        e_oq = sch.op(
                            "tensor", lambda e, hp=hp, hi=hi, b0=b0, o_ps=o_ps:
                            e.matmul(
                                o_ps[:, 0:512], kv_sb[b0:b0 + 64, hp, :],
                                qfeat[b0:b0 + 64, hp, :],
                                start=True, stop=True, tile_position=(b0, 0)),
                            waits=ev_max(e_qf_done, e_kv, e_oraw_evac_war),
                            inc="t")
                        h = 2 * hp + hi
                        evs.append(sch.op(
                            "scalar", lambda e, o_ps=o_ps, h=h: e.copy(
                                out=o_raw[:, h, :], in_=o_ps[0:64, 0:512]),
                            waits=ev_max(e_oq, e_o1_store)))
                        slot = (hp % 4) * 2 + hi
                        evs.append(sch.op(
                            "scalar", lambda e, o_ps=o_ps, slot=slot: e.copy(
                                out=nrm_sb[64:65, slot * 512:(slot + 1) * 512],
                                in_=o_ps[64:65, 0:512]),
                            waits=ev_max(e_nrm_user)))
                        e_oraw_evac_war = evs[-1]
                e_h1 = sch.dma("sync", nrm_scr[s_, qc, grp], nrm_sb[64:65, :],
                               "d_nbo", waits=ev_max(*evs))
                e_nrm_user = e_h1
                e_h2 = sch.dma("sync", rin_sb[:],
                               nrm_scr[s_, qc, grp].rearrange("(r j) -> r j", r=128),
                               "d_nbi", waits=(e_h1,))
                e_cl = sch.op("vector", lambda e: e.tensor_scalar(
                    out=rin_sb[:], in0=rin_sb[:], scalar1=1e-6, scalar2=None,
                    op0=ALU.max), waits=(e_h2,))
                e_rcp = sch.op("vector", lambda e: e.reciprocal(
                    out=rin_sb[:], in_=rin_sb[:]), waits=(e_cl,))
                e_h3 = sch.dma("sync",
                               nrm_scr[s_, qc, grp].rearrange("(r j) -> r j", r=128),
                               rin_sb[:], "d_nbo", waits=(e_rcp,))
                e_h4 = sch.dma("sync", nrm_sb[0:1, :],
                               nrm_scr[s_, qc, grp].rearrange("j -> () j"),
                               "d_nbi", waits=(e_h3,))
                for hpl in range(4):
                    hp = grp * 4 + hpl
                    for hi in range(2):
                        slot = hpl * 2 + hi
                        h = 2 * hp + hi
                        e_nbc = sch.op(
                            "tensor", lambda e, slot=slot: e.matmul(
                                nbc_ps[:, 0:512], ones_f[0:1, :],
                                nrm_sb[0:1, slot * 512:(slot + 1) * 512],
                                start=True, stop=True),
                            waits=ev_max(e_h4, e_nbc_free, e_ones_f), inc="t")
                        e_div = sch.op(
                            "vector", lambda e, h=h: e.tensor_tensor(
                                out=o_raw[:, h, :], in0=o_raw[:, h, :],
                                in1=nbc_ps[0:64, 0:512], op=ALU.mult),
                            waits=ev_max(e_nbc, e_o1_store))
                        e_nbc_free = e_div
                        e_div_all.append(e_div)
            e_o1_store = sch.dma(
                "sync", o1_scr[s_, qc].rearrange("hp h d j -> d hp h j"),
                o_raw[:].rearrange("d (hp h) j -> d hp h j", h=2), "d_o1o",
                waits=ev_max(*e_div_all))
            e_onfm = sch.dma(
                "sync", o1_nfm[:],
                o1_scr[s_, qc].rearrange("hp h d j -> (h d) hp j"), "d_o1i",
                waits=ev_max(e_o1_store, e_onfm_user))
            e_pj = None
            for tl in range(4):
                tt = qc * 4 + tl
                par = tl % 2
                t0 = tt * TS
                e_x1 = sch.dma("sync", x1t[:], x1_d[s_, t0:t0 + TS, :], "d_x1",
                               waits=ev_max(e_x1_user))
                e_xr = sch.dma("sync", xr_t[:, par],
                               xrfm_d[s_].rearrange(
                                   "c p t -> p c t")[:, :, t0:t0 + TS],
                               "d_xrt%d" % par, waits=ev_max(e_xrka_rel[par]))
                e_ka = sch.dma("sync", ka_t[:, par],
                               kafm_d[s_].rearrange(
                                   "c p t -> p c t")[:, :, t0:t0 + TS],
                               "d_kat%d" % par, waits=ev_max(e_xrka_rel[par]))
                for half in range(2):
                    for c in range(NCH):
                        sch.op("tensor", lambda e, c=c, half=half, tl=tl: e.matmul(
                            pr_ps[:, half], o1_nfm[:, c, tl * TS:(tl + 1) * TS],
                            wo_sb[:, c, half * 512:(half + 1) * 512],
                            start=(c == 0), stop=False),
                            waits=ev_max(e_onfm, e_effrow, e_ones,
                                         e_prps_free[half]) if c == 0 else ())
                    e_pj = sch.op("tensor", lambda e, half=half: e.matmul(
                        pr_ps[:, half], ones_r[0:1, :],
                        effrow_b[0:1, half * 512:(half + 1) * 512],
                        start=False, stop=True), inc="t")
                    for c in range(NCH):
                        e_rm = sch.op("tensor", lambda e, c=c, half=half, par=par:
                                      e.matmul(
                                          rv_ps[:, half], xr_t[:, par, c, :],
                                          wrt_sb[:, c, half * 512:(half + 1) * 512],
                                          start=(c == 0), stop=(c == NCH - 1)),
                                      waits=ev_max(e_xr, e_wrt,
                                                   e_rvps_ff[half])
                                      if c == 0 else (),
                                      inc="t" if c == NCH - 1 else None)
                    e_sg = sch.op("scalar", lambda e, half=half: e.activation(
                        out=rsg_sb[:, half * 512:(half + 1) * 512],
                        in_=rv_ps[:, half], func=AF.Sigmoid),
                        waits=ev_max(e_rm))
                    e_rvps_sig[half] = e_sg
                    e_a1 = sch.op("vector", lambda e, half=half: e.tensor_tensor(
                        out=x1t[:, half * 512:(half + 1) * 512],
                        in0=x1t[:, half * 512:(half + 1) * 512],
                        in1=pr_ps[:, half], op=ALU.add),
                        waits=ev_max(e_pj, e_x1))
                    e_prps_free[half] = e_a1
                for half in range(2):
                    for c in range(NCH):
                        e_om = sch.op("tensor", lambda e, c=c, half=half, par=par:
                                      e.matmul(
                                          rv_ps[:, half], ka_t[:, par, c, :],
                                          wvt_sb[:, c, half * 512:(half + 1) * 512],
                                          start=(c == 0), stop=(c == NCH - 1)),
                                      waits=ev_max(e_ka, e_wvt, e_rvps_sig[half])
                                      if c == 0 else (),
                                      inc="t" if c == NCH - 1 else None)
                    e_f = sch.op("vector", lambda e, half=half: e.tensor_tensor(
                        out=fv_sb[:, half * 512:(half + 1) * 512],
                        in0=rsg_sb[:, half * 512:(half + 1) * 512],
                        in1=rv_ps[:, half], op=ALU.mult),
                        waits=ev_max(e_om, e_out_store))
                    e_rvps_ff[half] = e_f
                    e_o = sch.op("vector", lambda e, half=half: e.tensor_tensor(
                        out=x1t[:, half * 512:(half + 1) * 512],
                        in0=x1t[:, half * 512:(half + 1) * 512],
                        in1=fv_sb[:, half * 512:(half + 1) * 512], op=ALU.add))
                e_xrka_rel[par] = ("t", sch.cnt["t"])
                e_out_store = sch.dma("sync", out_d[s_, t0:t0 + TS, :], x1t[:],
                                      "d_out", waits=(e_o,))
                e_x1_user = e_out_store
            e_onfm_user = e_pj
            e_xn_user = e_qmm
            e_qf_user = ("t", sch.cnt["t"])
        e_kv_user = ("t", sch.cnt["t"])
    sch.op("sync", None, waits=ev_max(sch.ev("d_out")))




def build_l1b(nc):
    x1_d = nc.dram_tensor("x1", [SEQ_PER_CORE, S, D], F32, kind="ExternalInput").ap()
    xn1fm_d = nc.dram_tensor("xn1fm", [SEQ_PER_CORE, NCH, 128, S], BF16,
                             kind="ExternalInput").ap()
    xrfm_d = nc.dram_tensor("xrfm", [SEQ_PER_CORE, NCH, 128, S], BF16,
                            kind="ExternalInput").ap()
    kafm_d = nc.dram_tensor("kafm", [SEQ_PER_CORE, NCH, 128, S], BF16,
                            kind="ExternalInput").ap()
    kv_d = nc.dram_tensor("kv", [SEQ_PER_CORE, 16, 64, 65], BF16,
                          kind="ExternalInput").ap()
    wqa_d = nc.dram_tensor("Wq_a", [D, D], F32, kind="ExternalInput").ap()
    bq_d = nc.dram_tensor("bq_pc", [128, NCH], F32, kind="ExternalInput").ap()
    wo_d = nc.dram_tensor("Wo_a", [D, D], F32, kind="ExternalInput").ap()
    bor_d = nc.dram_tensor("bo_row", [1, D], F32, kind="ExternalInput").ap()
    bv_d = nc.dram_tensor("bv_pc", [128, NCH], F32, kind="ExternalInput").ap()
    wrt_d = nc.dram_tensor("WrT_f", [D, D], F32, kind="ExternalInput").ap()
    wvt_d = nc.dram_tensor("WvT_f", [D, D], F32, kind="ExternalInput").ap()
    out_d = nc.dram_tensor("out_final", [SEQ_PER_CORE, S, D], F32,
                           kind="ExternalOutput").ap()
    dr = {"x1": x1_d, "xn1fm": xn1fm_d, "xrfm": xrfm_d, "kafm": kafm_d,
          "kv": kv_d, "Wq_a": wqa_d, "bq_pc": bq_d, "Wo_a": wo_d,
          "bo_row": bor_d, "bv_pc": bv_d, "WrT_f": wrt_d, "WvT_f": wvt_d,
          "out_final": out_d}
    with contextlib.ExitStack() as ctx:
        sch = Sched(nc, ctx)
        block = ctx.enter_context(nc.Block())
        with contextlib.ExitStack() as pctx:
            emit_l1b(nc, sch, pctx.enter_context, dr)
            sch.build(block)
    return nc


def build_fused(nc):
    EI, EO, IN = "ExternalInput", "ExternalOutput", "Internal"
    t = nc.dram_tensor
    x_d = t("x", [SEQ_PER_CORE, S, D], F32, kind=EI).ap()
    wg_d = t("w_gate", [I, D], F32, kind=EI).ap()
    wu_d = t("w_up", [I, D], F32, kind=EI).ap()
    wd_d = t("w_down", [D, I], F32, kind=EI).ap()
    ident_d = t("ident", [128, 128], F32, kind=EI).ap()
    qkvwT_d = t("qkv_wT", [D, 3 * D], F32, kind=EI).ap()
    qkvb_d = t("qkv_b_pc", [128, 24], F32, kind=EI).ap()
    projwT_d = t("proj_wT", [D, D], F32, kind=EI).ap()
    projbr_d = t("proj_b_row", [1, D], F32, kind=EI).ap()
    wka_d = t("Wk_a", [D, D], F32, kind=EI).ap()
    wva_d = t("Wv_a", [D, D], F32, kind=EI).ap()
    wktf_d = t("WkT_f", [D, D], F32, kind=EI).ap()
    bkr_d = t("bk_row", [1, D], F32, kind=EI).ap()
    mur_d = t("mu_r_pc", [128, NCH], F32, kind=EI).ap()
    muk_d = t("mu_k_pc", [128, NCH], F32, kind=EI).ap()
    wqa_d = t("Wq_a", [D, D], F32, kind=EI).ap()
    bq_d = t("bq_pc", [128, NCH], F32, kind=EI).ap()
    wo_d = t("Wo_a", [D, D], F32, kind=EI).ap()
    bor_d = t("bo_row", [1, D], F32, kind=EI).ap()
    bv_d = t("bv_pc", [128, NCH], F32, kind=EI).ap()
    wrt_d = t("WrT_f", [D, D], F32, kind=EI).ap()
    wvt_d = t("WvT_f", [D, D], F32, kind=EI).ap()
    out_d = t("out_final", [SEQ_PER_CORE, S, D], F32, kind=EO).ap()
    y_b = t("y_buf", [SEQ_PER_CORE, S, D], F32, kind=IN).ap()
    xnfm_b = t("xnfm_buf", [SEQ_PER_CORE, NCH, 128, S], BF16, kind=IN).ap()
    x1_b = t("x1_buf", [SEQ_PER_CORE, S, D], F32, kind=IN).ap()
    xn1fm_b = t("xn1fm_buf", [SEQ_PER_CORE, NCH, 128, S], BF16, kind=IN).ap()
    xrfm_b = t("xrfm_buf", [SEQ_PER_CORE, NCH, 128, S], BF16, kind=IN).ap()
    kafm_b = t("kafm_buf", [SEQ_PER_CORE, NCH, 128, S], BF16, kind=IN).ap()
    kv_b = t("kv_buf", [SEQ_PER_CORE, 16, 64, 65], BF16, kind=IN).ap()

    phases = [
        ("a_", emit_l0a, {"x": x_d, "w_gate": wg_d, "w_up": wu_d,
                          "w_down": wd_d, "ident": ident_d,
                          "y_out": y_b, "xnfm_out": xnfm_b}),
        ("b_", emit_l0b, {"xnfm": xnfm_b, "y_in": y_b, "qkv_wT": qkvwT_d,
                          "qkv_b_pc": qkvb_d, "proj_wT": projwT_d,
                          "proj_b_row": projbr_d, "x1_out": x1_b}),
        ("c_", emit_l1a, {"x1": x1_b, "Wk_a": wka_d, "Wv_a": wva_d,
                          "WkT_f": wktf_d, "bk_row": bkr_d, "mu_r_pc": mur_d,
                          "mu_k_pc": muk_d, "ident": ident_d,
                          "xn1fm_out": xn1fm_b, "xrfm_out": xrfm_b,
                          "kafm_out": kafm_b, "kv_out": kv_b}),
        ("d_", emit_l1b, {"x1": x1_b, "xn1fm": xn1fm_b, "xrfm": xrfm_b,
                          "kafm": kafm_b, "kv": kv_b, "Wq_a": wqa_d,
                          "bq_pc": bq_d, "Wo_a": wo_d, "bo_row": bor_d,
                          "bv_pc": bv_d, "WrT_f": wrt_d, "WvT_f": wvt_d,
                          "out_final": out_d}),
    ]
    with contextlib.ExitStack() as ctx:
        sch = Sched(nc, ctx)
        block = ctx.enter_context(nc.Block())
        for pfx, emitf, dr in phases:
            with contextlib.ExitStack() as pctx:
                emitf(nc, sch, pctx.enter_context, dr, pfx=pfx)
                sch.barrier()
                sch.build(block)
        if END_CLEAR:
            sch.end_clear()
            sch.build(block)
    return nc


def prep_in_maps(inputs):
    f32 = np.float32
    x_full = np.ascontiguousarray(inputs["L0_attn_x"], dtype=f32)
    qkv_b = np.asarray(inputs["L0_attn_qkv_b"], dtype=f32)
    sh = {
        "w_gate": np.ascontiguousarray(inputs["L0_ffn_w_gate"], dtype=f32),
        "w_up": np.ascontiguousarray(inputs["L0_ffn_w_up"], dtype=f32),
        "w_down": np.ascontiguousarray(inputs["L0_ffn_w_down"], dtype=f32),
        "ident": np.eye(128, dtype=f32),
        "qkv_wT": np.ascontiguousarray(np.asarray(inputs["L0_attn_qkv_w"], f32).T),
        "qkv_b_pc": np.ascontiguousarray(qkv_b.reshape(24, 128).T),
        "proj_wT": np.ascontiguousarray(np.asarray(inputs["L0_attn_proj_w"], f32).T),
        "proj_b_row": np.ascontiguousarray(
            np.asarray(inputs["L0_attn_proj_b"], f32).reshape(1, D)),
        "Wk_a": np.ascontiguousarray(np.asarray(inputs["L1_attn_Wk"], f32)),
        "Wv_a": np.ascontiguousarray(np.asarray(inputs["L1_attn_Wv"], f32)),
        "WkT_f": np.ascontiguousarray(np.asarray(inputs["L1_ffn_W_k"], f32).T),
        "bk_row": np.ascontiguousarray(
            np.asarray(inputs["L1_attn_bk"], f32).reshape(1, D)),
        "mu_r_pc": np.ascontiguousarray(
            np.asarray(inputs["L1_ffn_mu_r"], f32).reshape(8, 128).T),
        "mu_k_pc": np.ascontiguousarray(
            np.asarray(inputs["L1_ffn_mu_k"], f32).reshape(8, 128).T),
        "Wq_a": np.ascontiguousarray(np.asarray(inputs["L1_attn_Wq"], f32)),
        "bq_pc": np.ascontiguousarray(
            np.asarray(inputs["L1_attn_bq"], f32).reshape(8, 128).T),
        "Wo_a": np.ascontiguousarray(np.asarray(inputs["L1_attn_Wo"], f32)),
        "bo_row": np.ascontiguousarray(
            np.asarray(inputs["L1_attn_bo"], f32).reshape(1, D)),
        "bv_pc": np.ascontiguousarray(
            np.asarray(inputs["L1_attn_bv"], f32).reshape(8, 128).T),
        "WrT_f": np.ascontiguousarray(np.asarray(inputs["L1_ffn_W_r"], f32).T),
        "WvT_f": np.ascontiguousarray(np.asarray(inputs["L1_ffn_W_v"], f32).T),
    }
    return [dict(sh, x=x_full[2 * c:2 * c + 2]) for c in range(NCORES)]



_NCS = {}


def _get_nc(name, builder):
    if name not in _NCS:
        _NCS[name] = builder(bass.Bass("TRN2", target_bir_lowering=False))
    return _NCS[name]


def kernel(**inputs):
    in_maps = prep_in_maps(inputs)
    nc = _get_nc("fused", build_fused)
    res = run_bass_kernel_spmd(nc, in_maps, core_ids=list(range(NCORES)))
    return np.concatenate([np.asarray(r["out_final"]) for r in res.results], axis=0)


# version 10
# speedup vs baseline: 1.0933x; 1.0933x over previous
import contextlib
import numpy as np
import ml_dtypes
import concourse.bass as bass
import concourse.mybir as mybir
from concourse.bass import broadcast_tensor_aps
from concourse.bass_utils import run_bass_kernel_spmd

F32 = mybir.dt.float32
BF16 = mybir.dt.bfloat16
AF = mybir.ActivationFunctionType
ALU = mybir.AluOpType
AX = mybir.AxisListType
bfdt = ml_dtypes.bfloat16

NCORES = 8
SEQ_PER_CORE = 2
S = 2048
D = 1024
I = 2816
TS = 128
NT = S // TS
NCH = D // 128
EPS_LN = 1e-5

ENGS = ("sync", "gpsimd", "scalar", "vector", "tensor")
OWN_WAITS = True
DRAIN_FUSABLE = True
END_CLEAR = True
BATCH_INCS = ("tensor",)


class Sched:

    def __init__(self, nc, ctx, sem_names=("t", "v", "a")):
        self.nc = nc
        self.ctx = ctx
        self.pfx = ""
        self.swdge_sems = set()
        self.items = {e: [] for e in ENGS}
        self.sems = {}
        self.cnt = {}
        for n in sem_names:
            self.sem(n)

    def ev(self, name):
        if name.startswith("d_"):
            name = self.pfx + name
        return (name, self.cnt[name])

    def barrier(self):
        snap = tuple((n, c) for n, c in self.cnt.items() if c > 0)
        for e in ENGS:
            self.op(e, None, waits=snap)

    def end_clear(self):
        self.barrier()
        self.sem("fin")
        for e in ENGS:
            if e != "gpsimd":
                self.op(e, lambda en: en.nop(), inc="fin")
        fin_target = self.cnt["fin"]
        sems = [h for n, h in self.sems.items() if n not in self.swdge_sems]

        def fn(e, sems=sems):
            ins = None
            for h in sems:
                ins = e.sem_clear(h)
            return ins
        self.op("gpsimd", fn, waits=(("fin", fin_target),))

    def sem(self, name):
        if name not in self.sems:
            self.sems[name] = self.ctx.enter_context(
                self.nc.semaphore("sem_" + name))
            self.cnt[name] = 0
        return name

    def op(self, eng, fn, waits=(), inc=None, amt=1):
        own = {"vector": "v", "scalar": "a"}.get(eng)
        own_wait = None
        if own is not None and fn is not None:
            if inc is None:
                inc = own
            own_wait = (own, self.cnt[own])
        ev = None
        waits = tuple(w for w in waits if w is not None)
        abs_cnt = None
        if inc is not None:
            self.cnt[inc] += amt
            abs_cnt = self.cnt[inc]
            ev = (inc, abs_cnt)
        self.items[eng].append((fn, waits, inc, amt, own_wait, abs_cnt))
        return ev

    def dma(self, eng, out, in_, sem, waits=()):
        sem = self.pfx + sem
        self.sem(sem)
        if eng == "gpsimd":
            self.swdge_sems.add(sem)
        return self.op(eng, lambda e, o=out, i=in_: e.dma_start(out=o, in_=i),
                       waits=waits, inc=sem, amt=16)

    def build(self, block):
        sems = self.sems
        waited_vals = {}
        for e in ENGS:
            for fn, waits, inc, amt, own_wait, abs_cnt in self.items[e]:
                for (s_, v_) in waits:
                    waited_vals.setdefault(s_, set()).add(v_)
        for e in ENGS:
            items = self.items[e]
            if not items:
                continue
            use_drain = e in ("vector", "scalar")
            batch = e in BATCH_INCS
            last_idx = {}
            for i, (fn, waits, inc, amt, own_wait, abs_cnt) in enumerate(items):
                if inc is not None and fn is not None:
                    last_idx[inc] = i
            plan = []
            pending = {}
            for i, (fn, waits, inc, amt, own_wait, abs_cnt) in enumerate(items):
                em = None
                if inc is not None and fn is not None:
                    if amt == 16:
                        em = (inc, 16)
                    elif not batch:
                        em = (inc, amt)
                    else:
                        pending[inc] = pending.get(inc, 0) + amt
                        if (abs_cnt in waited_vals.get(inc, ()) or
                                i == last_idx[inc]):
                            em = (inc, pending[inc])
                            pending[inc] = 0
                plan.append(em)

            def make_emit(items=items, plan=plan, use_drain=use_drain):
                def emit(engobj):
                    waited = {}
                    for (fn, waits, inc, amt, own_wait, abs_cnt), em in zip(items, plan):
                        if OWN_WAITS and own_wait is not None:
                            waits = (own_wait,) + waits
                        for (s_, v_) in waits:
                            if v_ > 0 and waited.get(s_, -1) < v_:
                                engobj.wait_ge(sems[s_], v_)
                                waited[s_] = v_
                        if fn is None:
                            continue
                        instr = fn(engobj)
                        if em is not None:
                            sname, a2 = em
                            if a2 == 16 or not use_drain:
                                instr.then_inc(sems[sname], a2)
                            else:
                                engobj.drain(fusable=DRAIN_FUSABLE).then_inc(
                                    sems[sname], a2)
                return emit
            getattr(block, e)(make_emit())
        self.items = {e: [] for e in ENGS}


def ev_max(*evs):
    best = {}
    for ev in evs:
        if ev is None or (isinstance(ev, tuple) and len(ev) == 0):
            continue
        if isinstance(ev, tuple) and isinstance(ev[0], tuple):
            for (s_, v_) in ev:
                if best.get(s_, -1) < v_:
                    best[s_] = v_
        else:
            s_, v_ = ev
            if best.get(s_, -1) < v_:
                best[s_] = v_
    return tuple(best.items())



def emit_quant(sch, q, src, out_bf16, C, wv, wa):
    NB = C // 16
    absx, sgn, a1, c5 = q["absx"], q["sgn"], q["a1"], q["c5"]
    bm, sc, rs, hs = q["bm"], q["sc"], q["rs"], q["hs"]
    war = q.get("_last")
    wa = ev_max(wa, war)
    wv = ev_max(wv)
    e_absx = sch.op("scalar", lambda e: e.activation(
        out=absx[:, 0:C], in_=src, func=AF.Abs), waits=wa, inc="a")
    e_sgn = sch.op("scalar", lambda e: e.activation(
        out=sgn[:, 0:C], in_=src, func=AF.Sign), waits=wa, inc="a")
    e_bm = sch.op("vector", lambda e: e.tensor_reduce(
        out=bm[:, 0:NB].rearrange("p (b o) -> p b o", o=1),
        in_=src.rearrange("p (b s) -> p b s", s=16),
        axis=AX.X, op=ALU.max, apply_absolute_value=True), waits=wv, inc="v")
    e_sc = sch.op("vector", lambda e: e.tensor_scalar(
        out=sc[:, 0:NB], in0=bm[:, 0:NB], scalar1=1.0 / 6.0, scalar2=1e-12,
        op0=ALU.mult, op1=ALU.max), waits=(e_bm,), inc="v")
    e_rs = sch.op("vector", lambda e: e.reciprocal(
        out=rs[:, 0:NB], in_=sc[:, 0:NB]), waits=(e_sc,), inc="v")
    sch.op("vector", lambda e: e.tensor_scalar(
        out=hs[:, 0:NB], in0=sc[:, 0:NB], scalar1=0.5, scalar2=None, op0=ALU.mult))
    av = absx[:, 0:C].rearrange("p (b s) -> p b s", s=16)
    rv = rs[:, 0:NB].rearrange("p (b o) -> p b o", o=1)
    avb, rvb = broadcast_tensor_aps(av, rv)
    sch.op("vector", lambda e: e.tensor_tensor(out=av, in0=avb, in1=rvb, op=ALU.mult),
           waits=ev_max(e_absx, e_rs))
    u = absx[:, 0:C]
    sch.op("vector", lambda e: e.tensor_scalar(
        out=c5[:, 0:C], in0=u, scalar1=5.0, scalar2=2.0, op0=ALU.is_gt, op1=ALU.mult))
    for t_ in (3.5, 2.5):
        sch.op("vector", lambda e, t_=t_: e.scalar_tensor_tensor(
            out=c5[:, 0:C], in0=u, scalar=t_, in1=c5[:, 0:C],
            op0=ALU.is_gt, op1=ALU.add))
    sch.op("vector", lambda e: e.tensor_scalar(
        out=a1[:, 0:C], in0=u, scalar1=0.25, scalar2=None, op0=ALU.is_gt))
    for t_ in (0.75, 1.25, 1.75):
        sch.op("vector", lambda e, t_=t_: e.scalar_tensor_tensor(
            out=a1[:, 0:C], in0=u, scalar=t_, in1=a1[:, 0:C],
            op0=ALU.is_gt, op1=ALU.add))
    sch.op("vector", lambda e: e.scalar_tensor_tensor(
        out=a1[:, 0:C], in0=c5[:, 0:C], scalar=2.0, in1=a1[:, 0:C],
        op0=ALU.mult, op1=ALU.add))
    gv = sgn[:, 0:C].rearrange("p (b s) -> p b s", s=16)
    hv = hs[:, 0:NB].rearrange("p (b o) -> p b o", o=1)
    gvb, hvb = broadcast_tensor_aps(gv, hv)
    sch.op("vector", lambda e: e.tensor_tensor(out=gv, in0=gvb, in1=hvb, op=ALU.mult),
           waits=ev_max(e_sgn))
    ev = sch.op("vector", lambda e: e.tensor_tensor(
        out=out_bf16, in0=a1[:, 0:C], in1=sgn[:, 0:C], op=ALU.mult), inc="v")
    q["_last"] = ev
    return ev


def emit_ln(sch, b, x_src, out_xn, e_x):
    war = b.get("_last")
    wx = ev_max(e_x)
    wxa = ev_max(e_x, war)
    e_s1 = sch.op("vector", lambda e: e.reduce_sum(
        out=b["s1"][:], in_=x_src, axis=AX.X), waits=wx, inc="v")
    e_sq = sch.op("scalar", lambda e: e.activation(
        out=b["sqs"][:], in_=x_src, func=AF.Square), waits=wxa, inc="a")
    e_ssq = sch.op("vector", lambda e: e.reduce_sum(
        out=b["ssq"][:], in_=b["sqs"][:], axis=AX.X), waits=(e_sq,), inc="v")
    e_mean = sch.op("vector", lambda e: e.tensor_scalar(
        out=b["mean"][:], in0=b["s1"][:], scalar1=1.0 / D, scalar2=None,
        op0=ALU.mult), waits=ev_max(e_s1, e_ssq), inc="v")
    sch.op("vector", lambda e: e.tensor_scalar(
        out=b["ex2"][:], in0=b["ssq"][:], scalar1=1.0 / D, scalar2=EPS_LN,
        op0=ALU.mult, op1=ALU.add))
    sch.op("vector", lambda e: e.tensor_tensor(
        out=b["msq"][:], in0=b["mean"][:], in1=b["mean"][:], op=ALU.mult),
        waits=(e_mean,))
    e_var = sch.op("vector", lambda e: e.tensor_tensor(
        out=b["var"][:], in0=b["ex2"][:], in1=b["msq"][:], op=ALU.subtract), inc="v")
    e_sd = sch.op("scalar", lambda e: e.activation(
        out=b["sd"][:], in_=b["var"][:], func=AF.Sqrt),
        waits=ev_max(e_var, war), inc="a")
    e_rstd = sch.op("vector", lambda e: e.reciprocal(
        out=b["rstd"][:], in_=b["sd"][:]), waits=(e_sd,), inc="v")
    e_xn = sch.op("vector", lambda e: e.tensor_scalar(
        out=out_xn, in0=x_src, scalar1=b["mean"][:][:, 0:1],
        scalar2=b["rstd"][:][:, 0:1], op0=ALU.subtract, op1=ALU.mult),
        waits=(e_rstd,), inc="v")
    b["_last"] = e_xn
    return e_xn


def emit_transpose_evac(sch, psum, ident, srcs, dsts, w_tensor, w_evac,
                        evac_eng="scalar", dst_groups=None):
    evac_events = []
    n = len(srcs)
    last = None
    for cb in range(0, n, 4):
        nslot = min(4, n - cb)
        e_tps = None
        for s_i in range(nslot):
            src = srcs[cb + s_i]
            e_tps = sch.op("tensor", lambda e, src=src, s_i=s_i: e.transpose(
                psum[:, s_i, 0:128], src, ident), waits=w_tensor if s_i == 0 else (),
                inc="t" if s_i == nslot - 1 else None)
        if dst_groups is not None:
            dstg = dst_groups(cb, nslot)
            last = sch.op(evac_eng, lambda e, dstg=dstg, nslot=nslot: e.copy(
                out=dstg, in_=psum[:, 0:nslot, 0:128]),
                waits=ev_max(e_tps, w_evac), inc="a")
            evac_events.append(last)
        else:
            evs = []
            for s_i in range(nslot):
                dst = dsts[cb + s_i]
                evs.append(sch.op(evac_eng, lambda e, dst=dst, s_i=s_i: e.copy(
                    out=dst, in_=psum[:, s_i, 0:128]),
                    waits=ev_max(e_tps, w_evac) if s_i == 0 else (), inc="a"))
            last = evs[-1]
            evac_events.extend(evs)
        w_tensor = ev_max(w_tensor, last)
    return last, evac_events



def emit_l0a(nc, sch, E, dr, pfx=""):
    sch.pfx = pfx
    x_d = dr["x"]
    wg_d = dr["w_gate"]
    wu_d = dr["w_up"]
    wd_d = dr["w_down"]
    ident_d = dr["ident"]
    y_d = dr["y_out"]
    xnfm_d = dr["xnfm_out"]

    NIC = 6
    IC_OFF = [0, 512, 1024, 1536, 2048, 2560]
    IC_SZ = [512] * 5 + [256]
    NRT_G = I // 128
    NRT_D = D // 128

    wgqT = E(nc.sbuf_tensor(pfx + "wgqT", [128, NCH, I], BF16))
    wuqT = E(nc.sbuf_tensor(pfx + "wuqT", [128, NCH, I], BF16))
    wdqT = E(nc.sbuf_tensor(pfx + "wdqT", [128, NRT_G, D], BF16))

    q = {n: E(nc.sbuf_tensor(pfx + "q_" + n, [128, 1408], F32))
         for n in ("absx", "sgn", "a1", "c5")}
    q.update({n: E(nc.sbuf_tensor(pfx + "q_" + n, [128, 88], F32))
              for n in ("bm", "sc", "rs", "hs")})
    lnb = {n: E(nc.sbuf_tensor(pfx + "ln_" + n, [128, 1], F32))
           for n in ("s1", "ssq", "mean", "ex2", "msq", "var", "sd", "rstd")}
    xtile = E(nc.sbuf_tensor(pfx + "xtile", [128, 2, D], F32))
    xn_tm = E(nc.sbuf_tensor(pfx + "xn_tm", [128, D], F32))
    xn_stage = E(nc.sbuf_tensor(pfx + "xn_stage", [128, NCH, 128], BF16))
    xq_tm = E(nc.sbuf_tensor(pfx + "xq_tm", [128, D], BF16))
    xq_fm = E(nc.sbuf_tensor(pfx + "xq_fm", [128, NCH, 128], BF16))
    g_sb = E(nc.sbuf_tensor(pfx + "g_sb", [128, 512], F32))
    h_sb = E(nc.sbuf_tensor(pfx + "h_sb", [128, I], F32))
    hq_b = E(nc.sbuf_tensor(pfx + "hq_b", [128, I], BF16))
    hq_fm = E(nc.sbuf_tensor(pfx + "hq_fm", [128, NRT_G, 128], BF16))
    y_sb = E(nc.sbuf_tensor(pfx + "y_sb", [128, D], F32))
    lnb["sqs"] = y_sb
    wt_in = h_sb
    wq_b = hq_b
    idt = E(nc.sbuf_tensor(pfx + "idt", [128, 128], F32))
    idtb = E(nc.sbuf_tensor(pfx + "idtb", [128, 128], BF16))
    tp32 = E(nc.psum_tensor(pfx + "tp32", [128, 4, 128], F32))
    tpb16 = E(nc.psum_tensor(pfx + "tpb16", [128, 4, 256], BF16))
    g_ps = E(nc.psum_tensor(pfx + "g_ps", [128, 512], F32))
    u_ps = E(nc.psum_tensor(pfx + "u_ps", [128, 512], F32))
    d_ps = E(nc.psum_tensor(pfx + "d_ps", [128, 2, 512], F32))


    e_id = sch.dma("sync", idt[:], ident_d, "d_id")
    e_idb = sch.dma("gpsimd", idtb[:], ident_d, "d_idb")

    e_tpb_war = None
    wt_release = [None, None]
    slot = 0
    for (w_src, n_rt, row_c, dstT) in ((wg_d, NRT_G, D, wgqT),
                                       (wu_d, NRT_G, D, wuqT),
                                       (wd_d, NRT_D, I, wdqT)):
        n_chT = row_c // 128
        for r in range(n_rt):
            par = 0
            slot += 1
            e_q = None
            for c0 in range(0, row_c, 1024):
                c1 = min(row_c, c0 + 1024)
                e_ld = sch.dma("sync", wt_in[:, c0:c1],
                               w_src[r * 128:(r + 1) * 128, c0:c1],
                               "d_w%d" % par,
                               waits=ev_max(wt_release[par], e_q))
                e_q = emit_quant(sch, q, wt_in[:, c0:c1],
                                 wq_b[:, c0:c1], c1 - c0, e_ld, e_ld)
            wt_release[par] = e_q
            srcs = [wq_b[:, c * 128:(c + 1) * 128] for c in range(n_chT)]
            dsts = [dstT[:, c, r * 128:(r + 1) * 128] for c in range(n_chT)]
            last, _ = emit_transpose_evac(
                sch, tpb16, idtb[:], srcs, dsts,
                w_tensor=ev_max(e_q, e_idb, e_tpb_war), w_evac=())
            e_tpb_war = last
            q["_last"] = ev_max(q["_last"], ("t", sch.cnt["t"]))
    e_wts = ("a", sch.cnt["a"])

    xtile_release = [None, None]
    e_y_store = None
    e_xnstage_free = None
    e_tp32_war = None
    e_g0_free = None
    e_g1_free = None
    e_d_free = None
    e_hqfm_free = None
    e_xqfm_free = None
    e_hq_prev = None
    tiles = [(s_, t) for s_ in range(SEQ_PER_CORE) for t in range(NT)]
    e_load = {}

    def sched_load(idx):
        s_, t = tiles[idx]
        par = idx % 2
        e_load[idx] = sch.dma("sync", xtile[:, par],
                              x_d[s_, t * TS:(t + 1) * TS, :],
                              "d_x%d" % par,
                              waits=ev_max(xtile_release[par]))

    sched_load(0)
    for idx, (s_, t) in enumerate(tiles):
        par = idx % 2
        t0 = t * TS
        if idx + 1 < len(tiles):
            sched_load(idx + 1)
        e_x = e_load[idx]
        e_xn = emit_ln(sch, lnb, xtile[:, par], xn_tm[:], ev_max(e_x))
        srcs = [xn_tm[:, c * 128:(c + 1) * 128] for c in range(NCH)]
        dsts = [xn_stage[:, c, :] for c in range(NCH)]
        e_stg, _ = emit_transpose_evac(
            sch, tp32, idt[:], srcs, dsts,
            w_tensor=ev_max(e_xn, e_id, e_tp32_war),
            w_evac=ev_max(e_xnstage_free))
        e_tp32_war = e_stg
        e_xnstage_free = sch.dma(
            "sync", xnfm_d[s_].rearrange("c p t -> p c t")[:, :, t0:t0 + TS],
            xn_stage[:], "d_stg", waits=(e_stg,))
        e_xq = emit_quant(sch, q, xn_tm[:], xq_tm[:], D, e_xn, e_xn)
        srcs = [xq_tm[:, c * 128:(c + 1) * 128] for c in range(NCH)]
        dsts = [xq_fm[:, c, :] for c in range(NCH)]
        e_xqfm, _ = emit_transpose_evac(
            sch, tpb16, idtb[:], srcs, dsts,
            w_tensor=ev_max(e_xq, e_idb, e_tpb_war, e_wts, e_xqfm_free),
            w_evac=())
        e_tpb_war = e_xqfm
        e_up = None
        e_h = None
        for ic in range(NIC):
            io0, isz = IC_OFF[ic], IC_SZ[ic]
            for (ps, wT, e_free) in ((g_ps, wgqT, e_g0_free),
                                     (u_ps, wuqT, e_g1_free)):
                for c in range(NCH):
                    e_mm = sch.op(
                        "tensor", lambda e, c=c, ps=ps, wT=wT, io0=io0, isz=isz:
                        e.matmul(ps[:, 0:isz], xq_fm[:, c, :],
                                 wT[:, c, io0:io0 + isz],
                                 start=(c == 0), stop=(c == NCH - 1)),
                        waits=ev_max(e_xqfm, e_wts, e_free) if c == 0 else (),
                        inc="t" if c == NCH - 1 else None)
                if ps is g_ps:
                    e_gate = e_mm
                else:
                    e_up = e_mm
            e_silu = sch.op("scalar", lambda e, isz=isz: e.activation(
                out=g_sb[:, 0:isz], in_=g_ps[:, 0:isz], func=AF.Silu),
                waits=ev_max(e_gate, e_g0_free), inc="a")
            e_g0_free = e_silu
            e_h = sch.op("vector", lambda e, isz=isz, io0=io0: e.tensor_tensor(
                out=h_sb[:, io0:io0 + isz], in0=g_sb[:, 0:isz],
                in1=u_ps[:, 0:isz], op=ALU.mult),
                waits=ev_max(e_silu, e_up, e_hq_prev), inc="v")
            e_g1_free = e_h
        e_hq = None
        for half in range(2):
            e_hq = emit_quant(sch, q, h_sb[:, half * 1408:(half + 1) * 1408],
                              hq_b[:, half * 1408:(half + 1) * 1408], 1408,
                              e_h, e_h)
        e_hq_prev = e_hq
        srcs = [hq_b[:, c * 128:(c + 1) * 128] for c in range(NRT_G)]
        e_hqf, _ = emit_transpose_evac(
            sch, tpb16, idtb[:], srcs, None,
            w_tensor=ev_max(e_hq, e_tpb_war, e_hqfm_free), w_evac=(),
            dst_groups=lambda cb, ns: hq_fm[:, cb:cb + ns, :])
        e_tpb_war = e_hqf
        e_hq_done = [e_hqf]
        e_dn0 = None
        for half in range(2):
            for c in range(NRT_G):
                e_dn = sch.op(
                    "tensor", lambda e, c=c, half=half: e.matmul(
                        d_ps[:, half], hq_fm[:, c, :],
                        wdqT[:, c, half * 512:half * 512 + 512],
                        start=(c == 0), stop=(c == NRT_G - 1)),
                    waits=ev_max(*(e_hq_done + [e_d_free])) if c == 0 else (),
                    inc="t" if c == NRT_G - 1 else None)
            if half == 0:
                e_dn0 = e_dn
        e_hqfm_free = e_dn
        sch.op("vector", lambda e, par=par: e.tensor_tensor(
            out=y_sb[:, 0:512], in0=xtile[:, par, 0:512], in1=d_ps[:, 0],
            op=ALU.add), waits=ev_max(e_dn0, e_x, e_y_store), inc="v")
        e_yv = sch.op("vector", lambda e, par=par: e.tensor_tensor(
            out=y_sb[:, 512:1024], in0=xtile[:, par, 512:1024],
            in1=d_ps[:, 1], op=ALU.add), waits=ev_max(e_dn), inc="v")
        e_d_free = e_yv
        xtile_release[par] = e_yv
        e_y_store = sch.dma("sync", y_d[s_, t0:t0 + TS, :], y_sb[:],
                            "d_y", waits=(e_yv,))
        lnb["_last"] = ev_max(lnb.get("_last"), e_y_store)
    sch.op("sync", None, waits=ev_max(sch.ev("d_y"),
                                      sch.ev("d_stg")))




def build_l0a(nc):
    x_d = nc.dram_tensor("x", [SEQ_PER_CORE, S, D], F32, kind="ExternalInput").ap()
    wg_d = nc.dram_tensor("w_gate", [I, D], F32, kind="ExternalInput").ap()
    wu_d = nc.dram_tensor("w_up", [I, D], F32, kind="ExternalInput").ap()
    wd_d = nc.dram_tensor("w_down", [D, I], F32, kind="ExternalInput").ap()
    ident_d = nc.dram_tensor("ident", [128, 128], F32, kind="ExternalInput").ap()
    y_d = nc.dram_tensor("y_out", [SEQ_PER_CORE, S, D], F32, kind="ExternalOutput").ap()
    xnfm_d = nc.dram_tensor("xnfm_out", [SEQ_PER_CORE, NCH, 128, S], BF16,
                            kind="ExternalOutput").ap()
    dr = {"x": x_d, "w_gate": wg_d, "w_up": wu_d, "w_down": wd_d,
          "ident": ident_d, "y_out": y_d, "xnfm_out": xnfm_d}
    with contextlib.ExitStack() as ctx:
        sch = Sched(nc, ctx)
        block = ctx.enter_context(nc.Block())
        with contextlib.ExitStack() as pctx:
            emit_l0a(nc, sch, pctx.enter_context, dr)
            sch.build(block)
    return nc


def emit_l0b(nc, sch, E, dr, pfx=""):
    sch.pfx = pfx
    y_d = dr["y_in"]
    qkvwT_d = dr["qkv_wT"]
    qkvb_d = dr["qkv_b_pc"]
    projwT_d = dr["proj_wT"]
    projbr_d = dr["proj_b_row"]
    NH = 16
    NP = 8
    xnfm_d = dr["xnfm"]
    x1_d = dr["x1_out"]
    nrm_scr = nc.dram_tensor("nrm_scr", [SEQ_PER_CORE, NP, 8 * 512], F32,
                             kind="Internal").ap()
    onf_scr = nc.dram_tensor("onf_scr", [SEQ_PER_CORE, NP, 2, 64, S], BF16,
                             kind="Internal").ap()

    qkv_sb = E(nc.sbuf_tensor(pfx + "qkv_sb", [128, NCH, 3 * D], BF16))
    projw_sb = E(nc.sbuf_tensor(pfx + "projw_sb", [128, NCH, D], BF16))
    qkvb_sb = E(nc.sbuf_tensor(pfx + "qkvb_sb", [128, 24], F32))

    bvb = E(nc.sbuf_tensor(pfx + "bvb", [128, 8], BF16))
    ones_r = E(nc.sbuf_tensor(pfx + "ones_r", [1, 128], BF16))
    ones_f = E(nc.sbuf_tensor(pfx + "ones_f", [1, 64], F32))
    effrow_b = E(nc.sbuf_tensor(pfx + "effrow_b", [1, D], BF16))
    xnfm_sb = E(nc.sbuf_tensor(pfx + "xnfm_sb", [128, NCH, S], BF16))
    qf2 = E(nc.sbuf_tensor(pfx + "qf", [128, 2, S], BF16))
    kf2 = E(nc.sbuf_tensor(pfx + "kf", [128, 2, S], BF16))
    v_aug = E(nc.sbuf_tensor(pfx + "v_aug", [128, NT, NH * 65], BF16))
    p_sb = E(nc.sbuf_tensor(pfx + "p_sb", [128, 2, 2, 512], BF16))
    o_raw = E(nc.sbuf_tensor(pfx + "o_raw", [64, 2, 4, 512], BF16))
    nrm_sb = E(nc.sbuf_tensor(pfx + "nrm_sb", [128, 8 * 512], F32))
    rin_sb = E(nc.sbuf_tensor(pfx + "rin_sb", [128, 32], F32))
    o_nfm = E(nc.sbuf_tensor(pfx + "o_nfm", [128, NCH, S], BF16))
    projbr_b = o_nfm[0:1, 0, 0:D]
    y_tile = nrm_sb[:, 0:D]
    sT2 = E(nc.psum_tensor(pfx + "sT2", [128, 2, 512], F32))
    o_psA = E(nc.psum_tensor(pfx + "o_psA", [65, 512], F32))
    o_psB = E(nc.psum_tensor(pfx + "o_psB", [65, 512], F32))
    qk_ps = E(nc.psum_tensor(pfx + "qk_ps", [128, 512], F32))
    nbc_ps = E(nc.psum_tensor(pfx + "nbc_ps", [64, 512], F32))
    proj_ps = E(nc.psum_tensor(pfx + "proj_ps", [128, 2, 512], F32))


    e_qkvw = sch.dma("gpsimd", qkv_sb[:],
                     qkvwT_d.rearrange("(c p) o -> p c o", p=128), "d_qkvw")
    e_projw = sch.dma("gpsimd", projw_sb[:],
                      projwT_d.rearrange("(c p) o -> p c o", p=128), "d_projw")
    e_projb = sch.dma("gpsimd", projbr_b, projbr_d, "d_projb")
    e_qkvb = sch.dma("sync", qkvb_sb[:], qkvb_d, "d_qkvb")
    e_ones = sch.op("vector", lambda e: e.memset(ones_r[:], 1.0))
    e_ones_f = sch.op("vector", lambda e: e.memset(ones_f[:], 1.0))
    e_bvb = sch.op("vector", lambda e: e.tensor_copy(
        bvb[:], qkvb_sb[:, 16:24]), waits=(e_qkvb,))
    va_ones = v_aug[:].rearrange("p t (h c) -> p t h c", c=65)[:, :, :, 64:65]
    e_vones = sch.op("vector", lambda e: e.memset(va_ones, 1.0))

    e_eff = None
    for half in range(2):
        for c in range(NCH):
            sch.op("tensor", lambda e, c=c, half=half: e.matmul(
                proj_ps[0:1, half], bvb[:, c:c + 1],
                projw_sb[:, c, half * 512:half * 512 + 512],
                start=(c == 0), stop=False),
                waits=ev_max(e_bvb, e_projw) if (c == 0 and half == 0) else ())
        e_eff = sch.op("tensor", lambda e, half=half: e.matmul(
            proj_ps[0:1, half], ones_r[0:1, 0:1],
            projbr_b[0:1, half * 512:half * 512 + 512],
            start=False, stop=True),
            waits=ev_max(e_ones, e_projb) if half == 0 else (), inc="t")
    e_effrow = sch.op("scalar", lambda e: e.copy(
        out=effrow_b[:], in_=proj_ps[0:1, :].rearrange("p h j -> p (h j)")),
        waits=(e_eff,))

    e_xn_users = None
    e_onfm_users = None
    e_vaug_users = None
    e_x1_store = None
    e_onh_store = None
    e_nbc_free = None
    e_nbc_sb_free = {}
    for s_ in range(SEQ_PER_CORE):
        e_xn = sch.dma("sync", xnfm_sb[:],
                       xnfm_d[s_].rearrange("c p t -> p c t"), "d_xn",
                       waits=ev_max(e_xn_users))
        e_vevac = None
        e_vps_free = None
        e_vmm = None
        for tt in range(NT):
            for half in range(2):
                for c in range(NCH):
                    e_vmm = sch.op(
                        "tensor", lambda e, tt=tt, half=half, c=c: e.matmul(
                            qk_ps[:, 0:512],
                            xnfm_sb[:, c, tt * TS:(tt + 1) * TS],
                            qkv_sb[:, c, 2 * D + half * 512:2 * D + (half + 1) * 512],
                            start=(c == 0), stop=(c == NCH - 1)),
                        waits=ev_max(e_xn, e_qkvw, e_vps_free, e_vones)
                        if c == 0 else (),
                        inc="t" if c == NCH - 1 else None)
                vslice = v_aug[:].rearrange(
                    "p t (h c) -> p t h c", c=65)[:, tt, half * 8:(half + 1) * 8, 0:64]
                e_vevac = sch.op(
                    "scalar", lambda e, vslice=vslice: e.copy(
                        out=vslice,
                        in_=qk_ps[:, 0:512].rearrange("p (h c) -> p h c", c=64)),
                    waits=ev_max(e_vmm, e_vaug_users))
                e_vps_free = e_vevac
        e_vdone = e_vevac

        e_qkps_free = e_vps_free
        if s_ == 0:
            e_projps_free = {0: e_effrow, 1: e_effrow}
        e_qf_user = {0: None, 1: None}
        e_kf_user = {0: None, 1: None}
        e_praw_free = None
        e_sT_free = {0: None, 1: None}
        e_p_free = {}
        e_onh_user = None
        e_nrm_user = e_x1_store
        e_mm = None
        for hp in range(NP):
            hpar = hp % 2
            qf = qf2[:, hpar]
            kf = kf2[:, hpar]
            o0 = hp * 128
            e_qev = e_kev = None
            for which, (dst, bias_col, col0, e_user) in enumerate((
                    (qf, hp, o0, e_qf_user[hpar]),
                    (kf, 8 + hp, D + o0, e_kf_user[hpar]))):
                for qc in range(4):
                    for c in range(NCH):
                        e_mm = sch.op(
                            "tensor",
                            lambda e, c=c, qc=qc, col0=col0: e.matmul(
                                qk_ps[:, 0:512],
                                qkv_sb[:, c, col0:col0 + 128],
                                xnfm_sb[:, c, qc * 512:(qc + 1) * 512],
                                start=(c == 0), stop=(c == NCH - 1)),
                            waits=ev_max(e_xn, e_qkvw, e_qkps_free, e_user)
                            if c == 0 else (),
                            inc="t" if c == NCH - 1 else None)
                    e_ev = sch.op(
                        "scalar",
                        lambda e, dst=dst, qc=qc, bias_col=bias_col: e.activation(
                            out=dst[:, qc * 512:(qc + 1) * 512],
                            in_=qk_ps[:, 0:512], func=AF.Identity,
                            bias=qkvb_sb[:, bias_col:bias_col + 1]),
                        waits=ev_max(e_mm, e_qkvb))
                    e_qkps_free = e_ev
                if which == 0:
                    e_qev = e_ev
                else:
                    e_kev = e_ev
            e_att = {}
            for qc in range(4):
                for kt in range(NT):
                    par = kt % 2
                    sT = sT2 if par == 0 else proj_ps
                    for hi in range(2):
                        b0 = hi * 64
                        e_sc = sch.op(
                            "tensor",
                            lambda e, sT=sT, b0=b0, kt=kt, qc=qc, qf=qf, kf=kf,
                            hi=hi: e.matmul(
                                sT[:, hi, 0:512],
                                kf[b0:b0 + 64, kt * TS:(kt + 1) * TS],
                                qf[b0:b0 + 64, qc * 512:(qc + 1) * 512],
                                start=True, stop=True,
                                tile_position=(b0, 0)),
                            waits=ev_max(e_qev, e_kev, e_sT_free[par],
                                         (e_projps_free[0] if par == 1 else None),
                                         (e_projps_free[1] if par == 1 else None)),
                            inc="t")
                    e_exp = sch.op(
                        "scalar",
                        lambda e, sT=sT, par=par: e.activation(
                            out=p_sb[:, par].rearrange("p h j -> p (h j)"),
                            in_=sT[:, :, 0:512].rearrange("p h j -> p (h j)"),
                            func=AF.Exp, scale=0.125),
                        waits=ev_max(e_sc, e_p_free.get(par)))
                    e_sT_free[par] = e_exp
                    for hi, o_ps in enumerate((o_psA, o_psB)):
                        h = 2 * hp + hi
                        e_att[(qc, hi)] = sch.op(
                            "tensor",
                            lambda e, o_ps=o_ps, kt=kt, par=par, hi=hi, h=h:
                            e.matmul(
                                o_ps[:, 0:512],
                                v_aug[:].rearrange(
                                    "p t (hh c) -> p t hh c", c=65)[:, kt, h, :],
                                p_sb[:, par, hi, :],
                                start=(kt == 0), stop=(kt == NT - 1)),
                            waits=ev_max(e_exp, e_vdone,
                                         (e_praw_free if kt == 0 else None)),
                            inc="t")
                    e_p_free[par] = e_att[(qc, 1)]
                evs = []
                for hi, o_ps in enumerate((o_psA, o_psB)):
                    evs.append(sch.op(
                        "scalar", lambda e, o_ps=o_ps, hi=hi, qc=qc: e.copy(
                            out=o_raw[:, hi, qc, :], in_=o_ps[0:64, 0:512]),
                        waits=ev_max(e_att[(qc, hi)], e_onh_user)))
                    evs.append(sch.op(
                        "scalar", lambda e, o_ps=o_ps, hi=hi, qc=qc: e.copy(
                            out=nrm_sb[64:65, (qc * 2 + hi) * 512:
                                       (qc * 2 + hi + 1) * 512],
                            in_=o_ps[64:65, 0:512]),
                        waits=ev_max(e_nrm_user)))
                e_praw_free = ev_max(*evs)
            e_qf_user[hpar] = e_kf_user[hpar] = ("t", sch.cnt["t"])
            e_h1 = sch.dma("sync", nrm_scr[s_, hp], nrm_sb[64:65, :], "d_nbo",
                           waits=ev_max(e_praw_free))
            e_nrm_user = e_h1
            e_h2 = sch.dma("sync", rin_sb[:],
                           nrm_scr[s_, hp].rearrange("(r j) -> r j", r=128),
                           "d_nbi", waits=(e_h1,))
            e_rcp = sch.op("vector", lambda e: e.reciprocal(
                out=rin_sb[:], in_=rin_sb[:]), waits=(e_h2,))
            e_h3 = sch.dma("sync", nrm_scr[s_, hp].rearrange("(r j) -> r j", r=128),
                           rin_sb[:], "d_nbo", waits=(e_rcp,))
            e_h4 = sch.dma("sync", nrm_sb[0:1, :],
                           nrm_scr[s_, hp].rearrange("j -> () j"),
                           "d_nbi", waits=(e_h3,))
            e_div_last = None
            for qc in range(4):
                for hi in range(2):
                    slot = qc * 2 + hi
                    e_nbc = sch.op(
                        "tensor", lambda e, slot=slot: e.matmul(
                            nbc_ps[:, 0:512], ones_f[0:1, :],
                            nrm_sb[0:1, slot * 512:(slot + 1) * 512],
                            start=True, stop=True),
                        waits=ev_max(e_h4, e_nbc_free, e_ones_f), inc="t")
                    e_div = sch.op(
                        "vector", lambda e, hi=hi, qc=qc: e.tensor_tensor(
                            out=o_raw[:, hi, qc, :], in0=o_raw[:, hi, qc, :],
                            in1=nbc_ps[0:64, 0:512], op=ALU.mult),
                        waits=ev_max(e_nbc, e_onh_store))
                    e_nbc_free = e_div
                    e_div_last = e_div
            e_onh_store = sch.dma(
                "sync",
                onf_scr[s_, hp].rearrange("h d (qc j) -> d h qc j", j=512),
                o_raw[:], "d_onh", waits=ev_max(e_div_last))
            e_onh_user = e_onh_store
        e_onfm = sch.dma(
            "sync", o_nfm[:],
            onf_scr[s_].rearrange("hp h d t -> (h d) hp t"), "d_onfm",
            waits=ev_max(e_onh_store, e_onfm_users, e_effrow))
        e_x1v = None
        e_pj = None
        for tt in range(NT):
            e_yld = sch.dma("sync", y_tile, y_d[s_, tt * TS:(tt + 1) * TS, :],
                            "d_yl", waits=ev_max(e_x1_store, e_onfm))
            for half in range(2):
                for c in range(NCH):
                    sch.op("tensor", lambda e, c=c, half=half, tt=tt: e.matmul(
                        proj_ps[:, half], o_nfm[:, c, tt * TS:(tt + 1) * TS],
                        projw_sb[:, c, half * 512:(half + 1) * 512],
                        start=(c == 0), stop=False),
                        waits=ev_max(e_onfm, e_effrow, e_ones,
                                     e_projps_free[half]) if c == 0 else ())
                e_pj = sch.op("tensor", lambda e, half=half: e.matmul(
                    proj_ps[:, half], ones_r[0:1, :],
                    effrow_b[0:1, half * 512:(half + 1) * 512],
                    start=False, stop=True), inc="t")
                e_add = sch.op(
                    "vector", lambda e, half=half: e.tensor_tensor(
                        out=y_tile[:, half * 512:(half + 1) * 512],
                        in0=y_tile[:, half * 512:(half + 1) * 512],
                        in1=proj_ps[:, half], op=ALU.add),
                    waits=ev_max(e_pj, e_yld))
                e_projps_free[half] = e_add
                e_x1v = e_add
            e_x1_store = sch.dma("sync", x1_d[s_, tt * TS:(tt + 1) * TS, :],
                                 y_tile, "d_x1", waits=(e_x1v,))
        e_xn_users = ev_max(e_vmm, e_mm)
        e_onfm_users = e_pj
        e_vaug_users = ("t", sch.cnt["t"])
    sch.op("sync", None, waits=ev_max(sch.ev("d_x1")))




def build_l0b(nc):
    xnfm_d = nc.dram_tensor("xnfm", [SEQ_PER_CORE, NCH, 128, S], BF16,
                            kind="ExternalInput").ap()
    y_d = nc.dram_tensor("y_in", [SEQ_PER_CORE, S, D], F32, kind="ExternalInput").ap()
    qkvwT_d = nc.dram_tensor("qkv_wT", [D, 3 * D], F32, kind="ExternalInput").ap()
    qkvb_d = nc.dram_tensor("qkv_b_pc", [128, 24], F32, kind="ExternalInput").ap()
    projwT_d = nc.dram_tensor("proj_wT", [D, D], F32, kind="ExternalInput").ap()
    projbr_d = nc.dram_tensor("proj_b_row", [1, D], F32, kind="ExternalInput").ap()
    x1_d = nc.dram_tensor("x1_out", [SEQ_PER_CORE, S, D], F32,
                          kind="ExternalOutput").ap()
    dr = {"xnfm": xnfm_d, "y_in": y_d, "qkv_wT": qkvwT_d, "qkv_b_pc": qkvb_d,
          "proj_wT": projwT_d, "proj_b_row": projbr_d, "x1_out": x1_d}
    with contextlib.ExitStack() as ctx:
        sch = Sched(nc, ctx)
        block = ctx.enter_context(nc.Block())
        with contextlib.ExitStack() as pctx:
            emit_l0b(nc, sch, pctx.enter_context, dr)
            sch.build(block)
    return nc


def emit_l1a(nc, sch, E, dr, pfx=""):
    sch.pfx = pfx
    x1_d = dr["x1"]
    wka_d = dr["Wk_a"]
    wva_d = dr["Wv_a"]
    wktf_d = dr["WkT_f"]
    bkr_d = dr["bk_row"]
    mur_d = dr["mu_r_pc"]
    muk_d = dr["mu_k_pc"]
    ident_d = dr["ident"]
    NH = 16
    xn1fm_d = dr["xn1fm_out"]
    xrfm_d = dr["xrfm_out"]
    kafm_d = dr["kafm_out"]
    kv_d = dr["kv_out"]

    wka_sb = E(nc.sbuf_tensor(pfx + "wka_sb", [128, NCH, D], BF16))
    wva_sb = E(nc.sbuf_tensor(pfx + "wva_sb", [128, NCH, D], BF16))
    wktf_sb = E(nc.sbuf_tensor(pfx + "wktf_sb", [128, NCH, D], BF16))
    bkr_b = E(nc.sbuf_tensor(pfx + "bkr_b", [1, D], BF16))
    mur_sb = E(nc.sbuf_tensor(pfx + "mur_sb", [128, NCH], F32))
    muk_sb = E(nc.sbuf_tensor(pfx + "muk_sb", [128, NCH], F32))
    ones_r = E(nc.sbuf_tensor(pfx + "ones_r", [1, 128], BF16))
    idt = E(nc.sbuf_tensor(pfx + "idt", [128, 128], F32))
    lnb = {n: E(nc.sbuf_tensor(pfx + "ln_" + n, [128, 1], F32))
           for n in ("s1", "ssq", "mean", "ex2", "msq", "var", "sd", "rstd")}
    lnb["sqs"] = E(nc.sbuf_tensor(pfx + "ln_sqs", [128, D], F32))
    x1t = E(nc.sbuf_tensor(pfx + "x1t", [128, 2, D], F32))
    xn1_tm = E(nc.sbuf_tensor(pfx + "xn1_tm", [128, D], F32))
    win = E(nc.sbuf_tensor(pfx + "win", [128, NCH, 129], F32))
    stg = E(nc.sbuf_tensor(pfx + "stg", [128, NCH, 128], BF16))
    xr_stg = E(nc.sbuf_tensor(pfx + "xr_stg", [128, NCH, 128], BF16))
    dmix = E(nc.sbuf_tensor(pfx + "dmix", [128, NCH, 128], F32))
    tmix = E(nc.sbuf_tensor(pfx + "tmix", [128, NCH, 128], F32))
    xk_grp = E(nc.sbuf_tensor(pfx + "xk_grp", [128, NCH, 512], BF16))
    rl_sb = E(nc.sbuf_tensor(pfx + "rl_sb", [128, 512], F32))
    ka_stg = E(nc.sbuf_tensor(pfx + "ka_stg", [128, NCH, 512], BF16))
    kfeat = E(nc.sbuf_tensor(pfx + "kfeat", [128, NT, D], BF16))
    v_aug = E(nc.sbuf_tensor(pfx + "v_aug", [128, NT, NH * 65], BF16))
    e_sb = E(nc.sbuf_tensor(pfx + "e_sb", [128, D], F32))
    r_sb = E(nc.sbuf_tensor(pfx + "r_sb", [128, D], F32))
    kv_stg = E(nc.sbuf_tensor(pfx + "kv_stg", [64, NH, 65], BF16))
    tp32 = E(nc.psum_tensor(pfx + "tp32", [128, 4, 128], F32))
    k_ps = E(nc.psum_tensor(pfx + "k_ps", [128, 512], F32))
    v_ps = E(nc.psum_tensor(pfx + "v_ps", [128, 512], F32))
    kf_ps = E(nc.psum_tensor(pfx + "kf_ps", [128, 2, 512], F32))
    kv_ps = E(nc.psum_tensor(pfx + "kv_ps", [64, 2, 512], F32))


    e_wka = sch.dma("gpsimd", wka_sb[:],
                    wka_d.rearrange("(c p) o -> p c o", p=128), "d_wka")
    e_wva = sch.dma("gpsimd", wva_sb[:],
                    wva_d.rearrange("(c p) o -> p c o", p=128), "d_wva")
    e_wktf = sch.dma("gpsimd", wktf_sb[:],
                     wktf_d.rearrange("(c p) o -> p c o", p=128), "d_wktf")
    e_bkr = sch.dma("gpsimd", bkr_b[:], bkr_d, "d_bkr")
    e_mur = sch.dma("sync", mur_sb[:], mur_d, "d_mur")
    e_muk = sch.dma("sync", muk_sb[:], muk_d, "d_muk")
    e_id = sch.dma("sync", idt[:], ident_d, "d_id")
    e_ones = sch.op("vector", lambda e: e.memset(ones_r[:], 1.0))
    va_ones = v_aug[:].rearrange("p t (h c) -> p t h c", c=65)[:, :, :, 64:65]
    e_vones = sch.op("vector", lambda e: e.memset(va_ones, 1.0))

    x1_release = [None, None]
    e_stg_user = None
    e_xr_user = None
    e_ka_user = None
    e_tp32_war = None
    e_kps_free = None
    e_vps_free = None
    e_kf_free = None
    e_dmix_user = None
    e_win_user = None
    e_kfeat_users = None
    e_vaug_users = None
    e_kvstg_free = None
    e_er_user = None
    e_load = {}
    tiles = [(s_, t) for s_ in range(SEQ_PER_CORE) for t in range(NT)]

    def sched_load(idx):
        s_, t = tiles[idx]
        par = idx % 2
        e_load[idx] = sch.dma("sync", x1t[:, par],
                              x1_d[s_, t * TS:(t + 1) * TS, :],
                              "d_x%d" % par, waits=ev_max(x1_release[par]))

    sched_load(0)
    for idx, (s_, t) in enumerate(tiles):
        par = idx % 2
        t0 = t * TS
        g = t % 4
        if idx + 1 < len(tiles):
            sched_load(idx + 1)
        e_x = e_load[idx]
        if t == 0:
            e_z = sch.op("vector", lambda e: e.memset(win[:, :, 0:1], 0.0),
                         waits=ev_max(e_win_user))
        else:
            e_z = sch.op("vector", lambda e: e.tensor_copy(
                win[:, :, 0:1], win[:, :, 128:129]), waits=ev_max(e_win_user))
        e_xn = emit_ln(sch, lnb, x1t[:, par], xn1_tm[:], e_x)
        x1_release[par] = e_xn
        evs = []
        for cb in (0, 4):
            e_tps = None
            for s_i in range(4):
                c = cb + s_i
                e_tps = sch.op("tensor", lambda e, c=c, s_i=s_i: e.transpose(
                    tp32[:, s_i, 0:128], xn1_tm[:, c * 128:(c + 1) * 128], idt[:]),
                    waits=ev_max(e_xn, e_id, e_tp32_war) if s_i == 0 else (),
                    inc="t" if s_i == 3 else None)
            for s_i in range(4):
                c = cb + s_i
                sch.op("scalar", lambda e, c=c, s_i=s_i: e.copy(
                    out=win[:, c, 1:129], in_=tp32[:, s_i, 0:128]),
                    waits=ev_max(e_tps, e_z, e_dmix_user) if s_i == 0 else ())
                evs.append(sch.op("scalar", lambda e, c=c, s_i=s_i: e.copy(
                    out=stg[:, c, :], in_=tp32[:, s_i, 0:128]),
                    waits=ev_max(e_stg_user)))
            e_tp32_war = evs[-1]
        e_evac = evs[-1]
        e_win_user = None
        e_stg_user = sch.dma(
            "sync", xn1fm_d[s_].rearrange("c p t -> p c t")[:, :, t0:t0 + TS],
            stg[:], "d_stg", waits=(e_evac,))
        e_d = sch.op("vector", lambda e: e.tensor_tensor(
            out=dmix[:], in0=win[:, :, 1:129], in1=win[:, :, 0:128],
            op=ALU.subtract), waits=ev_max(e_evac))
        mu_r_bc = mur_sb[:].rearrange("p (c o) -> p c o", o=1)
        mu_k_bc = muk_sb[:].rearrange("p (c o) -> p c o", o=1)
        dv = dmix[:]
        dvb, murb = broadcast_tensor_aps(dv, mu_r_bc)
        _, mukb = broadcast_tensor_aps(dv, mu_k_bc)
        e_t1 = sch.op("vector", lambda e: e.tensor_tensor(
            out=tmix[:], in0=dvb, in1=murb, op=ALU.mult),
            waits=ev_max(e_d, e_mur))
        e_xr = sch.op("vector", lambda e: e.tensor_tensor(
            out=xr_stg[:], in0=tmix[:], in1=win[:, :, 0:128], op=ALU.add),
            waits=ev_max(e_xr_user))
        e_xr_user = sch.dma(
            "sync", xrfm_d[s_].rearrange("c p t -> p c t")[:, :, t0:t0 + TS],
            xr_stg[:], "d_xr", waits=(e_xr,))
        e_t2 = sch.op("vector", lambda e: e.tensor_tensor(
            out=tmix[:], in0=dvb, in1=mukb, op=ALU.mult), waits=ev_max(e_muk))
        e_xk = sch.op("vector", lambda e, g=g: e.tensor_tensor(
            out=xk_grp[:, :, g * 128:(g + 1) * 128], in0=tmix[:],
            in1=win[:, :, 0:128], op=ALU.add), waits=ev_max(e_kf_free))
        e_win_user = e_xk
        e_dmix_user = e_t2
        kf_evs = []
        for half in range(2):
            for c in range(NCH):
                e_kmm = sch.op("tensor", lambda e, c=c, half=half: e.matmul(
                    k_ps[:, 0:512], stg[:, c, :],
                    wka_sb[:, c, half * 512:(half + 1) * 512],
                    start=(c == 0), stop=False),
                    waits=ev_max(e_evac, e_wka, e_kps_free) if c == 0 else ())
            e_kmm = sch.op("tensor", lambda e, half=half: e.matmul(
                k_ps[:, 0:512], ones_r[0:1, :],
                bkr_b[0:1, half * 512:(half + 1) * 512],
                start=False, stop=True),
                waits=ev_max(e_ones, e_bkr), inc="t")
            e_e = sch.op("scalar", lambda e, half=half: e.activation(
                out=e_sb[:, half * 512:(half + 1) * 512], in_=k_ps[:, 0:512],
                func=AF.Exp), waits=ev_max(e_kmm, e_er_user))
            e_r = sch.op("scalar", lambda e, half=half: e.activation(
                out=r_sb[:, half * 512:(half + 1) * 512], in_=k_ps[:, 0:512],
                func=AF.Relu), waits=())
            e_kf = sch.op("vector", lambda e, half=half, t=t: e.scalar_tensor_tensor(
                out=kfeat[:, t, half * 512:(half + 1) * 512],
                in0=e_sb[:, half * 512:(half + 1) * 512], scalar=1.0,
                in1=r_sb[:, half * 512:(half + 1) * 512],
                op0=ALU.min, op1=ALU.add),
                waits=ev_max(e_r, e_kfeat_users))
            e_kps_free = e_r
            kf_evs.append(e_kf)
        e_kf_done = kf_evs[-1]
        e_er_user = e_kf_done
        v_evs = []
        for half in range(2):
            for c in range(NCH):
                e_vmm = sch.op("tensor", lambda e, c=c, half=half: e.matmul(
                    v_ps[:, 0:512], stg[:, c, :],
                    wva_sb[:, c, half * 512:(half + 1) * 512],
                    start=(c == 0), stop=(c == NCH - 1)),
                    waits=ev_max(e_evac, e_wva, e_vps_free, e_vones)
                    if c == 0 else (),
                    inc="t" if c == NCH - 1 else None)
            vslice = v_aug[:].rearrange(
                "p t (h c) -> p t h c", c=65)[:, t, half * 8:(half + 1) * 8, 0:64]
            e_vev = sch.op("scalar", lambda e, vslice=vslice: e.copy(
                out=vslice,
                in_=v_ps[:, 0:512].rearrange("p (h c) -> p h c", c=64)),
                waits=ev_max(e_vmm, e_vaug_users))
            e_vps_free = e_vev
            v_evs.append(e_vev)
        e_v_done = v_evs[-1]
        e_stg_user = ev_max(e_stg_user, ("t", sch.cnt["t"]))
        if g == 3:
            gi = t // 4
            ka_evs = []
            for dc in range(NCH):
                for cc in range(NCH):
                    e_km = sch.op("tensor", lambda e, dc=dc, cc=cc: e.matmul(
                        kf_ps[:, dc % 2], wktf_sb[:, cc, dc * 128:(dc + 1) * 128],
                        xk_grp[:, cc, :], start=(cc == 0), stop=(cc == NCH - 1)),
                        waits=ev_max(e_xk, e_wktf,
                                     ka_evs[-2] if len(ka_evs) >= 2 else None)
                        if cc == 0 else (),
                        inc="t" if cc == NCH - 1 else None)
                e_rl = sch.op("scalar", lambda e, dc=dc: e.activation(
                    out=rl_sb[:], in_=kf_ps[:, dc % 2], func=AF.Relu),
                    waits=ev_max(e_km))
                e_ka = sch.op("scalar", lambda e, dc=dc: e.activation(
                    out=ka_stg[:, dc, :], in_=rl_sb[:], func=AF.Square),
                    waits=ev_max(e_ka_user))
                ka_evs.append(e_ka)
            e_ka_user = sch.dma(
                "sync",
                kafm_d[s_].rearrange("c p t -> p c t")[:, :, gi * 512:(gi + 1) * 512],
                ka_stg[:], "d_ka", waits=(ka_evs[-1],))
            e_kf_free = ("a", sch.cnt["a"])
        if t == NT - 1:
            e_kv_evs = []
            e_kvps_free = {}
            for h in range(NH):
                pb = h % 2
                for tt in range(NT):
                    e_kvm = sch.op(
                        "tensor", lambda e, h=h, tt=tt, pb=pb: e.matmul(
                            kv_ps[:, pb, 0:65],
                            kfeat[:, tt, (h // 2) * 128 + (h % 2) * 64:
                                  (h // 2) * 128 + (h % 2) * 64 + 64],
                            v_aug[:].rearrange(
                                "p t (hh c) -> p t hh c", c=65)[:, tt, h, :],
                            start=(tt == 0), stop=(tt == NT - 1)),
                        waits=ev_max(e_kf_done, e_v_done, e_kvps_free.get(pb))
                        if tt == 0 else (),
                        inc="t" if tt == NT - 1 else None)
                e_kve = sch.op("scalar", lambda e, h=h, pb=pb: e.copy(
                    out=kv_stg[:, h, :], in_=kv_ps[0:64, pb, 0:65]),
                    waits=ev_max(e_kvm, e_kvstg_free))
                e_kvps_free[pb] = e_kve
                e_kv_evs.append(e_kve)
            e_kvstg_free = sch.dma(
                "sync", kv_d[s_].rearrange("h d c -> d h c"), kv_stg[:],
                "d_kv", waits=(e_kv_evs[-1],))
            e_kfeat_users = ("t", sch.cnt["t"])
            e_vaug_users = ("t", sch.cnt["t"])
    sch.op("sync", None, waits=ev_max(
        sch.ev("d_kv"), sch.ev("d_stg"),
        sch.ev("d_xr"), sch.ev("d_ka")))




def build_l1a(nc):
    x1_d = nc.dram_tensor("x1", [SEQ_PER_CORE, S, D], F32, kind="ExternalInput").ap()
    wka_d = nc.dram_tensor("Wk_a", [D, D], F32, kind="ExternalInput").ap()
    wva_d = nc.dram_tensor("Wv_a", [D, D], F32, kind="ExternalInput").ap()
    wktf_d = nc.dram_tensor("WkT_f", [D, D], F32, kind="ExternalInput").ap()
    bkr_d = nc.dram_tensor("bk_row", [1, D], F32, kind="ExternalInput").ap()
    mur_d = nc.dram_tensor("mu_r_pc", [128, NCH], F32, kind="ExternalInput").ap()
    muk_d = nc.dram_tensor("mu_k_pc", [128, NCH], F32, kind="ExternalInput").ap()
    ident_d = nc.dram_tensor("ident", [128, 128], F32, kind="ExternalInput").ap()
    xn1fm_d = nc.dram_tensor("xn1fm_out", [SEQ_PER_CORE, NCH, 128, S], BF16,
                             kind="ExternalOutput").ap()
    xrfm_d = nc.dram_tensor("xrfm_out", [SEQ_PER_CORE, NCH, 128, S], BF16,
                            kind="ExternalOutput").ap()
    kafm_d = nc.dram_tensor("kafm_out", [SEQ_PER_CORE, NCH, 128, S], BF16,
                            kind="ExternalOutput").ap()
    kv_d = nc.dram_tensor("kv_out", [SEQ_PER_CORE, 16, 64, 65], BF16,
                          kind="ExternalOutput").ap()
    dr = {"x1": x1_d, "Wk_a": wka_d, "Wv_a": wva_d, "WkT_f": wktf_d,
          "bk_row": bkr_d, "mu_r_pc": mur_d, "mu_k_pc": muk_d, "ident": ident_d,
          "xn1fm_out": xn1fm_d, "xrfm_out": xrfm_d, "kafm_out": kafm_d,
          "kv_out": kv_d}
    with contextlib.ExitStack() as ctx:
        sch = Sched(nc, ctx)
        block = ctx.enter_context(nc.Block())
        with contextlib.ExitStack() as pctx:
            emit_l1a(nc, sch, pctx.enter_context, dr)
            sch.build(block)
    return nc


def emit_l1b(nc, sch, E, dr, pfx=""):
    sch.pfx = pfx
    x1_d = dr["x1"]
    wqa_d = dr["Wq_a"]
    bq_d = dr["bq_pc"]
    wo_d = dr["Wo_a"]
    bor_d = dr["bo_row"]
    bv_d = dr["bv_pc"]
    wrt_d = dr["WrT_f"]
    wvt_d = dr["WvT_f"]
    NH = 16
    NP = 8
    xn1fm_d = dr["xn1fm"]
    xrfm_d = dr["xrfm"]
    kafm_d = dr["kafm"]
    kv_d = dr["kv"]
    out_d = dr["out_final"]
    nrm_scr = nc.dram_tensor("nrm2_scr", [SEQ_PER_CORE, 4, 2, 8 * 512], F32,
                             kind="Internal").ap()
    o1_scr = nc.dram_tensor("o1_scr", [SEQ_PER_CORE, 4, NP, 2, 64, 512], BF16,
                            kind="Internal").ap()

    wqa_sb = E(nc.sbuf_tensor(pfx + "wqa_sb", [128, NCH, D], BF16))
    wo_sb = E(nc.sbuf_tensor(pfx + "wo_sb", [128, NCH, D], BF16))
    wrt_sb = E(nc.sbuf_tensor(pfx + "wrt_sb", [128, NCH, D], BF16))
    wvt_sb = E(nc.sbuf_tensor(pfx + "wvt_sb", [128, NCH, D], BF16))
    bq_sb = E(nc.sbuf_tensor(pfx + "bq_sb", [128, NCH], F32))
    bv_sb = E(nc.sbuf_tensor(pfx + "bv_sb", [128, NCH], F32))
    bor_b = E(nc.sbuf_tensor(pfx + "bor_b", [1, D], BF16))
    bvb = E(nc.sbuf_tensor(pfx + "bvb", [128, NCH], BF16))
    ones_r = E(nc.sbuf_tensor(pfx + "ones_r", [1, 128], BF16))
    ones_f = E(nc.sbuf_tensor(pfx + "ones_f", [1, 64], F32))
    effrow_b = E(nc.sbuf_tensor(pfx + "effrow_b", [1, D], BF16))
    kv_sb = E(nc.sbuf_tensor(pfx + "kv_sb", [128, NP, 65], BF16))
    xn1_qc = E(nc.sbuf_tensor(pfx + "xn1_qc", [128, NCH, 512], BF16))
    qfeat = E(nc.sbuf_tensor(pfx + "qfeat", [128, NCH, 512], BF16))
    e_sb = E(nc.sbuf_tensor(pfx + "e_sb", [128, 512], F32))
    r_sb = E(nc.sbuf_tensor(pfx + "r_sb", [128, 512], F32))
    o_raw = E(nc.sbuf_tensor(pfx + "o_raw", [64, NH, 512], BF16))
    nrm_sb = E(nc.sbuf_tensor(pfx + "nrm_sb", [128, 8 * 512], F32))
    rin_sb = E(nc.sbuf_tensor(pfx + "rin_sb", [128, 32], F32))
    o1_nfm = E(nc.sbuf_tensor(pfx + "o1_nfm", [128, NCH, 512], BF16))
    x1t = E(nc.sbuf_tensor(pfx + "x1t", [128, D], F32))
    xr_t = E(nc.sbuf_tensor(pfx + "xr_t", [128, 2, NCH, 128], BF16))
    ka_t = E(nc.sbuf_tensor(pfx + "ka_t", [128, 2, NCH, 128], BF16))
    rsg_sb = E(nc.sbuf_tensor(pfx + "rsg_sb", [128, D], F32))
    fv_sb = E(nc.sbuf_tensor(pfx + "fv_sb", [128, D], F32))
    q_ps = E(nc.psum_tensor(pfx + "q_ps", [128, 512], F32))
    o1A = E(nc.psum_tensor(pfx + "o1A", [65, 512], F32))
    o1B = E(nc.psum_tensor(pfx + "o1B", [65, 512], F32))
    nbc_ps = E(nc.psum_tensor(pfx + "nbc_ps", [64, 512], F32))
    pr_ps = E(nc.psum_tensor(pfx + "pr_ps", [128, 2, 512], F32))
    rv_ps = E(nc.psum_tensor(pfx + "rv_ps", [128, 2, 512], F32))


    e_wqa = sch.dma("gpsimd", wqa_sb[:],
                    wqa_d.rearrange("(c p) o -> p c o", p=128), "d_wqa")
    e_wo = sch.dma("gpsimd", wo_sb[:],
                   wo_d.rearrange("(c p) o -> p c o", p=128), "d_wo")
    e_wrt = sch.dma("gpsimd", wrt_sb[:],
                    wrt_d.rearrange("(c p) o -> p c o", p=128), "d_wrt")
    e_wvt = sch.dma("gpsimd", wvt_sb[:],
                    wvt_d.rearrange("(c p) o -> p c o", p=128), "d_wvt")
    e_bor = sch.dma("gpsimd", bor_b[:], bor_d, "d_bor")
    e_bq = sch.dma("sync", bq_sb[:], bq_d, "d_bq")
    e_bv = sch.dma("sync", bv_sb[:], bv_d, "d_bv")
    e_ones = sch.op("vector", lambda e: e.memset(ones_r[:], 1.0))
    e_ones_f = sch.op("vector", lambda e: e.memset(ones_f[:], 1.0))
    e_bvb = sch.op("vector", lambda e: e.tensor_copy(bvb[:], bv_sb[:]),
                   waits=(e_bv,))

    e_eff = None
    for half in range(2):
        for c in range(NCH):
            sch.op("tensor", lambda e, c=c, half=half: e.matmul(
                pr_ps[0:1, half], bvb[:, c:c + 1],
                wo_sb[:, c, half * 512:half * 512 + 512],
                start=(c == 0), stop=False),
                waits=ev_max(e_bvb, e_wo) if (c == 0 and half == 0) else ())
        e_eff = sch.op("tensor", lambda e, half=half: e.matmul(
            pr_ps[0:1, half], ones_r[0:1, 0:1],
            bor_b[0:1, half * 512:half * 512 + 512],
            start=False, stop=True),
            waits=ev_max(e_ones, e_bor) if half == 0 else (), inc="t")
    e_effrow = sch.op("scalar", lambda e: e.copy(
        out=effrow_b[:], in_=pr_ps[0:1, :].rearrange("p h j -> p (h j)")),
        waits=(e_eff,))

    e_kv_user = None
    e_xn_user = None
    e_qf_user = None
    e_er_user = None
    e_oraw_evac_war = None
    e_nrm_user = None
    e_nbc_free = None
    e_o1_store = None
    e_onfm_user = None
    e_x1_user = None
    e_xrka_rel = [None, None]
    e_out_store = None
    e_qps_free = None
    e_prps_free = {0: None, 1: None}
    e_rvps_sig = {0: None, 1: None}
    e_rvps_ff = {0: None, 1: None}
    for s_ in range(SEQ_PER_CORE):
        e_kv = sch.dma("sync", kv_sb[:],
                       kv_d[s_].rearrange("(hp h2) d c -> (h2 d) hp c", h2=2),
                       "d_kv", waits=ev_max(e_kv_user))
        for qc in range(4):
            e_xn = sch.dma("sync", xn1_qc[:],
                           xn1fm_d[s_].rearrange(
                               "c p t -> p c t")[:, :, qc * 512:(qc + 1) * 512],
                           "d_xn", waits=ev_max(e_xn_user))
            for dc in range(NCH):
                for c in range(NCH):
                    e_qmm = sch.op("tensor", lambda e, dc=dc, c=c: e.matmul(
                        q_ps[:, 0:512], wqa_sb[:, c, dc * 128:(dc + 1) * 128],
                        xn1_qc[:, c, :], start=(c == 0), stop=(c == NCH - 1)),
                        waits=ev_max(e_xn, e_wqa, e_qps_free) if c == 0 else (),
                        inc="t" if c == NCH - 1 else None)
                e_e = sch.op("scalar", lambda e, dc=dc: e.activation(
                    out=e_sb[:], in_=q_ps[:, 0:512], func=AF.Exp,
                    bias=bq_sb[:, dc:dc + 1]),
                    waits=ev_max(e_qmm, e_bq, e_er_user))
                e_r = sch.op("scalar", lambda e, dc=dc: e.activation(
                    out=r_sb[:], in_=q_ps[:, 0:512], func=AF.Relu,
                    bias=bq_sb[:, dc:dc + 1]), waits=())
                e_qf = sch.op("vector", lambda e, dc=dc: e.scalar_tensor_tensor(
                    out=qfeat[:, dc, :], in0=e_sb[:], scalar=1.0,
                    in1=r_sb[:], op0=ALU.min, op1=ALU.add),
                    waits=ev_max(e_r, e_qf_user))
                e_er_user = e_qf
                e_qps_free = e_r
            e_qf_done = e_qf
            e_div_all = []
            for grp in range(2):
                evs = []
                for hpl in range(4):
                    hp = grp * 4 + hpl
                    for hi, o_ps in enumerate((o1A, o1B)):
                        b0 = hi * 64
                        e_oq = sch.op(
                            "tensor", lambda e, hp=hp, hi=hi, b0=b0, o_ps=o_ps:
                            e.matmul(
                                o_ps[:, 0:512], kv_sb[b0:b0 + 64, hp, :],
                                qfeat[b0:b0 + 64, hp, :],
                                start=True, stop=True, tile_position=(b0, 0)),
                            waits=ev_max(e_qf_done, e_kv, e_oraw_evac_war),
                            inc="t")
                        h = 2 * hp + hi
                        evs.append(sch.op(
                            "scalar", lambda e, o_ps=o_ps, h=h: e.copy(
                                out=o_raw[:, h, :], in_=o_ps[0:64, 0:512]),
                            waits=ev_max(e_oq, e_o1_store)))
                        slot = (hp % 4) * 2 + hi
                        evs.append(sch.op(
                            "scalar", lambda e, o_ps=o_ps, slot=slot: e.copy(
                                out=nrm_sb[64:65, slot * 512:(slot + 1) * 512],
                                in_=o_ps[64:65, 0:512]),
                            waits=ev_max(e_nrm_user)))
                        e_oraw_evac_war = evs[-1]
                e_h1 = sch.dma("sync", nrm_scr[s_, qc, grp], nrm_sb[64:65, :],
                               "d_nbo", waits=ev_max(*evs))
                e_nrm_user = e_h1
                e_h2 = sch.dma("sync", rin_sb[:],
                               nrm_scr[s_, qc, grp].rearrange("(r j) -> r j", r=128),
                               "d_nbi", waits=(e_h1,))
                e_cl = sch.op("vector", lambda e: e.tensor_scalar(
                    out=rin_sb[:], in0=rin_sb[:], scalar1=1e-6, scalar2=None,
                    op0=ALU.max), waits=(e_h2,))
                e_rcp = sch.op("vector", lambda e: e.reciprocal(
                    out=rin_sb[:], in_=rin_sb[:]), waits=(e_cl,))
                e_h3 = sch.dma("sync",
                               nrm_scr[s_, qc, grp].rearrange("(r j) -> r j", r=128),
                               rin_sb[:], "d_nbo", waits=(e_rcp,))
                e_h4 = sch.dma("sync", nrm_sb[0:1, :],
                               nrm_scr[s_, qc, grp].rearrange("j -> () j"),
                               "d_nbi", waits=(e_h3,))
                for hpl in range(4):
                    hp = grp * 4 + hpl
                    for hi in range(2):
                        slot = hpl * 2 + hi
                        h = 2 * hp + hi
                        e_nbc = sch.op(
                            "tensor", lambda e, slot=slot: e.matmul(
                                nbc_ps[:, 0:512], ones_f[0:1, :],
                                nrm_sb[0:1, slot * 512:(slot + 1) * 512],
                                start=True, stop=True),
                            waits=ev_max(e_h4, e_nbc_free, e_ones_f), inc="t")
                        e_div = sch.op(
                            "vector", lambda e, h=h: e.tensor_tensor(
                                out=o_raw[:, h, :], in0=o_raw[:, h, :],
                                in1=nbc_ps[0:64, 0:512], op=ALU.mult),
                            waits=ev_max(e_nbc, e_o1_store))
                        e_nbc_free = e_div
                        e_div_all.append(e_div)
            e_o1_store = sch.dma(
                "sync", o1_scr[s_, qc].rearrange("hp h d j -> d hp h j"),
                o_raw[:].rearrange("d (hp h) j -> d hp h j", h=2), "d_o1o",
                waits=ev_max(*e_div_all))
            e_onfm = sch.dma(
                "sync", o1_nfm[:],
                o1_scr[s_, qc].rearrange("hp h d j -> (h d) hp j"), "d_o1i",
                waits=ev_max(e_o1_store, e_onfm_user))
            e_pj = None
            for tl in range(4):
                tt = qc * 4 + tl
                par = tl % 2
                t0 = tt * TS
                e_x1 = sch.dma("sync", x1t[:], x1_d[s_, t0:t0 + TS, :], "d_x1",
                               waits=ev_max(e_x1_user))
                e_xr = sch.dma("sync", xr_t[:, par],
                               xrfm_d[s_].rearrange(
                                   "c p t -> p c t")[:, :, t0:t0 + TS],
                               "d_xrt%d" % par, waits=ev_max(e_xrka_rel[par]))
                e_ka = sch.dma("sync", ka_t[:, par],
                               kafm_d[s_].rearrange(
                                   "c p t -> p c t")[:, :, t0:t0 + TS],
                               "d_kat%d" % par, waits=ev_max(e_xrka_rel[par]))
                for half in range(2):
                    for c in range(NCH):
                        sch.op("tensor", lambda e, c=c, half=half, tl=tl: e.matmul(
                            pr_ps[:, half], o1_nfm[:, c, tl * TS:(tl + 1) * TS],
                            wo_sb[:, c, half * 512:(half + 1) * 512],
                            start=(c == 0), stop=False),
                            waits=ev_max(e_onfm, e_effrow, e_ones,
                                         e_prps_free[half]) if c == 0 else ())
                    e_pj = sch.op("tensor", lambda e, half=half: e.matmul(
                        pr_ps[:, half], ones_r[0:1, :],
                        effrow_b[0:1, half * 512:(half + 1) * 512],
                        start=False, stop=True), inc="t")
                    for c in range(NCH):
                        e_rm = sch.op("tensor", lambda e, c=c, half=half, par=par:
                                      e.matmul(
                                          rv_ps[:, half], xr_t[:, par, c, :],
                                          wrt_sb[:, c, half * 512:(half + 1) * 512],
                                          start=(c == 0), stop=(c == NCH - 1)),
                                      waits=ev_max(e_xr, e_wrt,
                                                   e_rvps_ff[half])
                                      if c == 0 else (),
                                      inc="t" if c == NCH - 1 else None)
                    e_sg = sch.op("scalar", lambda e, half=half: e.activation(
                        out=rsg_sb[:, half * 512:(half + 1) * 512],
                        in_=rv_ps[:, half], func=AF.Sigmoid),
                        waits=ev_max(e_rm))
                    e_rvps_sig[half] = e_sg
                    e_a1 = sch.op("vector", lambda e, half=half: e.tensor_tensor(
                        out=x1t[:, half * 512:(half + 1) * 512],
                        in0=x1t[:, half * 512:(half + 1) * 512],
                        in1=pr_ps[:, half], op=ALU.add),
                        waits=ev_max(e_pj, e_x1))
                    e_prps_free[half] = e_a1
                for half in range(2):
                    for c in range(NCH):
                        e_om = sch.op("tensor", lambda e, c=c, half=half, par=par:
                                      e.matmul(
                                          rv_ps[:, half], ka_t[:, par, c, :],
                                          wvt_sb[:, c, half * 512:(half + 1) * 512],
                                          start=(c == 0), stop=(c == NCH - 1)),
                                      waits=ev_max(e_ka, e_wvt, e_rvps_sig[half])
                                      if c == 0 else (),
                                      inc="t" if c == NCH - 1 else None)
                    e_f = sch.op("vector", lambda e, half=half: e.tensor_tensor(
                        out=fv_sb[:, half * 512:(half + 1) * 512],
                        in0=rsg_sb[:, half * 512:(half + 1) * 512],
                        in1=rv_ps[:, half], op=ALU.mult),
                        waits=ev_max(e_om, e_out_store))
                    e_rvps_ff[half] = e_f
                    e_o = sch.op("vector", lambda e, half=half: e.tensor_tensor(
                        out=x1t[:, half * 512:(half + 1) * 512],
                        in0=x1t[:, half * 512:(half + 1) * 512],
                        in1=fv_sb[:, half * 512:(half + 1) * 512], op=ALU.add))
                e_xrka_rel[par] = ("t", sch.cnt["t"])
                e_out_store = sch.dma("sync", out_d[s_, t0:t0 + TS, :], x1t[:],
                                      "d_out", waits=(e_o,))
                e_x1_user = e_out_store
            e_onfm_user = e_pj
            e_xn_user = e_qmm
            e_qf_user = ("t", sch.cnt["t"])
        e_kv_user = ("t", sch.cnt["t"])
    sch.op("sync", None, waits=ev_max(sch.ev("d_out")))




def build_l1b(nc):
    x1_d = nc.dram_tensor("x1", [SEQ_PER_CORE, S, D], F32, kind="ExternalInput").ap()
    xn1fm_d = nc.dram_tensor("xn1fm", [SEQ_PER_CORE, NCH, 128, S], BF16,
                             kind="ExternalInput").ap()
    xrfm_d = nc.dram_tensor("xrfm", [SEQ_PER_CORE, NCH, 128, S], BF16,
                            kind="ExternalInput").ap()
    kafm_d = nc.dram_tensor("kafm", [SEQ_PER_CORE, NCH, 128, S], BF16,
                            kind="ExternalInput").ap()
    kv_d = nc.dram_tensor("kv", [SEQ_PER_CORE, 16, 64, 65], BF16,
                          kind="ExternalInput").ap()
    wqa_d = nc.dram_tensor("Wq_a", [D, D], F32, kind="ExternalInput").ap()
    bq_d = nc.dram_tensor("bq_pc", [128, NCH], F32, kind="ExternalInput").ap()
    wo_d = nc.dram_tensor("Wo_a", [D, D], F32, kind="ExternalInput").ap()
    bor_d = nc.dram_tensor("bo_row", [1, D], F32, kind="ExternalInput").ap()
    bv_d = nc.dram_tensor("bv_pc", [128, NCH], F32, kind="ExternalInput").ap()
    wrt_d = nc.dram_tensor("WrT_f", [D, D], F32, kind="ExternalInput").ap()
    wvt_d = nc.dram_tensor("WvT_f", [D, D], F32, kind="ExternalInput").ap()
    out_d = nc.dram_tensor("out_final", [SEQ_PER_CORE, S, D], F32,
                           kind="ExternalOutput").ap()
    dr = {"x1": x1_d, "xn1fm": xn1fm_d, "xrfm": xrfm_d, "kafm": kafm_d,
          "kv": kv_d, "Wq_a": wqa_d, "bq_pc": bq_d, "Wo_a": wo_d,
          "bo_row": bor_d, "bv_pc": bv_d, "WrT_f": wrt_d, "WvT_f": wvt_d,
          "out_final": out_d}
    with contextlib.ExitStack() as ctx:
        sch = Sched(nc, ctx)
        block = ctx.enter_context(nc.Block())
        with contextlib.ExitStack() as pctx:
            emit_l1b(nc, sch, pctx.enter_context, dr)
            sch.build(block)
    return nc


def build_fused(nc):
    EI, EO, IN = "ExternalInput", "ExternalOutput", "Internal"
    t = nc.dram_tensor
    x_d = t("x", [SEQ_PER_CORE, S, D], F32, kind=EI).ap()
    wg_d = t("w_gate", [I, D], F32, kind=EI).ap()
    wu_d = t("w_up", [I, D], F32, kind=EI).ap()
    wd_d = t("w_down", [D, I], F32, kind=EI).ap()
    ident_d = t("ident", [128, 128], F32, kind=EI).ap()
    qkvwT_d = t("qkv_wT", [D, 3 * D], F32, kind=EI).ap()
    qkvb_d = t("qkv_b_pc", [128, 24], F32, kind=EI).ap()
    projwT_d = t("proj_wT", [D, D], F32, kind=EI).ap()
    projbr_d = t("proj_b_row", [1, D], F32, kind=EI).ap()
    wka_d = t("Wk_a", [D, D], F32, kind=EI).ap()
    wva_d = t("Wv_a", [D, D], F32, kind=EI).ap()
    wktf_d = t("WkT_f", [D, D], F32, kind=EI).ap()
    bkr_d = t("bk_row", [1, D], F32, kind=EI).ap()
    mur_d = t("mu_r_pc", [128, NCH], F32, kind=EI).ap()
    muk_d = t("mu_k_pc", [128, NCH], F32, kind=EI).ap()
    wqa_d = t("Wq_a", [D, D], F32, kind=EI).ap()
    bq_d = t("bq_pc", [128, NCH], F32, kind=EI).ap()
    wo_d = t("Wo_a", [D, D], F32, kind=EI).ap()
    bor_d = t("bo_row", [1, D], F32, kind=EI).ap()
    bv_d = t("bv_pc", [128, NCH], F32, kind=EI).ap()
    wrt_d = t("WrT_f", [D, D], F32, kind=EI).ap()
    wvt_d = t("WvT_f", [D, D], F32, kind=EI).ap()
    out_d = t("out_final", [SEQ_PER_CORE, S, D], F32, kind=EO).ap()
    y_b = t("y_buf", [SEQ_PER_CORE, S, D], F32, kind=IN).ap()
    xnfm_b = t("xnfm_buf", [SEQ_PER_CORE, NCH, 128, S], BF16, kind=IN).ap()
    x1_b = t("x1_buf", [SEQ_PER_CORE, S, D], F32, kind=IN).ap()
    xn1fm_b = t("xn1fm_buf", [SEQ_PER_CORE, NCH, 128, S], BF16, kind=IN).ap()
    xrfm_b = t("xrfm_buf", [SEQ_PER_CORE, NCH, 128, S], BF16, kind=IN).ap()
    kafm_b = t("kafm_buf", [SEQ_PER_CORE, NCH, 128, S], BF16, kind=IN).ap()
    kv_b = t("kv_buf", [SEQ_PER_CORE, 16, 64, 65], BF16, kind=IN).ap()

    phases = [
        ("a_", emit_l0a, {"x": x_d, "w_gate": wg_d, "w_up": wu_d,
                          "w_down": wd_d, "ident": ident_d,
                          "y_out": y_b, "xnfm_out": xnfm_b}),
        ("b_", emit_l0b, {"xnfm": xnfm_b, "y_in": y_b, "qkv_wT": qkvwT_d,
                          "qkv_b_pc": qkvb_d, "proj_wT": projwT_d,
                          "proj_b_row": projbr_d, "x1_out": x1_b}),
        ("c_", emit_l1a, {"x1": x1_b, "Wk_a": wka_d, "Wv_a": wva_d,
                          "WkT_f": wktf_d, "bk_row": bkr_d, "mu_r_pc": mur_d,
                          "mu_k_pc": muk_d, "ident": ident_d,
                          "xn1fm_out": xn1fm_b, "xrfm_out": xrfm_b,
                          "kafm_out": kafm_b, "kv_out": kv_b}),
        ("d_", emit_l1b, {"x1": x1_b, "xn1fm": xn1fm_b, "xrfm": xrfm_b,
                          "kafm": kafm_b, "kv": kv_b, "Wq_a": wqa_d,
                          "bq_pc": bq_d, "Wo_a": wo_d, "bo_row": bor_d,
                          "bv_pc": bv_d, "WrT_f": wrt_d, "WvT_f": wvt_d,
                          "out_final": out_d}),
    ]
    with contextlib.ExitStack() as ctx:
        sch = Sched(nc, ctx)
        block = ctx.enter_context(nc.Block())
        for pfx, emitf, dr in phases:
            with contextlib.ExitStack() as pctx:
                emitf(nc, sch, pctx.enter_context, dr, pfx=pfx)
                sch.barrier()
                sch.build(block)
        if END_CLEAR:
            sch.end_clear()
            sch.build(block)
    return nc


def prep_in_maps(inputs):
    f32 = np.float32
    x_full = np.ascontiguousarray(inputs["L0_attn_x"], dtype=f32)
    qkv_b = np.asarray(inputs["L0_attn_qkv_b"], dtype=f32)
    sh = {
        "w_gate": np.ascontiguousarray(inputs["L0_ffn_w_gate"], dtype=f32),
        "w_up": np.ascontiguousarray(inputs["L0_ffn_w_up"], dtype=f32),
        "w_down": np.ascontiguousarray(inputs["L0_ffn_w_down"], dtype=f32),
        "ident": np.eye(128, dtype=f32),
        "qkv_wT": np.ascontiguousarray(np.asarray(inputs["L0_attn_qkv_w"], f32).T),
        "qkv_b_pc": np.ascontiguousarray(qkv_b.reshape(24, 128).T),
        "proj_wT": np.ascontiguousarray(np.asarray(inputs["L0_attn_proj_w"], f32).T),
        "proj_b_row": np.ascontiguousarray(
            np.asarray(inputs["L0_attn_proj_b"], f32).reshape(1, D)),
        "Wk_a": np.ascontiguousarray(np.asarray(inputs["L1_attn_Wk"], f32)),
        "Wv_a": np.ascontiguousarray(np.asarray(inputs["L1_attn_Wv"], f32)),
        "WkT_f": np.ascontiguousarray(np.asarray(inputs["L1_ffn_W_k"], f32).T),
        "bk_row": np.ascontiguousarray(
            np.asarray(inputs["L1_attn_bk"], f32).reshape(1, D)),
        "mu_r_pc": np.ascontiguousarray(
            np.asarray(inputs["L1_ffn_mu_r"], f32).reshape(8, 128).T),
        "mu_k_pc": np.ascontiguousarray(
            np.asarray(inputs["L1_ffn_mu_k"], f32).reshape(8, 128).T),
        "Wq_a": np.ascontiguousarray(np.asarray(inputs["L1_attn_Wq"], f32)),
        "bq_pc": np.ascontiguousarray(
            np.asarray(inputs["L1_attn_bq"], f32).reshape(8, 128).T),
        "Wo_a": np.ascontiguousarray(np.asarray(inputs["L1_attn_Wo"], f32)),
        "bo_row": np.ascontiguousarray(
            np.asarray(inputs["L1_attn_bo"], f32).reshape(1, D)),
        "bv_pc": np.ascontiguousarray(
            np.asarray(inputs["L1_attn_bv"], f32).reshape(8, 128).T),
        "WrT_f": np.ascontiguousarray(np.asarray(inputs["L1_ffn_W_r"], f32).T),
        "WvT_f": np.ascontiguousarray(np.asarray(inputs["L1_ffn_W_v"], f32).T),
    }
    return [dict(sh, x=x_full[2 * c:2 * c + 2]) for c in range(NCORES)]



_NCS = {}


def _get_nc(name, builder):
    if name not in _NCS:
        _NCS[name] = builder(bass.Bass("TRN2", target_bir_lowering=False))
    return _NCS[name]


def kernel(**inputs):
    in_maps = prep_in_maps(inputs)
    nc = _get_nc("fused", build_fused)
    res = run_bass_kernel_spmd(nc, in_maps, core_ids=list(range(NCORES)))
    return np.concatenate([np.asarray(r["out_final"]) for r in res.results], axis=0)


# version 11
# speedup vs baseline: 1.1008x; 1.0069x over previous
import contextlib
import numpy as np
import ml_dtypes
import concourse.bass as bass
import concourse.mybir as mybir
from concourse.bass import broadcast_tensor_aps
from concourse.bass_utils import run_bass_kernel_spmd

F32 = mybir.dt.float32
BF16 = mybir.dt.bfloat16
AF = mybir.ActivationFunctionType
ALU = mybir.AluOpType
AX = mybir.AxisListType
bfdt = ml_dtypes.bfloat16

NCORES = 8
SEQ_PER_CORE = 2
S = 2048
D = 1024
I = 2816
TS = 128
NT = S // TS
NCH = D // 128
EPS_LN = 1e-5

ENGS = ("sync", "gpsimd", "scalar", "vector", "tensor")
OWN_WAITS = True
DRAIN_FUSABLE = True
END_CLEAR = True
BATCH_INCS = ("tensor",)


class Sched:

    def __init__(self, nc, ctx, sem_names=("t", "v", "a")):
        self.nc = nc
        self.ctx = ctx
        self.pfx = ""
        self.swdge_sems = set()
        self.items = {e: [] for e in ENGS}
        self.sems = {}
        self.cnt = {}
        for n in sem_names:
            self.sem(n)

    def ev(self, name):
        if name.startswith("d_"):
            name = self.pfx + name
        return (name, self.cnt[name])

    def barrier(self):
        snap = tuple((n, c) for n, c in self.cnt.items() if c > 0)
        for e in ENGS:
            self.op(e, None, waits=snap)

    def end_clear(self):
        self.barrier()
        self.sem("fin")
        for e in ENGS:
            if e != "gpsimd":
                self.op(e, lambda en: en.nop(), inc="fin")
        fin_target = self.cnt["fin"]
        sems = [h for n, h in self.sems.items() if n not in self.swdge_sems]

        def fn(e, sems=sems):
            ins = None
            for h in sems:
                ins = e.sem_clear(h)
            return ins
        self.op("gpsimd", fn, waits=(("fin", fin_target),))

    def sem(self, name):
        if name not in self.sems:
            self.sems[name] = self.ctx.enter_context(
                self.nc.semaphore("sem_" + name))
            self.cnt[name] = 0
        return name

    def op(self, eng, fn, waits=(), inc=None, amt=1):
        own = {"vector": "v", "scalar": "a"}.get(eng)
        own_wait = None
        if own is not None and fn is not None:
            if inc is None:
                inc = own
            own_wait = (own, self.cnt[own])
        ev = None
        waits = tuple(w for w in waits if w is not None)
        abs_cnt = None
        if inc is not None:
            self.cnt[inc] += amt
            abs_cnt = self.cnt[inc]
            ev = (inc, abs_cnt)
        self.items[eng].append((fn, waits, inc, amt, own_wait, abs_cnt))
        return ev

    def dma(self, eng, out, in_, sem, waits=()):
        sem = self.pfx + sem
        self.sem(sem)
        if eng == "gpsimd":
            self.swdge_sems.add(sem)
        return self.op(eng, lambda e, o=out, i=in_: e.dma_start(out=o, in_=i),
                       waits=waits, inc=sem, amt=16)

    def build(self, block):
        sems = self.sems
        waited_vals = {}
        for e in ENGS:
            for fn, waits, inc, amt, own_wait, abs_cnt in self.items[e]:
                for (s_, v_) in waits:
                    waited_vals.setdefault(s_, set()).add(v_)
        for e in ENGS:
            items = self.items[e]
            if not items:
                continue
            use_drain = e in ("vector", "scalar")
            batch = e in BATCH_INCS
            last_idx = {}
            for i, (fn, waits, inc, amt, own_wait, abs_cnt) in enumerate(items):
                if inc is not None and fn is not None:
                    last_idx[inc] = i
            plan = []
            pending = {}
            for i, (fn, waits, inc, amt, own_wait, abs_cnt) in enumerate(items):
                em = None
                if inc is not None and fn is not None:
                    if amt == 16:
                        em = (inc, 16)
                    elif not batch:
                        em = (inc, amt)
                    else:
                        pending[inc] = pending.get(inc, 0) + amt
                        if (abs_cnt in waited_vals.get(inc, ()) or
                                i == last_idx[inc]):
                            em = (inc, pending[inc])
                            pending[inc] = 0
                plan.append(em)

            def make_emit(items=items, plan=plan, use_drain=use_drain):
                def emit(engobj):
                    waited = {}
                    for (fn, waits, inc, amt, own_wait, abs_cnt), em in zip(items, plan):
                        if OWN_WAITS and own_wait is not None:
                            waits = (own_wait,) + waits
                        for (s_, v_) in waits:
                            if v_ > 0 and waited.get(s_, -1) < v_:
                                engobj.wait_ge(sems[s_], v_)
                                waited[s_] = v_
                        if fn is None:
                            continue
                        instr = fn(engobj)
                        if em is not None:
                            sname, a2 = em
                            if a2 == 16 or not use_drain:
                                instr.then_inc(sems[sname], a2)
                            else:
                                engobj.drain(fusable=DRAIN_FUSABLE).then_inc(
                                    sems[sname], a2)
                return emit
            getattr(block, e)(make_emit())
        self.items = {e: [] for e in ENGS}


def ev_max(*evs):
    best = {}
    for ev in evs:
        if ev is None or (isinstance(ev, tuple) and len(ev) == 0):
            continue
        if isinstance(ev, tuple) and isinstance(ev[0], tuple):
            for (s_, v_) in ev:
                if best.get(s_, -1) < v_:
                    best[s_] = v_
        else:
            s_, v_ = ev
            if best.get(s_, -1) < v_:
                best[s_] = v_
    return tuple(best.items())



def emit_quant(sch, q, src, out_bf16, C, wv, wa):
    NB = C // 16
    absx, sgn, a1, c5 = q["absx"], q["sgn"], q["a1"], q["c5"]
    bm, sc, rs, hs = q["bm"], q["sc"], q["rs"], q["hs"]
    war = q.get("_last")
    wa = ev_max(wa, war)
    wv = ev_max(wv)
    e_absx = sch.op("scalar", lambda e: e.activation(
        out=absx[:, 0:C], in_=src, func=AF.Abs), waits=wa, inc="a")
    e_sgn = sch.op("scalar", lambda e: e.activation(
        out=sgn[:, 0:C], in_=src, func=AF.Sign), waits=wa, inc="a")
    e_bm = sch.op("vector", lambda e: e.tensor_reduce(
        out=bm[:, 0:NB].rearrange("p (b o) -> p b o", o=1),
        in_=src.rearrange("p (b s) -> p b s", s=16),
        axis=AX.X, op=ALU.max, apply_absolute_value=True), waits=wv, inc="v")
    e_sc = sch.op("vector", lambda e: e.tensor_scalar(
        out=sc[:, 0:NB], in0=bm[:, 0:NB], scalar1=1.0 / 6.0, scalar2=1e-12,
        op0=ALU.mult, op1=ALU.max), waits=(e_bm,), inc="v")
    e_rs = sch.op("vector", lambda e: e.reciprocal(
        out=rs[:, 0:NB], in_=sc[:, 0:NB]), waits=(e_sc,), inc="v")
    sch.op("vector", lambda e: e.tensor_scalar(
        out=hs[:, 0:NB], in0=sc[:, 0:NB], scalar1=0.5, scalar2=None, op0=ALU.mult))
    av = absx[:, 0:C].rearrange("p (b s) -> p b s", s=16)
    rv = rs[:, 0:NB].rearrange("p (b o) -> p b o", o=1)
    avb, rvb = broadcast_tensor_aps(av, rv)
    sch.op("vector", lambda e: e.tensor_tensor(out=av, in0=avb, in1=rvb, op=ALU.mult),
           waits=ev_max(e_absx, e_rs))
    u = absx[:, 0:C]
    sch.op("vector", lambda e: e.tensor_scalar(
        out=c5[:, 0:C], in0=u, scalar1=5.0, scalar2=2.0, op0=ALU.is_gt, op1=ALU.mult))
    for t_ in (3.5, 2.5):
        sch.op("vector", lambda e, t_=t_: e.scalar_tensor_tensor(
            out=c5[:, 0:C], in0=u, scalar=t_, in1=c5[:, 0:C],
            op0=ALU.is_gt, op1=ALU.add))
    sch.op("vector", lambda e: e.tensor_scalar(
        out=a1[:, 0:C], in0=u, scalar1=0.25, scalar2=None, op0=ALU.is_gt))
    for t_ in (0.75, 1.25, 1.75):
        sch.op("vector", lambda e, t_=t_: e.scalar_tensor_tensor(
            out=a1[:, 0:C], in0=u, scalar=t_, in1=a1[:, 0:C],
            op0=ALU.is_gt, op1=ALU.add))
    sch.op("vector", lambda e: e.scalar_tensor_tensor(
        out=a1[:, 0:C], in0=c5[:, 0:C], scalar=2.0, in1=a1[:, 0:C],
        op0=ALU.mult, op1=ALU.add))
    gv = sgn[:, 0:C].rearrange("p (b s) -> p b s", s=16)
    hv = hs[:, 0:NB].rearrange("p (b o) -> p b o", o=1)
    gvb, hvb = broadcast_tensor_aps(gv, hv)
    sch.op("vector", lambda e: e.tensor_tensor(out=gv, in0=gvb, in1=hvb, op=ALU.mult),
           waits=ev_max(e_sgn))
    ev = sch.op("vector", lambda e: e.tensor_tensor(
        out=out_bf16, in0=a1[:, 0:C], in1=sgn[:, 0:C], op=ALU.mult), inc="v")
    q["_last"] = ev
    return ev


def emit_ln(sch, b, x_src, out_xn, e_x):
    war = b.get("_last")
    wx = ev_max(e_x)
    wxa = ev_max(e_x, war)
    e_s1 = sch.op("vector", lambda e: e.reduce_sum(
        out=b["s1"][:], in_=x_src, axis=AX.X), waits=wx, inc="v")
    e_sq = sch.op("scalar", lambda e: e.activation(
        out=b["sqs"][:], in_=x_src, func=AF.Square), waits=wxa, inc="a")
    e_ssq = sch.op("vector", lambda e: e.reduce_sum(
        out=b["ssq"][:], in_=b["sqs"][:], axis=AX.X), waits=(e_sq,), inc="v")
    e_mean = sch.op("vector", lambda e: e.tensor_scalar(
        out=b["mean"][:], in0=b["s1"][:], scalar1=1.0 / D, scalar2=None,
        op0=ALU.mult), waits=ev_max(e_s1, e_ssq), inc="v")
    sch.op("vector", lambda e: e.tensor_scalar(
        out=b["ex2"][:], in0=b["ssq"][:], scalar1=1.0 / D, scalar2=EPS_LN,
        op0=ALU.mult, op1=ALU.add))
    sch.op("vector", lambda e: e.tensor_tensor(
        out=b["msq"][:], in0=b["mean"][:], in1=b["mean"][:], op=ALU.mult),
        waits=(e_mean,))
    e_var = sch.op("vector", lambda e: e.tensor_tensor(
        out=b["var"][:], in0=b["ex2"][:], in1=b["msq"][:], op=ALU.subtract), inc="v")
    e_sd = sch.op("scalar", lambda e: e.activation(
        out=b["sd"][:], in_=b["var"][:], func=AF.Sqrt),
        waits=ev_max(e_var, war), inc="a")
    e_rstd = sch.op("vector", lambda e: e.reciprocal(
        out=b["rstd"][:], in_=b["sd"][:]), waits=(e_sd,), inc="v")
    e_xn = sch.op("vector", lambda e: e.tensor_scalar(
        out=out_xn, in0=x_src, scalar1=b["mean"][:][:, 0:1],
        scalar2=b["rstd"][:][:, 0:1], op0=ALU.subtract, op1=ALU.mult),
        waits=(e_rstd,), inc="v")
    b["_last"] = e_xn
    return e_xn


def emit_transpose_evac(sch, psum, ident, srcs, dsts, w_tensor, w_evac,
                        evac_eng="scalar", dst_groups=None):
    evac_events = []
    n = len(srcs)
    last = None
    for cb in range(0, n, 4):
        nslot = min(4, n - cb)
        e_tps = None
        for s_i in range(nslot):
            src = srcs[cb + s_i]
            e_tps = sch.op("tensor", lambda e, src=src, s_i=s_i: e.transpose(
                psum[:, s_i, 0:128], src, ident), waits=w_tensor if s_i == 0 else (),
                inc="t" if s_i == nslot - 1 else None)
        if dst_groups is not None:
            dstg = dst_groups(cb, nslot)
            last = sch.op(evac_eng, lambda e, dstg=dstg, nslot=nslot: e.copy(
                out=dstg, in_=psum[:, 0:nslot, 0:128]),
                waits=ev_max(e_tps, w_evac), inc="a")
            evac_events.append(last)
        else:
            evs = []
            for s_i in range(nslot):
                dst = dsts[cb + s_i]
                evs.append(sch.op(evac_eng, lambda e, dst=dst, s_i=s_i: e.copy(
                    out=dst, in_=psum[:, s_i, 0:128]),
                    waits=ev_max(e_tps, w_evac) if s_i == 0 else (), inc="a"))
            last = evs[-1]
            evac_events.extend(evs)
        w_tensor = ev_max(w_tensor, last)
    return last, evac_events



def emit_l0a(nc, sch, E, dr, pfx=""):
    sch.pfx = pfx
    x_d = dr["x"]
    wg_d = dr["w_gate"]
    wu_d = dr["w_up"]
    wd_d = dr["w_down"]
    ident_d = dr["ident"]
    y_d = dr["y_out"]
    xnfm_d = dr["xnfm_out"]

    NIC = 6
    IC_OFF = [0, 512, 1024, 1536, 2048, 2560]
    IC_SZ = [512] * 5 + [256]
    NRT_G = I // 128
    NRT_D = D // 128

    wgqT = E(nc.sbuf_tensor(pfx + "wgqT", [128, NCH, I], BF16))
    wuqT = E(nc.sbuf_tensor(pfx + "wuqT", [128, NCH, I], BF16))
    wdqT = E(nc.sbuf_tensor(pfx + "wdqT", [128, NRT_G, D], BF16))

    q = {n: E(nc.sbuf_tensor(pfx + "q_" + n, [128, 1408], F32))
         for n in ("absx", "sgn", "a1", "c5")}
    q.update({n: E(nc.sbuf_tensor(pfx + "q_" + n, [128, 88], F32))
              for n in ("bm", "sc", "rs", "hs")})
    lnb = {n: E(nc.sbuf_tensor(pfx + "ln_" + n, [128, 1], F32))
           for n in ("s1", "ssq", "mean", "ex2", "msq", "var", "sd", "rstd")}
    xtile = E(nc.sbuf_tensor(pfx + "xtile", [128, 2, D], F32))
    xn_tm = E(nc.sbuf_tensor(pfx + "xn_tm", [128, D], F32))
    xn_stage = E(nc.sbuf_tensor(pfx + "xn_stage", [128, NCH, 128], BF16))
    xq_tm = E(nc.sbuf_tensor(pfx + "xq_tm", [128, D], BF16))
    xq_fm = E(nc.sbuf_tensor(pfx + "xq_fm", [128, NCH, 128], BF16))
    g_sb = E(nc.sbuf_tensor(pfx + "g_sb", [128, 512], F32))
    h_sb = E(nc.sbuf_tensor(pfx + "h_sb", [128, I], F32))
    hq_b = E(nc.sbuf_tensor(pfx + "hq_b", [128, I], BF16))
    hq_fm = E(nc.sbuf_tensor(pfx + "hq_fm", [128, NRT_G, 128], BF16))
    y_sb = E(nc.sbuf_tensor(pfx + "y_sb", [128, D], F32))
    lnb["sqs"] = y_sb
    wt_in = h_sb
    wq_b = hq_b
    idt = E(nc.sbuf_tensor(pfx + "idt", [128, 128], F32))
    idtb = E(nc.sbuf_tensor(pfx + "idtb", [128, 128], BF16))
    tp32 = E(nc.psum_tensor(pfx + "tp32", [128, 4, 128], F32))
    tpb16 = E(nc.psum_tensor(pfx + "tpb16", [128, 4, 256], BF16))
    g_ps = E(nc.psum_tensor(pfx + "g_ps", [128, 512], F32))
    u_ps = E(nc.psum_tensor(pfx + "u_ps", [128, 512], F32))
    d_ps = E(nc.psum_tensor(pfx + "d_ps", [128, 2, 512], F32))


    e_id = sch.dma("sync", idt[:], ident_d, "d_id")
    e_idb = sch.dma("gpsimd", idtb[:], ident_d, "d_idb")

    e_tpb_war = None
    wt_release = [None, None]
    slot = 0
    for (w_src, n_rt, row_c, dstT) in ((wg_d, NRT_G, D, wgqT),
                                       (wu_d, NRT_G, D, wuqT),
                                       (wd_d, NRT_D, I, wdqT)):
        n_chT = row_c // 128
        for r in range(n_rt):
            par = 0
            slot += 1
            e_q = None
            for c0 in range(0, row_c, 1024):
                c1 = min(row_c, c0 + 1024)
                e_ld = sch.dma("sync", wt_in[:, c0:c1],
                               w_src[r * 128:(r + 1) * 128, c0:c1],
                               "d_w%d" % par,
                               waits=ev_max(wt_release[par], e_q))
                e_q = emit_quant(sch, q, wt_in[:, c0:c1],
                                 wq_b[:, c0:c1], c1 - c0, e_ld, e_ld)
            wt_release[par] = e_q
            srcs = [wq_b[:, c * 128:(c + 1) * 128] for c in range(n_chT)]
            dsts = [dstT[:, c, r * 128:(r + 1) * 128] for c in range(n_chT)]
            last, _ = emit_transpose_evac(
                sch, tpb16, idtb[:], srcs, dsts,
                w_tensor=ev_max(e_q, e_idb, e_tpb_war), w_evac=())
            e_tpb_war = last
            q["_last"] = ev_max(q["_last"], ("t", sch.cnt["t"]))
    e_wts = ("a", sch.cnt["a"])

    tiles = [(s_, t) for s_ in range(SEQ_PER_CORE) for t in range(NT)]
    ntile = len(tiles)
    e_load = {}
    st = {}
    cs = {"stg_tp": None, "xq_tp": None, "tp32_war": None,
          "tpb_war": e_tpb_war, "xnstage_free": None,
          "d_free": None, "hq_prev": None, "hq_tp": None}
    xtile_release = [None, None]

    def sched_load(idx):
        s_, t = tiles[idx]
        par = idx % 2
        e_load[idx] = sch.dma("sync", xtile[:, par],
                              x_d[s_, t * TS:(t + 1) * TS, :],
                              "d_x%d" % par,
                              waits=ev_max(xtile_release[par]))

    def stageA(idx):
        s_, t = tiles[idx]
        par = idx % 2
        t0 = t * TS
        e_x = e_load[idx]
        lnb["_last"] = ev_max(lnb.get("_last"), cs["stg_tp"])
        e_xn = emit_ln(sch, lnb, xtile[:, par], xn_tm[:], ev_max(e_x))
        srcs = [xn_tm[:, c * 128:(c + 1) * 128] for c in range(NCH)]
        dsts = [xn_stage[:, c, :] for c in range(NCH)]
        e_stg, _ = emit_transpose_evac(
            sch, tp32, idt[:], srcs, dsts,
            w_tensor=ev_max(e_xn, e_id, cs["tp32_war"]),
            w_evac=ev_max(cs["xnstage_free"]))
        cs["tp32_war"] = e_stg
        cs["stg_tp"] = e_stg
        cs["xnstage_free"] = sch.dma(
            "sync", xnfm_d[s_].rearrange("c p t -> p c t")[:, :, t0:t0 + TS],
            xn_stage[:], "d_stg", waits=(e_stg,))
        q["_last"] = ev_max(q["_last"], cs["xq_tp"])
        e_xq = emit_quant(sch, q, xn_tm[:], xq_tm[:], D, e_xn, e_xn)
        srcs = [xq_tm[:, c * 128:(c + 1) * 128] for c in range(NCH)]
        dsts = [xq_fm[:, c, :] for c in range(NCH)]
        e_xqfm, _ = emit_transpose_evac(
            sch, tpb16, idtb[:], srcs, dsts,
            w_tensor=ev_max(e_xq, e_idb, cs["tpb_war"], e_wts),
            w_evac=())
        cs["tpb_war"] = e_xqfm
        cs["xq_tp"] = e_xqfm
        st[idx] = {"xqfm": e_xqfm}

    def stageB_act(idx):
        Tb = sch.cnt["t"]
        st[idx]["Tb"] = Tb
        prev_u = st[idx - 1]["mults"][-1] if idx > 0 else None
        e_mult = None
        silus = []
        mults = []
        for ic in range(NIC):
            io0, isz = IC_OFF[ic], IC_SZ[ic]
            e_gate_f = ("t", Tb + 2 * ic + 1)
            e_up_f = ("t", Tb + 2 * ic + 2)
            e_silu = sch.op("scalar", lambda e, isz=isz: e.activation(
                out=g_sb[:, 0:isz], in_=g_ps[:, 0:isz], func=AF.Silu),
                waits=ev_max(e_gate_f,
                             e_mult if ic > 0 else prev_u), inc="a")
            e_mult = sch.op(
                "vector", lambda e, isz=isz, io0=io0: e.tensor_tensor(
                    out=h_sb[:, io0:io0 + isz], in0=g_sb[:, 0:isz],
                    in1=u_ps[:, 0:isz], op=ALU.mult),
                waits=ev_max(e_silu, e_up_f,
                             cs["hq_prev"] if ic == 0 else None), inc="v")
            silus.append(e_silu)
            mults.append(e_mult)
        st[idx]["silus"] = silus
        st[idx]["mults"] = mults
        st[idx]["h"] = e_mult

    def stageB_mm(idx):
        e_xqfm = st[idx]["xqfm"]
        silus = st[idx]["silus"]
        mults = st[idx]["mults"]
        assert sch.cnt["t"] == st[idx]["Tb"]
        prev_g = st[idx - 1]["silus"][-1] if idx > 0 else None
        prev_u = st[idx - 1]["mults"][-1] if idx > 0 else None
        for ic in range(NIC):
            io0, isz = IC_OFF[ic], IC_SZ[ic]
            for (ps, wT, e_free) in (
                    (g_ps, wgqT, silus[ic - 1] if ic > 0 else prev_g),
                    (u_ps, wuqT, mults[ic - 1] if ic > 0 else prev_u)):
                for c in range(NCH):
                    sch.op(
                        "tensor", lambda e, c=c, ps=ps, wT=wT, io0=io0, isz=isz:
                        e.matmul(ps[:, 0:isz], xq_fm[:, c, :],
                                 wT[:, c, io0:io0 + isz],
                                 start=(c == 0), stop=(c == NCH - 1)),
                        waits=ev_max(e_xqfm, e_wts, e_free) if c == 0 else (),
                        inc="t" if c == NCH - 1 else None)
        assert sch.cnt["t"] == st[idx]["Tb"] + 2 * NIC

    def stageC_v(idx):
        e_h = st[idx]["h"]
        q["_last"] = ev_max(q["_last"], cs["hq_tp"])
        e_hq = None
        for half in range(2):
            e_hq = emit_quant(sch, q, h_sb[:, half * 1408:(half + 1) * 1408],
                              hq_b[:, half * 1408:(half + 1) * 1408], 1408,
                              e_h, e_h)
        cs["hq_prev"] = e_hq
        st[idx]["hq"] = e_hq

    def stageC_t(idx):
        srcs = [hq_b[:, c * 128:(c + 1) * 128] for c in range(NRT_G)]
        e_hqf, _ = emit_transpose_evac(
            sch, tpb16, idtb[:], srcs, None,
            w_tensor=ev_max(st[idx]["hq"], cs["tpb_war"]), w_evac=(),
            dst_groups=lambda cb, ns: hq_fm[:, cb:cb + ns, :])
        cs["tpb_war"] = e_hqf
        cs["hq_tp"] = e_hqf
        st[idx]["hqf"] = e_hqf

    def stageD(idx):
        s_, t = tiles[idx]
        par = idx % 2
        t0 = t * TS
        e_dn0 = None
        e_dn = None
        for half in range(2):
            for c in range(NRT_G):
                e_dn = sch.op(
                    "tensor", lambda e, c=c, half=half: e.matmul(
                        d_ps[:, half], hq_fm[:, c, :],
                        wdqT[:, c, half * 512:half * 512 + 512],
                        start=(c == 0), stop=(c == NRT_G - 1)),
                    waits=ev_max(st[idx]["hqf"], cs["d_free"]) if c == 0 else (),
                    inc="t" if c == NRT_G - 1 else None)
            if half == 0:
                e_dn0 = e_dn
        e_ya0 = sch.op("vector", lambda e, par=par: e.tensor_tensor(
            out=xtile[:, par, 0:512], in0=xtile[:, par, 0:512],
            in1=d_ps[:, 0], op=ALU.add), waits=ev_max(e_dn0), inc="v")
        e_ya = sch.op("vector", lambda e, par=par: e.tensor_tensor(
            out=xtile[:, par, 512:1024], in0=xtile[:, par, 512:1024],
            in1=d_ps[:, 1], op=ALU.add), waits=ev_max(e_dn), inc="v")
        cs["d_free"] = e_ya
        e_ys = sch.dma("sync", y_d[s_, t0:t0 + TS, :], xtile[:, par],
                       "d_y", waits=ev_max(e_ya0, e_ya))
        xtile_release[par] = e_ys

    sched_load(0)
    stageA(0)
    stageB_act(0)
    stageB_mm(0)
    for idx in range(ntile):
        if idx + 1 < ntile:
            sched_load(idx + 1)
            stageA(idx + 1)
        stageC_v(idx)
        if idx + 1 < ntile:
            stageB_act(idx + 1)
            stageB_mm(idx + 1)
        stageC_t(idx)
        stageD(idx)
    sch.op("sync", None, waits=ev_max(sch.ev("d_y"),
                                      sch.ev("d_stg")))




def build_l0a(nc):
    x_d = nc.dram_tensor("x", [SEQ_PER_CORE, S, D], F32, kind="ExternalInput").ap()
    wg_d = nc.dram_tensor("w_gate", [I, D], F32, kind="ExternalInput").ap()
    wu_d = nc.dram_tensor("w_up", [I, D], F32, kind="ExternalInput").ap()
    wd_d = nc.dram_tensor("w_down", [D, I], F32, kind="ExternalInput").ap()
    ident_d = nc.dram_tensor("ident", [128, 128], F32, kind="ExternalInput").ap()
    y_d = nc.dram_tensor("y_out", [SEQ_PER_CORE, S, D], F32, kind="ExternalOutput").ap()
    xnfm_d = nc.dram_tensor("xnfm_out", [SEQ_PER_CORE, NCH, 128, S], BF16,
                            kind="ExternalOutput").ap()
    dr = {"x": x_d, "w_gate": wg_d, "w_up": wu_d, "w_down": wd_d,
          "ident": ident_d, "y_out": y_d, "xnfm_out": xnfm_d}
    with contextlib.ExitStack() as ctx:
        sch = Sched(nc, ctx)
        block = ctx.enter_context(nc.Block())
        with contextlib.ExitStack() as pctx:
            emit_l0a(nc, sch, pctx.enter_context, dr)
            sch.build(block)
    return nc


def emit_l0b(nc, sch, E, dr, pfx=""):
    sch.pfx = pfx
    y_d = dr["y_in"]
    qkvwT_d = dr["qkv_wT"]
    qkvb_d = dr["qkv_b_pc"]
    projwT_d = dr["proj_wT"]
    projbr_d = dr["proj_b_row"]
    NH = 16
    NP = 8
    xnfm_d = dr["xnfm"]
    x1_d = dr["x1_out"]
    nrm_scr = nc.dram_tensor("nrm_scr", [SEQ_PER_CORE, NP, 8 * 512], F32,
                             kind="Internal").ap()
    onf_scr = nc.dram_tensor("onf_scr", [SEQ_PER_CORE, NP, 2, 64, S], BF16,
                             kind="Internal").ap()

    qkv_sb = E(nc.sbuf_tensor(pfx + "qkv_sb", [128, NCH, 3 * D], BF16))
    projw_sb = E(nc.sbuf_tensor(pfx + "projw_sb", [128, NCH, D], BF16))
    qkvb_sb = E(nc.sbuf_tensor(pfx + "qkvb_sb", [128, 24], F32))

    bvb = E(nc.sbuf_tensor(pfx + "bvb", [128, 8], BF16))
    ones_r = E(nc.sbuf_tensor(pfx + "ones_r", [1, 128], BF16))
    ones_f = E(nc.sbuf_tensor(pfx + "ones_f", [1, 64], F32))
    effrow_b = E(nc.sbuf_tensor(pfx + "effrow_b", [1, D], BF16))
    xnfm_sb = E(nc.sbuf_tensor(pfx + "xnfm_sb", [128, NCH, S], BF16))
    qf2 = E(nc.sbuf_tensor(pfx + "qf", [128, 2, S], BF16))
    kf2 = E(nc.sbuf_tensor(pfx + "kf", [128, 2, S], BF16))
    v_aug = E(nc.sbuf_tensor(pfx + "v_aug", [128, NT, NH * 65], BF16))
    p_sb = E(nc.sbuf_tensor(pfx + "p_sb", [128, 2, 2, 512], BF16))
    o_raw = E(nc.sbuf_tensor(pfx + "o_raw", [64, 2, 4, 512], BF16))
    nrm_sb = E(nc.sbuf_tensor(pfx + "nrm_sb", [128, 8 * 512], F32))
    rin_sb = E(nc.sbuf_tensor(pfx + "rin_sb", [128, 32], F32))
    o_nfm = E(nc.sbuf_tensor(pfx + "o_nfm", [128, NCH, S], BF16))
    projbr_b = o_nfm[0:1, 0, 0:D]
    y_tile = nrm_sb[:, 0:D]
    sT2 = E(nc.psum_tensor(pfx + "sT2", [128, 2, 512], F32))
    o_psA = E(nc.psum_tensor(pfx + "o_psA", [65, 512], F32))
    o_psB = E(nc.psum_tensor(pfx + "o_psB", [65, 512], F32))
    qk_ps = E(nc.psum_tensor(pfx + "qk_ps", [128, 512], F32))
    nbc_ps = E(nc.psum_tensor(pfx + "nbc_ps", [64, 512], F32))
    proj_ps = E(nc.psum_tensor(pfx + "proj_ps", [128, 2, 512], F32))


    e_qkvw = sch.dma("gpsimd", qkv_sb[:],
                     qkvwT_d.rearrange("(c p) o -> p c o", p=128), "d_qkvw")
    e_projw = sch.dma("gpsimd", projw_sb[:],
                      projwT_d.rearrange("(c p) o -> p c o", p=128), "d_projw")
    e_projb = sch.dma("gpsimd", projbr_b, projbr_d, "d_projb")
    e_qkvb = sch.dma("sync", qkvb_sb[:], qkvb_d, "d_qkvb")
    e_ones = sch.op("vector", lambda e: e.memset(ones_r[:], 1.0))
    e_ones_f = sch.op("vector", lambda e: e.memset(ones_f[:], 1.0))
    e_bvb = sch.op("vector", lambda e: e.tensor_copy(
        bvb[:], qkvb_sb[:, 16:24]), waits=(e_qkvb,))
    va_ones = v_aug[:].rearrange("p t (h c) -> p t h c", c=65)[:, :, :, 64:65]
    e_vones = sch.op("vector", lambda e: e.memset(va_ones, 1.0))

    e_eff = None
    for half in range(2):
        for c in range(NCH):
            sch.op("tensor", lambda e, c=c, half=half: e.matmul(
                proj_ps[0:1, half], bvb[:, c:c + 1],
                projw_sb[:, c, half * 512:half * 512 + 512],
                start=(c == 0), stop=False),
                waits=ev_max(e_bvb, e_projw) if (c == 0 and half == 0) else ())
        e_eff = sch.op("tensor", lambda e, half=half: e.matmul(
            proj_ps[0:1, half], ones_r[0:1, 0:1],
            projbr_b[0:1, half * 512:half * 512 + 512],
            start=False, stop=True),
            waits=ev_max(e_ones, e_projb) if half == 0 else (), inc="t")
    e_effrow = sch.op("scalar", lambda e: e.copy(
        out=effrow_b[:], in_=proj_ps[0:1, :].rearrange("p h j -> p (h j)")),
        waits=(e_eff,))

    e_xn_users = None
    e_onfm_users = None
    e_vaug_users = None
    e_x1_store = None
    e_onh_store = None
    e_nbc_free = None
    e_nbc_sb_free = {}
    for s_ in range(SEQ_PER_CORE):
        e_xn = sch.dma("sync", xnfm_sb[:],
                       xnfm_d[s_].rearrange("c p t -> p c t"), "d_xn",
                       waits=ev_max(e_xn_users))
        e_vevac = None
        e_vps_free = None
        e_vmm = None
        for tt in range(NT):
            for half in range(2):
                for c in range(NCH):
                    e_vmm = sch.op(
                        "tensor", lambda e, tt=tt, half=half, c=c: e.matmul(
                            qk_ps[:, 0:512],
                            xnfm_sb[:, c, tt * TS:(tt + 1) * TS],
                            qkv_sb[:, c, 2 * D + half * 512:2 * D + (half + 1) * 512],
                            start=(c == 0), stop=(c == NCH - 1)),
                        waits=ev_max(e_xn, e_qkvw, e_vps_free, e_vones)
                        if c == 0 else (),
                        inc="t" if c == NCH - 1 else None)
                vslice = v_aug[:].rearrange(
                    "p t (h c) -> p t h c", c=65)[:, tt, half * 8:(half + 1) * 8, 0:64]
                e_vevac = sch.op(
                    "scalar", lambda e, vslice=vslice: e.copy(
                        out=vslice,
                        in_=qk_ps[:, 0:512].rearrange("p (h c) -> p h c", c=64)),
                    waits=ev_max(e_vmm, e_vaug_users))
                e_vps_free = e_vevac
        e_vdone = e_vevac

        e_qkps_free = e_vps_free
        if s_ == 0:
            e_projps_free = {0: e_effrow, 1: e_effrow}
        e_qf_user = {0: None, 1: None}
        e_kf_user = {0: None, 1: None}
        e_praw_free = None
        e_sT_free = {0: None, 1: None}
        e_p_free = {}
        e_onh_user = None
        e_nrm_user = e_x1_store
        e_mm = None
        for hp in range(NP):
            hpar = hp % 2
            qf = qf2[:, hpar]
            kf = kf2[:, hpar]
            o0 = hp * 128
            e_qev = e_kev = None
            for which, (dst, bias_col, col0, e_user) in enumerate((
                    (qf, hp, o0, e_qf_user[hpar]),
                    (kf, 8 + hp, D + o0, e_kf_user[hpar]))):
                for qc in range(4):
                    for c in range(NCH):
                        e_mm = sch.op(
                            "tensor",
                            lambda e, c=c, qc=qc, col0=col0: e.matmul(
                                qk_ps[:, 0:512],
                                qkv_sb[:, c, col0:col0 + 128],
                                xnfm_sb[:, c, qc * 512:(qc + 1) * 512],
                                start=(c == 0), stop=(c == NCH - 1)),
                            waits=ev_max(e_xn, e_qkvw, e_qkps_free, e_user)
                            if c == 0 else (),
                            inc="t" if c == NCH - 1 else None)
                    e_ev = sch.op(
                        "scalar",
                        lambda e, dst=dst, qc=qc, bias_col=bias_col: e.activation(
                            out=dst[:, qc * 512:(qc + 1) * 512],
                            in_=qk_ps[:, 0:512], func=AF.Identity,
                            bias=qkvb_sb[:, bias_col:bias_col + 1]),
                        waits=ev_max(e_mm, e_qkvb))
                    e_qkps_free = e_ev
                if which == 0:
                    e_qev = e_ev
                else:
                    e_kev = e_ev
            e_att = {}
            for qc in range(4):
                for kt in range(NT):
                    par = kt % 2
                    sT = sT2 if par == 0 else proj_ps
                    for hi in range(2):
                        b0 = hi * 64
                        e_sc = sch.op(
                            "tensor",
                            lambda e, sT=sT, b0=b0, kt=kt, qc=qc, qf=qf, kf=kf,
                            hi=hi: e.matmul(
                                sT[:, hi, 0:512],
                                kf[b0:b0 + 64, kt * TS:(kt + 1) * TS],
                                qf[b0:b0 + 64, qc * 512:(qc + 1) * 512],
                                start=True, stop=True,
                                tile_position=(b0, 0)),
                            waits=ev_max(e_qev, e_kev, e_sT_free[par],
                                         (e_projps_free[0] if par == 1 else None),
                                         (e_projps_free[1] if par == 1 else None)),
                            inc="t")
                    e_exp = sch.op(
                        "scalar",
                        lambda e, sT=sT, par=par: e.activation(
                            out=p_sb[:, par].rearrange("p h j -> p (h j)"),
                            in_=sT[:, :, 0:512].rearrange("p h j -> p (h j)"),
                            func=AF.Exp, scale=0.125),
                        waits=ev_max(e_sc, e_p_free.get(par)))
                    e_sT_free[par] = e_exp
                    for hi, o_ps in enumerate((o_psA, o_psB)):
                        h = 2 * hp + hi
                        e_att[(qc, hi)] = sch.op(
                            "tensor",
                            lambda e, o_ps=o_ps, kt=kt, par=par, hi=hi, h=h:
                            e.matmul(
                                o_ps[:, 0:512],
                                v_aug[:].rearrange(
                                    "p t (hh c) -> p t hh c", c=65)[:, kt, h, :],
                                p_sb[:, par, hi, :],
                                start=(kt == 0), stop=(kt == NT - 1)),
                            waits=ev_max(e_exp, e_vdone,
                                         (e_praw_free if kt == 0 else None)),
                            inc="t")
                    e_p_free[par] = e_att[(qc, 1)]
                evs = []
                for hi, o_ps in enumerate((o_psA, o_psB)):
                    evs.append(sch.op(
                        "scalar", lambda e, o_ps=o_ps, hi=hi, qc=qc: e.copy(
                            out=o_raw[:, hi, qc, :], in_=o_ps[0:64, 0:512]),
                        waits=ev_max(e_att[(qc, hi)], e_onh_user)))
                    evs.append(sch.op(
                        "scalar", lambda e, o_ps=o_ps, hi=hi, qc=qc: e.copy(
                            out=nrm_sb[64:65, (qc * 2 + hi) * 512:
                                       (qc * 2 + hi + 1) * 512],
                            in_=o_ps[64:65, 0:512]),
                        waits=ev_max(e_nrm_user)))
                e_praw_free = ev_max(*evs)
            e_qf_user[hpar] = e_kf_user[hpar] = ("t", sch.cnt["t"])
            e_h1 = sch.dma("sync", nrm_scr[s_, hp], nrm_sb[64:65, :], "d_nbo",
                           waits=ev_max(e_praw_free))
            e_nrm_user = e_h1
            e_h2 = sch.dma("sync", rin_sb[:],
                           nrm_scr[s_, hp].rearrange("(r j) -> r j", r=128),
                           "d_nbi", waits=(e_h1,))
            e_rcp = sch.op("vector", lambda e: e.reciprocal(
                out=rin_sb[:], in_=rin_sb[:]), waits=(e_h2,))
            e_h3 = sch.dma("sync", nrm_scr[s_, hp].rearrange("(r j) -> r j", r=128),
                           rin_sb[:], "d_nbo", waits=(e_rcp,))
            e_h4 = sch.dma("sync", nrm_sb[0:1, :],
                           nrm_scr[s_, hp].rearrange("j -> () j"),
                           "d_nbi", waits=(e_h3,))
            e_div_last = None
            for qc in range(4):
                for hi in range(2):
                    slot = qc * 2 + hi
                    e_nbc = sch.op(
                        "tensor", lambda e, slot=slot: e.matmul(
                            nbc_ps[:, 0:512], ones_f[0:1, :],
                            nrm_sb[0:1, slot * 512:(slot + 1) * 512],
                            start=True, stop=True),
                        waits=ev_max(e_h4, e_nbc_free, e_ones_f), inc="t")
                    e_div = sch.op(
                        "vector", lambda e, hi=hi, qc=qc: e.tensor_tensor(
                            out=o_raw[:, hi, qc, :], in0=o_raw[:, hi, qc, :],
                            in1=nbc_ps[0:64, 0:512], op=ALU.mult),
                        waits=ev_max(e_nbc, e_onh_store))
                    e_nbc_free = e_div
                    e_div_last = e_div
            e_onh_store = sch.dma(
                "sync",
                onf_scr[s_, hp].rearrange("h d (qc j) -> d h qc j", j=512),
                o_raw[:], "d_onh", waits=ev_max(e_div_last))
            e_onh_user = e_onh_store
        e_onfm = sch.dma(
            "sync", o_nfm[:],
            onf_scr[s_].rearrange("hp h d t -> (h d) hp t"), "d_onfm",
            waits=ev_max(e_onh_store, e_onfm_users, e_effrow))
        e_x1v = None
        e_pj = None
        for tt in range(NT):
            e_yld = sch.dma("sync", y_tile, y_d[s_, tt * TS:(tt + 1) * TS, :],
                            "d_yl", waits=ev_max(e_x1_store, e_onfm))
            for half in range(2):
                for c in range(NCH):
                    sch.op("tensor", lambda e, c=c, half=half, tt=tt: e.matmul(
                        proj_ps[:, half], o_nfm[:, c, tt * TS:(tt + 1) * TS],
                        projw_sb[:, c, half * 512:(half + 1) * 512],
                        start=(c == 0), stop=False),
                        waits=ev_max(e_onfm, e_effrow, e_ones,
                                     e_projps_free[half]) if c == 0 else ())
                e_pj = sch.op("tensor", lambda e, half=half: e.matmul(
                    proj_ps[:, half], ones_r[0:1, :],
                    effrow_b[0:1, half * 512:(half + 1) * 512],
                    start=False, stop=True), inc="t")
                e_add = sch.op(
                    "vector", lambda e, half=half: e.tensor_tensor(
                        out=y_tile[:, half * 512:(half + 1) * 512],
                        in0=y_tile[:, half * 512:(half + 1) * 512],
                        in1=proj_ps[:, half], op=ALU.add),
                    waits=ev_max(e_pj, e_yld))
                e_projps_free[half] = e_add
                e_x1v = e_add
            e_x1_store = sch.dma("sync", x1_d[s_, tt * TS:(tt + 1) * TS, :],
                                 y_tile, "d_x1", waits=(e_x1v,))
        e_xn_users = ev_max(e_vmm, e_mm)
        e_onfm_users = e_pj
        e_vaug_users = ("t", sch.cnt["t"])
    sch.op("sync", None, waits=ev_max(sch.ev("d_x1")))




def build_l0b(nc):
    xnfm_d = nc.dram_tensor("xnfm", [SEQ_PER_CORE, NCH, 128, S], BF16,
                            kind="ExternalInput").ap()
    y_d = nc.dram_tensor("y_in", [SEQ_PER_CORE, S, D], F32, kind="ExternalInput").ap()
    qkvwT_d = nc.dram_tensor("qkv_wT", [D, 3 * D], F32, kind="ExternalInput").ap()
    qkvb_d = nc.dram_tensor("qkv_b_pc", [128, 24], F32, kind="ExternalInput").ap()
    projwT_d = nc.dram_tensor("proj_wT", [D, D], F32, kind="ExternalInput").ap()
    projbr_d = nc.dram_tensor("proj_b_row", [1, D], F32, kind="ExternalInput").ap()
    x1_d = nc.dram_tensor("x1_out", [SEQ_PER_CORE, S, D], F32,
                          kind="ExternalOutput").ap()
    dr = {"xnfm": xnfm_d, "y_in": y_d, "qkv_wT": qkvwT_d, "qkv_b_pc": qkvb_d,
          "proj_wT": projwT_d, "proj_b_row": projbr_d, "x1_out": x1_d}
    with contextlib.ExitStack() as ctx:
        sch = Sched(nc, ctx)
        block = ctx.enter_context(nc.Block())
        with contextlib.ExitStack() as pctx:
            emit_l0b(nc, sch, pctx.enter_context, dr)
            sch.build(block)
    return nc


def emit_l1a(nc, sch, E, dr, pfx=""):
    sch.pfx = pfx
    x1_d = dr["x1"]
    wka_d = dr["Wk_a"]
    wva_d = dr["Wv_a"]
    wktf_d = dr["WkT_f"]
    bkr_d = dr["bk_row"]
    mur_d = dr["mu_r_pc"]
    muk_d = dr["mu_k_pc"]
    ident_d = dr["ident"]
    NH = 16
    xn1fm_d = dr["xn1fm_out"]
    xrfm_d = dr["xrfm_out"]
    kafm_d = dr["kafm_out"]
    kv_d = dr["kv_out"]

    wka_sb = E(nc.sbuf_tensor(pfx + "wka_sb", [128, NCH, D], BF16))
    wva_sb = E(nc.sbuf_tensor(pfx + "wva_sb", [128, NCH, D], BF16))
    wktf_sb = E(nc.sbuf_tensor(pfx + "wktf_sb", [128, NCH, D], BF16))
    bkr_b = E(nc.sbuf_tensor(pfx + "bkr_b", [1, D], BF16))
    mur_sb = E(nc.sbuf_tensor(pfx + "mur_sb", [128, NCH], F32))
    muk_sb = E(nc.sbuf_tensor(pfx + "muk_sb", [128, NCH], F32))
    ones_r = E(nc.sbuf_tensor(pfx + "ones_r", [1, 128], BF16))
    idt = E(nc.sbuf_tensor(pfx + "idt", [128, 128], F32))
    lnb = {n: E(nc.sbuf_tensor(pfx + "ln_" + n, [128, 1], F32))
           for n in ("s1", "ssq", "mean", "ex2", "msq", "var", "sd", "rstd")}
    lnb["sqs"] = E(nc.sbuf_tensor(pfx + "ln_sqs", [128, D], F32))
    x1t = E(nc.sbuf_tensor(pfx + "x1t", [128, 2, D], F32))
    xn1_tm = E(nc.sbuf_tensor(pfx + "xn1_tm", [128, D], F32))
    win = E(nc.sbuf_tensor(pfx + "win", [128, NCH, 129], F32))
    stg = E(nc.sbuf_tensor(pfx + "stg", [128, NCH, 128], BF16))
    xr_stg = E(nc.sbuf_tensor(pfx + "xr_stg", [128, NCH, 128], BF16))
    dmix = E(nc.sbuf_tensor(pfx + "dmix", [128, NCH, 128], F32))
    tmix = E(nc.sbuf_tensor(pfx + "tmix", [128, NCH, 128], F32))
    xk_grp = E(nc.sbuf_tensor(pfx + "xk_grp", [128, NCH, 512], BF16))
    rl_sb = E(nc.sbuf_tensor(pfx + "rl_sb", [128, 512], F32))
    ka_stg = E(nc.sbuf_tensor(pfx + "ka_stg", [128, NCH, 512], BF16))
    kfeat = E(nc.sbuf_tensor(pfx + "kfeat", [128, NT, D], BF16))
    v_aug = E(nc.sbuf_tensor(pfx + "v_aug", [128, NT, NH * 65], BF16))
    e_sb = E(nc.sbuf_tensor(pfx + "e_sb", [128, D], F32))
    r_sb = E(nc.sbuf_tensor(pfx + "r_sb", [128, D], F32))
    kv_stg = E(nc.sbuf_tensor(pfx + "kv_stg", [64, NH, 65], BF16))
    tp32 = E(nc.psum_tensor(pfx + "tp32", [128, 4, 128], F32))
    k_ps = E(nc.psum_tensor(pfx + "k_ps", [128, 512], F32))
    v_ps = E(nc.psum_tensor(pfx + "v_ps", [128, 512], F32))
    kf_ps = E(nc.psum_tensor(pfx + "kf_ps", [128, 2, 512], F32))
    kv_ps = E(nc.psum_tensor(pfx + "kv_ps", [64, 2, 512], F32))


    e_wka = sch.dma("gpsimd", wka_sb[:],
                    wka_d.rearrange("(c p) o -> p c o", p=128), "d_wka")
    e_wva = sch.dma("gpsimd", wva_sb[:],
                    wva_d.rearrange("(c p) o -> p c o", p=128), "d_wva")
    e_wktf = sch.dma("gpsimd", wktf_sb[:],
                     wktf_d.rearrange("(c p) o -> p c o", p=128), "d_wktf")
    e_bkr = sch.dma("gpsimd", bkr_b[:], bkr_d, "d_bkr")
    e_mur = sch.dma("sync", mur_sb[:], mur_d, "d_mur")
    e_muk = sch.dma("sync", muk_sb[:], muk_d, "d_muk")
    e_id = sch.dma("sync", idt[:], ident_d, "d_id")
    e_ones = sch.op("vector", lambda e: e.memset(ones_r[:], 1.0))
    va_ones = v_aug[:].rearrange("p t (h c) -> p t h c", c=65)[:, :, :, 64:65]
    e_vones = sch.op("vector", lambda e: e.memset(va_ones, 1.0))

    x1_release = [None, None]
    e_stg_user = None
    e_xr_user = None
    e_ka_user = None
    e_tp32_war = None
    e_kps_free = None
    e_vps_free = None
    e_kf_free = None
    e_dmix_user = None
    e_win_user = None
    e_kfeat_users = None
    e_vaug_users = None
    e_kvstg_free = None
    e_er_user = None
    e_load = {}
    tiles = [(s_, t) for s_ in range(SEQ_PER_CORE) for t in range(NT)]

    def sched_load(idx):
        s_, t = tiles[idx]
        par = idx % 2
        e_load[idx] = sch.dma("sync", x1t[:, par],
                              x1_d[s_, t * TS:(t + 1) * TS, :],
                              "d_x%d" % par, waits=ev_max(x1_release[par]))

    sched_load(0)
    for idx, (s_, t) in enumerate(tiles):
        par = idx % 2
        t0 = t * TS
        g = t % 4
        if idx + 1 < len(tiles):
            sched_load(idx + 1)
        e_x = e_load[idx]
        if t == 0:
            e_z = sch.op("vector", lambda e: e.memset(win[:, :, 0:1], 0.0),
                         waits=ev_max(e_win_user))
        else:
            e_z = sch.op("vector", lambda e: e.tensor_copy(
                win[:, :, 0:1], win[:, :, 128:129]), waits=ev_max(e_win_user))
        e_xn = emit_ln(sch, lnb, x1t[:, par], xn1_tm[:], e_x)
        x1_release[par] = e_xn
        evs = []
        for cb in (0, 4):
            e_tps = None
            for s_i in range(4):
                c = cb + s_i
                e_tps = sch.op("tensor", lambda e, c=c, s_i=s_i: e.transpose(
                    tp32[:, s_i, 0:128], xn1_tm[:, c * 128:(c + 1) * 128], idt[:]),
                    waits=ev_max(e_xn, e_id, e_tp32_war) if s_i == 0 else (),
                    inc="t" if s_i == 3 else None)
            for s_i in range(4):
                c = cb + s_i
                sch.op("scalar", lambda e, c=c, s_i=s_i: e.copy(
                    out=win[:, c, 1:129], in_=tp32[:, s_i, 0:128]),
                    waits=ev_max(e_tps, e_z, e_dmix_user) if s_i == 0 else ())
                evs.append(sch.op("scalar", lambda e, c=c, s_i=s_i: e.copy(
                    out=stg[:, c, :], in_=tp32[:, s_i, 0:128]),
                    waits=ev_max(e_stg_user)))
            e_tp32_war = evs[-1]
        e_evac = evs[-1]
        e_win_user = None
        e_stg_user = sch.dma(
            "sync", xn1fm_d[s_].rearrange("c p t -> p c t")[:, :, t0:t0 + TS],
            stg[:], "d_stg", waits=(e_evac,))
        e_d = sch.op("vector", lambda e: e.tensor_tensor(
            out=dmix[:], in0=win[:, :, 1:129], in1=win[:, :, 0:128],
            op=ALU.subtract), waits=ev_max(e_evac))
        mu_r_bc = mur_sb[:].rearrange("p (c o) -> p c o", o=1)
        mu_k_bc = muk_sb[:].rearrange("p (c o) -> p c o", o=1)
        dv = dmix[:]
        dvb, murb = broadcast_tensor_aps(dv, mu_r_bc)
        _, mukb = broadcast_tensor_aps(dv, mu_k_bc)
        e_t1 = sch.op("vector", lambda e: e.tensor_tensor(
            out=tmix[:], in0=dvb, in1=murb, op=ALU.mult),
            waits=ev_max(e_d, e_mur))
        e_xr = sch.op("vector", lambda e: e.tensor_tensor(
            out=xr_stg[:], in0=tmix[:], in1=win[:, :, 0:128], op=ALU.add),
            waits=ev_max(e_xr_user))
        e_xr_user = sch.dma(
            "sync", xrfm_d[s_].rearrange("c p t -> p c t")[:, :, t0:t0 + TS],
            xr_stg[:], "d_xr", waits=(e_xr,))
        e_t2 = sch.op("vector", lambda e: e.tensor_tensor(
            out=tmix[:], in0=dvb, in1=mukb, op=ALU.mult), waits=ev_max(e_muk))
        e_xk = sch.op("vector", lambda e, g=g: e.tensor_tensor(
            out=xk_grp[:, :, g * 128:(g + 1) * 128], in0=tmix[:],
            in1=win[:, :, 0:128], op=ALU.add), waits=ev_max(e_kf_free))
        e_win_user = e_xk
        e_dmix_user = e_t2
        kf_evs = []
        for half in range(2):
            for c in range(NCH):
                e_kmm = sch.op("tensor", lambda e, c=c, half=half: e.matmul(
                    k_ps[:, 0:512], stg[:, c, :],
                    wka_sb[:, c, half * 512:(half + 1) * 512],
                    start=(c == 0), stop=False),
                    waits=ev_max(e_evac, e_wka, e_kps_free) if c == 0 else ())
            e_kmm = sch.op("tensor", lambda e, half=half: e.matmul(
                k_ps[:, 0:512], ones_r[0:1, :],
                bkr_b[0:1, half * 512:(half + 1) * 512],
                start=False, stop=True),
                waits=ev_max(e_ones, e_bkr), inc="t")
            e_e = sch.op("scalar", lambda e, half=half: e.activation(
                out=e_sb[:, half * 512:(half + 1) * 512], in_=k_ps[:, 0:512],
                func=AF.Exp), waits=ev_max(e_kmm, e_er_user))
            e_r = sch.op("scalar", lambda e, half=half: e.activation(
                out=r_sb[:, half * 512:(half + 1) * 512], in_=k_ps[:, 0:512],
                func=AF.Relu), waits=())
            e_kf = sch.op("vector", lambda e, half=half, t=t: e.scalar_tensor_tensor(
                out=kfeat[:, t, half * 512:(half + 1) * 512],
                in0=e_sb[:, half * 512:(half + 1) * 512], scalar=1.0,
                in1=r_sb[:, half * 512:(half + 1) * 512],
                op0=ALU.min, op1=ALU.add),
                waits=ev_max(e_r, e_kfeat_users))
            e_kps_free = e_r
            kf_evs.append(e_kf)
        e_kf_done = kf_evs[-1]
        e_er_user = e_kf_done
        v_evs = []
        for half in range(2):
            for c in range(NCH):
                e_vmm = sch.op("tensor", lambda e, c=c, half=half: e.matmul(
                    v_ps[:, 0:512], stg[:, c, :],
                    wva_sb[:, c, half * 512:(half + 1) * 512],
                    start=(c == 0), stop=(c == NCH - 1)),
                    waits=ev_max(e_evac, e_wva, e_vps_free, e_vones)
                    if c == 0 else (),
                    inc="t" if c == NCH - 1 else None)
            vslice = v_aug[:].rearrange(
                "p t (h c) -> p t h c", c=65)[:, t, half * 8:(half + 1) * 8, 0:64]
            e_vev = sch.op("scalar", lambda e, vslice=vslice: e.copy(
                out=vslice,
                in_=v_ps[:, 0:512].rearrange("p (h c) -> p h c", c=64)),
                waits=ev_max(e_vmm, e_vaug_users))
            e_vps_free = e_vev
            v_evs.append(e_vev)
        e_v_done = v_evs[-1]
        e_stg_user = ev_max(e_stg_user, ("t", sch.cnt["t"]))
        if g == 3:
            gi = t // 4
            ka_evs = []
            for dc in range(NCH):
                for cc in range(NCH):
                    e_km = sch.op("tensor", lambda e, dc=dc, cc=cc: e.matmul(
                        kf_ps[:, dc % 2], wktf_sb[:, cc, dc * 128:(dc + 1) * 128],
                        xk_grp[:, cc, :], start=(cc == 0), stop=(cc == NCH - 1)),
                        waits=ev_max(e_xk, e_wktf,
                                     ka_evs[-2] if len(ka_evs) >= 2 else None)
                        if cc == 0 else (),
                        inc="t" if cc == NCH - 1 else None)
                e_rl = sch.op("scalar", lambda e, dc=dc: e.activation(
                    out=rl_sb[:], in_=kf_ps[:, dc % 2], func=AF.Relu),
                    waits=ev_max(e_km))
                e_ka = sch.op("scalar", lambda e, dc=dc: e.activation(
                    out=ka_stg[:, dc, :], in_=rl_sb[:], func=AF.Square),
                    waits=ev_max(e_ka_user))
                ka_evs.append(e_ka)
            e_ka_user = sch.dma(
                "sync",
                kafm_d[s_].rearrange("c p t -> p c t")[:, :, gi * 512:(gi + 1) * 512],
                ka_stg[:], "d_ka", waits=(ka_evs[-1],))
            e_kf_free = ("a", sch.cnt["a"])
        if t == NT - 1:
            e_kv_evs = []
            e_kvps_free = {}
            for h in range(NH):
                pb = h % 2
                for tt in range(NT):
                    e_kvm = sch.op(
                        "tensor", lambda e, h=h, tt=tt, pb=pb: e.matmul(
                            kv_ps[:, pb, 0:65],
                            kfeat[:, tt, (h // 2) * 128 + (h % 2) * 64:
                                  (h // 2) * 128 + (h % 2) * 64 + 64],
                            v_aug[:].rearrange(
                                "p t (hh c) -> p t hh c", c=65)[:, tt, h, :],
                            start=(tt == 0), stop=(tt == NT - 1)),
                        waits=ev_max(e_kf_done, e_v_done, e_kvps_free.get(pb))
                        if tt == 0 else (),
                        inc="t" if tt == NT - 1 else None)
                e_kve = sch.op("scalar", lambda e, h=h, pb=pb: e.copy(
                    out=kv_stg[:, h, :], in_=kv_ps[0:64, pb, 0:65]),
                    waits=ev_max(e_kvm, e_kvstg_free))
                e_kvps_free[pb] = e_kve
                e_kv_evs.append(e_kve)
            e_kvstg_free = sch.dma(
                "sync", kv_d[s_].rearrange("h d c -> d h c"), kv_stg[:],
                "d_kv", waits=(e_kv_evs[-1],))
            e_kfeat_users = ("t", sch.cnt["t"])
            e_vaug_users = ("t", sch.cnt["t"])
    sch.op("sync", None, waits=ev_max(
        sch.ev("d_kv"), sch.ev("d_stg"),
        sch.ev("d_xr"), sch.ev("d_ka")))




def build_l1a(nc):
    x1_d = nc.dram_tensor("x1", [SEQ_PER_CORE, S, D], F32, kind="ExternalInput").ap()
    wka_d = nc.dram_tensor("Wk_a", [D, D], F32, kind="ExternalInput").ap()
    wva_d = nc.dram_tensor("Wv_a", [D, D], F32, kind="ExternalInput").ap()
    wktf_d = nc.dram_tensor("WkT_f", [D, D], F32, kind="ExternalInput").ap()
    bkr_d = nc.dram_tensor("bk_row", [1, D], F32, kind="ExternalInput").ap()
    mur_d = nc.dram_tensor("mu_r_pc", [128, NCH], F32, kind="ExternalInput").ap()
    muk_d = nc.dram_tensor("mu_k_pc", [128, NCH], F32, kind="ExternalInput").ap()
    ident_d = nc.dram_tensor("ident", [128, 128], F32, kind="ExternalInput").ap()
    xn1fm_d = nc.dram_tensor("xn1fm_out", [SEQ_PER_CORE, NCH, 128, S], BF16,
                             kind="ExternalOutput").ap()
    xrfm_d = nc.dram_tensor("xrfm_out", [SEQ_PER_CORE, NCH, 128, S], BF16,
                            kind="ExternalOutput").ap()
    kafm_d = nc.dram_tensor("kafm_out", [SEQ_PER_CORE, NCH, 128, S], BF16,
                            kind="ExternalOutput").ap()
    kv_d = nc.dram_tensor("kv_out", [SEQ_PER_CORE, 16, 64, 65], BF16,
                          kind="ExternalOutput").ap()
    dr = {"x1": x1_d, "Wk_a": wka_d, "Wv_a": wva_d, "WkT_f": wktf_d,
          "bk_row": bkr_d, "mu_r_pc": mur_d, "mu_k_pc": muk_d, "ident": ident_d,
          "xn1fm_out": xn1fm_d, "xrfm_out": xrfm_d, "kafm_out": kafm_d,
          "kv_out": kv_d}
    with contextlib.ExitStack() as ctx:
        sch = Sched(nc, ctx)
        block = ctx.enter_context(nc.Block())
        with contextlib.ExitStack() as pctx:
            emit_l1a(nc, sch, pctx.enter_context, dr)
            sch.build(block)
    return nc


def emit_l1b(nc, sch, E, dr, pfx=""):
    sch.pfx = pfx
    x1_d = dr["x1"]
    wqa_d = dr["Wq_a"]
    bq_d = dr["bq_pc"]
    wo_d = dr["Wo_a"]
    bor_d = dr["bo_row"]
    bv_d = dr["bv_pc"]
    wrt_d = dr["WrT_f"]
    wvt_d = dr["WvT_f"]
    NH = 16
    NP = 8
    xn1fm_d = dr["xn1fm"]
    xrfm_d = dr["xrfm"]
    kafm_d = dr["kafm"]
    kv_d = dr["kv"]
    out_d = dr["out_final"]
    nrm_scr = nc.dram_tensor("nrm2_scr", [SEQ_PER_CORE, 4, 2, 8 * 512], F32,
                             kind="Internal").ap()
    o1_scr = nc.dram_tensor("o1_scr", [SEQ_PER_CORE, 4, NP, 2, 64, 512], BF16,
                            kind="Internal").ap()

    wqa_sb = E(nc.sbuf_tensor(pfx + "wqa_sb", [128, NCH, D], BF16))
    wo_sb = E(nc.sbuf_tensor(pfx + "wo_sb", [128, NCH, D], BF16))
    wrt_sb = E(nc.sbuf_tensor(pfx + "wrt_sb", [128, NCH, D], BF16))
    wvt_sb = E(nc.sbuf_tensor(pfx + "wvt_sb", [128, NCH, D], BF16))
    bq_sb = E(nc.sbuf_tensor(pfx + "bq_sb", [128, NCH], F32))
    bv_sb = E(nc.sbuf_tensor(pfx + "bv_sb", [128, NCH], F32))
    bor_b = E(nc.sbuf_tensor(pfx + "bor_b", [1, D], BF16))
    bvb = E(nc.sbuf_tensor(pfx + "bvb", [128, NCH], BF16))
    ones_r = E(nc.sbuf_tensor(pfx + "ones_r", [1, 128], BF16))
    ones_f = E(nc.sbuf_tensor(pfx + "ones_f", [1, 64], F32))
    effrow_b = E(nc.sbuf_tensor(pfx + "effrow_b", [1, D], BF16))
    kv_sb = E(nc.sbuf_tensor(pfx + "kv_sb", [128, NP, 65], BF16))
    xn1_qc = E(nc.sbuf_tensor(pfx + "xn1_qc", [128, NCH, 512], BF16))
    qfeat = E(nc.sbuf_tensor(pfx + "qfeat", [128, NCH, 512], BF16))
    e_sb = E(nc.sbuf_tensor(pfx + "e_sb", [128, 512], F32))
    r_sb = E(nc.sbuf_tensor(pfx + "r_sb", [128, 512], F32))
    o_raw = E(nc.sbuf_tensor(pfx + "o_raw", [64, NH, 512], BF16))
    nrm_sb = E(nc.sbuf_tensor(pfx + "nrm_sb", [128, 8 * 512], F32))
    rin_sb = E(nc.sbuf_tensor(pfx + "rin_sb", [128, 32], F32))
    o1_nfm = E(nc.sbuf_tensor(pfx + "o1_nfm", [128, NCH, 512], BF16))
    x1t = E(nc.sbuf_tensor(pfx + "x1t", [128, D], F32))
    xr_t = E(nc.sbuf_tensor(pfx + "xr_t", [128, 2, NCH, 128], BF16))
    ka_t = E(nc.sbuf_tensor(pfx + "ka_t", [128, 2, NCH, 128], BF16))
    rsg_sb = E(nc.sbuf_tensor(pfx + "rsg_sb", [128, D], F32))
    fv_sb = E(nc.sbuf_tensor(pfx + "fv_sb", [128, D], F32))
    q_ps = E(nc.psum_tensor(pfx + "q_ps", [128, 512], F32))
    o1A = E(nc.psum_tensor(pfx + "o1A", [65, 512], F32))
    o1B = E(nc.psum_tensor(pfx + "o1B", [65, 512], F32))
    nbc_ps = E(nc.psum_tensor(pfx + "nbc_ps", [64, 512], F32))
    pr_ps = E(nc.psum_tensor(pfx + "pr_ps", [128, 2, 512], F32))
    rv_ps = E(nc.psum_tensor(pfx + "rv_ps", [128, 2, 512], F32))


    e_wqa = sch.dma("gpsimd", wqa_sb[:],
                    wqa_d.rearrange("(c p) o -> p c o", p=128), "d_wqa")
    e_wo = sch.dma("gpsimd", wo_sb[:],
                   wo_d.rearrange("(c p) o -> p c o", p=128), "d_wo")
    e_wrt = sch.dma("gpsimd", wrt_sb[:],
                    wrt_d.rearrange("(c p) o -> p c o", p=128), "d_wrt")
    e_wvt = sch.dma("gpsimd", wvt_sb[:],
                    wvt_d.rearrange("(c p) o -> p c o", p=128), "d_wvt")
    e_bor = sch.dma("gpsimd", bor_b[:], bor_d, "d_bor")
    e_bq = sch.dma("sync", bq_sb[:], bq_d, "d_bq")
    e_bv = sch.dma("sync", bv_sb[:], bv_d, "d_bv")
    e_ones = sch.op("vector", lambda e: e.memset(ones_r[:], 1.0))
    e_ones_f = sch.op("vector", lambda e: e.memset(ones_f[:], 1.0))
    e_bvb = sch.op("vector", lambda e: e.tensor_copy(bvb[:], bv_sb[:]),
                   waits=(e_bv,))

    e_eff = None
    for half in range(2):
        for c in range(NCH):
            sch.op("tensor", lambda e, c=c, half=half: e.matmul(
                pr_ps[0:1, half], bvb[:, c:c + 1],
                wo_sb[:, c, half * 512:half * 512 + 512],
                start=(c == 0), stop=False),
                waits=ev_max(e_bvb, e_wo) if (c == 0 and half == 0) else ())
        e_eff = sch.op("tensor", lambda e, half=half: e.matmul(
            pr_ps[0:1, half], ones_r[0:1, 0:1],
            bor_b[0:1, half * 512:half * 512 + 512],
            start=False, stop=True),
            waits=ev_max(e_ones, e_bor) if half == 0 else (), inc="t")
    e_effrow = sch.op("scalar", lambda e: e.copy(
        out=effrow_b[:], in_=pr_ps[0:1, :].rearrange("p h j -> p (h j)")),
        waits=(e_eff,))

    e_kv_user = None
    e_xn_user = None
    e_qf_user = None
    e_er_user = None
    e_oraw_evac_war = None
    e_nrm_user = None
    e_nbc_free = None
    e_o1_store = None
    e_onfm_user = None
    e_x1_user = None
    e_xrka_rel = [None, None]
    e_out_store = None
    e_qps_free = None
    e_prps_free = {0: None, 1: None}
    e_rvps_sig = {0: None, 1: None}
    e_rvps_ff = {0: None, 1: None}
    for s_ in range(SEQ_PER_CORE):
        e_kv = sch.dma("sync", kv_sb[:],
                       kv_d[s_].rearrange("(hp h2) d c -> (h2 d) hp c", h2=2),
                       "d_kv", waits=ev_max(e_kv_user))
        for qc in range(4):
            e_xn = sch.dma("sync", xn1_qc[:],
                           xn1fm_d[s_].rearrange(
                               "c p t -> p c t")[:, :, qc * 512:(qc + 1) * 512],
                           "d_xn", waits=ev_max(e_xn_user))
            for dc in range(NCH):
                for c in range(NCH):
                    e_qmm = sch.op("tensor", lambda e, dc=dc, c=c: e.matmul(
                        q_ps[:, 0:512], wqa_sb[:, c, dc * 128:(dc + 1) * 128],
                        xn1_qc[:, c, :], start=(c == 0), stop=(c == NCH - 1)),
                        waits=ev_max(e_xn, e_wqa, e_qps_free) if c == 0 else (),
                        inc="t" if c == NCH - 1 else None)
                e_e = sch.op("scalar", lambda e, dc=dc: e.activation(
                    out=e_sb[:], in_=q_ps[:, 0:512], func=AF.Exp,
                    bias=bq_sb[:, dc:dc + 1]),
                    waits=ev_max(e_qmm, e_bq, e_er_user))
                e_r = sch.op("scalar", lambda e, dc=dc: e.activation(
                    out=r_sb[:], in_=q_ps[:, 0:512], func=AF.Relu,
                    bias=bq_sb[:, dc:dc + 1]), waits=())
                e_qf = sch.op("vector", lambda e, dc=dc: e.scalar_tensor_tensor(
                    out=qfeat[:, dc, :], in0=e_sb[:], scalar=1.0,
                    in1=r_sb[:], op0=ALU.min, op1=ALU.add),
                    waits=ev_max(e_r, e_qf_user))
                e_er_user = e_qf
                e_qps_free = e_r
            e_qf_done = e_qf
            e_div_all = []
            for grp in range(2):
                evs = []
                for hpl in range(4):
                    hp = grp * 4 + hpl
                    for hi, o_ps in enumerate((o1A, o1B)):
                        b0 = hi * 64
                        e_oq = sch.op(
                            "tensor", lambda e, hp=hp, hi=hi, b0=b0, o_ps=o_ps:
                            e.matmul(
                                o_ps[:, 0:512], kv_sb[b0:b0 + 64, hp, :],
                                qfeat[b0:b0 + 64, hp, :],
                                start=True, stop=True, tile_position=(b0, 0)),
                            waits=ev_max(e_qf_done, e_kv, e_oraw_evac_war),
                            inc="t")
                        h = 2 * hp + hi
                        evs.append(sch.op(
                            "scalar", lambda e, o_ps=o_ps, h=h: e.copy(
                                out=o_raw[:, h, :], in_=o_ps[0:64, 0:512]),
                            waits=ev_max(e_oq, e_o1_store)))
                        slot = (hp % 4) * 2 + hi
                        evs.append(sch.op(
                            "scalar", lambda e, o_ps=o_ps, slot=slot: e.copy(
                                out=nrm_sb[64:65, slot * 512:(slot + 1) * 512],
                                in_=o_ps[64:65, 0:512]),
                            waits=ev_max(e_nrm_user)))
                        e_oraw_evac_war = evs[-1]
                e_h1 = sch.dma("sync", nrm_scr[s_, qc, grp], nrm_sb[64:65, :],
                               "d_nbo", waits=ev_max(*evs))
                e_nrm_user = e_h1
                e_h2 = sch.dma("sync", rin_sb[:],
                               nrm_scr[s_, qc, grp].rearrange("(r j) -> r j", r=128),
                               "d_nbi", waits=(e_h1,))
                e_cl = sch.op("vector", lambda e: e.tensor_scalar(
                    out=rin_sb[:], in0=rin_sb[:], scalar1=1e-6, scalar2=None,
                    op0=ALU.max), waits=(e_h2,))
                e_rcp = sch.op("vector", lambda e: e.reciprocal(
                    out=rin_sb[:], in_=rin_sb[:]), waits=(e_cl,))
                e_h3 = sch.dma("sync",
                               nrm_scr[s_, qc, grp].rearrange("(r j) -> r j", r=128),
                               rin_sb[:], "d_nbo", waits=(e_rcp,))
                e_h4 = sch.dma("sync", nrm_sb[0:1, :],
                               nrm_scr[s_, qc, grp].rearrange("j -> () j"),
                               "d_nbi", waits=(e_h3,))
                for hpl in range(4):
                    hp = grp * 4 + hpl
                    for hi in range(2):
                        slot = hpl * 2 + hi
                        h = 2 * hp + hi
                        e_nbc = sch.op(
                            "tensor", lambda e, slot=slot: e.matmul(
                                nbc_ps[:, 0:512], ones_f[0:1, :],
                                nrm_sb[0:1, slot * 512:(slot + 1) * 512],
                                start=True, stop=True),
                            waits=ev_max(e_h4, e_nbc_free, e_ones_f), inc="t")
                        e_div = sch.op(
                            "vector", lambda e, h=h: e.tensor_tensor(
                                out=o_raw[:, h, :], in0=o_raw[:, h, :],
                                in1=nbc_ps[0:64, 0:512], op=ALU.mult),
                            waits=ev_max(e_nbc, e_o1_store))
                        e_nbc_free = e_div
                        e_div_all.append(e_div)
            e_o1_store = sch.dma(
                "sync", o1_scr[s_, qc].rearrange("hp h d j -> d hp h j"),
                o_raw[:].rearrange("d (hp h) j -> d hp h j", h=2), "d_o1o",
                waits=ev_max(*e_div_all))
            e_onfm = sch.dma(
                "sync", o1_nfm[:],
                o1_scr[s_, qc].rearrange("hp h d j -> (h d) hp j"), "d_o1i",
                waits=ev_max(e_o1_store, e_onfm_user))
            e_pj = None
            for tl in range(4):
                tt = qc * 4 + tl
                par = tl % 2
                t0 = tt * TS
                e_x1 = sch.dma("sync", x1t[:], x1_d[s_, t0:t0 + TS, :], "d_x1",
                               waits=ev_max(e_x1_user))
                e_xr = sch.dma("sync", xr_t[:, par],
                               xrfm_d[s_].rearrange(
                                   "c p t -> p c t")[:, :, t0:t0 + TS],
                               "d_xrt%d" % par, waits=ev_max(e_xrka_rel[par]))
                e_ka = sch.dma("sync", ka_t[:, par],
                               kafm_d[s_].rearrange(
                                   "c p t -> p c t")[:, :, t0:t0 + TS],
                               "d_kat%d" % par, waits=ev_max(e_xrka_rel[par]))
                for half in range(2):
                    for c in range(NCH):
                        sch.op("tensor", lambda e, c=c, half=half, tl=tl: e.matmul(
                            pr_ps[:, half], o1_nfm[:, c, tl * TS:(tl + 1) * TS],
                            wo_sb[:, c, half * 512:(half + 1) * 512],
                            start=(c == 0), stop=False),
                            waits=ev_max(e_onfm, e_effrow, e_ones,
                                         e_prps_free[half]) if c == 0 else ())
                    e_pj = sch.op("tensor", lambda e, half=half: e.matmul(
                        pr_ps[:, half], ones_r[0:1, :],
                        effrow_b[0:1, half * 512:(half + 1) * 512],
                        start=False, stop=True), inc="t")
                    for c in range(NCH):
                        e_rm = sch.op("tensor", lambda e, c=c, half=half, par=par:
                                      e.matmul(
                                          rv_ps[:, half], xr_t[:, par, c, :],
                                          wrt_sb[:, c, half * 512:(half + 1) * 512],
                                          start=(c == 0), stop=(c == NCH - 1)),
                                      waits=ev_max(e_xr, e_wrt,
                                                   e_rvps_ff[half])
                                      if c == 0 else (),
                                      inc="t" if c == NCH - 1 else None)
                    e_sg = sch.op("scalar", lambda e, half=half: e.activation(
                        out=rsg_sb[:, half * 512:(half + 1) * 512],
                        in_=rv_ps[:, half], func=AF.Sigmoid),
                        waits=ev_max(e_rm))
                    e_rvps_sig[half] = e_sg
                    e_a1 = sch.op("vector", lambda e, half=half: e.tensor_tensor(
                        out=x1t[:, half * 512:(half + 1) * 512],
                        in0=x1t[:, half * 512:(half + 1) * 512],
                        in1=pr_ps[:, half], op=ALU.add),
                        waits=ev_max(e_pj, e_x1))
                    e_prps_free[half] = e_a1
                for half in range(2):
                    for c in range(NCH):
                        e_om = sch.op("tensor", lambda e, c=c, half=half, par=par:
                                      e.matmul(
                                          rv_ps[:, half], ka_t[:, par, c, :],
                                          wvt_sb[:, c, half * 512:(half + 1) * 512],
                                          start=(c == 0), stop=(c == NCH - 1)),
                                      waits=ev_max(e_ka, e_wvt, e_rvps_sig[half])
                                      if c == 0 else (),
                                      inc="t" if c == NCH - 1 else None)
                    e_f = sch.op("vector", lambda e, half=half: e.tensor_tensor(
                        out=fv_sb[:, half * 512:(half + 1) * 512],
                        in0=rsg_sb[:, half * 512:(half + 1) * 512],
                        in1=rv_ps[:, half], op=ALU.mult),
                        waits=ev_max(e_om, e_out_store))
                    e_rvps_ff[half] = e_f
                    e_o = sch.op("vector", lambda e, half=half: e.tensor_tensor(
                        out=x1t[:, half * 512:(half + 1) * 512],
                        in0=x1t[:, half * 512:(half + 1) * 512],
                        in1=fv_sb[:, half * 512:(half + 1) * 512], op=ALU.add))
                e_xrka_rel[par] = ("t", sch.cnt["t"])
                e_out_store = sch.dma("sync", out_d[s_, t0:t0 + TS, :], x1t[:],
                                      "d_out", waits=(e_o,))
                e_x1_user = e_out_store
            e_onfm_user = e_pj
            e_xn_user = e_qmm
            e_qf_user = ("t", sch.cnt["t"])
        e_kv_user = ("t", sch.cnt["t"])
    sch.op("sync", None, waits=ev_max(sch.ev("d_out")))




def build_l1b(nc):
    x1_d = nc.dram_tensor("x1", [SEQ_PER_CORE, S, D], F32, kind="ExternalInput").ap()
    xn1fm_d = nc.dram_tensor("xn1fm", [SEQ_PER_CORE, NCH, 128, S], BF16,
                             kind="ExternalInput").ap()
    xrfm_d = nc.dram_tensor("xrfm", [SEQ_PER_CORE, NCH, 128, S], BF16,
                            kind="ExternalInput").ap()
    kafm_d = nc.dram_tensor("kafm", [SEQ_PER_CORE, NCH, 128, S], BF16,
                            kind="ExternalInput").ap()
    kv_d = nc.dram_tensor("kv", [SEQ_PER_CORE, 16, 64, 65], BF16,
                          kind="ExternalInput").ap()
    wqa_d = nc.dram_tensor("Wq_a", [D, D], F32, kind="ExternalInput").ap()
    bq_d = nc.dram_tensor("bq_pc", [128, NCH], F32, kind="ExternalInput").ap()
    wo_d = nc.dram_tensor("Wo_a", [D, D], F32, kind="ExternalInput").ap()
    bor_d = nc.dram_tensor("bo_row", [1, D], F32, kind="ExternalInput").ap()
    bv_d = nc.dram_tensor("bv_pc", [128, NCH], F32, kind="ExternalInput").ap()
    wrt_d = nc.dram_tensor("WrT_f", [D, D], F32, kind="ExternalInput").ap()
    wvt_d = nc.dram_tensor("WvT_f", [D, D], F32, kind="ExternalInput").ap()
    out_d = nc.dram_tensor("out_final", [SEQ_PER_CORE, S, D], F32,
                           kind="ExternalOutput").ap()
    dr = {"x1": x1_d, "xn1fm": xn1fm_d, "xrfm": xrfm_d, "kafm": kafm_d,
          "kv": kv_d, "Wq_a": wqa_d, "bq_pc": bq_d, "Wo_a": wo_d,
          "bo_row": bor_d, "bv_pc": bv_d, "WrT_f": wrt_d, "WvT_f": wvt_d,
          "out_final": out_d}
    with contextlib.ExitStack() as ctx:
        sch = Sched(nc, ctx)
        block = ctx.enter_context(nc.Block())
        with contextlib.ExitStack() as pctx:
            emit_l1b(nc, sch, pctx.enter_context, dr)
            sch.build(block)
    return nc


def build_fused(nc):
    EI, EO, IN = "ExternalInput", "ExternalOutput", "Internal"
    t = nc.dram_tensor
    x_d = t("x", [SEQ_PER_CORE, S, D], F32, kind=EI).ap()
    wg_d = t("w_gate", [I, D], F32, kind=EI).ap()
    wu_d = t("w_up", [I, D], F32, kind=EI).ap()
    wd_d = t("w_down", [D, I], F32, kind=EI).ap()
    ident_d = t("ident", [128, 128], F32, kind=EI).ap()
    qkvwT_d = t("qkv_wT", [D, 3 * D], F32, kind=EI).ap()
    qkvb_d = t("qkv_b_pc", [128, 24], F32, kind=EI).ap()
    projwT_d = t("proj_wT", [D, D], F32, kind=EI).ap()
    projbr_d = t("proj_b_row", [1, D], F32, kind=EI).ap()
    wka_d = t("Wk_a", [D, D], F32, kind=EI).ap()
    wva_d = t("Wv_a", [D, D], F32, kind=EI).ap()
    wktf_d = t("WkT_f", [D, D], F32, kind=EI).ap()
    bkr_d = t("bk_row", [1, D], F32, kind=EI).ap()
    mur_d = t("mu_r_pc", [128, NCH], F32, kind=EI).ap()
    muk_d = t("mu_k_pc", [128, NCH], F32, kind=EI).ap()
    wqa_d = t("Wq_a", [D, D], F32, kind=EI).ap()
    bq_d = t("bq_pc", [128, NCH], F32, kind=EI).ap()
    wo_d = t("Wo_a", [D, D], F32, kind=EI).ap()
    bor_d = t("bo_row", [1, D], F32, kind=EI).ap()
    bv_d = t("bv_pc", [128, NCH], F32, kind=EI).ap()
    wrt_d = t("WrT_f", [D, D], F32, kind=EI).ap()
    wvt_d = t("WvT_f", [D, D], F32, kind=EI).ap()
    out_d = t("out_final", [SEQ_PER_CORE, S, D], F32, kind=EO).ap()
    y_b = t("y_buf", [SEQ_PER_CORE, S, D], F32, kind=IN).ap()
    xnfm_b = t("xnfm_buf", [SEQ_PER_CORE, NCH, 128, S], BF16, kind=IN).ap()
    x1_b = t("x1_buf", [SEQ_PER_CORE, S, D], F32, kind=IN).ap()
    xn1fm_b = t("xn1fm_buf", [SEQ_PER_CORE, NCH, 128, S], BF16, kind=IN).ap()
    xrfm_b = t("xrfm_buf", [SEQ_PER_CORE, NCH, 128, S], BF16, kind=IN).ap()
    kafm_b = t("kafm_buf", [SEQ_PER_CORE, NCH, 128, S], BF16, kind=IN).ap()
    kv_b = t("kv_buf", [SEQ_PER_CORE, 16, 64, 65], BF16, kind=IN).ap()

    phases = [
        ("a_", emit_l0a, {"x": x_d, "w_gate": wg_d, "w_up": wu_d,
                          "w_down": wd_d, "ident": ident_d,
                          "y_out": y_b, "xnfm_out": xnfm_b}),
        ("b_", emit_l0b, {"xnfm": xnfm_b, "y_in": y_b, "qkv_wT": qkvwT_d,
                          "qkv_b_pc": qkvb_d, "proj_wT": projwT_d,
                          "proj_b_row": projbr_d, "x1_out": x1_b}),
        ("c_", emit_l1a, {"x1": x1_b, "Wk_a": wka_d, "Wv_a": wva_d,
                          "WkT_f": wktf_d, "bk_row": bkr_d, "mu_r_pc": mur_d,
                          "mu_k_pc": muk_d, "ident": ident_d,
                          "xn1fm_out": xn1fm_b, "xrfm_out": xrfm_b,
                          "kafm_out": kafm_b, "kv_out": kv_b}),
        ("d_", emit_l1b, {"x1": x1_b, "xn1fm": xn1fm_b, "xrfm": xrfm_b,
                          "kafm": kafm_b, "kv": kv_b, "Wq_a": wqa_d,
                          "bq_pc": bq_d, "Wo_a": wo_d, "bo_row": bor_d,
                          "bv_pc": bv_d, "WrT_f": wrt_d, "WvT_f": wvt_d,
                          "out_final": out_d}),
    ]
    with contextlib.ExitStack() as ctx:
        sch = Sched(nc, ctx)
        block = ctx.enter_context(nc.Block())
        for pfx, emitf, dr in phases:
            with contextlib.ExitStack() as pctx:
                emitf(nc, sch, pctx.enter_context, dr, pfx=pfx)
                sch.barrier()
                sch.build(block)
        if END_CLEAR:
            sch.end_clear()
            sch.build(block)
    return nc


def prep_in_maps(inputs):
    f32 = np.float32
    x_full = np.ascontiguousarray(inputs["L0_attn_x"], dtype=f32)
    qkv_b = np.asarray(inputs["L0_attn_qkv_b"], dtype=f32)
    sh = {
        "w_gate": np.ascontiguousarray(inputs["L0_ffn_w_gate"], dtype=f32),
        "w_up": np.ascontiguousarray(inputs["L0_ffn_w_up"], dtype=f32),
        "w_down": np.ascontiguousarray(inputs["L0_ffn_w_down"], dtype=f32),
        "ident": np.eye(128, dtype=f32),
        "qkv_wT": np.ascontiguousarray(np.asarray(inputs["L0_attn_qkv_w"], f32).T),
        "qkv_b_pc": np.ascontiguousarray(qkv_b.reshape(24, 128).T),
        "proj_wT": np.ascontiguousarray(np.asarray(inputs["L0_attn_proj_w"], f32).T),
        "proj_b_row": np.ascontiguousarray(
            np.asarray(inputs["L0_attn_proj_b"], f32).reshape(1, D)),
        "Wk_a": np.ascontiguousarray(np.asarray(inputs["L1_attn_Wk"], f32)),
        "Wv_a": np.ascontiguousarray(np.asarray(inputs["L1_attn_Wv"], f32)),
        "WkT_f": np.ascontiguousarray(np.asarray(inputs["L1_ffn_W_k"], f32).T),
        "bk_row": np.ascontiguousarray(
            np.asarray(inputs["L1_attn_bk"], f32).reshape(1, D)),
        "mu_r_pc": np.ascontiguousarray(
            np.asarray(inputs["L1_ffn_mu_r"], f32).reshape(8, 128).T),
        "mu_k_pc": np.ascontiguousarray(
            np.asarray(inputs["L1_ffn_mu_k"], f32).reshape(8, 128).T),
        "Wq_a": np.ascontiguousarray(np.asarray(inputs["L1_attn_Wq"], f32)),
        "bq_pc": np.ascontiguousarray(
            np.asarray(inputs["L1_attn_bq"], f32).reshape(8, 128).T),
        "Wo_a": np.ascontiguousarray(np.asarray(inputs["L1_attn_Wo"], f32)),
        "bo_row": np.ascontiguousarray(
            np.asarray(inputs["L1_attn_bo"], f32).reshape(1, D)),
        "bv_pc": np.ascontiguousarray(
            np.asarray(inputs["L1_attn_bv"], f32).reshape(8, 128).T),
        "WrT_f": np.ascontiguousarray(np.asarray(inputs["L1_ffn_W_r"], f32).T),
        "WvT_f": np.ascontiguousarray(np.asarray(inputs["L1_ffn_W_v"], f32).T),
    }
    return [dict(sh, x=x_full[2 * c:2 * c + 2]) for c in range(NCORES)]



_NCS = {}


def _get_nc(name, builder):
    if name not in _NCS:
        _NCS[name] = builder(bass.Bass("TRN2", target_bir_lowering=False))
    return _NCS[name]


def kernel(**inputs):
    in_maps = prep_in_maps(inputs)
    nc = _get_nc("fused", build_fused)
    res = run_bass_kernel_spmd(nc, in_maps, core_ids=list(range(NCORES)))
    return np.concatenate([np.asarray(r["out_final"]) for r in res.results], axis=0)


# version 12
# speedup vs baseline: 1.1105x; 1.0088x over previous
import contextlib
import numpy as np
import ml_dtypes
import concourse.bass as bass
import concourse.mybir as mybir
from concourse.bass import broadcast_tensor_aps
from concourse.bass_utils import run_bass_kernel_spmd

F32 = mybir.dt.float32
BF16 = mybir.dt.bfloat16
AF = mybir.ActivationFunctionType
ALU = mybir.AluOpType
AX = mybir.AxisListType
bfdt = ml_dtypes.bfloat16

NCORES = 8
SEQ_PER_CORE = 2
S = 2048
D = 1024
I = 2816
TS = 128
NT = S // TS
NCH = D // 128
EPS_LN = 1e-5

ENGS = ("sync", "gpsimd", "scalar", "vector", "tensor")
OWN_WAITS = True
DRAIN_FUSABLE = True
END_CLEAR = True
BATCH_INCS = ("tensor",)


class Sched:

    def __init__(self, nc, ctx, sem_names=("t", "v", "a")):
        self.nc = nc
        self.ctx = ctx
        self.pfx = ""
        self.swdge_sems = set()
        self.items = {e: [] for e in ENGS}
        self.sems = {}
        self.cnt = {}
        for n in sem_names:
            self.sem(n)

    def ev(self, name):
        if name.startswith("d_"):
            name = self.pfx + name
        return (name, self.cnt[name])

    def barrier(self):
        snap = tuple((n, c) for n, c in self.cnt.items() if c > 0)
        for e in ENGS:
            self.op(e, None, waits=snap)

    def end_clear(self):
        self.barrier()
        self.sem("fin")
        for e in ENGS:
            if e != "gpsimd":
                self.op(e, lambda en: en.nop(), inc="fin")
        fin_target = self.cnt["fin"]
        sems = [h for n, h in self.sems.items() if n not in self.swdge_sems]

        def fn(e, sems=sems):
            ins = None
            for h in sems:
                ins = e.sem_clear(h)
            return ins
        self.op("gpsimd", fn, waits=(("fin", fin_target),))

    def sem(self, name):
        if name not in self.sems:
            self.sems[name] = self.ctx.enter_context(
                self.nc.semaphore("sem_" + name))
            self.cnt[name] = 0
        return name

    def op(self, eng, fn, waits=(), inc=None, amt=1):
        own = {"vector": "v", "scalar": "a"}.get(eng)
        own_wait = None
        if own is not None and fn is not None:
            if inc is None:
                inc = own
            own_wait = (own, self.cnt[own])
        ev = None
        waits = tuple(w for w in waits if w is not None)
        abs_cnt = None
        if inc is not None:
            self.cnt[inc] += amt
            abs_cnt = self.cnt[inc]
            ev = (inc, abs_cnt)
        self.items[eng].append((fn, waits, inc, amt, own_wait, abs_cnt))
        return ev

    def dma(self, eng, out, in_, sem, waits=()):
        sem = self.pfx + sem
        self.sem(sem)
        if eng == "gpsimd":
            self.swdge_sems.add(sem)
        return self.op(eng, lambda e, o=out, i=in_: e.dma_start(out=o, in_=i),
                       waits=waits, inc=sem, amt=16)

    def build(self, block):
        sems = self.sems
        waited_vals = {}
        for e in ENGS:
            for fn, waits, inc, amt, own_wait, abs_cnt in self.items[e]:
                for (s_, v_) in waits:
                    waited_vals.setdefault(s_, set()).add(v_)
        for e in ENGS:
            items = self.items[e]
            if not items:
                continue
            use_drain = e in ("vector", "scalar")
            batch = e in BATCH_INCS
            last_idx = {}
            for i, (fn, waits, inc, amt, own_wait, abs_cnt) in enumerate(items):
                if inc is not None and fn is not None:
                    last_idx[inc] = i
            plan = []
            pending = {}
            for i, (fn, waits, inc, amt, own_wait, abs_cnt) in enumerate(items):
                em = None
                if inc is not None and fn is not None:
                    if amt == 16:
                        em = (inc, 16)
                    elif not batch:
                        em = (inc, amt)
                    else:
                        pending[inc] = pending.get(inc, 0) + amt
                        if (abs_cnt in waited_vals.get(inc, ()) or
                                i == last_idx[inc]):
                            em = (inc, pending[inc])
                            pending[inc] = 0
                plan.append(em)

            def make_emit(items=items, plan=plan, use_drain=use_drain):
                def emit(engobj):
                    waited = {}
                    for (fn, waits, inc, amt, own_wait, abs_cnt), em in zip(items, plan):
                        if OWN_WAITS and own_wait is not None:
                            waits = (own_wait,) + waits
                        for (s_, v_) in waits:
                            if v_ > 0 and waited.get(s_, -1) < v_:
                                engobj.wait_ge(sems[s_], v_)
                                waited[s_] = v_
                        if fn is None:
                            continue
                        instr = fn(engobj)
                        if em is not None:
                            sname, a2 = em
                            if a2 == 16 or not use_drain:
                                instr.then_inc(sems[sname], a2)
                            else:
                                engobj.drain(fusable=DRAIN_FUSABLE).then_inc(
                                    sems[sname], a2)
                return emit
            getattr(block, e)(make_emit())
        self.items = {e: [] for e in ENGS}


def ev_max(*evs):
    best = {}
    for ev in evs:
        if ev is None or (isinstance(ev, tuple) and len(ev) == 0):
            continue
        if isinstance(ev, tuple) and isinstance(ev[0], tuple):
            for (s_, v_) in ev:
                if best.get(s_, -1) < v_:
                    best[s_] = v_
        else:
            s_, v_ = ev
            if best.get(s_, -1) < v_:
                best[s_] = v_
    return tuple(best.items())



def emit_quant(sch, q, src, out_bf16, C, wv, wa):
    NB = C // 16
    absx, sgn, a1, c5 = q["absx"], q["sgn"], q["a1"], q["c5"]
    bm, sc, rs = q["bm"], q["sc"], q["rs"]
    war = q.get("_last")
    wa = ev_max(wa, war)
    wv = ev_max(wv)
    e_absx = sch.op("scalar", lambda e: e.activation(
        out=absx[:, 0:C], in_=src, func=AF.Abs), waits=wa, inc="a")
    e_sgn = sch.op("scalar", lambda e: e.activation(
        out=sgn[:, 0:C], in_=src, func=AF.Sign), waits=wa, inc="a")
    e_bm = sch.op("vector", lambda e: e.tensor_reduce(
        out=bm[:, 0:NB].rearrange("p (b o) -> p b o", o=1),
        in_=src.rearrange("p (b s) -> p b s", s=16),
        axis=AX.X, op=ALU.max, apply_absolute_value=True), waits=wv, inc="v")
    e_sc = sch.op("vector", lambda e: e.tensor_scalar(
        out=sc[:, 0:NB], in0=bm[:, 0:NB], scalar1=1.0 / 6.0, scalar2=1e-12,
        op0=ALU.mult, op1=ALU.max), waits=(e_bm,), inc="v")
    e_rs = sch.op("vector", lambda e: e.reciprocal(
        out=rs[:, 0:NB], in_=sc[:, 0:NB]), waits=(e_sc,), inc="v")
    av = absx[:, 0:C].rearrange("p (b s) -> p b s", s=16)
    rv = rs[:, 0:NB].rearrange("p (b o) -> p b o", o=1)
    avb, rvb = broadcast_tensor_aps(av, rv)
    sch.op("vector", lambda e: e.tensor_tensor(out=av, in0=avb, in1=rvb, op=ALU.mult),
           waits=ev_max(e_absx, e_rs))
    u = absx[:, 0:C]
    sch.op("vector", lambda e: e.tensor_scalar(
        out=c5[:, 0:C], in0=u, scalar1=5.0, scalar2=2.0, op0=ALU.is_gt, op1=ALU.mult))
    for t_ in (3.5, 2.5):
        sch.op("vector", lambda e, t_=t_: e.scalar_tensor_tensor(
            out=c5[:, 0:C], in0=u, scalar=t_, in1=c5[:, 0:C],
            op0=ALU.is_gt, op1=ALU.add))
    sch.op("vector", lambda e: e.tensor_scalar(
        out=a1[:, 0:C], in0=u, scalar1=0.25, scalar2=None, op0=ALU.is_gt))
    for t_ in (0.75, 1.25, 1.75):
        sch.op("vector", lambda e, t_=t_: e.scalar_tensor_tensor(
            out=a1[:, 0:C], in0=u, scalar=t_, in1=a1[:, 0:C],
            op0=ALU.is_gt, op1=ALU.add))
    sch.op("vector", lambda e: e.scalar_tensor_tensor(
        out=a1[:, 0:C], in0=c5[:, 0:C], scalar=2.0, in1=a1[:, 0:C],
        op0=ALU.mult, op1=ALU.add))
    gv = sgn[:, 0:C].rearrange("p (b s) -> p b s", s=16)
    hv = sc[:, 0:NB].rearrange("p (b o) -> p b o", o=1)
    gvb, hvb = broadcast_tensor_aps(gv, hv)
    sch.op("vector", lambda e: e.tensor_tensor(out=gv, in0=gvb, in1=hvb, op=ALU.mult),
           waits=ev_max(e_sgn))
    ev = sch.op("vector", lambda e: e.tensor_tensor(
        out=out_bf16, in0=a1[:, 0:C], in1=sgn[:, 0:C], op=ALU.mult), inc="v")
    q["_last"] = ev
    return ev


def emit_ln(sch, b, x_src, out_xn, e_x):
    war = b.get("_last")
    wx = ev_max(e_x)
    wxa = ev_max(e_x, war)
    e_s1 = sch.op("vector", lambda e: e.reduce_sum(
        out=b["s1"][:], in_=x_src, axis=AX.X), waits=wx, inc="v")
    e_sq = sch.op("scalar", lambda e: e.activation(
        out=b["sqs"][:], in_=x_src, func=AF.Square), waits=wxa, inc="a")
    e_ssq = sch.op("vector", lambda e: e.reduce_sum(
        out=b["ssq"][:], in_=b["sqs"][:], axis=AX.X), waits=(e_sq,), inc="v")
    e_mean = sch.op("vector", lambda e: e.tensor_scalar(
        out=b["mean"][:], in0=b["s1"][:], scalar1=1.0 / D, scalar2=None,
        op0=ALU.mult), waits=ev_max(e_s1, e_ssq), inc="v")
    sch.op("vector", lambda e: e.tensor_scalar(
        out=b["ex2"][:], in0=b["ssq"][:], scalar1=1.0 / D, scalar2=EPS_LN,
        op0=ALU.mult, op1=ALU.add))
    sch.op("vector", lambda e: e.tensor_tensor(
        out=b["msq"][:], in0=b["mean"][:], in1=b["mean"][:], op=ALU.mult),
        waits=(e_mean,))
    e_var = sch.op("vector", lambda e: e.tensor_tensor(
        out=b["var"][:], in0=b["ex2"][:], in1=b["msq"][:], op=ALU.subtract), inc="v")
    e_sd = sch.op("scalar", lambda e: e.activation(
        out=b["sd"][:], in_=b["var"][:], func=AF.Sqrt),
        waits=ev_max(e_var, war), inc="a")
    e_rstd = sch.op("vector", lambda e: e.reciprocal(
        out=b["rstd"][:], in_=b["sd"][:]), waits=(e_sd,), inc="v")
    e_xn = sch.op("vector", lambda e: e.tensor_scalar(
        out=out_xn, in0=x_src, scalar1=b["mean"][:][:, 0:1],
        scalar2=b["rstd"][:][:, 0:1], op0=ALU.subtract, op1=ALU.mult),
        waits=(e_rstd,), inc="v")
    b["_last"] = e_xn
    return e_xn


def emit_transpose_evac(sch, psum, ident, srcs, dsts, w_tensor, w_evac,
                        evac_eng="scalar", dst_groups=None):
    evac_events = []
    n = len(srcs)
    last = None
    for cb in range(0, n, 4):
        nslot = min(4, n - cb)
        e_tps = None
        for s_i in range(nslot):
            src = srcs[cb + s_i]
            e_tps = sch.op("tensor", lambda e, src=src, s_i=s_i: e.transpose(
                psum[:, s_i, 0:128], src, ident), waits=w_tensor if s_i == 0 else (),
                inc="t" if s_i == nslot - 1 else None)
        if dst_groups is not None:
            dstg = dst_groups(cb, nslot)
            last = sch.op(evac_eng, lambda e, dstg=dstg, nslot=nslot: e.copy(
                out=dstg, in_=psum[:, 0:nslot, 0:128]),
                waits=ev_max(e_tps, w_evac), inc="a")
            evac_events.append(last)
        else:
            evs = []
            for s_i in range(nslot):
                dst = dsts[cb + s_i]
                evs.append(sch.op(evac_eng, lambda e, dst=dst, s_i=s_i: e.copy(
                    out=dst, in_=psum[:, s_i, 0:128]),
                    waits=ev_max(e_tps, w_evac) if s_i == 0 else (), inc="a"))
            last = evs[-1]
            evac_events.extend(evs)
        w_tensor = ev_max(w_tensor, last)
    return last, evac_events



def emit_l0a(nc, sch, E, dr, pfx=""):
    sch.pfx = pfx
    x_d = dr["x"]
    wg_d = dr["w_gate"]
    wu_d = dr["w_up"]
    wd_d = dr["w_down"]
    ident_d = dr["ident"]
    y_d = dr["y_out"]
    xnfm_d = dr["xnfm_out"]

    NIC = 6
    IC_OFF = [0, 512, 1024, 1536, 2048, 2560]
    IC_SZ = [512] * 5 + [256]
    NRT_G = I // 128
    NRT_D = D // 128

    wgqT = E(nc.sbuf_tensor(pfx + "wgqT", [128, NCH, I], BF16))
    wuqT = E(nc.sbuf_tensor(pfx + "wuqT", [128, NCH, I], BF16))
    wdqT = E(nc.sbuf_tensor(pfx + "wdqT", [128, NRT_G, D], BF16))

    q = {n: E(nc.sbuf_tensor(pfx + "q_" + n, [128, 1408], F32))
         for n in ("absx", "sgn", "a1", "c5")}
    q.update({n: E(nc.sbuf_tensor(pfx + "q_" + n, [128, 88], F32))
              for n in ("bm", "sc", "rs", "hs")})
    lnb = {n: E(nc.sbuf_tensor(pfx + "ln_" + n, [128, 1], F32))
           for n in ("s1", "ssq", "mean", "ex2", "msq", "var", "sd", "rstd")}
    xtile = E(nc.sbuf_tensor(pfx + "xtile", [128, 2, D], F32))
    xn_tm = E(nc.sbuf_tensor(pfx + "xn_tm", [128, D], F32))
    xn_stage = E(nc.sbuf_tensor(pfx + "xn_stage", [128, NCH, 128], BF16))
    xq_tm = E(nc.sbuf_tensor(pfx + "xq_tm", [128, D], BF16))
    xq_fm = E(nc.sbuf_tensor(pfx + "xq_fm", [128, NCH, 128], BF16))
    g_sb = E(nc.sbuf_tensor(pfx + "g_sb", [128, 512], F32))
    h_sb = E(nc.sbuf_tensor(pfx + "h_sb", [128, I], F32))
    hq_b = E(nc.sbuf_tensor(pfx + "hq_b", [128, I], BF16))
    hq_fm = E(nc.sbuf_tensor(pfx + "hq_fm", [128, NRT_G, 128], BF16))
    y_sb = E(nc.sbuf_tensor(pfx + "y_sb", [128, D], F32))
    lnb["sqs"] = y_sb
    wt_in = h_sb
    wq_b = hq_b
    idt = E(nc.sbuf_tensor(pfx + "idt", [128, 128], F32))
    idtb = E(nc.sbuf_tensor(pfx + "idtb", [128, 128], BF16))
    tp32 = E(nc.psum_tensor(pfx + "tp32", [128, 4, 128], F32))
    tpb16 = E(nc.psum_tensor(pfx + "tpb16", [128, 4, 256], BF16))
    g_ps = E(nc.psum_tensor(pfx + "g_ps", [128, 512], F32))
    u_ps = E(nc.psum_tensor(pfx + "u_ps", [128, 512], F32))
    d_ps = E(nc.psum_tensor(pfx + "d_ps", [128, 2, 512], F32))


    e_id = sch.dma("sync", idt[:], ident_d, "d_id")
    e_idb = sch.dma("gpsimd", idtb[:], ident_d, "d_idb")

    e_tpb_war = None
    wt_release = None
    for (w_src, n_rt, row_c, dstT, grp) in ((wg_d, NRT_G, D, wgqT, 2),
                                            (wu_d, NRT_G, D, wuqT, 2),
                                            (wd_d, NRT_D, I, wdqT, 1)):
        n_chT = row_c // 128
        for r0 in range(0, n_rt, grp):
            g_rows = min(grp, n_rt - r0)
            W = g_rows * row_c
            e_lds = []
            for gi in range(g_rows):
                e_lds.append(sch.dma(
                    "sync", wt_in[:, gi * row_c:(gi + 1) * row_c],
                    w_src[(r0 + gi) * 128:(r0 + gi + 1) * 128, :],
                    "d_w0", waits=ev_max(wt_release)))
            e_ld = ev_max(*e_lds)
            e_q = None
            for c0 in range(0, W, 1408):
                c1 = min(W, c0 + 1408)
                e_q = emit_quant(sch, q, wt_in[:, c0:c1],
                                 wq_b[:, c0:c1], c1 - c0, e_ld, e_ld)
            wt_release = e_q
            nch_g = W // 128
            srcs = [wq_b[:, c * 128:(c + 1) * 128] for c in range(nch_g)]
            dsts = [dstT[:, j % n_chT, (r0 + j // n_chT) * 128:
                         (r0 + j // n_chT + 1) * 128] for j in range(nch_g)]
            last, _ = emit_transpose_evac(
                sch, tpb16, idtb[:], srcs, dsts,
                w_tensor=ev_max(e_q, e_idb, e_tpb_war), w_evac=())
            e_tpb_war = last
            q["_last"] = ev_max(q["_last"], ("t", sch.cnt["t"]))
    e_wts = ("a", sch.cnt["a"])

    tiles = [(s_, t) for s_ in range(SEQ_PER_CORE) for t in range(NT)]
    ntile = len(tiles)
    e_load = {}
    st = {}
    cs = {"stg_tp": None, "xq_tp": None, "tp32_war": None,
          "tpb_war": e_tpb_war, "xnstage_free": None,
          "d_free": None, "hq_prev": None, "hq_tp": None}
    xtile_release = [None, None]

    def sched_load(idx):
        s_, t = tiles[idx]
        par = idx % 2
        e_load[idx] = sch.dma("sync", xtile[:, par],
                              x_d[s_, t * TS:(t + 1) * TS, :],
                              "d_x%d" % par,
                              waits=ev_max(xtile_release[par]))

    def stageA(idx):
        s_, t = tiles[idx]
        par = idx % 2
        t0 = t * TS
        e_x = e_load[idx]
        lnb["_last"] = ev_max(lnb.get("_last"), cs["stg_tp"])
        e_xn = emit_ln(sch, lnb, xtile[:, par], xn_tm[:], ev_max(e_x))
        srcs = [xn_tm[:, c * 128:(c + 1) * 128] for c in range(NCH)]
        dsts = [xn_stage[:, c, :] for c in range(NCH)]
        e_stg, _ = emit_transpose_evac(
            sch, tp32, idt[:], srcs, dsts,
            w_tensor=ev_max(e_xn, e_id, cs["tp32_war"]),
            w_evac=ev_max(cs["xnstage_free"]))
        cs["tp32_war"] = e_stg
        cs["stg_tp"] = e_stg
        cs["xnstage_free"] = sch.dma(
            "sync", xnfm_d[s_].rearrange("c p t -> p c t")[:, :, t0:t0 + TS],
            xn_stage[:], "d_stg", waits=(e_stg,))
        q["_last"] = ev_max(q["_last"], cs["xq_tp"])
        e_xq = emit_quant(sch, q, xn_tm[:], xq_tm[:], D, e_xn, e_xn)
        srcs = [xq_tm[:, c * 128:(c + 1) * 128] for c in range(NCH)]
        dsts = [xq_fm[:, c, :] for c in range(NCH)]
        e_xqfm, _ = emit_transpose_evac(
            sch, tpb16, idtb[:], srcs, dsts,
            w_tensor=ev_max(e_xq, e_idb, cs["tpb_war"], e_wts),
            w_evac=())
        cs["tpb_war"] = e_xqfm
        cs["xq_tp"] = e_xqfm
        st[idx] = {"xqfm": e_xqfm}

    def stageB_act(idx):
        Tb = sch.cnt["t"]
        st[idx]["Tb"] = Tb
        prev_u = st[idx - 1]["mults"][-1] if idx > 0 else None
        e_mult = None
        silus = []
        mults = []
        for ic in range(NIC):
            io0, isz = IC_OFF[ic], IC_SZ[ic]
            e_gate_f = ("t", Tb + 2 * ic + 1)
            e_up_f = ("t", Tb + 2 * ic + 2)
            e_silu = sch.op("scalar", lambda e, isz=isz: e.activation(
                out=g_sb[:, 0:isz], in_=g_ps[:, 0:isz], func=AF.Silu,
                scale=0.25),
                waits=ev_max(e_gate_f,
                             e_mult if ic > 0 else prev_u), inc="a")
            e_mult = sch.op(
                "vector", lambda e, isz=isz, io0=io0: e.scalar_tensor_tensor(
                    out=h_sb[:, io0:io0 + isz], in0=u_ps[:, 0:isz],
                    scalar=0.25, in1=g_sb[:, 0:isz],
                    op0=ALU.mult, op1=ALU.mult),
                waits=ev_max(e_silu, e_up_f,
                             cs["hq_prev"] if ic == 0 else None), inc="v")
            silus.append(e_silu)
            mults.append(e_mult)
        st[idx]["silus"] = silus
        st[idx]["mults"] = mults
        st[idx]["h"] = e_mult

    def stageB_mm(idx):
        e_xqfm = st[idx]["xqfm"]
        silus = st[idx]["silus"]
        mults = st[idx]["mults"]
        assert sch.cnt["t"] == st[idx]["Tb"]
        prev_g = st[idx - 1]["silus"][-1] if idx > 0 else None
        prev_u = st[idx - 1]["mults"][-1] if idx > 0 else None
        for ic in range(NIC):
            io0, isz = IC_OFF[ic], IC_SZ[ic]
            for (ps, wT, e_free) in (
                    (g_ps, wgqT, silus[ic - 1] if ic > 0 else prev_g),
                    (u_ps, wuqT, mults[ic - 1] if ic > 0 else prev_u)):
                for c in range(NCH):
                    sch.op(
                        "tensor", lambda e, c=c, ps=ps, wT=wT, io0=io0, isz=isz:
                        e.matmul(ps[:, 0:isz], xq_fm[:, c, :],
                                 wT[:, c, io0:io0 + isz],
                                 start=(c == 0), stop=(c == NCH - 1)),
                        waits=ev_max(e_xqfm, e_wts, e_free) if c == 0 else (),
                        inc="t" if c == NCH - 1 else None)
        assert sch.cnt["t"] == st[idx]["Tb"] + 2 * NIC

    def stageC_v(idx):
        e_h = st[idx]["h"]
        q["_last"] = ev_max(q["_last"], cs["hq_tp"])
        e_hq = None
        for half in range(2):
            e_hq = emit_quant(sch, q, h_sb[:, half * 1408:(half + 1) * 1408],
                              hq_b[:, half * 1408:(half + 1) * 1408], 1408,
                              e_h, e_h)
        cs["hq_prev"] = e_hq
        st[idx]["hq"] = e_hq

    def stageC_t(idx):
        srcs = [hq_b[:, c * 128:(c + 1) * 128] for c in range(NRT_G)]
        e_hqf, _ = emit_transpose_evac(
            sch, tpb16, idtb[:], srcs, None,
            w_tensor=ev_max(st[idx]["hq"], cs["tpb_war"]), w_evac=(),
            dst_groups=lambda cb, ns: hq_fm[:, cb:cb + ns, :])
        cs["tpb_war"] = e_hqf
        cs["hq_tp"] = e_hqf
        st[idx]["hqf"] = e_hqf

    def stageD(idx):
        s_, t = tiles[idx]
        par = idx % 2
        t0 = t * TS
        e_dn0 = None
        e_dn = None
        for half in range(2):
            for c in range(NRT_G):
                e_dn = sch.op(
                    "tensor", lambda e, c=c, half=half: e.matmul(
                        d_ps[:, half], hq_fm[:, c, :],
                        wdqT[:, c, half * 512:half * 512 + 512],
                        start=(c == 0), stop=(c == NRT_G - 1)),
                    waits=ev_max(st[idx]["hqf"], cs["d_free"]) if c == 0 else (),
                    inc="t" if c == NRT_G - 1 else None)
            if half == 0:
                e_dn0 = e_dn
        e_ya0 = sch.op("vector", lambda e, par=par: e.scalar_tensor_tensor(
            out=xtile[:, par, 0:512], in0=d_ps[:, 0], scalar=0.25,
            in1=xtile[:, par, 0:512], op0=ALU.mult, op1=ALU.add),
            waits=ev_max(e_dn0), inc="v")
        e_ya = sch.op("vector", lambda e, par=par: e.scalar_tensor_tensor(
            out=xtile[:, par, 512:1024], in0=d_ps[:, 1], scalar=0.25,
            in1=xtile[:, par, 512:1024], op0=ALU.mult, op1=ALU.add),
            waits=ev_max(e_dn), inc="v")
        cs["d_free"] = e_ya
        e_ys = sch.dma("sync", y_d[s_, t0:t0 + TS, :], xtile[:, par],
                       "d_y", waits=ev_max(e_ya0, e_ya))
        xtile_release[par] = e_ys

    sched_load(0)
    stageA(0)
    stageB_act(0)
    stageB_mm(0)
    for idx in range(ntile):
        if idx + 1 < ntile:
            sched_load(idx + 1)
            stageA(idx + 1)
        stageC_v(idx)
        if idx + 1 < ntile:
            stageB_act(idx + 1)
            stageB_mm(idx + 1)
        stageC_t(idx)
        stageD(idx)
    sch.op("sync", None, waits=ev_max(sch.ev("d_y"),
                                      sch.ev("d_stg")))




def build_l0a(nc):
    x_d = nc.dram_tensor("x", [SEQ_PER_CORE, S, D], F32, kind="ExternalInput").ap()
    wg_d = nc.dram_tensor("w_gate", [I, D], F32, kind="ExternalInput").ap()
    wu_d = nc.dram_tensor("w_up", [I, D], F32, kind="ExternalInput").ap()
    wd_d = nc.dram_tensor("w_down", [D, I], F32, kind="ExternalInput").ap()
    ident_d = nc.dram_tensor("ident", [128, 128], F32, kind="ExternalInput").ap()
    y_d = nc.dram_tensor("y_out", [SEQ_PER_CORE, S, D], F32, kind="ExternalOutput").ap()
    xnfm_d = nc.dram_tensor("xnfm_out", [SEQ_PER_CORE, NCH, 128, S], BF16,
                            kind="ExternalOutput").ap()
    dr = {"x": x_d, "w_gate": wg_d, "w_up": wu_d, "w_down": wd_d,
          "ident": ident_d, "y_out": y_d, "xnfm_out": xnfm_d}
    with contextlib.ExitStack() as ctx:
        sch = Sched(nc, ctx)
        block = ctx.enter_context(nc.Block())
        with contextlib.ExitStack() as pctx:
            emit_l0a(nc, sch, pctx.enter_context, dr)
            sch.build(block)
    return nc


def emit_l0b(nc, sch, E, dr, pfx=""):
    sch.pfx = pfx
    y_d = dr["y_in"]
    qkvwT_d = dr["qkv_wT"]
    qkvb_d = dr["qkv_b_pc"]
    projwT_d = dr["proj_wT"]
    projbr_d = dr["proj_b_row"]
    NH = 16
    NP = 8
    xnfm_d = dr["xnfm"]
    x1_d = dr["x1_out"]
    nrm_scr = nc.dram_tensor("nrm_scr", [SEQ_PER_CORE, NP, 8 * 512], F32,
                             kind="Internal").ap()
    onf_scr = nc.dram_tensor("onf_scr", [SEQ_PER_CORE, NP, 2, 64, S], BF16,
                             kind="Internal").ap()

    qkv_sb = E(nc.sbuf_tensor(pfx + "qkv_sb", [128, NCH, 3 * D], BF16))
    projw_sb = E(nc.sbuf_tensor(pfx + "projw_sb", [128, NCH, D], BF16))
    qkvb_sb = E(nc.sbuf_tensor(pfx + "qkvb_sb", [128, 24], F32))

    bvb = E(nc.sbuf_tensor(pfx + "bvb", [128, 8], BF16))
    ones_r = E(nc.sbuf_tensor(pfx + "ones_r", [1, 128], BF16))
    ones_f = E(nc.sbuf_tensor(pfx + "ones_f", [1, 64], F32))
    effrow_b = E(nc.sbuf_tensor(pfx + "effrow_b", [1, D], BF16))
    xnfm_sb = E(nc.sbuf_tensor(pfx + "xnfm_sb", [128, NCH, S], BF16))
    qf2 = E(nc.sbuf_tensor(pfx + "qf", [128, 2, S], BF16))
    kf2 = E(nc.sbuf_tensor(pfx + "kf", [128, 2, S], BF16))
    v_aug = E(nc.sbuf_tensor(pfx + "v_aug", [128, NT, NH * 65], BF16))
    p_sb = E(nc.sbuf_tensor(pfx + "p_sb", [128, 2, 2, 512], BF16))
    o_raw = E(nc.sbuf_tensor(pfx + "o_raw", [64, 2, 4, 512], BF16))
    nrm_sb = E(nc.sbuf_tensor(pfx + "nrm_sb", [128, 8 * 512], F32))
    rin_sb = E(nc.sbuf_tensor(pfx + "rin_sb", [128, 32], F32))
    o_nfm = E(nc.sbuf_tensor(pfx + "o_nfm", [128, NCH, S], BF16))
    projbr_b = o_nfm[0:1, 0, 0:D]
    y_tile = nrm_sb[:, 0:D]
    sT2 = E(nc.psum_tensor(pfx + "sT2", [128, 2, 512], F32))
    o_psA = E(nc.psum_tensor(pfx + "o_psA", [65, 512], F32))
    o_psB = E(nc.psum_tensor(pfx + "o_psB", [65, 512], F32))
    qk_ps = E(nc.psum_tensor(pfx + "qk_ps", [128, 512], F32))
    nbc_ps = E(nc.psum_tensor(pfx + "nbc_ps", [64, 512], F32))
    proj_ps = E(nc.psum_tensor(pfx + "proj_ps", [128, 2, 512], F32))


    e_qkvw = sch.dma("gpsimd", qkv_sb[:],
                     qkvwT_d.rearrange("(c p) o -> p c o", p=128), "d_qkvw")
    e_projw = sch.dma("gpsimd", projw_sb[:],
                      projwT_d.rearrange("(c p) o -> p c o", p=128), "d_projw")
    e_projb = sch.dma("gpsimd", projbr_b, projbr_d, "d_projb")
    e_qkvb = sch.dma("sync", qkvb_sb[:], qkvb_d, "d_qkvb")
    e_ones = sch.op("vector", lambda e: e.memset(ones_r[:], 1.0))
    e_ones_f = sch.op("vector", lambda e: e.memset(ones_f[:], 1.0))
    e_bvb = sch.op("vector", lambda e: e.tensor_copy(
        bvb[:], qkvb_sb[:, 16:24]), waits=(e_qkvb,))
    va_ones = v_aug[:].rearrange("p t (h c) -> p t h c", c=65)[:, :, :, 64:65]
    e_vones = sch.op("vector", lambda e: e.memset(va_ones, 1.0))

    e_eff = None
    for half in range(2):
        for c in range(NCH):
            sch.op("tensor", lambda e, c=c, half=half: e.matmul(
                proj_ps[0:1, half], bvb[:, c:c + 1],
                projw_sb[:, c, half * 512:half * 512 + 512],
                start=(c == 0), stop=False),
                waits=ev_max(e_bvb, e_projw) if (c == 0 and half == 0) else ())
        e_eff = sch.op("tensor", lambda e, half=half: e.matmul(
            proj_ps[0:1, half], ones_r[0:1, 0:1],
            projbr_b[0:1, half * 512:half * 512 + 512],
            start=False, stop=True),
            waits=ev_max(e_ones, e_projb) if half == 0 else (), inc="t")
    e_effrow = sch.op("scalar", lambda e: e.copy(
        out=effrow_b[:], in_=proj_ps[0:1, :].rearrange("p h j -> p (h j)")),
        waits=(e_eff,))

    e_xn_users = None
    e_onfm_users = None
    e_vaug_users = None
    e_x1_store = None
    e_onh_store = None
    e_nbc_free = None
    e_nbc_sb_free = {}
    for s_ in range(SEQ_PER_CORE):
        e_xn = sch.dma("sync", xnfm_sb[:],
                       xnfm_d[s_].rearrange("c p t -> p c t"), "d_xn",
                       waits=ev_max(e_xn_users))
        e_vevac = None
        e_vps_free = None
        e_vmm = None
        for tt in range(NT):
            for half in range(2):
                for c in range(NCH):
                    e_vmm = sch.op(
                        "tensor", lambda e, tt=tt, half=half, c=c: e.matmul(
                            qk_ps[:, 0:512],
                            xnfm_sb[:, c, tt * TS:(tt + 1) * TS],
                            qkv_sb[:, c, 2 * D + half * 512:2 * D + (half + 1) * 512],
                            start=(c == 0), stop=(c == NCH - 1)),
                        waits=ev_max(e_xn, e_qkvw, e_vps_free, e_vones)
                        if c == 0 else (),
                        inc="t" if c == NCH - 1 else None)
                vslice = v_aug[:].rearrange(
                    "p t (h c) -> p t h c", c=65)[:, tt, half * 8:(half + 1) * 8, 0:64]
                e_vevac = sch.op(
                    "scalar", lambda e, vslice=vslice: e.copy(
                        out=vslice,
                        in_=qk_ps[:, 0:512].rearrange("p (h c) -> p h c", c=64)),
                    waits=ev_max(e_vmm, e_vaug_users))
                e_vps_free = e_vevac
        e_vdone = e_vevac

        e_qkps_free = e_vps_free
        if s_ == 0:
            e_projps_free = {0: e_effrow, 1: e_effrow}
        e_qf_user = {0: None, 1: None}
        e_kf_user = {0: None, 1: None}
        e_praw_free = None
        e_sT_free = {0: None, 1: None}
        e_p_free = {}
        e_onh_user = None
        e_nrm_user = e_x1_store
        e_mm = None
        for hp in range(NP):
            hpar = hp % 2
            qf = qf2[:, hpar]
            kf = kf2[:, hpar]
            o0 = hp * 128
            e_qev = e_kev = None
            for which, (dst, bias_col, col0, e_user) in enumerate((
                    (qf, hp, o0, e_qf_user[hpar]),
                    (kf, 8 + hp, D + o0, e_kf_user[hpar]))):
                for qc in range(4):
                    for c in range(NCH):
                        e_mm = sch.op(
                            "tensor",
                            lambda e, c=c, qc=qc, col0=col0: e.matmul(
                                qk_ps[:, 0:512],
                                qkv_sb[:, c, col0:col0 + 128],
                                xnfm_sb[:, c, qc * 512:(qc + 1) * 512],
                                start=(c == 0), stop=(c == NCH - 1)),
                            waits=ev_max(e_xn, e_qkvw, e_qkps_free, e_user)
                            if c == 0 else (),
                            inc="t" if c == NCH - 1 else None)
                    e_ev = sch.op(
                        "scalar",
                        lambda e, dst=dst, qc=qc, bias_col=bias_col: e.activation(
                            out=dst[:, qc * 512:(qc + 1) * 512],
                            in_=qk_ps[:, 0:512], func=AF.Identity,
                            bias=qkvb_sb[:, bias_col:bias_col + 1]),
                        waits=ev_max(e_mm, e_qkvb))
                    e_qkps_free = e_ev
                if which == 0:
                    e_qev = e_ev
                else:
                    e_kev = e_ev
            e_att = {}
            for qc in range(4):
                for kt in range(NT):
                    par = kt % 2
                    sT = sT2 if par == 0 else proj_ps
                    for hi in range(2):
                        b0 = hi * 64
                        e_sc = sch.op(
                            "tensor",
                            lambda e, sT=sT, b0=b0, kt=kt, qc=qc, qf=qf, kf=kf,
                            hi=hi: e.matmul(
                                sT[:, hi, 0:512],
                                kf[b0:b0 + 64, kt * TS:(kt + 1) * TS],
                                qf[b0:b0 + 64, qc * 512:(qc + 1) * 512],
                                start=True, stop=True,
                                tile_position=(b0, 0)),
                            waits=ev_max(e_qev, e_kev, e_sT_free[par],
                                         (e_projps_free[0] if par == 1 else None),
                                         (e_projps_free[1] if par == 1 else None)),
                            inc="t")
                    e_exp = sch.op(
                        "scalar",
                        lambda e, sT=sT, par=par: e.activation(
                            out=p_sb[:, par].rearrange("p h j -> p (h j)"),
                            in_=sT[:, :, 0:512].rearrange("p h j -> p (h j)"),
                            func=AF.Exp, scale=0.125),
                        waits=ev_max(e_sc, e_p_free.get(par)))
                    e_sT_free[par] = e_exp
                    for hi, o_ps in enumerate((o_psA, o_psB)):
                        h = 2 * hp + hi
                        e_att[(qc, hi)] = sch.op(
                            "tensor",
                            lambda e, o_ps=o_ps, kt=kt, par=par, hi=hi, h=h:
                            e.matmul(
                                o_ps[:, 0:512],
                                v_aug[:].rearrange(
                                    "p t (hh c) -> p t hh c", c=65)[:, kt, h, :],
                                p_sb[:, par, hi, :],
                                start=(kt == 0), stop=(kt == NT - 1)),
                            waits=ev_max(e_exp, e_vdone,
                                         (e_praw_free if kt == 0 else None)),
                            inc="t")
                    e_p_free[par] = e_att[(qc, 1)]
                evs = []
                for hi, o_ps in enumerate((o_psA, o_psB)):
                    evs.append(sch.op(
                        "scalar", lambda e, o_ps=o_ps, hi=hi, qc=qc: e.copy(
                            out=o_raw[:, hi, qc, :], in_=o_ps[0:64, 0:512]),
                        waits=ev_max(e_att[(qc, hi)], e_onh_user)))
                    evs.append(sch.op(
                        "scalar", lambda e, o_ps=o_ps, hi=hi, qc=qc: e.copy(
                            out=nrm_sb[64:65, (qc * 2 + hi) * 512:
                                       (qc * 2 + hi + 1) * 512],
                            in_=o_ps[64:65, 0:512]),
                        waits=ev_max(e_nrm_user)))
                e_praw_free = ev_max(*evs)
            e_qf_user[hpar] = e_kf_user[hpar] = ("t", sch.cnt["t"])
            e_h1 = sch.dma("sync", nrm_scr[s_, hp], nrm_sb[64:65, :], "d_nbo",
                           waits=ev_max(e_praw_free))
            e_nrm_user = e_h1
            e_h2 = sch.dma("sync", rin_sb[:],
                           nrm_scr[s_, hp].rearrange("(r j) -> r j", r=128),
                           "d_nbi", waits=(e_h1,))
            e_rcp = sch.op("vector", lambda e: e.reciprocal(
                out=rin_sb[:], in_=rin_sb[:]), waits=(e_h2,))
            e_h3 = sch.dma("sync", nrm_scr[s_, hp].rearrange("(r j) -> r j", r=128),
                           rin_sb[:], "d_nbo", waits=(e_rcp,))
            e_h4 = sch.dma("sync", nrm_sb[0:1, :],
                           nrm_scr[s_, hp].rearrange("j -> () j"),
                           "d_nbi", waits=(e_h3,))
            e_div_last = None
            for qc in range(4):
                for hi in range(2):
                    slot = qc * 2 + hi
                    e_nbc = sch.op(
                        "tensor", lambda e, slot=slot: e.matmul(
                            nbc_ps[:, 0:512], ones_f[0:1, :],
                            nrm_sb[0:1, slot * 512:(slot + 1) * 512],
                            start=True, stop=True),
                        waits=ev_max(e_h4, e_nbc_free, e_ones_f), inc="t")
                    e_div = sch.op(
                        "vector", lambda e, hi=hi, qc=qc: e.tensor_tensor(
                            out=o_raw[:, hi, qc, :], in0=o_raw[:, hi, qc, :],
                            in1=nbc_ps[0:64, 0:512], op=ALU.mult),
                        waits=ev_max(e_nbc, e_onh_store))
                    e_nbc_free = e_div
                    e_div_last = e_div
            e_onh_store = sch.dma(
                "sync",
                onf_scr[s_, hp].rearrange("h d (qc j) -> d h qc j", j=512),
                o_raw[:], "d_onh", waits=ev_max(e_div_last))
            e_onh_user = e_onh_store
        e_onfm = sch.dma(
            "sync", o_nfm[:],
            onf_scr[s_].rearrange("hp h d t -> (h d) hp t"), "d_onfm",
            waits=ev_max(e_onh_store, e_onfm_users, e_effrow))
        e_x1v = None
        e_pj = None
        for tt in range(NT):
            e_yld = sch.dma("sync", y_tile, y_d[s_, tt * TS:(tt + 1) * TS, :],
                            "d_yl", waits=ev_max(e_x1_store, e_onfm))
            for half in range(2):
                for c in range(NCH):
                    sch.op("tensor", lambda e, c=c, half=half, tt=tt: e.matmul(
                        proj_ps[:, half], o_nfm[:, c, tt * TS:(tt + 1) * TS],
                        projw_sb[:, c, half * 512:(half + 1) * 512],
                        start=(c == 0), stop=False),
                        waits=ev_max(e_onfm, e_effrow, e_ones,
                                     e_projps_free[half]) if c == 0 else ())
                e_pj = sch.op("tensor", lambda e, half=half: e.matmul(
                    proj_ps[:, half], ones_r[0:1, :],
                    effrow_b[0:1, half * 512:(half + 1) * 512],
                    start=False, stop=True), inc="t")
                e_add = sch.op(
                    "vector", lambda e, half=half: e.tensor_tensor(
                        out=y_tile[:, half * 512:(half + 1) * 512],
                        in0=y_tile[:, half * 512:(half + 1) * 512],
                        in1=proj_ps[:, half], op=ALU.add),
                    waits=ev_max(e_pj, e_yld))
                e_projps_free[half] = e_add
                e_x1v = e_add
            e_x1_store = sch.dma("sync", x1_d[s_, tt * TS:(tt + 1) * TS, :],
                                 y_tile, "d_x1", waits=(e_x1v,))
        e_xn_users = ev_max(e_vmm, e_mm)
        e_onfm_users = e_pj
        e_vaug_users = ("t", sch.cnt["t"])
    sch.op("sync", None, waits=ev_max(sch.ev("d_x1")))




def build_l0b(nc):
    xnfm_d = nc.dram_tensor("xnfm", [SEQ_PER_CORE, NCH, 128, S], BF16,
                            kind="ExternalInput").ap()
    y_d = nc.dram_tensor("y_in", [SEQ_PER_CORE, S, D], F32, kind="ExternalInput").ap()
    qkvwT_d = nc.dram_tensor("qkv_wT", [D, 3 * D], F32, kind="ExternalInput").ap()
    qkvb_d = nc.dram_tensor("qkv_b_pc", [128, 24], F32, kind="ExternalInput").ap()
    projwT_d = nc.dram_tensor("proj_wT", [D, D], F32, kind="ExternalInput").ap()
    projbr_d = nc.dram_tensor("proj_b_row", [1, D], F32, kind="ExternalInput").ap()
    x1_d = nc.dram_tensor("x1_out", [SEQ_PER_CORE, S, D], F32,
                          kind="ExternalOutput").ap()
    dr = {"xnfm": xnfm_d, "y_in": y_d, "qkv_wT": qkvwT_d, "qkv_b_pc": qkvb_d,
          "proj_wT": projwT_d, "proj_b_row": projbr_d, "x1_out": x1_d}
    with contextlib.ExitStack() as ctx:
        sch = Sched(nc, ctx)
        block = ctx.enter_context(nc.Block())
        with contextlib.ExitStack() as pctx:
            emit_l0b(nc, sch, pctx.enter_context, dr)
            sch.build(block)
    return nc


def emit_l1a(nc, sch, E, dr, pfx=""):
    sch.pfx = pfx
    x1_d = dr["x1"]
    wka_d = dr["Wk_a"]
    wva_d = dr["Wv_a"]
    wktf_d = dr["WkT_f"]
    bkr_d = dr["bk_row"]
    mur_d = dr["mu_r_pc"]
    muk_d = dr["mu_k_pc"]
    ident_d = dr["ident"]
    NH = 16
    xn1fm_d = dr["xn1fm_out"]
    xrfm_d = dr["xrfm_out"]
    kafm_d = dr["kafm_out"]
    kv_d = dr["kv_out"]

    wka_sb = E(nc.sbuf_tensor(pfx + "wka_sb", [128, NCH, D], BF16))
    wva_sb = E(nc.sbuf_tensor(pfx + "wva_sb", [128, NCH, D], BF16))
    wktf_sb = E(nc.sbuf_tensor(pfx + "wktf_sb", [128, NCH, D], BF16))
    bkr_b = E(nc.sbuf_tensor(pfx + "bkr_b", [1, D], BF16))
    mur_sb = E(nc.sbuf_tensor(pfx + "mur_sb", [128, NCH], F32))
    muk_sb = E(nc.sbuf_tensor(pfx + "muk_sb", [128, NCH], F32))
    ones_r = E(nc.sbuf_tensor(pfx + "ones_r", [1, 128], BF16))
    idt = E(nc.sbuf_tensor(pfx + "idt", [128, 128], F32))
    lnb = {n: E(nc.sbuf_tensor(pfx + "ln_" + n, [128, 1], F32))
           for n in ("s1", "ssq", "mean", "ex2", "msq", "var", "sd", "rstd")}
    lnb["sqs"] = E(nc.sbuf_tensor(pfx + "ln_sqs", [128, D], F32))
    x1t = E(nc.sbuf_tensor(pfx + "x1t", [128, 2, D], F32))
    xn1_tm = E(nc.sbuf_tensor(pfx + "xn1_tm", [128, D], F32))
    win = E(nc.sbuf_tensor(pfx + "win", [128, NCH, 129], F32))
    stg = E(nc.sbuf_tensor(pfx + "stg", [128, NCH, 128], BF16))
    xr_stg = E(nc.sbuf_tensor(pfx + "xr_stg", [128, NCH, 128], BF16))
    dmix = E(nc.sbuf_tensor(pfx + "dmix", [128, NCH, 128], F32))
    tmix = E(nc.sbuf_tensor(pfx + "tmix", [128, NCH, 128], F32))
    xk_grp = E(nc.sbuf_tensor(pfx + "xk_grp", [128, NCH, 512], BF16))
    rl_sb = E(nc.sbuf_tensor(pfx + "rl_sb", [128, 512], F32))
    ka_stg = E(nc.sbuf_tensor(pfx + "ka_stg", [128, NCH, 512], BF16))
    kfeat = E(nc.sbuf_tensor(pfx + "kfeat", [128, NT, D], BF16))
    v_aug = E(nc.sbuf_tensor(pfx + "v_aug", [128, NT, NH * 65], BF16))
    e_sb = E(nc.sbuf_tensor(pfx + "e_sb", [128, D], F32))
    r_sb = E(nc.sbuf_tensor(pfx + "r_sb", [128, D], F32))
    kv_stg = E(nc.sbuf_tensor(pfx + "kv_stg", [64, NH, 65], BF16))
    tp32 = E(nc.psum_tensor(pfx + "tp32", [128, 4, 128], F32))
    k_ps = E(nc.psum_tensor(pfx + "k_ps", [128, 512], F32))
    v_ps = E(nc.psum_tensor(pfx + "v_ps", [128, 512], F32))
    kf_ps = E(nc.psum_tensor(pfx + "kf_ps", [128, 2, 512], F32))
    kv_ps = E(nc.psum_tensor(pfx + "kv_ps", [64, 2, 512], F32))


    e_wka = sch.dma("gpsimd", wka_sb[:],
                    wka_d.rearrange("(c p) o -> p c o", p=128), "d_wka")
    e_wva = sch.dma("gpsimd", wva_sb[:],
                    wva_d.rearrange("(c p) o -> p c o", p=128), "d_wva")
    e_wktf = sch.dma("gpsimd", wktf_sb[:],
                     wktf_d.rearrange("(c p) o -> p c o", p=128), "d_wktf")
    e_bkr = sch.dma("gpsimd", bkr_b[:], bkr_d, "d_bkr")
    e_mur = sch.dma("sync", mur_sb[:], mur_d, "d_mur")
    e_muk = sch.dma("sync", muk_sb[:], muk_d, "d_muk")
    e_id = sch.dma("sync", idt[:], ident_d, "d_id")
    e_ones = sch.op("vector", lambda e: e.memset(ones_r[:], 1.0))
    va_ones = v_aug[:].rearrange("p t (h c) -> p t h c", c=65)[:, :, :, 64:65]
    e_vones = sch.op("vector", lambda e: e.memset(va_ones, 1.0))

    x1_release = [None, None]
    e_stg_user = None
    e_xr_user = None
    e_ka_user = None
    e_tp32_war = None
    e_kps_free = None
    e_vps_free = None
    e_kf_free = None
    e_dmix_user = None
    e_win_user = None
    e_kfeat_users = None
    e_vaug_users = None
    e_kvstg_free = None
    e_er_user = None
    e_load = {}
    tiles = [(s_, t) for s_ in range(SEQ_PER_CORE) for t in range(NT)]

    def sched_load(idx):
        s_, t = tiles[idx]
        par = idx % 2
        e_load[idx] = sch.dma("sync", x1t[:, par],
                              x1_d[s_, t * TS:(t + 1) * TS, :],
                              "d_x%d" % par, waits=ev_max(x1_release[par]))

    sched_load(0)
    for idx, (s_, t) in enumerate(tiles):
        par = idx % 2
        t0 = t * TS
        g = t % 4
        if idx + 1 < len(tiles):
            sched_load(idx + 1)
        e_x = e_load[idx]
        if t == 0:
            e_z = sch.op("vector", lambda e: e.memset(win[:, :, 0:1], 0.0),
                         waits=ev_max(e_win_user))
        else:
            e_z = sch.op("vector", lambda e: e.tensor_copy(
                win[:, :, 0:1], win[:, :, 128:129]), waits=ev_max(e_win_user))
        e_xn = emit_ln(sch, lnb, x1t[:, par], xn1_tm[:], e_x)
        x1_release[par] = e_xn
        evs = []
        for cb in (0, 4):
            e_tps = None
            for s_i in range(4):
                c = cb + s_i
                e_tps = sch.op("tensor", lambda e, c=c, s_i=s_i: e.transpose(
                    tp32[:, s_i, 0:128], xn1_tm[:, c * 128:(c + 1) * 128], idt[:]),
                    waits=ev_max(e_xn, e_id, e_tp32_war) if s_i == 0 else (),
                    inc="t" if s_i == 3 else None)
            for s_i in range(4):
                c = cb + s_i
                sch.op("scalar", lambda e, c=c, s_i=s_i: e.copy(
                    out=win[:, c, 1:129], in_=tp32[:, s_i, 0:128]),
                    waits=ev_max(e_tps, e_z, e_dmix_user) if s_i == 0 else ())
                evs.append(sch.op("scalar", lambda e, c=c, s_i=s_i: e.copy(
                    out=stg[:, c, :], in_=tp32[:, s_i, 0:128]),
                    waits=ev_max(e_stg_user)))
            e_tp32_war = evs[-1]
        e_evac = evs[-1]
        e_win_user = None
        e_stg_user = sch.dma(
            "sync", xn1fm_d[s_].rearrange("c p t -> p c t")[:, :, t0:t0 + TS],
            stg[:], "d_stg", waits=(e_evac,))
        e_d = sch.op("vector", lambda e: e.tensor_tensor(
            out=dmix[:], in0=win[:, :, 1:129], in1=win[:, :, 0:128],
            op=ALU.subtract), waits=ev_max(e_evac))
        mu_r_bc = mur_sb[:].rearrange("p (c o) -> p c o", o=1)
        mu_k_bc = muk_sb[:].rearrange("p (c o) -> p c o", o=1)
        dv = dmix[:]
        dvb, murb = broadcast_tensor_aps(dv, mu_r_bc)
        _, mukb = broadcast_tensor_aps(dv, mu_k_bc)
        e_t1 = sch.op("vector", lambda e: e.tensor_tensor(
            out=tmix[:], in0=dvb, in1=murb, op=ALU.mult),
            waits=ev_max(e_d, e_mur))
        e_xr = sch.op("vector", lambda e: e.tensor_tensor(
            out=xr_stg[:], in0=tmix[:], in1=win[:, :, 0:128], op=ALU.add),
            waits=ev_max(e_xr_user))
        e_xr_user = sch.dma(
            "sync", xrfm_d[s_].rearrange("c p t -> p c t")[:, :, t0:t0 + TS],
            xr_stg[:], "d_xr", waits=(e_xr,))
        e_t2 = sch.op("vector", lambda e: e.tensor_tensor(
            out=tmix[:], in0=dvb, in1=mukb, op=ALU.mult), waits=ev_max(e_muk))
        e_xk = sch.op("vector", lambda e, g=g: e.tensor_tensor(
            out=xk_grp[:, :, g * 128:(g + 1) * 128], in0=tmix[:],
            in1=win[:, :, 0:128], op=ALU.add), waits=ev_max(e_kf_free))
        e_win_user = e_xk
        e_dmix_user = e_t2
        kf_evs = []
        for half in range(2):
            for c in range(NCH):
                e_kmm = sch.op("tensor", lambda e, c=c, half=half: e.matmul(
                    k_ps[:, 0:512], stg[:, c, :],
                    wka_sb[:, c, half * 512:(half + 1) * 512],
                    start=(c == 0), stop=False),
                    waits=ev_max(e_evac, e_wka, e_kps_free) if c == 0 else ())
            e_kmm = sch.op("tensor", lambda e, half=half: e.matmul(
                k_ps[:, 0:512], ones_r[0:1, :],
                bkr_b[0:1, half * 512:(half + 1) * 512],
                start=False, stop=True),
                waits=ev_max(e_ones, e_bkr), inc="t")
            e_e = sch.op("scalar", lambda e, half=half: e.activation(
                out=e_sb[:, half * 512:(half + 1) * 512], in_=k_ps[:, 0:512],
                func=AF.Exp), waits=ev_max(e_kmm, e_er_user))
            e_r = sch.op("scalar", lambda e, half=half: e.activation(
                out=r_sb[:, half * 512:(half + 1) * 512], in_=k_ps[:, 0:512],
                func=AF.Relu), waits=())
            e_kf = sch.op("vector", lambda e, half=half, t=t: e.scalar_tensor_tensor(
                out=kfeat[:, t, half * 512:(half + 1) * 512],
                in0=e_sb[:, half * 512:(half + 1) * 512], scalar=1.0,
                in1=r_sb[:, half * 512:(half + 1) * 512],
                op0=ALU.min, op1=ALU.add),
                waits=ev_max(e_r, e_kfeat_users))
            e_kps_free = e_r
            kf_evs.append(e_kf)
        e_kf_done = kf_evs[-1]
        e_er_user = e_kf_done
        v_evs = []
        for half in range(2):
            for c in range(NCH):
                e_vmm = sch.op("tensor", lambda e, c=c, half=half: e.matmul(
                    v_ps[:, 0:512], stg[:, c, :],
                    wva_sb[:, c, half * 512:(half + 1) * 512],
                    start=(c == 0), stop=(c == NCH - 1)),
                    waits=ev_max(e_evac, e_wva, e_vps_free, e_vones)
                    if c == 0 else (),
                    inc="t" if c == NCH - 1 else None)
            vslice = v_aug[:].rearrange(
                "p t (h c) -> p t h c", c=65)[:, t, half * 8:(half + 1) * 8, 0:64]
            e_vev = sch.op("scalar", lambda e, vslice=vslice: e.copy(
                out=vslice,
                in_=v_ps[:, 0:512].rearrange("p (h c) -> p h c", c=64)),
                waits=ev_max(e_vmm, e_vaug_users))
            e_vps_free = e_vev
            v_evs.append(e_vev)
        e_v_done = v_evs[-1]
        e_stg_user = ev_max(e_stg_user, ("t", sch.cnt["t"]))
        if g == 3:
            gi = t // 4
            ka_evs = []
            for dc in range(NCH):
                for cc in range(NCH):
                    e_km = sch.op("tensor", lambda e, dc=dc, cc=cc: e.matmul(
                        kf_ps[:, dc % 2], wktf_sb[:, cc, dc * 128:(dc + 1) * 128],
                        xk_grp[:, cc, :], start=(cc == 0), stop=(cc == NCH - 1)),
                        waits=ev_max(e_xk, e_wktf,
                                     ka_evs[-2] if len(ka_evs) >= 2 else None)
                        if cc == 0 else (),
                        inc="t" if cc == NCH - 1 else None)
                e_rl = sch.op("scalar", lambda e, dc=dc: e.activation(
                    out=rl_sb[:], in_=kf_ps[:, dc % 2], func=AF.Relu),
                    waits=ev_max(e_km))
                e_ka = sch.op("scalar", lambda e, dc=dc: e.activation(
                    out=ka_stg[:, dc, :], in_=rl_sb[:], func=AF.Square),
                    waits=ev_max(e_ka_user))
                ka_evs.append(e_ka)
            e_ka_user = sch.dma(
                "sync",
                kafm_d[s_].rearrange("c p t -> p c t")[:, :, gi * 512:(gi + 1) * 512],
                ka_stg[:], "d_ka", waits=(ka_evs[-1],))
            e_kf_free = ("a", sch.cnt["a"])
        if t == NT - 1:
            e_kv_evs = []
            e_kvps_free = {}
            for h in range(NH):
                pb = h % 2
                for tt in range(NT):
                    e_kvm = sch.op(
                        "tensor", lambda e, h=h, tt=tt, pb=pb: e.matmul(
                            kv_ps[:, pb, 0:65],
                            kfeat[:, tt, (h // 2) * 128 + (h % 2) * 64:
                                  (h // 2) * 128 + (h % 2) * 64 + 64],
                            v_aug[:].rearrange(
                                "p t (hh c) -> p t hh c", c=65)[:, tt, h, :],
                            start=(tt == 0), stop=(tt == NT - 1)),
                        waits=ev_max(e_kf_done, e_v_done, e_kvps_free.get(pb))
                        if tt == 0 else (),
                        inc="t" if tt == NT - 1 else None)
                e_kve = sch.op("scalar", lambda e, h=h, pb=pb: e.copy(
                    out=kv_stg[:, h, :], in_=kv_ps[0:64, pb, 0:65]),
                    waits=ev_max(e_kvm, e_kvstg_free))
                e_kvps_free[pb] = e_kve
                e_kv_evs.append(e_kve)
            e_kvstg_free = sch.dma(
                "sync", kv_d[s_].rearrange("h d c -> d h c"), kv_stg[:],
                "d_kv", waits=(e_kv_evs[-1],))
            e_kfeat_users = ("t", sch.cnt["t"])
            e_vaug_users = ("t", sch.cnt["t"])
    sch.op("sync", None, waits=ev_max(
        sch.ev("d_kv"), sch.ev("d_stg"),
        sch.ev("d_xr"), sch.ev("d_ka")))




def build_l1a(nc):
    x1_d = nc.dram_tensor("x1", [SEQ_PER_CORE, S, D], F32, kind="ExternalInput").ap()
    wka_d = nc.dram_tensor("Wk_a", [D, D], F32, kind="ExternalInput").ap()
    wva_d = nc.dram_tensor("Wv_a", [D, D], F32, kind="ExternalInput").ap()
    wktf_d = nc.dram_tensor("WkT_f", [D, D], F32, kind="ExternalInput").ap()
    bkr_d = nc.dram_tensor("bk_row", [1, D], F32, kind="ExternalInput").ap()
    mur_d = nc.dram_tensor("mu_r_pc", [128, NCH], F32, kind="ExternalInput").ap()
    muk_d = nc.dram_tensor("mu_k_pc", [128, NCH], F32, kind="ExternalInput").ap()
    ident_d = nc.dram_tensor("ident", [128, 128], F32, kind="ExternalInput").ap()
    xn1fm_d = nc.dram_tensor("xn1fm_out", [SEQ_PER_CORE, NCH, 128, S], BF16,
                             kind="ExternalOutput").ap()
    xrfm_d = nc.dram_tensor("xrfm_out", [SEQ_PER_CORE, NCH, 128, S], BF16,
                            kind="ExternalOutput").ap()
    kafm_d = nc.dram_tensor("kafm_out", [SEQ_PER_CORE, NCH, 128, S], BF16,
                            kind="ExternalOutput").ap()
    kv_d = nc.dram_tensor("kv_out", [SEQ_PER_CORE, 16, 64, 65], BF16,
                          kind="ExternalOutput").ap()
    dr = {"x1": x1_d, "Wk_a": wka_d, "Wv_a": wva_d, "WkT_f": wktf_d,
          "bk_row": bkr_d, "mu_r_pc": mur_d, "mu_k_pc": muk_d, "ident": ident_d,
          "xn1fm_out": xn1fm_d, "xrfm_out": xrfm_d, "kafm_out": kafm_d,
          "kv_out": kv_d}
    with contextlib.ExitStack() as ctx:
        sch = Sched(nc, ctx)
        block = ctx.enter_context(nc.Block())
        with contextlib.ExitStack() as pctx:
            emit_l1a(nc, sch, pctx.enter_context, dr)
            sch.build(block)
    return nc


def emit_l1b(nc, sch, E, dr, pfx=""):
    sch.pfx = pfx
    x1_d = dr["x1"]
    wqa_d = dr["Wq_a"]
    bq_d = dr["bq_pc"]
    wo_d = dr["Wo_a"]
    bor_d = dr["bo_row"]
    bv_d = dr["bv_pc"]
    wrt_d = dr["WrT_f"]
    wvt_d = dr["WvT_f"]
    NH = 16
    NP = 8
    xn1fm_d = dr["xn1fm"]
    xrfm_d = dr["xrfm"]
    kafm_d = dr["kafm"]
    kv_d = dr["kv"]
    out_d = dr["out_final"]
    nrm_scr = nc.dram_tensor("nrm2_scr", [SEQ_PER_CORE, 4, 2, 8 * 512], F32,
                             kind="Internal").ap()
    o1_scr = nc.dram_tensor("o1_scr", [SEQ_PER_CORE, 4, NP, 2, 64, 512], BF16,
                            kind="Internal").ap()

    wqa_sb = E(nc.sbuf_tensor(pfx + "wqa_sb", [128, NCH, D], BF16))
    wo_sb = E(nc.sbuf_tensor(pfx + "wo_sb", [128, NCH, D], BF16))
    wrt_sb = E(nc.sbuf_tensor(pfx + "wrt_sb", [128, NCH, D], BF16))
    wvt_sb = E(nc.sbuf_tensor(pfx + "wvt_sb", [128, NCH, D], BF16))
    bq_sb = E(nc.sbuf_tensor(pfx + "bq_sb", [128, NCH], F32))
    bv_sb = E(nc.sbuf_tensor(pfx + "bv_sb", [128, NCH], F32))
    bor_b = E(nc.sbuf_tensor(pfx + "bor_b", [1, D], BF16))
    bvb = E(nc.sbuf_tensor(pfx + "bvb", [128, NCH], BF16))
    ones_r = E(nc.sbuf_tensor(pfx + "ones_r", [1, 128], BF16))
    ones_f = E(nc.sbuf_tensor(pfx + "ones_f", [1, 64], F32))
    effrow_b = E(nc.sbuf_tensor(pfx + "effrow_b", [1, D], BF16))
    kv_sb = E(nc.sbuf_tensor(pfx + "kv_sb", [128, NP, 65], BF16))
    xn1_qc = E(nc.sbuf_tensor(pfx + "xn1_qc", [128, NCH, 512], BF16))
    qfeat = E(nc.sbuf_tensor(pfx + "qfeat", [128, NCH, 512], BF16))
    e_sb = E(nc.sbuf_tensor(pfx + "e_sb", [128, 512], F32))
    r_sb = E(nc.sbuf_tensor(pfx + "r_sb", [128, 512], F32))
    o_raw = E(nc.sbuf_tensor(pfx + "o_raw", [64, NH, 512], BF16))
    nrm_sb = E(nc.sbuf_tensor(pfx + "nrm_sb", [128, 8 * 512], F32))
    rin_sb = E(nc.sbuf_tensor(pfx + "rin_sb", [128, 32], F32))
    o1_nfm = E(nc.sbuf_tensor(pfx + "o1_nfm", [128, NCH, 512], BF16))
    x1t = E(nc.sbuf_tensor(pfx + "x1t", [128, D], F32))
    xr_t = E(nc.sbuf_tensor(pfx + "xr_t", [128, 2, NCH, 128], BF16))
    ka_t = E(nc.sbuf_tensor(pfx + "ka_t", [128, 2, NCH, 128], BF16))
    rsg_sb = E(nc.sbuf_tensor(pfx + "rsg_sb", [128, D], F32))
    fv_sb = E(nc.sbuf_tensor(pfx + "fv_sb", [128, D], F32))
    q_ps = E(nc.psum_tensor(pfx + "q_ps", [128, 512], F32))
    o1A = E(nc.psum_tensor(pfx + "o1A", [65, 512], F32))
    o1B = E(nc.psum_tensor(pfx + "o1B", [65, 512], F32))
    nbc_ps = E(nc.psum_tensor(pfx + "nbc_ps", [64, 512], F32))
    pr_ps = E(nc.psum_tensor(pfx + "pr_ps", [128, 2, 512], F32))
    rv_ps = E(nc.psum_tensor(pfx + "rv_ps", [128, 2, 512], F32))


    e_wqa = sch.dma("gpsimd", wqa_sb[:],
                    wqa_d.rearrange("(c p) o -> p c o", p=128), "d_wqa")
    e_wo = sch.dma("gpsimd", wo_sb[:],
                   wo_d.rearrange("(c p) o -> p c o", p=128), "d_wo")
    e_wrt = sch.dma("gpsimd", wrt_sb[:],
                    wrt_d.rearrange("(c p) o -> p c o", p=128), "d_wrt")
    e_wvt = sch.dma("gpsimd", wvt_sb[:],
                    wvt_d.rearrange("(c p) o -> p c o", p=128), "d_wvt")
    e_bor = sch.dma("gpsimd", bor_b[:], bor_d, "d_bor")
    e_bq = sch.dma("sync", bq_sb[:], bq_d, "d_bq")
    e_bv = sch.dma("sync", bv_sb[:], bv_d, "d_bv")
    e_ones = sch.op("vector", lambda e: e.memset(ones_r[:], 1.0))
    e_ones_f = sch.op("vector", lambda e: e.memset(ones_f[:], 1.0))
    e_bvb = sch.op("vector", lambda e: e.tensor_copy(bvb[:], bv_sb[:]),
                   waits=(e_bv,))

    e_eff = None
    for half in range(2):
        for c in range(NCH):
            sch.op("tensor", lambda e, c=c, half=half: e.matmul(
                pr_ps[0:1, half], bvb[:, c:c + 1],
                wo_sb[:, c, half * 512:half * 512 + 512],
                start=(c == 0), stop=False),
                waits=ev_max(e_bvb, e_wo) if (c == 0 and half == 0) else ())
        e_eff = sch.op("tensor", lambda e, half=half: e.matmul(
            pr_ps[0:1, half], ones_r[0:1, 0:1],
            bor_b[0:1, half * 512:half * 512 + 512],
            start=False, stop=True),
            waits=ev_max(e_ones, e_bor) if half == 0 else (), inc="t")
    e_effrow = sch.op("scalar", lambda e: e.copy(
        out=effrow_b[:], in_=pr_ps[0:1, :].rearrange("p h j -> p (h j)")),
        waits=(e_eff,))

    e_kv_user = None
    e_xn_user = None
    e_qf_user = None
    e_er_user = None
    e_oraw_evac_war = None
    e_nrm_user = None
    e_nbc_free = None
    e_o1_store = None
    e_onfm_user = None
    e_x1_user = None
    e_xrka_rel = [None, None]
    e_out_store = None
    e_qps_free = None
    e_prps_free = {0: None, 1: None}
    e_rvps_sig = {0: None, 1: None}
    e_rvps_ff = {0: None, 1: None}
    for s_ in range(SEQ_PER_CORE):
        e_kv = sch.dma("sync", kv_sb[:],
                       kv_d[s_].rearrange("(hp h2) d c -> (h2 d) hp c", h2=2),
                       "d_kv", waits=ev_max(e_kv_user))
        for qc in range(4):
            e_xn = sch.dma("sync", xn1_qc[:],
                           xn1fm_d[s_].rearrange(
                               "c p t -> p c t")[:, :, qc * 512:(qc + 1) * 512],
                           "d_xn", waits=ev_max(e_xn_user))
            for dc in range(NCH):
                for c in range(NCH):
                    e_qmm = sch.op("tensor", lambda e, dc=dc, c=c: e.matmul(
                        q_ps[:, 0:512], wqa_sb[:, c, dc * 128:(dc + 1) * 128],
                        xn1_qc[:, c, :], start=(c == 0), stop=(c == NCH - 1)),
                        waits=ev_max(e_xn, e_wqa, e_qps_free) if c == 0 else (),
                        inc="t" if c == NCH - 1 else None)
                e_e = sch.op("scalar", lambda e, dc=dc: e.activation(
                    out=e_sb[:], in_=q_ps[:, 0:512], func=AF.Exp,
                    bias=bq_sb[:, dc:dc + 1]),
                    waits=ev_max(e_qmm, e_bq, e_er_user))
                e_r = sch.op("scalar", lambda e, dc=dc: e.activation(
                    out=r_sb[:], in_=q_ps[:, 0:512], func=AF.Relu,
                    bias=bq_sb[:, dc:dc + 1]), waits=())
                e_qf = sch.op("vector", lambda e, dc=dc: e.scalar_tensor_tensor(
                    out=qfeat[:, dc, :], in0=e_sb[:], scalar=1.0,
                    in1=r_sb[:], op0=ALU.min, op1=ALU.add),
                    waits=ev_max(e_r, e_qf_user))
                e_er_user = e_qf
                e_qps_free = e_r
            e_qf_done = e_qf
            e_div_all = []
            for grp in range(2):
                evs = []
                for hpl in range(4):
                    hp = grp * 4 + hpl
                    for hi, o_ps in enumerate((o1A, o1B)):
                        b0 = hi * 64
                        e_oq = sch.op(
                            "tensor", lambda e, hp=hp, hi=hi, b0=b0, o_ps=o_ps:
                            e.matmul(
                                o_ps[:, 0:512], kv_sb[b0:b0 + 64, hp, :],
                                qfeat[b0:b0 + 64, hp, :],
                                start=True, stop=True, tile_position=(b0, 0)),
                            waits=ev_max(e_qf_done, e_kv, e_oraw_evac_war),
                            inc="t")
                        h = 2 * hp + hi
                        evs.append(sch.op(
                            "scalar", lambda e, o_ps=o_ps, h=h: e.copy(
                                out=o_raw[:, h, :], in_=o_ps[0:64, 0:512]),
                            waits=ev_max(e_oq, e_o1_store)))
                        slot = (hp % 4) * 2 + hi
                        evs.append(sch.op(
                            "scalar", lambda e, o_ps=o_ps, slot=slot: e.copy(
                                out=nrm_sb[64:65, slot * 512:(slot + 1) * 512],
                                in_=o_ps[64:65, 0:512]),
                            waits=ev_max(e_nrm_user)))
                        e_oraw_evac_war = evs[-1]
                e_h1 = sch.dma("sync", nrm_scr[s_, qc, grp], nrm_sb[64:65, :],
                               "d_nbo", waits=ev_max(*evs))
                e_nrm_user = e_h1
                e_h2 = sch.dma("sync", rin_sb[:],
                               nrm_scr[s_, qc, grp].rearrange("(r j) -> r j", r=128),
                               "d_nbi", waits=(e_h1,))
                e_cl = sch.op("vector", lambda e: e.tensor_scalar(
                    out=rin_sb[:], in0=rin_sb[:], scalar1=1e-6, scalar2=None,
                    op0=ALU.max), waits=(e_h2,))
                e_rcp = sch.op("vector", lambda e: e.reciprocal(
                    out=rin_sb[:], in_=rin_sb[:]), waits=(e_cl,))
                e_h3 = sch.dma("sync",
                               nrm_scr[s_, qc, grp].rearrange("(r j) -> r j", r=128),
                               rin_sb[:], "d_nbo", waits=(e_rcp,))
                e_h4 = sch.dma("sync", nrm_sb[0:1, :],
                               nrm_scr[s_, qc, grp].rearrange("j -> () j"),
                               "d_nbi", waits=(e_h3,))
                for hpl in range(4):
                    hp = grp * 4 + hpl
                    for hi in range(2):
                        slot = hpl * 2 + hi
                        h = 2 * hp + hi
                        e_nbc = sch.op(
                            "tensor", lambda e, slot=slot: e.matmul(
                                nbc_ps[:, 0:512], ones_f[0:1, :],
                                nrm_sb[0:1, slot * 512:(slot + 1) * 512],
                                start=True, stop=True),
                            waits=ev_max(e_h4, e_nbc_free, e_ones_f), inc="t")
                        e_div = sch.op(
                            "vector", lambda e, h=h: e.tensor_tensor(
                                out=o_raw[:, h, :], in0=o_raw[:, h, :],
                                in1=nbc_ps[0:64, 0:512], op=ALU.mult),
                            waits=ev_max(e_nbc, e_o1_store))
                        e_nbc_free = e_div
                        e_div_all.append(e_div)
            e_o1_store = sch.dma(
                "sync", o1_scr[s_, qc].rearrange("hp h d j -> d hp h j"),
                o_raw[:].rearrange("d (hp h) j -> d hp h j", h=2), "d_o1o",
                waits=ev_max(*e_div_all))
            e_onfm = sch.dma(
                "sync", o1_nfm[:],
                o1_scr[s_, qc].rearrange("hp h d j -> (h d) hp j"), "d_o1i",
                waits=ev_max(e_o1_store, e_onfm_user))
            e_pj = None
            for tl in range(4):
                tt = qc * 4 + tl
                par = tl % 2
                t0 = tt * TS
                e_x1 = sch.dma("sync", x1t[:], x1_d[s_, t0:t0 + TS, :], "d_x1",
                               waits=ev_max(e_x1_user))
                e_xr = sch.dma("sync", xr_t[:, par],
                               xrfm_d[s_].rearrange(
                                   "c p t -> p c t")[:, :, t0:t0 + TS],
                               "d_xrt%d" % par, waits=ev_max(e_xrka_rel[par]))
                e_ka = sch.dma("sync", ka_t[:, par],
                               kafm_d[s_].rearrange(
                                   "c p t -> p c t")[:, :, t0:t0 + TS],
                               "d_kat%d" % par, waits=ev_max(e_xrka_rel[par]))
                for half in range(2):
                    for c in range(NCH):
                        sch.op("tensor", lambda e, c=c, half=half, tl=tl: e.matmul(
                            pr_ps[:, half], o1_nfm[:, c, tl * TS:(tl + 1) * TS],
                            wo_sb[:, c, half * 512:(half + 1) * 512],
                            start=(c == 0), stop=False),
                            waits=ev_max(e_onfm, e_effrow, e_ones,
                                         e_prps_free[half]) if c == 0 else ())
                    e_pj = sch.op("tensor", lambda e, half=half: e.matmul(
                        pr_ps[:, half], ones_r[0:1, :],
                        effrow_b[0:1, half * 512:(half + 1) * 512],
                        start=False, stop=True), inc="t")
                    for c in range(NCH):
                        e_rm = sch.op("tensor", lambda e, c=c, half=half, par=par:
                                      e.matmul(
                                          rv_ps[:, half], xr_t[:, par, c, :],
                                          wrt_sb[:, c, half * 512:(half + 1) * 512],
                                          start=(c == 0), stop=(c == NCH - 1)),
                                      waits=ev_max(e_xr, e_wrt,
                                                   e_rvps_ff[half])
                                      if c == 0 else (),
                                      inc="t" if c == NCH - 1 else None)
                    e_sg = sch.op("scalar", lambda e, half=half: e.activation(
                        out=rsg_sb[:, half * 512:(half + 1) * 512],
                        in_=rv_ps[:, half], func=AF.Sigmoid),
                        waits=ev_max(e_rm))
                    e_rvps_sig[half] = e_sg
                    e_a1 = sch.op("vector", lambda e, half=half: e.tensor_tensor(
                        out=x1t[:, half * 512:(half + 1) * 512],
                        in0=x1t[:, half * 512:(half + 1) * 512],
                        in1=pr_ps[:, half], op=ALU.add),
                        waits=ev_max(e_pj, e_x1))
                    e_prps_free[half] = e_a1
                for half in range(2):
                    for c in range(NCH):
                        e_om = sch.op("tensor", lambda e, c=c, half=half, par=par:
                                      e.matmul(
                                          rv_ps[:, half], ka_t[:, par, c, :],
                                          wvt_sb[:, c, half * 512:(half + 1) * 512],
                                          start=(c == 0), stop=(c == NCH - 1)),
                                      waits=ev_max(e_ka, e_wvt, e_rvps_sig[half])
                                      if c == 0 else (),
                                      inc="t" if c == NCH - 1 else None)
                    e_f = sch.op("vector", lambda e, half=half: e.tensor_tensor(
                        out=fv_sb[:, half * 512:(half + 1) * 512],
                        in0=rsg_sb[:, half * 512:(half + 1) * 512],
                        in1=rv_ps[:, half], op=ALU.mult),
                        waits=ev_max(e_om, e_out_store))
                    e_rvps_ff[half] = e_f
                    e_o = sch.op("vector", lambda e, half=half: e.tensor_tensor(
                        out=x1t[:, half * 512:(half + 1) * 512],
                        in0=x1t[:, half * 512:(half + 1) * 512],
                        in1=fv_sb[:, half * 512:(half + 1) * 512], op=ALU.add))
                e_xrka_rel[par] = ("t", sch.cnt["t"])
                e_out_store = sch.dma("sync", out_d[s_, t0:t0 + TS, :], x1t[:],
                                      "d_out", waits=(e_o,))
                e_x1_user = e_out_store
            e_onfm_user = e_pj
            e_xn_user = e_qmm
            e_qf_user = ("t", sch.cnt["t"])
        e_kv_user = ("t", sch.cnt["t"])
    sch.op("sync", None, waits=ev_max(sch.ev("d_out")))




def build_l1b(nc):
    x1_d = nc.dram_tensor("x1", [SEQ_PER_CORE, S, D], F32, kind="ExternalInput").ap()
    xn1fm_d = nc.dram_tensor("xn1fm", [SEQ_PER_CORE, NCH, 128, S], BF16,
                             kind="ExternalInput").ap()
    xrfm_d = nc.dram_tensor("xrfm", [SEQ_PER_CORE, NCH, 128, S], BF16,
                            kind="ExternalInput").ap()
    kafm_d = nc.dram_tensor("kafm", [SEQ_PER_CORE, NCH, 128, S], BF16,
                            kind="ExternalInput").ap()
    kv_d = nc.dram_tensor("kv", [SEQ_PER_CORE, 16, 64, 65], BF16,
                          kind="ExternalInput").ap()
    wqa_d = nc.dram_tensor("Wq_a", [D, D], F32, kind="ExternalInput").ap()
    bq_d = nc.dram_tensor("bq_pc", [128, NCH], F32, kind="ExternalInput").ap()
    wo_d = nc.dram_tensor("Wo_a", [D, D], F32, kind="ExternalInput").ap()
    bor_d = nc.dram_tensor("bo_row", [1, D], F32, kind="ExternalInput").ap()
    bv_d = nc.dram_tensor("bv_pc", [128, NCH], F32, kind="ExternalInput").ap()
    wrt_d = nc.dram_tensor("WrT_f", [D, D], F32, kind="ExternalInput").ap()
    wvt_d = nc.dram_tensor("WvT_f", [D, D], F32, kind="ExternalInput").ap()
    out_d = nc.dram_tensor("out_final", [SEQ_PER_CORE, S, D], F32,
                           kind="ExternalOutput").ap()
    dr = {"x1": x1_d, "xn1fm": xn1fm_d, "xrfm": xrfm_d, "kafm": kafm_d,
          "kv": kv_d, "Wq_a": wqa_d, "bq_pc": bq_d, "Wo_a": wo_d,
          "bo_row": bor_d, "bv_pc": bv_d, "WrT_f": wrt_d, "WvT_f": wvt_d,
          "out_final": out_d}
    with contextlib.ExitStack() as ctx:
        sch = Sched(nc, ctx)
        block = ctx.enter_context(nc.Block())
        with contextlib.ExitStack() as pctx:
            emit_l1b(nc, sch, pctx.enter_context, dr)
            sch.build(block)
    return nc


def build_fused(nc):
    EI, EO, IN = "ExternalInput", "ExternalOutput", "Internal"
    t = nc.dram_tensor
    x_d = t("x", [SEQ_PER_CORE, S, D], F32, kind=EI).ap()
    wg_d = t("w_gate", [I, D], F32, kind=EI).ap()
    wu_d = t("w_up", [I, D], F32, kind=EI).ap()
    wd_d = t("w_down", [D, I], F32, kind=EI).ap()
    ident_d = t("ident", [128, 128], F32, kind=EI).ap()
    qkvwT_d = t("qkv_wT", [D, 3 * D], F32, kind=EI).ap()
    qkvb_d = t("qkv_b_pc", [128, 24], F32, kind=EI).ap()
    projwT_d = t("proj_wT", [D, D], F32, kind=EI).ap()
    projbr_d = t("proj_b_row", [1, D], F32, kind=EI).ap()
    wka_d = t("Wk_a", [D, D], F32, kind=EI).ap()
    wva_d = t("Wv_a", [D, D], F32, kind=EI).ap()
    wktf_d = t("WkT_f", [D, D], F32, kind=EI).ap()
    bkr_d = t("bk_row", [1, D], F32, kind=EI).ap()
    mur_d = t("mu_r_pc", [128, NCH], F32, kind=EI).ap()
    muk_d = t("mu_k_pc", [128, NCH], F32, kind=EI).ap()
    wqa_d = t("Wq_a", [D, D], F32, kind=EI).ap()
    bq_d = t("bq_pc", [128, NCH], F32, kind=EI).ap()
    wo_d = t("Wo_a", [D, D], F32, kind=EI).ap()
    bor_d = t("bo_row", [1, D], F32, kind=EI).ap()
    bv_d = t("bv_pc", [128, NCH], F32, kind=EI).ap()
    wrt_d = t("WrT_f", [D, D], F32, kind=EI).ap()
    wvt_d = t("WvT_f", [D, D], F32, kind=EI).ap()
    out_d = t("out_final", [SEQ_PER_CORE, S, D], F32, kind=EO).ap()
    y_b = t("y_buf", [SEQ_PER_CORE, S, D], F32, kind=IN).ap()
    xnfm_b = t("xnfm_buf", [SEQ_PER_CORE, NCH, 128, S], BF16, kind=IN).ap()
    x1_b = t("x1_buf", [SEQ_PER_CORE, S, D], F32, kind=IN).ap()
    xn1fm_b = t("xn1fm_buf", [SEQ_PER_CORE, NCH, 128, S], BF16, kind=IN).ap()
    xrfm_b = t("xrfm_buf", [SEQ_PER_CORE, NCH, 128, S], BF16, kind=IN).ap()
    kafm_b = t("kafm_buf", [SEQ_PER_CORE, NCH, 128, S], BF16, kind=IN).ap()
    kv_b = t("kv_buf", [SEQ_PER_CORE, 16, 64, 65], BF16, kind=IN).ap()

    phases = [
        ("a_", emit_l0a, {"x": x_d, "w_gate": wg_d, "w_up": wu_d,
                          "w_down": wd_d, "ident": ident_d,
                          "y_out": y_b, "xnfm_out": xnfm_b}),
        ("b_", emit_l0b, {"xnfm": xnfm_b, "y_in": y_b, "qkv_wT": qkvwT_d,
                          "qkv_b_pc": qkvb_d, "proj_wT": projwT_d,
                          "proj_b_row": projbr_d, "x1_out": x1_b}),
        ("c_", emit_l1a, {"x1": x1_b, "Wk_a": wka_d, "Wv_a": wva_d,
                          "WkT_f": wktf_d, "bk_row": bkr_d, "mu_r_pc": mur_d,
                          "mu_k_pc": muk_d, "ident": ident_d,
                          "xn1fm_out": xn1fm_b, "xrfm_out": xrfm_b,
                          "kafm_out": kafm_b, "kv_out": kv_b}),
        ("d_", emit_l1b, {"x1": x1_b, "xn1fm": xn1fm_b, "xrfm": xrfm_b,
                          "kafm": kafm_b, "kv": kv_b, "Wq_a": wqa_d,
                          "bq_pc": bq_d, "Wo_a": wo_d, "bo_row": bor_d,
                          "bv_pc": bv_d, "WrT_f": wrt_d, "WvT_f": wvt_d,
                          "out_final": out_d}),
    ]
    with contextlib.ExitStack() as ctx:
        sch = Sched(nc, ctx)
        block = ctx.enter_context(nc.Block())
        for pfx, emitf, dr in phases:
            with contextlib.ExitStack() as pctx:
                emitf(nc, sch, pctx.enter_context, dr, pfx=pfx)
                sch.barrier()
                sch.build(block)
        if END_CLEAR:
            sch.end_clear()
            sch.build(block)
    return nc


def prep_in_maps(inputs):
    f32 = np.float32
    x_full = np.ascontiguousarray(inputs["L0_attn_x"], dtype=f32)
    qkv_b = np.asarray(inputs["L0_attn_qkv_b"], dtype=f32)
    sh = {
        "w_gate": np.ascontiguousarray(inputs["L0_ffn_w_gate"], dtype=f32),
        "w_up": np.ascontiguousarray(inputs["L0_ffn_w_up"], dtype=f32),
        "w_down": np.ascontiguousarray(inputs["L0_ffn_w_down"], dtype=f32),
        "ident": np.eye(128, dtype=f32),
        "qkv_wT": np.ascontiguousarray(np.asarray(inputs["L0_attn_qkv_w"], f32).T),
        "qkv_b_pc": np.ascontiguousarray(qkv_b.reshape(24, 128).T),
        "proj_wT": np.ascontiguousarray(np.asarray(inputs["L0_attn_proj_w"], f32).T),
        "proj_b_row": np.ascontiguousarray(
            np.asarray(inputs["L0_attn_proj_b"], f32).reshape(1, D)),
        "Wk_a": np.ascontiguousarray(np.asarray(inputs["L1_attn_Wk"], f32)),
        "Wv_a": np.ascontiguousarray(np.asarray(inputs["L1_attn_Wv"], f32)),
        "WkT_f": np.ascontiguousarray(np.asarray(inputs["L1_ffn_W_k"], f32).T),
        "bk_row": np.ascontiguousarray(
            np.asarray(inputs["L1_attn_bk"], f32).reshape(1, D)),
        "mu_r_pc": np.ascontiguousarray(
            np.asarray(inputs["L1_ffn_mu_r"], f32).reshape(8, 128).T),
        "mu_k_pc": np.ascontiguousarray(
            np.asarray(inputs["L1_ffn_mu_k"], f32).reshape(8, 128).T),
        "Wq_a": np.ascontiguousarray(np.asarray(inputs["L1_attn_Wq"], f32)),
        "bq_pc": np.ascontiguousarray(
            np.asarray(inputs["L1_attn_bq"], f32).reshape(8, 128).T),
        "Wo_a": np.ascontiguousarray(np.asarray(inputs["L1_attn_Wo"], f32)),
        "bo_row": np.ascontiguousarray(
            np.asarray(inputs["L1_attn_bo"], f32).reshape(1, D)),
        "bv_pc": np.ascontiguousarray(
            np.asarray(inputs["L1_attn_bv"], f32).reshape(8, 128).T),
        "WrT_f": np.ascontiguousarray(np.asarray(inputs["L1_ffn_W_r"], f32).T),
        "WvT_f": np.ascontiguousarray(np.asarray(inputs["L1_ffn_W_v"], f32).T),
    }
    return [dict(sh, x=x_full[2 * c:2 * c + 2]) for c in range(NCORES)]



_NCS = {}


def _get_nc(name, builder):
    if name not in _NCS:
        _NCS[name] = builder(bass.Bass("TRN2", target_bir_lowering=False))
    return _NCS[name]


def kernel(**inputs):
    in_maps = prep_in_maps(inputs)
    nc = _get_nc("fused", build_fused)
    res = run_bass_kernel_spmd(nc, in_maps, core_ids=list(range(NCORES)))
    return np.concatenate([np.asarray(r["out_final"]) for r in res.results], axis=0)


# version 13 (final)
# speedup vs baseline: 1.1197x; 1.0083x over previous
import contextlib
import numpy as np
import ml_dtypes
import concourse.bass as bass
import concourse.mybir as mybir
from concourse.bass import broadcast_tensor_aps
from concourse.bass_utils import run_bass_kernel_spmd

F32 = mybir.dt.float32
BF16 = mybir.dt.bfloat16
AF = mybir.ActivationFunctionType
ALU = mybir.AluOpType
AX = mybir.AxisListType
bfdt = ml_dtypes.bfloat16

NCORES = 8
SEQ_PER_CORE = 2
S = 2048
D = 1024
I = 2816
TS = 128
NT = S // TS
NCH = D // 128
EPS_LN = 1e-5

ENGS = ("sync", "gpsimd", "scalar", "vector", "tensor")
OWN_WAITS = True
DRAIN_FUSABLE = True
END_CLEAR = True
BATCH_INCS = ("tensor",)


class Sched:

    def __init__(self, nc, ctx, sem_names=("t", "v", "a")):
        self.nc = nc
        self.ctx = ctx
        self.pfx = ""
        self.swdge_sems = set()
        self.items = {e: [] for e in ENGS}
        self.sems = {}
        self.cnt = {}
        for n in sem_names:
            self.sem(n)

    def ev(self, name):
        if name.startswith("d_"):
            name = self.pfx + name
        return (name, self.cnt[name])

    def barrier(self):
        snap = tuple((n, c) for n, c in self.cnt.items() if c > 0)
        for e in ENGS:
            self.op(e, None, waits=snap)

    def end_clear(self):
        self.barrier()
        self.sem("fin")
        for e in ENGS:
            if e != "gpsimd":
                self.op(e, lambda en: en.nop(), inc="fin")
        fin_target = self.cnt["fin"]
        sems = [h for n, h in self.sems.items() if n not in self.swdge_sems]

        def fn(e, sems=sems):
            ins = None
            for h in sems:
                ins = e.sem_clear(h)
            return ins
        self.op("gpsimd", fn, waits=(("fin", fin_target),))

    def sem(self, name):
        if name not in self.sems:
            self.sems[name] = self.ctx.enter_context(
                self.nc.semaphore("sem_" + name))
            self.cnt[name] = 0
        return name

    def op(self, eng, fn, waits=(), inc=None, amt=1):
        own = {"vector": "v", "scalar": "a"}.get(eng)
        own_wait = None
        if own is not None and fn is not None:
            if inc is None:
                inc = own
            own_wait = (own, self.cnt[own])
        ev = None
        waits = tuple(w for w in waits if w is not None)
        abs_cnt = None
        if inc is not None:
            self.cnt[inc] += amt
            abs_cnt = self.cnt[inc]
            ev = (inc, abs_cnt)
        self.items[eng].append((fn, waits, inc, amt, own_wait, abs_cnt))
        return ev

    def dma(self, eng, out, in_, sem, waits=()):
        sem = self.pfx + sem
        self.sem(sem)
        if eng == "gpsimd":
            self.swdge_sems.add(sem)
        return self.op(eng, lambda e, o=out, i=in_: e.dma_start(out=o, in_=i),
                       waits=waits, inc=sem, amt=16)

    def build(self, block):
        sems = self.sems
        waited_vals = {}
        for e in ENGS:
            for fn, waits, inc, amt, own_wait, abs_cnt in self.items[e]:
                for (s_, v_) in waits:
                    waited_vals.setdefault(s_, set()).add(v_)
        for e in ENGS:
            items = self.items[e]
            if not items:
                continue
            use_drain = e in ("vector", "scalar")
            batch = e in BATCH_INCS
            last_idx = {}
            for i, (fn, waits, inc, amt, own_wait, abs_cnt) in enumerate(items):
                if inc is not None and fn is not None:
                    last_idx[inc] = i
            plan = []
            pending = {}
            for i, (fn, waits, inc, amt, own_wait, abs_cnt) in enumerate(items):
                em = None
                if inc is not None and fn is not None:
                    if amt == 16:
                        em = (inc, 16)
                    elif not batch:
                        em = (inc, amt)
                    else:
                        pending[inc] = pending.get(inc, 0) + amt
                        if (abs_cnt in waited_vals.get(inc, ()) or
                                i == last_idx[inc]):
                            em = (inc, pending[inc])
                            pending[inc] = 0
                plan.append(em)

            def make_emit(items=items, plan=plan, use_drain=use_drain):
                def emit(engobj):
                    waited = {}
                    for (fn, waits, inc, amt, own_wait, abs_cnt), em in zip(items, plan):
                        if OWN_WAITS and own_wait is not None:
                            waits = (own_wait,) + waits
                        for (s_, v_) in waits:
                            if v_ > 0 and waited.get(s_, -1) < v_:
                                engobj.wait_ge(sems[s_], v_)
                                waited[s_] = v_
                        if fn is None:
                            continue
                        instr = fn(engobj)
                        if em is not None:
                            sname, a2 = em
                            if a2 == 16 or not use_drain:
                                instr.then_inc(sems[sname], a2)
                            else:
                                engobj.drain(fusable=DRAIN_FUSABLE).then_inc(
                                    sems[sname], a2)
                return emit
            getattr(block, e)(make_emit())
        self.items = {e: [] for e in ENGS}


def ev_max(*evs):
    best = {}
    for ev in evs:
        if ev is None or (isinstance(ev, tuple) and len(ev) == 0):
            continue
        if isinstance(ev, tuple) and isinstance(ev[0], tuple):
            for (s_, v_) in ev:
                if best.get(s_, -1) < v_:
                    best[s_] = v_
        else:
            s_, v_ = ev
            if best.get(s_, -1) < v_:
                best[s_] = v_
    return tuple(best.items())



def emit_quant(sch, q, src, out_bf16, C, wv, wa):
    NB = C // 16
    absx, sgn, a1, c5 = q["absx"], q["sgn"], q["a1"], q["c5"]
    bm, sc, rs = q["bm"], q["sc"], q["rs"]
    war = q.get("_last")
    wa = ev_max(wa, war)
    wv = ev_max(wv)
    e_absx = sch.op("scalar", lambda e: e.activation(
        out=absx[:, 0:C], in_=src, func=AF.Abs), waits=wa, inc="a")
    e_sgn = sch.op("scalar", lambda e: e.activation(
        out=sgn[:, 0:C], in_=src, func=AF.Sign), waits=wa, inc="a")
    e_bm = sch.op("vector", lambda e: e.tensor_reduce(
        out=bm[:, 0:NB].rearrange("p (b o) -> p b o", o=1),
        in_=src.rearrange("p (b s) -> p b s", s=16),
        axis=AX.X, op=ALU.max, apply_absolute_value=True), waits=wv, inc="v")
    e_sc = sch.op("vector", lambda e: e.tensor_scalar(
        out=sc[:, 0:NB], in0=bm[:, 0:NB], scalar1=1.0 / 6.0, scalar2=1e-12,
        op0=ALU.mult, op1=ALU.max), waits=(e_bm,), inc="v")
    e_rs = sch.op("vector", lambda e: e.reciprocal(
        out=rs[:, 0:NB], in_=sc[:, 0:NB]), waits=(e_sc,), inc="v")
    av = absx[:, 0:C].rearrange("p (b s) -> p b s", s=16)
    rv = rs[:, 0:NB].rearrange("p (b o) -> p b o", o=1)
    avb, rvb = broadcast_tensor_aps(av, rv)
    sch.op("vector", lambda e: e.tensor_tensor(out=av, in0=avb, in1=rvb, op=ALU.mult),
           waits=ev_max(e_absx, e_rs))
    u = absx[:, 0:C]
    sch.op("vector", lambda e: e.tensor_scalar(
        out=c5[:, 0:C], in0=u, scalar1=5.0, scalar2=2.0, op0=ALU.is_gt, op1=ALU.mult))
    for t_ in (3.5, 2.5):
        sch.op("vector", lambda e, t_=t_: e.scalar_tensor_tensor(
            out=c5[:, 0:C], in0=u, scalar=t_, in1=c5[:, 0:C],
            op0=ALU.is_gt, op1=ALU.add))
    sch.op("vector", lambda e: e.tensor_scalar(
        out=a1[:, 0:C], in0=u, scalar1=0.25, scalar2=None, op0=ALU.is_gt))
    for t_ in (0.75, 1.25, 1.75):
        sch.op("vector", lambda e, t_=t_: e.scalar_tensor_tensor(
            out=a1[:, 0:C], in0=u, scalar=t_, in1=a1[:, 0:C],
            op0=ALU.is_gt, op1=ALU.add))
    sch.op("vector", lambda e: e.scalar_tensor_tensor(
        out=a1[:, 0:C], in0=c5[:, 0:C], scalar=2.0, in1=a1[:, 0:C],
        op0=ALU.mult, op1=ALU.add))
    gv = sgn[:, 0:C].rearrange("p (b s) -> p b s", s=16)
    hv = sc[:, 0:NB].rearrange("p (b o) -> p b o", o=1)
    gvb, hvb = broadcast_tensor_aps(gv, hv)
    sch.op("vector", lambda e: e.tensor_tensor(out=gv, in0=gvb, in1=hvb, op=ALU.mult),
           waits=ev_max(e_sgn))
    ev = sch.op("vector", lambda e: e.tensor_tensor(
        out=out_bf16, in0=a1[:, 0:C], in1=sgn[:, 0:C], op=ALU.mult), inc="v")
    q["_last"] = ev
    return ev


def emit_ln(sch, b, x_src, out_xn, e_x):
    war = b.get("_last")
    wx = ev_max(e_x)
    wxa = ev_max(e_x, war)
    e_s1 = sch.op("vector", lambda e: e.reduce_sum(
        out=b["s1"][:], in_=x_src, axis=AX.X), waits=wx, inc="v")
    e_sq = sch.op("scalar", lambda e: e.activation(
        out=b["sqs"][:], in_=x_src, func=AF.Square,
        accum_out=b["ssq"][:]), waits=wxa, inc="a")
    e_mean = sch.op("vector", lambda e: e.tensor_scalar(
        out=b["mean"][:], in0=b["s1"][:], scalar1=1.0 / D, scalar2=None,
        op0=ALU.mult), waits=ev_max(e_s1, e_sq), inc="v")
    sch.op("vector", lambda e: e.tensor_scalar(
        out=b["ex2"][:], in0=b["ssq"][:], scalar1=1.0 / D, scalar2=EPS_LN,
        op0=ALU.mult, op1=ALU.add))
    e_msq = sch.op("scalar", lambda e: e.activation(
        out=b["msq"][:], in_=b["mean"][:], func=AF.Square),
        waits=(e_mean,), inc="a")
    e_var = sch.op("vector", lambda e: e.tensor_tensor(
        out=b["var"][:], in0=b["ex2"][:], in1=b["msq"][:], op=ALU.subtract),
        waits=(e_msq,), inc="v")
    e_sd = sch.op("scalar", lambda e: e.activation(
        out=b["sd"][:], in_=b["var"][:], func=AF.Sqrt),
        waits=ev_max(e_var, war), inc="a")
    e_rstd = sch.op("vector", lambda e: e.reciprocal(
        out=b["rstd"][:], in_=b["sd"][:]), waits=(e_sd,), inc="v")
    e_xn = sch.op("vector", lambda e: e.tensor_scalar(
        out=out_xn, in0=x_src, scalar1=b["mean"][:][:, 0:1],
        scalar2=b["rstd"][:][:, 0:1], op0=ALU.subtract, op1=ALU.mult),
        waits=(e_rstd,), inc="v")
    b["_last"] = e_xn
    return e_xn


def emit_transpose_evac(sch, psum, ident, srcs, dsts, w_tensor, w_evac,
                        evac_eng="scalar", dst_groups=None):
    evac_events = []
    n = len(srcs)
    last = None
    for cb in range(0, n, 4):
        nslot = min(4, n - cb)
        e_tps = None
        for s_i in range(nslot):
            src = srcs[cb + s_i]
            e_tps = sch.op("tensor", lambda e, src=src, s_i=s_i: e.transpose(
                psum[:, s_i, 0:128], src, ident), waits=w_tensor if s_i == 0 else (),
                inc="t" if s_i == nslot - 1 else None)
        if dst_groups is not None:
            dstg = dst_groups(cb, nslot)
            last = sch.op(evac_eng, lambda e, dstg=dstg, nslot=nslot: e.copy(
                out=dstg, in_=psum[:, 0:nslot, 0:128]),
                waits=ev_max(e_tps, w_evac), inc="a")
            evac_events.append(last)
        else:
            evs = []
            for s_i in range(nslot):
                dst = dsts[cb + s_i]
                evs.append(sch.op(evac_eng, lambda e, dst=dst, s_i=s_i: e.copy(
                    out=dst, in_=psum[:, s_i, 0:128]),
                    waits=ev_max(e_tps, w_evac) if s_i == 0 else (), inc="a"))
            last = evs[-1]
            evac_events.extend(evs)
        w_tensor = ev_max(w_tensor, last)
    return last, evac_events



def emit_l0a(nc, sch, E, dr, pfx=""):
    sch.pfx = pfx
    x_d = dr["x"]
    wg_d = dr["w_gate"]
    wu_d = dr["w_up"]
    wd_d = dr["w_down"]
    ident_d = dr["ident"]
    y_d = dr["y_out"]
    xnfm_d = dr["xnfm_out"]

    NIC = 6
    IC_OFF = [0, 512, 1024, 1536, 2048, 2560]
    IC_SZ = [512] * 5 + [256]
    NRT_G = I // 128
    NRT_D = D // 128

    wgqT = E(nc.sbuf_tensor(pfx + "wgqT", [128, NCH, I], BF16))
    wuqT = E(nc.sbuf_tensor(pfx + "wuqT", [128, NCH, I], BF16))
    wdqT = E(nc.sbuf_tensor(pfx + "wdqT", [128, NRT_G, D], BF16))

    q = {n: E(nc.sbuf_tensor(pfx + "q_" + n, [128, 1408], F32))
         for n in ("absx", "sgn", "a1", "c5")}
    q.update({n: E(nc.sbuf_tensor(pfx + "q_" + n, [128, 88], F32))
              for n in ("bm", "sc", "rs", "hs")})
    lnb = {n: E(nc.sbuf_tensor(pfx + "ln_" + n, [128, 1], F32))
           for n in ("s1", "ssq", "mean", "ex2", "msq", "var", "sd", "rstd")}
    xtile = E(nc.sbuf_tensor(pfx + "xtile", [128, 2, D], F32))
    xn_tm = E(nc.sbuf_tensor(pfx + "xn_tm", [128, D], F32))
    xn_stage = E(nc.sbuf_tensor(pfx + "xn_stage", [128, NCH, 128], BF16))
    xq_tm = E(nc.sbuf_tensor(pfx + "xq_tm", [128, D], BF16))
    xq_fm = E(nc.sbuf_tensor(pfx + "xq_fm", [128, NCH, 128], BF16))
    g_sb = E(nc.sbuf_tensor(pfx + "g_sb", [128, 512], F32))
    h_sb = E(nc.sbuf_tensor(pfx + "h_sb", [128, I], F32))
    hq_b = E(nc.sbuf_tensor(pfx + "hq_b", [128, I], BF16))
    hq_fm = E(nc.sbuf_tensor(pfx + "hq_fm", [128, NRT_G, 128], BF16))
    y_sb = E(nc.sbuf_tensor(pfx + "y_sb", [128, D], F32))
    lnb["sqs"] = y_sb
    wt_in = h_sb
    wq_b = hq_b
    idt = E(nc.sbuf_tensor(pfx + "idt", [128, 128], F32))
    idtb = E(nc.sbuf_tensor(pfx + "idtb", [128, 128], BF16))
    tp32 = E(nc.psum_tensor(pfx + "tp32", [128, 4, 128], F32))
    tpb16 = E(nc.psum_tensor(pfx + "tpb16", [128, 4, 256], BF16))
    g_ps = E(nc.psum_tensor(pfx + "g_ps", [128, 512], F32))
    u_ps = E(nc.psum_tensor(pfx + "u_ps", [128, 512], F32))
    d_ps = E(nc.psum_tensor(pfx + "d_ps", [128, 2, 512], F32))


    e_id = sch.dma("sync", idt[:], ident_d, "d_id")
    e_idb = sch.dma("gpsimd", idtb[:], ident_d, "d_idb")

    e_tpb_war = None
    wt_release = None
    for (w_src, n_rt, row_c, dstT, grp) in ((wg_d, NRT_G, D, wgqT, 2),
                                            (wu_d, NRT_G, D, wuqT, 2),
                                            (wd_d, NRT_D, I, wdqT, 1)):
        n_chT = row_c // 128
        for r0 in range(0, n_rt, grp):
            g_rows = min(grp, n_rt - r0)
            W = g_rows * row_c
            e_lds = []
            for gi in range(g_rows):
                e_lds.append(sch.dma(
                    "sync", wt_in[:, gi * row_c:(gi + 1) * row_c],
                    w_src[(r0 + gi) * 128:(r0 + gi + 1) * 128, :],
                    "d_w0", waits=ev_max(wt_release)))
            e_ld = ev_max(*e_lds)
            e_q = None
            for c0 in range(0, W, 1408):
                c1 = min(W, c0 + 1408)
                e_q = emit_quant(sch, q, wt_in[:, c0:c1],
                                 wq_b[:, c0:c1], c1 - c0, e_ld, e_ld)
            wt_release = e_q
            nch_g = W // 128
            srcs = [wq_b[:, c * 128:(c + 1) * 128] for c in range(nch_g)]
            dsts = [dstT[:, j % n_chT, (r0 + j // n_chT) * 128:
                         (r0 + j // n_chT + 1) * 128] for j in range(nch_g)]
            last, _ = emit_transpose_evac(
                sch, tpb16, idtb[:], srcs, dsts,
                w_tensor=ev_max(e_q, e_idb, e_tpb_war), w_evac=())
            e_tpb_war = last
            q["_last"] = ev_max(q["_last"], ("t", sch.cnt["t"]))
    e_wts = ("a", sch.cnt["a"])

    tiles = [(s_, t) for s_ in range(SEQ_PER_CORE) for t in range(NT)]
    ntile = len(tiles)
    e_load = {}
    st = {}
    cs = {"stg_tp": None, "xq_tp": None, "tp32_war": None,
          "tpb_war": e_tpb_war, "xnstage_free": None,
          "d_free": None, "hq_prev": None, "hq_tp": None}
    xtile_release = [None, None]

    def sched_load(idx):
        s_, t = tiles[idx]
        par = idx % 2
        e_load[idx] = sch.dma("sync", xtile[:, par],
                              x_d[s_, t * TS:(t + 1) * TS, :],
                              "d_x%d" % par,
                              waits=ev_max(xtile_release[par]))

    def stageA(idx):
        s_, t = tiles[idx]
        par = idx % 2
        t0 = t * TS
        e_x = e_load[idx]
        lnb["_last"] = ev_max(lnb.get("_last"), cs["stg_tp"])
        e_xn = emit_ln(sch, lnb, xtile[:, par], xn_tm[:], ev_max(e_x))
        srcs = [xn_tm[:, c * 128:(c + 1) * 128] for c in range(NCH)]
        dsts = [xn_stage[:, c, :] for c in range(NCH)]
        e_stg, _ = emit_transpose_evac(
            sch, tp32, idt[:], srcs, dsts,
            w_tensor=ev_max(e_xn, e_id, cs["tp32_war"]),
            w_evac=ev_max(cs["xnstage_free"]))
        cs["tp32_war"] = e_stg
        cs["stg_tp"] = e_stg
        cs["xnstage_free"] = sch.dma(
            "sync", xnfm_d[s_].rearrange("c p t -> p c t")[:, :, t0:t0 + TS],
            xn_stage[:], "d_stg", waits=(e_stg,))
        q["_last"] = ev_max(q["_last"], cs["xq_tp"])
        e_xq = emit_quant(sch, q, xn_tm[:], xq_tm[:], D, e_xn, e_xn)
        srcs = [xq_tm[:, c * 128:(c + 1) * 128] for c in range(NCH)]
        dsts = [xq_fm[:, c, :] for c in range(NCH)]
        e_xqfm, _ = emit_transpose_evac(
            sch, tpb16, idtb[:], srcs, dsts,
            w_tensor=ev_max(e_xq, e_idb, cs["tpb_war"], e_wts),
            w_evac=())
        cs["tpb_war"] = e_xqfm
        cs["xq_tp"] = e_xqfm
        st[idx] = {"xqfm": e_xqfm}

    def stageB_act(idx):
        Tb = sch.cnt["t"]
        st[idx]["Tb"] = Tb
        prev_u = st[idx - 1]["mults"][-1] if idx > 0 else None
        e_mult = None
        silus = []
        mults = []
        for ic in range(NIC):
            io0, isz = IC_OFF[ic], IC_SZ[ic]
            e_gate_f = ("t", Tb + 2 * ic + 1)
            e_up_f = ("t", Tb + 2 * ic + 2)
            e_silu = sch.op("scalar", lambda e, isz=isz: e.activation(
                out=g_sb[:, 0:isz], in_=g_ps[:, 0:isz], func=AF.Silu,
                scale=0.25),
                waits=ev_max(e_gate_f,
                             e_mult if ic > 0 else prev_u), inc="a")
            e_mult = sch.op(
                "vector", lambda e, isz=isz, io0=io0: e.scalar_tensor_tensor(
                    out=h_sb[:, io0:io0 + isz], in0=u_ps[:, 0:isz],
                    scalar=0.25, in1=g_sb[:, 0:isz],
                    op0=ALU.mult, op1=ALU.mult),
                waits=ev_max(e_silu, e_up_f,
                             cs["hq_prev"] if ic == 0 else None), inc="v")
            silus.append(e_silu)
            mults.append(e_mult)
        st[idx]["silus"] = silus
        st[idx]["mults"] = mults
        st[idx]["h"] = e_mult

    def stageB_mm(idx):
        e_xqfm = st[idx]["xqfm"]
        silus = st[idx]["silus"]
        mults = st[idx]["mults"]
        assert sch.cnt["t"] == st[idx]["Tb"]
        prev_g = st[idx - 1]["silus"][-1] if idx > 0 else None
        prev_u = st[idx - 1]["mults"][-1] if idx > 0 else None
        for ic in range(NIC):
            io0, isz = IC_OFF[ic], IC_SZ[ic]
            for (ps, wT, e_free) in (
                    (g_ps, wgqT, silus[ic - 1] if ic > 0 else prev_g),
                    (u_ps, wuqT, mults[ic - 1] if ic > 0 else prev_u)):
                for c in range(NCH):
                    sch.op(
                        "tensor", lambda e, c=c, ps=ps, wT=wT, io0=io0, isz=isz:
                        e.matmul(ps[:, 0:isz], xq_fm[:, c, :],
                                 wT[:, c, io0:io0 + isz],
                                 start=(c == 0), stop=(c == NCH - 1)),
                        waits=ev_max(e_xqfm, e_wts, e_free) if c == 0 else (),
                        inc="t" if c == NCH - 1 else None)
        assert sch.cnt["t"] == st[idx]["Tb"] + 2 * NIC

    def stageC_v(idx):
        e_h = st[idx]["h"]
        q["_last"] = ev_max(q["_last"], cs["hq_tp"])
        e_hq = None
        for half in range(2):
            e_hq = emit_quant(sch, q, h_sb[:, half * 1408:(half + 1) * 1408],
                              hq_b[:, half * 1408:(half + 1) * 1408], 1408,
                              e_h, e_h)
        cs["hq_prev"] = e_hq
        st[idx]["hq"] = e_hq

    def stageC_t(idx):
        srcs = [hq_b[:, c * 128:(c + 1) * 128] for c in range(NRT_G)]
        e_hqf, _ = emit_transpose_evac(
            sch, tpb16, idtb[:], srcs, None,
            w_tensor=ev_max(st[idx]["hq"], cs["tpb_war"]), w_evac=(),
            dst_groups=lambda cb, ns: hq_fm[:, cb:cb + ns, :])
        cs["tpb_war"] = e_hqf
        cs["hq_tp"] = e_hqf
        st[idx]["hqf"] = e_hqf

    def stageD(idx):
        s_, t = tiles[idx]
        par = idx % 2
        t0 = t * TS
        e_dn0 = None
        e_dn = None
        for half in range(2):
            for c in range(NRT_G):
                e_dn = sch.op(
                    "tensor", lambda e, c=c, half=half: e.matmul(
                        d_ps[:, half], hq_fm[:, c, :],
                        wdqT[:, c, half * 512:half * 512 + 512],
                        start=(c == 0), stop=(c == NRT_G - 1)),
                    waits=ev_max(st[idx]["hqf"], cs["d_free"]) if c == 0 else (),
                    inc="t" if c == NRT_G - 1 else None)
            if half == 0:
                e_dn0 = e_dn
        e_ya0 = sch.op("vector", lambda e, par=par: e.scalar_tensor_tensor(
            out=xtile[:, par, 0:512], in0=d_ps[:, 0], scalar=0.25,
            in1=xtile[:, par, 0:512], op0=ALU.mult, op1=ALU.add),
            waits=ev_max(e_dn0), inc="v")
        e_ya = sch.op("vector", lambda e, par=par: e.scalar_tensor_tensor(
            out=xtile[:, par, 512:1024], in0=d_ps[:, 1], scalar=0.25,
            in1=xtile[:, par, 512:1024], op0=ALU.mult, op1=ALU.add),
            waits=ev_max(e_dn), inc="v")
        cs["d_free"] = e_ya
        e_ys = sch.dma("sync", y_d[s_, t0:t0 + TS, :], xtile[:, par],
                       "d_y", waits=ev_max(e_ya0, e_ya))
        xtile_release[par] = e_ys

    sched_load(0)
    stageA(0)
    stageB_act(0)
    stageB_mm(0)
    for idx in range(ntile):
        if idx + 1 < ntile:
            sched_load(idx + 1)
            stageA(idx + 1)
        stageC_v(idx)
        if idx + 1 < ntile:
            stageB_act(idx + 1)
            stageB_mm(idx + 1)
        stageC_t(idx)
        stageD(idx)
    sch.op("sync", None, waits=ev_max(sch.ev("d_y"),
                                      sch.ev("d_stg")))




def build_l0a(nc):
    x_d = nc.dram_tensor("x", [SEQ_PER_CORE, S, D], F32, kind="ExternalInput").ap()
    wg_d = nc.dram_tensor("w_gate", [I, D], F32, kind="ExternalInput").ap()
    wu_d = nc.dram_tensor("w_up", [I, D], F32, kind="ExternalInput").ap()
    wd_d = nc.dram_tensor("w_down", [D, I], F32, kind="ExternalInput").ap()
    ident_d = nc.dram_tensor("ident", [128, 128], F32, kind="ExternalInput").ap()
    y_d = nc.dram_tensor("y_out", [SEQ_PER_CORE, S, D], F32, kind="ExternalOutput").ap()
    xnfm_d = nc.dram_tensor("xnfm_out", [SEQ_PER_CORE, NCH, 128, S], BF16,
                            kind="ExternalOutput").ap()
    dr = {"x": x_d, "w_gate": wg_d, "w_up": wu_d, "w_down": wd_d,
          "ident": ident_d, "y_out": y_d, "xnfm_out": xnfm_d}
    with contextlib.ExitStack() as ctx:
        sch = Sched(nc, ctx)
        block = ctx.enter_context(nc.Block())
        with contextlib.ExitStack() as pctx:
            emit_l0a(nc, sch, pctx.enter_context, dr)
            sch.build(block)
    return nc


def emit_l0b(nc, sch, E, dr, pfx=""):
    sch.pfx = pfx
    y_d = dr["y_in"]
    qkvwT_d = dr["qkv_wT"]
    qkvb_d = dr["qkv_b_pc"]
    projwT_d = dr["proj_wT"]
    projbr_d = dr["proj_b_row"]
    NH = 16
    NP = 8
    xnfm_d = dr["xnfm"]
    x1_d = dr["x1_out"]
    nrm_scr = nc.dram_tensor("nrm_scr", [SEQ_PER_CORE, NP, 8 * 512], F32,
                             kind="Internal").ap()
    onf_scr = nc.dram_tensor("onf_scr", [SEQ_PER_CORE, NP, 2, 64, S], BF16,
                             kind="Internal").ap()

    qkv_sb = E(nc.sbuf_tensor(pfx + "qkv_sb", [128, NCH, 3 * D], BF16))
    projw_sb = E(nc.sbuf_tensor(pfx + "projw_sb", [128, NCH, D], BF16))
    qkvb_sb = E(nc.sbuf_tensor(pfx + "qkvb_sb", [128, 24], F32))

    bvb = E(nc.sbuf_tensor(pfx + "bvb", [128, 8], BF16))
    ones_r = E(nc.sbuf_tensor(pfx + "ones_r", [1, 128], BF16))
    ones_f = E(nc.sbuf_tensor(pfx + "ones_f", [1, 64], F32))
    effrow_b = E(nc.sbuf_tensor(pfx + "effrow_b", [1, D], BF16))
    xnfm_sb = E(nc.sbuf_tensor(pfx + "xnfm_sb", [128, NCH, S], BF16))
    qf2 = E(nc.sbuf_tensor(pfx + "qf", [128, 2, S], BF16))
    kf2 = E(nc.sbuf_tensor(pfx + "kf", [128, 2, S], BF16))
    v_aug = E(nc.sbuf_tensor(pfx + "v_aug", [128, NT, NH * 65], BF16))
    p_sb = E(nc.sbuf_tensor(pfx + "p_sb", [128, 2, 2, 512], BF16))
    o_raw = E(nc.sbuf_tensor(pfx + "o_raw", [64, 2, 4, 512], BF16))
    nrm_sb = E(nc.sbuf_tensor(pfx + "nrm_sb", [128, 8 * 512], F32))
    rin_sb = E(nc.sbuf_tensor(pfx + "rin_sb", [128, 32], F32))
    o_nfm = E(nc.sbuf_tensor(pfx + "o_nfm", [128, NCH, S], BF16))
    projbr_b = o_nfm[0:1, 0, 0:D]
    y_tile = nrm_sb[:, 0:D]
    sT2 = E(nc.psum_tensor(pfx + "sT2", [128, 2, 512], F32))
    o_psA = E(nc.psum_tensor(pfx + "o_psA", [65, 512], F32))
    o_psB = E(nc.psum_tensor(pfx + "o_psB", [65, 512], F32))
    qk_ps = E(nc.psum_tensor(pfx + "qk_ps", [128, 512], F32))
    nbc_ps = E(nc.psum_tensor(pfx + "nbc_ps", [64, 512], F32))
    proj_ps = E(nc.psum_tensor(pfx + "proj_ps", [128, 2, 512], F32))


    e_qkvw = sch.dma("gpsimd", qkv_sb[:],
                     qkvwT_d.rearrange("(c p) o -> p c o", p=128), "d_qkvw")
    e_projw = sch.dma("gpsimd", projw_sb[:],
                      projwT_d.rearrange("(c p) o -> p c o", p=128), "d_projw")
    e_projb = sch.dma("gpsimd", projbr_b, projbr_d, "d_projb")
    e_qkvb = sch.dma("sync", qkvb_sb[:], qkvb_d, "d_qkvb")
    e_ones = sch.op("vector", lambda e: e.memset(ones_r[:], 1.0))
    e_ones_f = sch.op("vector", lambda e: e.memset(ones_f[:], 1.0))
    e_bvb = sch.op("vector", lambda e: e.tensor_copy(
        bvb[:], qkvb_sb[:, 16:24]), waits=(e_qkvb,))
    va_ones = v_aug[:].rearrange("p t (h c) -> p t h c", c=65)[:, :, :, 64:65]
    e_vones = sch.op("vector", lambda e: e.memset(va_ones, 1.0))

    e_eff = None
    for half in range(2):
        for c in range(NCH):
            sch.op("tensor", lambda e, c=c, half=half: e.matmul(
                proj_ps[0:1, half], bvb[:, c:c + 1],
                projw_sb[:, c, half * 512:half * 512 + 512],
                start=(c == 0), stop=False),
                waits=ev_max(e_bvb, e_projw) if (c == 0 and half == 0) else ())
        e_eff = sch.op("tensor", lambda e, half=half: e.matmul(
            proj_ps[0:1, half], ones_r[0:1, 0:1],
            projbr_b[0:1, half * 512:half * 512 + 512],
            start=False, stop=True),
            waits=ev_max(e_ones, e_projb) if half == 0 else (), inc="t")
    e_effrow = sch.op("scalar", lambda e: e.copy(
        out=effrow_b[:], in_=proj_ps[0:1, :].rearrange("p h j -> p (h j)")),
        waits=(e_eff,))

    e_xn_users = None
    e_onfm_users = None
    e_vaug_users = None
    e_x1_store = None
    e_onh_store = None
    e_nbc_free = None
    e_nbc_sb_free = {}
    for s_ in range(SEQ_PER_CORE):
        e_xn = sch.dma("sync", xnfm_sb[:],
                       xnfm_d[s_].rearrange("c p t -> p c t"), "d_xn",
                       waits=ev_max(e_xn_users))
        e_vevac = None
        e_vps_free = None
        e_vmm = None
        for tt in range(NT):
            for half in range(2):
                for c in range(NCH):
                    e_vmm = sch.op(
                        "tensor", lambda e, tt=tt, half=half, c=c: e.matmul(
                            qk_ps[:, 0:512],
                            xnfm_sb[:, c, tt * TS:(tt + 1) * TS],
                            qkv_sb[:, c, 2 * D + half * 512:2 * D + (half + 1) * 512],
                            start=(c == 0), stop=(c == NCH - 1)),
                        waits=ev_max(e_xn, e_qkvw, e_vps_free, e_vones)
                        if c == 0 else (),
                        inc="t" if c == NCH - 1 else None)
                vslice = v_aug[:].rearrange(
                    "p t (h c) -> p t h c", c=65)[:, tt, half * 8:(half + 1) * 8, 0:64]
                e_vevac = sch.op(
                    "scalar", lambda e, vslice=vslice: e.copy(
                        out=vslice,
                        in_=qk_ps[:, 0:512].rearrange("p (h c) -> p h c", c=64)),
                    waits=ev_max(e_vmm, e_vaug_users))
                e_vps_free = e_vevac
        e_vdone = e_vevac

        e_qkps_free = e_vps_free
        if s_ == 0:
            e_projps_free = {0: e_effrow, 1: e_effrow}
        e_qf_user = {0: None, 1: None}
        e_kf_user = {0: None, 1: None}
        e_praw_free = None
        e_sT_free = {0: None, 1: None}
        e_p_free = {}
        e_onh_user = None
        e_nrm_user = e_x1_store
        e_mm = None
        for hp in range(NP):
            hpar = hp % 2
            qf = qf2[:, hpar]
            kf = kf2[:, hpar]
            o0 = hp * 128
            e_qev = e_kev = None
            for which, (dst, bias_col, col0, e_user) in enumerate((
                    (qf, hp, o0, e_qf_user[hpar]),
                    (kf, 8 + hp, D + o0, e_kf_user[hpar]))):
                for qc in range(4):
                    for c in range(NCH):
                        e_mm = sch.op(
                            "tensor",
                            lambda e, c=c, qc=qc, col0=col0: e.matmul(
                                qk_ps[:, 0:512],
                                qkv_sb[:, c, col0:col0 + 128],
                                xnfm_sb[:, c, qc * 512:(qc + 1) * 512],
                                start=(c == 0), stop=(c == NCH - 1)),
                            waits=ev_max(e_xn, e_qkvw, e_qkps_free, e_user)
                            if c == 0 else (),
                            inc="t" if c == NCH - 1 else None)
                    e_ev = sch.op(
                        "scalar",
                        lambda e, dst=dst, qc=qc, bias_col=bias_col: e.activation(
                            out=dst[:, qc * 512:(qc + 1) * 512],
                            in_=qk_ps[:, 0:512], func=AF.Identity,
                            bias=qkvb_sb[:, bias_col:bias_col + 1]),
                        waits=ev_max(e_mm, e_qkvb))
                    e_qkps_free = e_ev
                if which == 0:
                    e_qev = e_ev
                else:
                    e_kev = e_ev
            e_att = {}
            for qc in range(4):
                for kt in range(NT):
                    par = kt % 2
                    sT = sT2 if par == 0 else proj_ps
                    for hi in range(2):
                        b0 = hi * 64
                        e_sc = sch.op(
                            "tensor",
                            lambda e, sT=sT, b0=b0, kt=kt, qc=qc, qf=qf, kf=kf,
                            hi=hi: e.matmul(
                                sT[:, hi, 0:512],
                                kf[b0:b0 + 64, kt * TS:(kt + 1) * TS],
                                qf[b0:b0 + 64, qc * 512:(qc + 1) * 512],
                                start=True, stop=True,
                                tile_position=(b0, 0)),
                            waits=ev_max(e_qev, e_kev, e_sT_free[par],
                                         (e_projps_free[0] if par == 1 else None),
                                         (e_projps_free[1] if par == 1 else None)),
                            inc="t")
                    e_exp = sch.op(
                        "scalar",
                        lambda e, sT=sT, par=par: e.activation(
                            out=p_sb[:, par].rearrange("p h j -> p (h j)"),
                            in_=sT[:, :, 0:512].rearrange("p h j -> p (h j)"),
                            func=AF.Exp, scale=0.125),
                        waits=ev_max(e_sc, e_p_free.get(par)))
                    e_sT_free[par] = e_exp
                    for hi, o_ps in enumerate((o_psA, o_psB)):
                        h = 2 * hp + hi
                        e_att[(qc, hi)] = sch.op(
                            "tensor",
                            lambda e, o_ps=o_ps, kt=kt, par=par, hi=hi, h=h:
                            e.matmul(
                                o_ps[:, 0:512],
                                v_aug[:].rearrange(
                                    "p t (hh c) -> p t hh c", c=65)[:, kt, h, :],
                                p_sb[:, par, hi, :],
                                start=(kt == 0), stop=(kt == NT - 1)),
                            waits=ev_max(e_exp, e_vdone,
                                         (e_praw_free if kt == 0 else None)),
                            inc="t")
                    e_p_free[par] = e_att[(qc, 1)]
                evs = []
                for hi, o_ps in enumerate((o_psA, o_psB)):
                    evs.append(sch.op(
                        "scalar", lambda e, o_ps=o_ps, hi=hi, qc=qc: e.copy(
                            out=o_raw[:, hi, qc, :], in_=o_ps[0:64, 0:512]),
                        waits=ev_max(e_att[(qc, hi)], e_onh_user)))
                    evs.append(sch.op(
                        "scalar", lambda e, o_ps=o_ps, hi=hi, qc=qc: e.copy(
                            out=nrm_sb[64:65, (qc * 2 + hi) * 512:
                                       (qc * 2 + hi + 1) * 512],
                            in_=o_ps[64:65, 0:512]),
                        waits=ev_max(e_nrm_user)))
                e_praw_free = ev_max(*evs)
            e_qf_user[hpar] = e_kf_user[hpar] = ("t", sch.cnt["t"])
            e_h1 = sch.dma("sync", nrm_scr[s_, hp], nrm_sb[64:65, :], "d_nbo",
                           waits=ev_max(e_praw_free))
            e_nrm_user = e_h1
            e_h2 = sch.dma("sync", rin_sb[:],
                           nrm_scr[s_, hp].rearrange("(r j) -> r j", r=128),
                           "d_nbi", waits=(e_h1,))
            e_rcp = sch.op("vector", lambda e: e.reciprocal(
                out=rin_sb[:], in_=rin_sb[:]), waits=(e_h2,))
            e_h3 = sch.dma("sync", nrm_scr[s_, hp].rearrange("(r j) -> r j", r=128),
                           rin_sb[:], "d_nbo", waits=(e_rcp,))
            e_h4 = sch.dma("sync", nrm_sb[0:1, :],
                           nrm_scr[s_, hp].rearrange("j -> () j"),
                           "d_nbi", waits=(e_h3,))
            e_div_last = None
            for qc in range(4):
                for hi in range(2):
                    slot = qc * 2 + hi
                    e_nbc = sch.op(
                        "tensor", lambda e, slot=slot: e.matmul(
                            nbc_ps[:, 0:512], ones_f[0:1, :],
                            nrm_sb[0:1, slot * 512:(slot + 1) * 512],
                            start=True, stop=True),
                        waits=ev_max(e_h4, e_nbc_free, e_ones_f), inc="t")
                    e_div = sch.op(
                        "vector", lambda e, hi=hi, qc=qc: e.tensor_tensor(
                            out=o_raw[:, hi, qc, :], in0=o_raw[:, hi, qc, :],
                            in1=nbc_ps[0:64, 0:512], op=ALU.mult),
                        waits=ev_max(e_nbc, e_onh_store))
                    e_nbc_free = e_div
                    e_div_last = e_div
            e_onh_store = sch.dma(
                "sync",
                onf_scr[s_, hp].rearrange("h d (qc j) -> d h qc j", j=512),
                o_raw[:], "d_onh", waits=ev_max(e_div_last))
            e_onh_user = e_onh_store
        e_onfm = sch.dma(
            "sync", o_nfm[:],
            onf_scr[s_].rearrange("hp h d t -> (h d) hp t"), "d_onfm",
            waits=ev_max(e_onh_store, e_onfm_users, e_effrow))
        e_x1v = None
        e_pj = None
        for tt in range(NT):
            e_yld = sch.dma("sync", y_tile, y_d[s_, tt * TS:(tt + 1) * TS, :],
                            "d_yl", waits=ev_max(e_x1_store, e_onfm))
            for half in range(2):
                for c in range(NCH):
                    sch.op("tensor", lambda e, c=c, half=half, tt=tt: e.matmul(
                        proj_ps[:, half], o_nfm[:, c, tt * TS:(tt + 1) * TS],
                        projw_sb[:, c, half * 512:(half + 1) * 512],
                        start=(c == 0), stop=False),
                        waits=ev_max(e_onfm, e_effrow, e_ones,
                                     e_projps_free[half]) if c == 0 else ())
                e_pj = sch.op("tensor", lambda e, half=half: e.matmul(
                    proj_ps[:, half], ones_r[0:1, :],
                    effrow_b[0:1, half * 512:(half + 1) * 512],
                    start=False, stop=True), inc="t")
                e_add = sch.op(
                    "vector", lambda e, half=half: e.tensor_tensor(
                        out=y_tile[:, half * 512:(half + 1) * 512],
                        in0=y_tile[:, half * 512:(half + 1) * 512],
                        in1=proj_ps[:, half], op=ALU.add),
                    waits=ev_max(e_pj, e_yld))
                e_projps_free[half] = e_add
                e_x1v = e_add
            e_x1_store = sch.dma("sync", x1_d[s_, tt * TS:(tt + 1) * TS, :],
                                 y_tile, "d_x1", waits=(e_x1v,))
        e_xn_users = ev_max(e_vmm, e_mm)
        e_onfm_users = e_pj
        e_vaug_users = ("t", sch.cnt["t"])
    sch.op("sync", None, waits=ev_max(sch.ev("d_x1")))




def build_l0b(nc):
    xnfm_d = nc.dram_tensor("xnfm", [SEQ_PER_CORE, NCH, 128, S], BF16,
                            kind="ExternalInput").ap()
    y_d = nc.dram_tensor("y_in", [SEQ_PER_CORE, S, D], F32, kind="ExternalInput").ap()
    qkvwT_d = nc.dram_tensor("qkv_wT", [D, 3 * D], F32, kind="ExternalInput").ap()
    qkvb_d = nc.dram_tensor("qkv_b_pc", [128, 24], F32, kind="ExternalInput").ap()
    projwT_d = nc.dram_tensor("proj_wT", [D, D], F32, kind="ExternalInput").ap()
    projbr_d = nc.dram_tensor("proj_b_row", [1, D], F32, kind="ExternalInput").ap()
    x1_d = nc.dram_tensor("x1_out", [SEQ_PER_CORE, S, D], F32,
                          kind="ExternalOutput").ap()
    dr = {"xnfm": xnfm_d, "y_in": y_d, "qkv_wT": qkvwT_d, "qkv_b_pc": qkvb_d,
          "proj_wT": projwT_d, "proj_b_row": projbr_d, "x1_out": x1_d}
    with contextlib.ExitStack() as ctx:
        sch = Sched(nc, ctx)
        block = ctx.enter_context(nc.Block())
        with contextlib.ExitStack() as pctx:
            emit_l0b(nc, sch, pctx.enter_context, dr)
            sch.build(block)
    return nc


def emit_l1a(nc, sch, E, dr, pfx=""):
    sch.pfx = pfx
    x1_d = dr["x1"]
    wka_d = dr["Wk_a"]
    wva_d = dr["Wv_a"]
    wktf_d = dr["WkT_f"]
    bkr_d = dr["bk_row"]
    mur_d = dr["mu_r_pc"]
    muk_d = dr["mu_k_pc"]
    ident_d = dr["ident"]
    NH = 16
    xn1fm_d = dr["xn1fm_out"]
    xrfm_d = dr["xrfm_out"]
    kafm_d = dr["kafm_out"]
    kv_d = dr["kv_out"]

    wka_sb = E(nc.sbuf_tensor(pfx + "wka_sb", [128, NCH, D], BF16))
    wva_sb = E(nc.sbuf_tensor(pfx + "wva_sb", [128, NCH, D], BF16))
    wktf_sb = E(nc.sbuf_tensor(pfx + "wktf_sb", [128, NCH, D], BF16))
    bkr_b = E(nc.sbuf_tensor(pfx + "bkr_b", [1, D], BF16))
    mur_sb = E(nc.sbuf_tensor(pfx + "mur_sb", [128, NCH], F32))
    muk_sb = E(nc.sbuf_tensor(pfx + "muk_sb", [128, NCH], F32))
    ones_r = E(nc.sbuf_tensor(pfx + "ones_r", [1, 128], BF16))
    idt = E(nc.sbuf_tensor(pfx + "idt", [128, 128], F32))
    lnb = {n: E(nc.sbuf_tensor(pfx + "ln_" + n, [128, 1], F32))
           for n in ("s1", "ssq", "mean", "ex2", "msq", "var", "sd", "rstd")}
    lnb["sqs"] = E(nc.sbuf_tensor(pfx + "ln_sqs", [128, D], F32))
    x1t = E(nc.sbuf_tensor(pfx + "x1t", [128, 2, D], F32))
    xn1_tm = E(nc.sbuf_tensor(pfx + "xn1_tm", [128, D], F32))
    win = E(nc.sbuf_tensor(pfx + "win", [128, NCH, 129], F32))
    stg = E(nc.sbuf_tensor(pfx + "stg", [128, NCH, 128], BF16))
    xr_stg = E(nc.sbuf_tensor(pfx + "xr_stg", [128, NCH, 128], BF16))
    dmix = E(nc.sbuf_tensor(pfx + "dmix", [128, NCH, 128], F32))
    tmix = E(nc.sbuf_tensor(pfx + "tmix", [128, NCH, 128], F32))
    xk_grp = E(nc.sbuf_tensor(pfx + "xk_grp", [128, NCH, 512], BF16))
    rl_sb = E(nc.sbuf_tensor(pfx + "rl_sb", [128, 512], F32))
    ka_stg = E(nc.sbuf_tensor(pfx + "ka_stg", [128, NCH, 512], BF16))
    kfeat = E(nc.sbuf_tensor(pfx + "kfeat", [128, NT, D], BF16))
    v_aug = E(nc.sbuf_tensor(pfx + "v_aug", [128, NT, NH * 65], BF16))
    e_sb = E(nc.sbuf_tensor(pfx + "e_sb", [128, D], F32))
    r_sb = E(nc.sbuf_tensor(pfx + "r_sb", [128, D], F32))
    kv_stg = E(nc.sbuf_tensor(pfx + "kv_stg", [64, NH, 65], BF16))
    tp32 = E(nc.psum_tensor(pfx + "tp32", [128, 4, 128], F32))
    k_ps = E(nc.psum_tensor(pfx + "k_ps", [128, 512], F32))
    v_ps = E(nc.psum_tensor(pfx + "v_ps", [128, 512], F32))
    kf_ps = E(nc.psum_tensor(pfx + "kf_ps", [128, 2, 512], F32))
    kv_ps = E(nc.psum_tensor(pfx + "kv_ps", [64, 2, 512], F32))


    e_wka = sch.dma("gpsimd", wka_sb[:],
                    wka_d.rearrange("(c p) o -> p c o", p=128), "d_wka")
    e_wva = sch.dma("gpsimd", wva_sb[:],
                    wva_d.rearrange("(c p) o -> p c o", p=128), "d_wva")
    e_wktf = sch.dma("gpsimd", wktf_sb[:],
                     wktf_d.rearrange("(c p) o -> p c o", p=128), "d_wktf")
    e_bkr = sch.dma("gpsimd", bkr_b[:], bkr_d, "d_bkr")
    e_mur = sch.dma("sync", mur_sb[:], mur_d, "d_mur")
    e_muk = sch.dma("sync", muk_sb[:], muk_d, "d_muk")
    e_id = sch.dma("sync", idt[:], ident_d, "d_id")
    e_ones = sch.op("vector", lambda e: e.memset(ones_r[:], 1.0))
    va_ones = v_aug[:].rearrange("p t (h c) -> p t h c", c=65)[:, :, :, 64:65]
    e_vones = sch.op("vector", lambda e: e.memset(va_ones, 1.0))

    x1_release = [None, None]
    e_stg_user = None
    e_xr_user = None
    e_ka_user = None
    e_tp32_war = None
    e_kps_free = None
    e_vps_free = None
    e_kf_free = None
    e_dmix_user = None
    e_win_user = None
    e_kfeat_users = None
    e_vaug_users = None
    e_kvstg_free = None
    e_er_user = None
    e_load = {}
    tiles = [(s_, t) for s_ in range(SEQ_PER_CORE) for t in range(NT)]

    def sched_load(idx):
        s_, t = tiles[idx]
        par = idx % 2
        e_load[idx] = sch.dma("sync", x1t[:, par],
                              x1_d[s_, t * TS:(t + 1) * TS, :],
                              "d_x%d" % par, waits=ev_max(x1_release[par]))

    sched_load(0)
    for idx, (s_, t) in enumerate(tiles):
        par = idx % 2
        t0 = t * TS
        g = t % 4
        if idx + 1 < len(tiles):
            sched_load(idx + 1)
        e_x = e_load[idx]
        if t == 0:
            e_z = sch.op("vector", lambda e: e.memset(win[:, :, 0:1], 0.0),
                         waits=ev_max(e_win_user))
        else:
            e_z = sch.op("vector", lambda e: e.tensor_copy(
                win[:, :, 0:1], win[:, :, 128:129]), waits=ev_max(e_win_user))
        e_xn = emit_ln(sch, lnb, x1t[:, par], xn1_tm[:], e_x)
        x1_release[par] = e_xn
        evs = []
        for cb in (0, 4):
            e_tps = None
            for s_i in range(4):
                c = cb + s_i
                e_tps = sch.op("tensor", lambda e, c=c, s_i=s_i: e.transpose(
                    tp32[:, s_i, 0:128], xn1_tm[:, c * 128:(c + 1) * 128], idt[:]),
                    waits=ev_max(e_xn, e_id, e_tp32_war) if s_i == 0 else (),
                    inc="t" if s_i == 3 else None)
            for s_i in range(4):
                c = cb + s_i
                sch.op("scalar", lambda e, c=c, s_i=s_i: e.copy(
                    out=win[:, c, 1:129], in_=tp32[:, s_i, 0:128]),
                    waits=ev_max(e_tps, e_z, e_dmix_user) if s_i == 0 else ())
                evs.append(sch.op("scalar", lambda e, c=c, s_i=s_i: e.copy(
                    out=stg[:, c, :], in_=tp32[:, s_i, 0:128]),
                    waits=ev_max(e_stg_user)))
            e_tp32_war = evs[-1]
        e_evac = evs[-1]
        e_win_user = None
        e_stg_user = sch.dma(
            "sync", xn1fm_d[s_].rearrange("c p t -> p c t")[:, :, t0:t0 + TS],
            stg[:], "d_stg", waits=(e_evac,))
        e_d = sch.op("vector", lambda e: e.tensor_tensor(
            out=dmix[:], in0=win[:, :, 1:129], in1=win[:, :, 0:128],
            op=ALU.subtract), waits=ev_max(e_evac))
        mu_r_bc = mur_sb[:].rearrange("p (c o) -> p c o", o=1)
        mu_k_bc = muk_sb[:].rearrange("p (c o) -> p c o", o=1)
        dv = dmix[:]
        dvb, murb = broadcast_tensor_aps(dv, mu_r_bc)
        _, mukb = broadcast_tensor_aps(dv, mu_k_bc)
        e_t1 = sch.op("vector", lambda e: e.tensor_tensor(
            out=tmix[:], in0=dvb, in1=murb, op=ALU.mult),
            waits=ev_max(e_d, e_mur))
        e_xr = sch.op("vector", lambda e: e.tensor_tensor(
            out=xr_stg[:], in0=tmix[:], in1=win[:, :, 0:128], op=ALU.add),
            waits=ev_max(e_xr_user))
        e_xr_user = sch.dma(
            "sync", xrfm_d[s_].rearrange("c p t -> p c t")[:, :, t0:t0 + TS],
            xr_stg[:], "d_xr", waits=(e_xr,))
        e_t2 = sch.op("vector", lambda e: e.tensor_tensor(
            out=tmix[:], in0=dvb, in1=mukb, op=ALU.mult), waits=ev_max(e_muk))
        e_xk = sch.op("vector", lambda e, g=g: e.tensor_tensor(
            out=xk_grp[:, :, g * 128:(g + 1) * 128], in0=tmix[:],
            in1=win[:, :, 0:128], op=ALU.add), waits=ev_max(e_kf_free))
        e_win_user = e_xk
        e_dmix_user = e_t2
        kf_evs = []
        for half in range(2):
            for c in range(NCH):
                e_kmm = sch.op("tensor", lambda e, c=c, half=half: e.matmul(
                    k_ps[:, 0:512], stg[:, c, :],
                    wka_sb[:, c, half * 512:(half + 1) * 512],
                    start=(c == 0), stop=False),
                    waits=ev_max(e_evac, e_wka, e_kps_free) if c == 0 else ())
            e_kmm = sch.op("tensor", lambda e, half=half: e.matmul(
                k_ps[:, 0:512], ones_r[0:1, :],
                bkr_b[0:1, half * 512:(half + 1) * 512],
                start=False, stop=True),
                waits=ev_max(e_ones, e_bkr), inc="t")
            e_e = sch.op("scalar", lambda e, half=half: e.activation(
                out=e_sb[:, half * 512:(half + 1) * 512], in_=k_ps[:, 0:512],
                func=AF.Exp), waits=ev_max(e_kmm, e_er_user))
            e_r = sch.op("scalar", lambda e, half=half: e.activation(
                out=r_sb[:, half * 512:(half + 1) * 512], in_=k_ps[:, 0:512],
                func=AF.Relu), waits=())
            e_kf = sch.op("vector", lambda e, half=half, t=t: e.scalar_tensor_tensor(
                out=kfeat[:, t, half * 512:(half + 1) * 512],
                in0=e_sb[:, half * 512:(half + 1) * 512], scalar=1.0,
                in1=r_sb[:, half * 512:(half + 1) * 512],
                op0=ALU.min, op1=ALU.add),
                waits=ev_max(e_r, e_kfeat_users))
            e_kps_free = e_r
            kf_evs.append(e_kf)
        e_kf_done = kf_evs[-1]
        e_er_user = e_kf_done
        v_evs = []
        for half in range(2):
            for c in range(NCH):
                e_vmm = sch.op("tensor", lambda e, c=c, half=half: e.matmul(
                    v_ps[:, 0:512], stg[:, c, :],
                    wva_sb[:, c, half * 512:(half + 1) * 512],
                    start=(c == 0), stop=(c == NCH - 1)),
                    waits=ev_max(e_evac, e_wva, e_vps_free, e_vones)
                    if c == 0 else (),
                    inc="t" if c == NCH - 1 else None)
            vslice = v_aug[:].rearrange(
                "p t (h c) -> p t h c", c=65)[:, t, half * 8:(half + 1) * 8, 0:64]
            e_vev = sch.op("scalar", lambda e, vslice=vslice: e.copy(
                out=vslice,
                in_=v_ps[:, 0:512].rearrange("p (h c) -> p h c", c=64)),
                waits=ev_max(e_vmm, e_vaug_users))
            e_vps_free = e_vev
            v_evs.append(e_vev)
        e_v_done = v_evs[-1]
        e_stg_user = ev_max(e_stg_user, ("t", sch.cnt["t"]))
        if g == 3:
            gi = t // 4
            ka_evs = []
            for dc in range(NCH):
                for cc in range(NCH):
                    e_km = sch.op("tensor", lambda e, dc=dc, cc=cc: e.matmul(
                        kf_ps[:, dc % 2], wktf_sb[:, cc, dc * 128:(dc + 1) * 128],
                        xk_grp[:, cc, :], start=(cc == 0), stop=(cc == NCH - 1)),
                        waits=ev_max(e_xk, e_wktf,
                                     ka_evs[-2] if len(ka_evs) >= 2 else None)
                        if cc == 0 else (),
                        inc="t" if cc == NCH - 1 else None)
                e_rl = sch.op("scalar", lambda e, dc=dc: e.activation(
                    out=rl_sb[:], in_=kf_ps[:, dc % 2], func=AF.Relu),
                    waits=ev_max(e_km))
                e_ka = sch.op("scalar", lambda e, dc=dc: e.activation(
                    out=ka_stg[:, dc, :], in_=rl_sb[:], func=AF.Square),
                    waits=ev_max(e_ka_user))
                ka_evs.append(e_ka)
            e_ka_user = sch.dma(
                "sync",
                kafm_d[s_].rearrange("c p t -> p c t")[:, :, gi * 512:(gi + 1) * 512],
                ka_stg[:], "d_ka", waits=(ka_evs[-1],))
            e_kf_free = ("a", sch.cnt["a"])
        if t == NT - 1:
            e_kv_evs = []
            e_kvps_free = {}
            for h in range(NH):
                pb = h % 2
                for tt in range(NT):
                    e_kvm = sch.op(
                        "tensor", lambda e, h=h, tt=tt, pb=pb: e.matmul(
                            kv_ps[:, pb, 0:65],
                            kfeat[:, tt, (h // 2) * 128 + (h % 2) * 64:
                                  (h // 2) * 128 + (h % 2) * 64 + 64],
                            v_aug[:].rearrange(
                                "p t (hh c) -> p t hh c", c=65)[:, tt, h, :],
                            start=(tt == 0), stop=(tt == NT - 1)),
                        waits=ev_max(e_kf_done, e_v_done, e_kvps_free.get(pb))
                        if tt == 0 else (),
                        inc="t" if tt == NT - 1 else None)
                e_kve = sch.op("scalar", lambda e, h=h, pb=pb: e.copy(
                    out=kv_stg[:, h, :], in_=kv_ps[0:64, pb, 0:65]),
                    waits=ev_max(e_kvm, e_kvstg_free))
                e_kvps_free[pb] = e_kve
                e_kv_evs.append(e_kve)
            e_kvstg_free = sch.dma(
                "sync", kv_d[s_].rearrange("h d c -> d h c"), kv_stg[:],
                "d_kv", waits=(e_kv_evs[-1],))
            e_kfeat_users = ("t", sch.cnt["t"])
            e_vaug_users = ("t", sch.cnt["t"])
    sch.op("sync", None, waits=ev_max(
        sch.ev("d_kv"), sch.ev("d_stg"),
        sch.ev("d_xr"), sch.ev("d_ka")))




def build_l1a(nc):
    x1_d = nc.dram_tensor("x1", [SEQ_PER_CORE, S, D], F32, kind="ExternalInput").ap()
    wka_d = nc.dram_tensor("Wk_a", [D, D], F32, kind="ExternalInput").ap()
    wva_d = nc.dram_tensor("Wv_a", [D, D], F32, kind="ExternalInput").ap()
    wktf_d = nc.dram_tensor("WkT_f", [D, D], F32, kind="ExternalInput").ap()
    bkr_d = nc.dram_tensor("bk_row", [1, D], F32, kind="ExternalInput").ap()
    mur_d = nc.dram_tensor("mu_r_pc", [128, NCH], F32, kind="ExternalInput").ap()
    muk_d = nc.dram_tensor("mu_k_pc", [128, NCH], F32, kind="ExternalInput").ap()
    ident_d = nc.dram_tensor("ident", [128, 128], F32, kind="ExternalInput").ap()
    xn1fm_d = nc.dram_tensor("xn1fm_out", [SEQ_PER_CORE, NCH, 128, S], BF16,
                             kind="ExternalOutput").ap()
    xrfm_d = nc.dram_tensor("xrfm_out", [SEQ_PER_CORE, NCH, 128, S], BF16,
                            kind="ExternalOutput").ap()
    kafm_d = nc.dram_tensor("kafm_out", [SEQ_PER_CORE, NCH, 128, S], BF16,
                            kind="ExternalOutput").ap()
    kv_d = nc.dram_tensor("kv_out", [SEQ_PER_CORE, 16, 64, 65], BF16,
                          kind="ExternalOutput").ap()
    dr = {"x1": x1_d, "Wk_a": wka_d, "Wv_a": wva_d, "WkT_f": wktf_d,
          "bk_row": bkr_d, "mu_r_pc": mur_d, "mu_k_pc": muk_d, "ident": ident_d,
          "xn1fm_out": xn1fm_d, "xrfm_out": xrfm_d, "kafm_out": kafm_d,
          "kv_out": kv_d}
    with contextlib.ExitStack() as ctx:
        sch = Sched(nc, ctx)
        block = ctx.enter_context(nc.Block())
        with contextlib.ExitStack() as pctx:
            emit_l1a(nc, sch, pctx.enter_context, dr)
            sch.build(block)
    return nc


def emit_l1b(nc, sch, E, dr, pfx=""):
    sch.pfx = pfx
    x1_d = dr["x1"]
    wqa_d = dr["Wq_a"]
    bq_d = dr["bq_pc"]
    wo_d = dr["Wo_a"]
    bor_d = dr["bo_row"]
    bv_d = dr["bv_pc"]
    wrt_d = dr["WrT_f"]
    wvt_d = dr["WvT_f"]
    NH = 16
    NP = 8
    xn1fm_d = dr["xn1fm"]
    xrfm_d = dr["xrfm"]
    kafm_d = dr["kafm"]
    kv_d = dr["kv"]
    out_d = dr["out_final"]
    nrm_scr = nc.dram_tensor("nrm2_scr", [SEQ_PER_CORE, 4, 2, 8 * 512], F32,
                             kind="Internal").ap()
    o1_scr = nc.dram_tensor("o1_scr", [SEQ_PER_CORE, 4, NP, 2, 64, 512], BF16,
                            kind="Internal").ap()

    wqa_sb = E(nc.sbuf_tensor(pfx + "wqa_sb", [128, NCH, D], BF16))
    wo_sb = E(nc.sbuf_tensor(pfx + "wo_sb", [128, NCH, D], BF16))
    wrt_sb = E(nc.sbuf_tensor(pfx + "wrt_sb", [128, NCH, D], BF16))
    wvt_sb = E(nc.sbuf_tensor(pfx + "wvt_sb", [128, NCH, D], BF16))
    bq_sb = E(nc.sbuf_tensor(pfx + "bq_sb", [128, NCH], F32))
    bv_sb = E(nc.sbuf_tensor(pfx + "bv_sb", [128, NCH], F32))
    bor_b = E(nc.sbuf_tensor(pfx + "bor_b", [1, D], BF16))
    bvb = E(nc.sbuf_tensor(pfx + "bvb", [128, NCH], BF16))
    ones_r = E(nc.sbuf_tensor(pfx + "ones_r", [1, 128], BF16))
    ones_f = E(nc.sbuf_tensor(pfx + "ones_f", [1, 64], F32))
    effrow_b = E(nc.sbuf_tensor(pfx + "effrow_b", [1, D], BF16))
    kv_sb = E(nc.sbuf_tensor(pfx + "kv_sb", [128, NP, 65], BF16))
    xn1_qc = E(nc.sbuf_tensor(pfx + "xn1_qc", [128, NCH, 512], BF16))
    qfeat = E(nc.sbuf_tensor(pfx + "qfeat", [128, NCH, 512], BF16))
    e_sb = E(nc.sbuf_tensor(pfx + "e_sb", [128, 512], F32))
    r_sb = E(nc.sbuf_tensor(pfx + "r_sb", [128, 512], F32))
    o_raw = E(nc.sbuf_tensor(pfx + "o_raw", [64, NH, 512], BF16))
    nrm_sb = E(nc.sbuf_tensor(pfx + "nrm_sb", [128, 8 * 512], F32))
    rin_sb = E(nc.sbuf_tensor(pfx + "rin_sb", [128, 32], F32))
    o1_nfm = E(nc.sbuf_tensor(pfx + "o1_nfm", [128, NCH, 512], BF16))
    x1t = E(nc.sbuf_tensor(pfx + "x1t", [128, D], F32))
    xr_t = E(nc.sbuf_tensor(pfx + "xr_t", [128, 2, NCH, 128], BF16))
    ka_t = E(nc.sbuf_tensor(pfx + "ka_t", [128, 2, NCH, 128], BF16))
    rsg_sb = E(nc.sbuf_tensor(pfx + "rsg_sb", [128, D], F32))
    fv_sb = E(nc.sbuf_tensor(pfx + "fv_sb", [128, D], F32))
    q_ps = E(nc.psum_tensor(pfx + "q_ps", [128, 512], F32))
    o1A = E(nc.psum_tensor(pfx + "o1A", [65, 512], F32))
    o1B = E(nc.psum_tensor(pfx + "o1B", [65, 512], F32))
    nbc_ps = E(nc.psum_tensor(pfx + "nbc_ps", [64, 512], F32))
    pr_ps = E(nc.psum_tensor(pfx + "pr_ps", [128, 2, 512], F32))
    rv_ps = E(nc.psum_tensor(pfx + "rv_ps", [128, 2, 512], F32))


    e_wqa = sch.dma("gpsimd", wqa_sb[:],
                    wqa_d.rearrange("(c p) o -> p c o", p=128), "d_wqa")
    e_wo = sch.dma("gpsimd", wo_sb[:],
                   wo_d.rearrange("(c p) o -> p c o", p=128), "d_wo")
    e_wrt = sch.dma("gpsimd", wrt_sb[:],
                    wrt_d.rearrange("(c p) o -> p c o", p=128), "d_wrt")
    e_wvt = sch.dma("gpsimd", wvt_sb[:],
                    wvt_d.rearrange("(c p) o -> p c o", p=128), "d_wvt")
    e_bor = sch.dma("gpsimd", bor_b[:], bor_d, "d_bor")
    e_bq = sch.dma("sync", bq_sb[:], bq_d, "d_bq")
    e_bv = sch.dma("sync", bv_sb[:], bv_d, "d_bv")
    e_ones = sch.op("vector", lambda e: e.memset(ones_r[:], 1.0))
    e_ones_f = sch.op("vector", lambda e: e.memset(ones_f[:], 1.0))
    e_bvb = sch.op("vector", lambda e: e.tensor_copy(bvb[:], bv_sb[:]),
                   waits=(e_bv,))

    e_eff = None
    for half in range(2):
        for c in range(NCH):
            sch.op("tensor", lambda e, c=c, half=half: e.matmul(
                pr_ps[0:1, half], bvb[:, c:c + 1],
                wo_sb[:, c, half * 512:half * 512 + 512],
                start=(c == 0), stop=False),
                waits=ev_max(e_bvb, e_wo) if (c == 0 and half == 0) else ())
        e_eff = sch.op("tensor", lambda e, half=half: e.matmul(
            pr_ps[0:1, half], ones_r[0:1, 0:1],
            bor_b[0:1, half * 512:half * 512 + 512],
            start=False, stop=True),
            waits=ev_max(e_ones, e_bor) if half == 0 else (), inc="t")
    e_effrow = sch.op("scalar", lambda e: e.copy(
        out=effrow_b[:], in_=pr_ps[0:1, :].rearrange("p h j -> p (h j)")),
        waits=(e_eff,))

    e_kv_user = None
    e_xn_user = None
    e_qf_user = None
    e_er_user = None
    e_oraw_evac_war = None
    e_nrm_user = None
    e_nbc_free = None
    e_o1_store = None
    e_onfm_user = None
    e_x1_user = None
    e_xrka_rel = [None, None]
    e_out_store = None
    e_qps_free = None
    e_prps_free = {0: None, 1: None}
    e_rvps_sig = {0: None, 1: None}
    e_rvps_ff = {0: None, 1: None}
    for s_ in range(SEQ_PER_CORE):
        e_kv = sch.dma("sync", kv_sb[:],
                       kv_d[s_].rearrange("(hp h2) d c -> (h2 d) hp c", h2=2),
                       "d_kv", waits=ev_max(e_kv_user))
        for qc in range(4):
            e_xn = sch.dma("sync", xn1_qc[:],
                           xn1fm_d[s_].rearrange(
                               "c p t -> p c t")[:, :, qc * 512:(qc + 1) * 512],
                           "d_xn", waits=ev_max(e_xn_user))
            for dc in range(NCH):
                for c in range(NCH):
                    e_qmm = sch.op("tensor", lambda e, dc=dc, c=c: e.matmul(
                        q_ps[:, 0:512], wqa_sb[:, c, dc * 128:(dc + 1) * 128],
                        xn1_qc[:, c, :], start=(c == 0), stop=(c == NCH - 1)),
                        waits=ev_max(e_xn, e_wqa, e_qps_free) if c == 0 else (),
                        inc="t" if c == NCH - 1 else None)
                e_e = sch.op("scalar", lambda e, dc=dc: e.activation(
                    out=e_sb[:], in_=q_ps[:, 0:512], func=AF.Exp,
                    bias=bq_sb[:, dc:dc + 1]),
                    waits=ev_max(e_qmm, e_bq, e_er_user))
                e_r = sch.op("scalar", lambda e, dc=dc: e.activation(
                    out=r_sb[:], in_=q_ps[:, 0:512], func=AF.Relu,
                    bias=bq_sb[:, dc:dc + 1]), waits=())
                e_qf = sch.op("vector", lambda e, dc=dc: e.scalar_tensor_tensor(
                    out=qfeat[:, dc, :], in0=e_sb[:], scalar=1.0,
                    in1=r_sb[:], op0=ALU.min, op1=ALU.add),
                    waits=ev_max(e_r, e_qf_user))
                e_er_user = e_qf
                e_qps_free = e_r
            e_qf_done = e_qf
            e_div_all = []
            for grp in range(2):
                evs = []
                for hpl in range(4):
                    hp = grp * 4 + hpl
                    for hi, o_ps in enumerate((o1A, o1B)):
                        b0 = hi * 64
                        e_oq = sch.op(
                            "tensor", lambda e, hp=hp, hi=hi, b0=b0, o_ps=o_ps:
                            e.matmul(
                                o_ps[:, 0:512], kv_sb[b0:b0 + 64, hp, :],
                                qfeat[b0:b0 + 64, hp, :],
                                start=True, stop=True, tile_position=(b0, 0)),
                            waits=ev_max(e_qf_done, e_kv, e_oraw_evac_war),
                            inc="t")
                        h = 2 * hp + hi
                        evs.append(sch.op(
                            "scalar", lambda e, o_ps=o_ps, h=h: e.copy(
                                out=o_raw[:, h, :], in_=o_ps[0:64, 0:512]),
                            waits=ev_max(e_oq, e_o1_store)))
                        slot = (hp % 4) * 2 + hi
                        evs.append(sch.op(
                            "scalar", lambda e, o_ps=o_ps, slot=slot: e.copy(
                                out=nrm_sb[64:65, slot * 512:(slot + 1) * 512],
                                in_=o_ps[64:65, 0:512]),
                            waits=ev_max(e_nrm_user)))
                        e_oraw_evac_war = evs[-1]
                e_h1 = sch.dma("sync", nrm_scr[s_, qc, grp], nrm_sb[64:65, :],
                               "d_nbo", waits=ev_max(*evs))
                e_nrm_user = e_h1
                e_h2 = sch.dma("sync", rin_sb[:],
                               nrm_scr[s_, qc, grp].rearrange("(r j) -> r j", r=128),
                               "d_nbi", waits=(e_h1,))
                e_cl = sch.op("vector", lambda e: e.tensor_scalar(
                    out=rin_sb[:], in0=rin_sb[:], scalar1=1e-6, scalar2=None,
                    op0=ALU.max), waits=(e_h2,))
                e_rcp = sch.op("vector", lambda e: e.reciprocal(
                    out=rin_sb[:], in_=rin_sb[:]), waits=(e_cl,))
                e_h3 = sch.dma("sync",
                               nrm_scr[s_, qc, grp].rearrange("(r j) -> r j", r=128),
                               rin_sb[:], "d_nbo", waits=(e_rcp,))
                e_h4 = sch.dma("sync", nrm_sb[0:1, :],
                               nrm_scr[s_, qc, grp].rearrange("j -> () j"),
                               "d_nbi", waits=(e_h3,))
                for hpl in range(4):
                    hp = grp * 4 + hpl
                    for hi in range(2):
                        slot = hpl * 2 + hi
                        h = 2 * hp + hi
                        e_nbc = sch.op(
                            "tensor", lambda e, slot=slot: e.matmul(
                                nbc_ps[:, 0:512], ones_f[0:1, :],
                                nrm_sb[0:1, slot * 512:(slot + 1) * 512],
                                start=True, stop=True),
                            waits=ev_max(e_h4, e_nbc_free, e_ones_f), inc="t")
                        e_div = sch.op(
                            "vector", lambda e, h=h: e.tensor_tensor(
                                out=o_raw[:, h, :], in0=o_raw[:, h, :],
                                in1=nbc_ps[0:64, 0:512], op=ALU.mult),
                            waits=ev_max(e_nbc, e_o1_store))
                        e_nbc_free = e_div
                        e_div_all.append(e_div)
            e_o1_store = sch.dma(
                "sync", o1_scr[s_, qc].rearrange("hp h d j -> d hp h j"),
                o_raw[:].rearrange("d (hp h) j -> d hp h j", h=2), "d_o1o",
                waits=ev_max(*e_div_all))
            e_onfm = sch.dma(
                "sync", o1_nfm[:],
                o1_scr[s_, qc].rearrange("hp h d j -> (h d) hp j"), "d_o1i",
                waits=ev_max(e_o1_store, e_onfm_user))
            e_pj = None
            for tl in range(4):
                tt = qc * 4 + tl
                par = tl % 2
                t0 = tt * TS
                e_x1 = sch.dma("sync", x1t[:], x1_d[s_, t0:t0 + TS, :], "d_x1",
                               waits=ev_max(e_x1_user))
                e_xr = sch.dma("sync", xr_t[:, par],
                               xrfm_d[s_].rearrange(
                                   "c p t -> p c t")[:, :, t0:t0 + TS],
                               "d_xrt%d" % par, waits=ev_max(e_xrka_rel[par]))
                e_ka = sch.dma("sync", ka_t[:, par],
                               kafm_d[s_].rearrange(
                                   "c p t -> p c t")[:, :, t0:t0 + TS],
                               "d_kat%d" % par, waits=ev_max(e_xrka_rel[par]))
                for half in range(2):
                    for c in range(NCH):
                        sch.op("tensor", lambda e, c=c, half=half, tl=tl: e.matmul(
                            pr_ps[:, half], o1_nfm[:, c, tl * TS:(tl + 1) * TS],
                            wo_sb[:, c, half * 512:(half + 1) * 512],
                            start=(c == 0), stop=False),
                            waits=ev_max(e_onfm, e_effrow, e_ones,
                                         e_prps_free[half]) if c == 0 else ())
                    e_pj = sch.op("tensor", lambda e, half=half: e.matmul(
                        pr_ps[:, half], ones_r[0:1, :],
                        effrow_b[0:1, half * 512:(half + 1) * 512],
                        start=False, stop=True), inc="t")
                    for c in range(NCH):
                        e_rm = sch.op("tensor", lambda e, c=c, half=half, par=par:
                                      e.matmul(
                                          rv_ps[:, half], xr_t[:, par, c, :],
                                          wrt_sb[:, c, half * 512:(half + 1) * 512],
                                          start=(c == 0), stop=(c == NCH - 1)),
                                      waits=ev_max(e_xr, e_wrt,
                                                   e_rvps_ff[half])
                                      if c == 0 else (),
                                      inc="t" if c == NCH - 1 else None)
                    e_sg = sch.op("scalar", lambda e, half=half: e.activation(
                        out=rsg_sb[:, half * 512:(half + 1) * 512],
                        in_=rv_ps[:, half], func=AF.Sigmoid),
                        waits=ev_max(e_rm))
                    e_rvps_sig[half] = e_sg
                    e_a1 = sch.op("vector", lambda e, half=half: e.tensor_tensor(
                        out=x1t[:, half * 512:(half + 1) * 512],
                        in0=x1t[:, half * 512:(half + 1) * 512],
                        in1=pr_ps[:, half], op=ALU.add),
                        waits=ev_max(e_pj, e_x1))
                    e_prps_free[half] = e_a1
                for half in range(2):
                    for c in range(NCH):
                        e_om = sch.op("tensor", lambda e, c=c, half=half, par=par:
                                      e.matmul(
                                          rv_ps[:, half], ka_t[:, par, c, :],
                                          wvt_sb[:, c, half * 512:(half + 1) * 512],
                                          start=(c == 0), stop=(c == NCH - 1)),
                                      waits=ev_max(e_ka, e_wvt, e_rvps_sig[half])
                                      if c == 0 else (),
                                      inc="t" if c == NCH - 1 else None)
                    e_f = sch.op("vector", lambda e, half=half: e.tensor_tensor(
                        out=fv_sb[:, half * 512:(half + 1) * 512],
                        in0=rsg_sb[:, half * 512:(half + 1) * 512],
                        in1=rv_ps[:, half], op=ALU.mult),
                        waits=ev_max(e_om, e_out_store))
                    e_rvps_ff[half] = e_f
                    e_o = sch.op("vector", lambda e, half=half: e.tensor_tensor(
                        out=x1t[:, half * 512:(half + 1) * 512],
                        in0=x1t[:, half * 512:(half + 1) * 512],
                        in1=fv_sb[:, half * 512:(half + 1) * 512], op=ALU.add))
                e_xrka_rel[par] = ("t", sch.cnt["t"])
                e_out_store = sch.dma("sync", out_d[s_, t0:t0 + TS, :], x1t[:],
                                      "d_out", waits=(e_o,))
                e_x1_user = e_out_store
            e_onfm_user = e_pj
            e_xn_user = e_qmm
            e_qf_user = ("t", sch.cnt["t"])
        e_kv_user = ("t", sch.cnt["t"])
    sch.op("sync", None, waits=ev_max(sch.ev("d_out")))




def build_l1b(nc):
    x1_d = nc.dram_tensor("x1", [SEQ_PER_CORE, S, D], F32, kind="ExternalInput").ap()
    xn1fm_d = nc.dram_tensor("xn1fm", [SEQ_PER_CORE, NCH, 128, S], BF16,
                             kind="ExternalInput").ap()
    xrfm_d = nc.dram_tensor("xrfm", [SEQ_PER_CORE, NCH, 128, S], BF16,
                            kind="ExternalInput").ap()
    kafm_d = nc.dram_tensor("kafm", [SEQ_PER_CORE, NCH, 128, S], BF16,
                            kind="ExternalInput").ap()
    kv_d = nc.dram_tensor("kv", [SEQ_PER_CORE, 16, 64, 65], BF16,
                          kind="ExternalInput").ap()
    wqa_d = nc.dram_tensor("Wq_a", [D, D], F32, kind="ExternalInput").ap()
    bq_d = nc.dram_tensor("bq_pc", [128, NCH], F32, kind="ExternalInput").ap()
    wo_d = nc.dram_tensor("Wo_a", [D, D], F32, kind="ExternalInput").ap()
    bor_d = nc.dram_tensor("bo_row", [1, D], F32, kind="ExternalInput").ap()
    bv_d = nc.dram_tensor("bv_pc", [128, NCH], F32, kind="ExternalInput").ap()
    wrt_d = nc.dram_tensor("WrT_f", [D, D], F32, kind="ExternalInput").ap()
    wvt_d = nc.dram_tensor("WvT_f", [D, D], F32, kind="ExternalInput").ap()
    out_d = nc.dram_tensor("out_final", [SEQ_PER_CORE, S, D], F32,
                           kind="ExternalOutput").ap()
    dr = {"x1": x1_d, "xn1fm": xn1fm_d, "xrfm": xrfm_d, "kafm": kafm_d,
          "kv": kv_d, "Wq_a": wqa_d, "bq_pc": bq_d, "Wo_a": wo_d,
          "bo_row": bor_d, "bv_pc": bv_d, "WrT_f": wrt_d, "WvT_f": wvt_d,
          "out_final": out_d}
    with contextlib.ExitStack() as ctx:
        sch = Sched(nc, ctx)
        block = ctx.enter_context(nc.Block())
        with contextlib.ExitStack() as pctx:
            emit_l1b(nc, sch, pctx.enter_context, dr)
            sch.build(block)
    return nc


def build_fused(nc):
    EI, EO, IN = "ExternalInput", "ExternalOutput", "Internal"
    t = nc.dram_tensor
    x_d = t("x", [SEQ_PER_CORE, S, D], F32, kind=EI).ap()
    wg_d = t("w_gate", [I, D], F32, kind=EI).ap()
    wu_d = t("w_up", [I, D], F32, kind=EI).ap()
    wd_d = t("w_down", [D, I], F32, kind=EI).ap()
    ident_d = t("ident", [128, 128], F32, kind=EI).ap()
    qkvwT_d = t("qkv_wT", [D, 3 * D], F32, kind=EI).ap()
    qkvb_d = t("qkv_b_pc", [128, 24], F32, kind=EI).ap()
    projwT_d = t("proj_wT", [D, D], F32, kind=EI).ap()
    projbr_d = t("proj_b_row", [1, D], F32, kind=EI).ap()
    wka_d = t("Wk_a", [D, D], F32, kind=EI).ap()
    wva_d = t("Wv_a", [D, D], F32, kind=EI).ap()
    wktf_d = t("WkT_f", [D, D], F32, kind=EI).ap()
    bkr_d = t("bk_row", [1, D], F32, kind=EI).ap()
    mur_d = t("mu_r_pc", [128, NCH], F32, kind=EI).ap()
    muk_d = t("mu_k_pc", [128, NCH], F32, kind=EI).ap()
    wqa_d = t("Wq_a", [D, D], F32, kind=EI).ap()
    bq_d = t("bq_pc", [128, NCH], F32, kind=EI).ap()
    wo_d = t("Wo_a", [D, D], F32, kind=EI).ap()
    bor_d = t("bo_row", [1, D], F32, kind=EI).ap()
    bv_d = t("bv_pc", [128, NCH], F32, kind=EI).ap()
    wrt_d = t("WrT_f", [D, D], F32, kind=EI).ap()
    wvt_d = t("WvT_f", [D, D], F32, kind=EI).ap()
    out_d = t("out_final", [SEQ_PER_CORE, S, D], F32, kind=EO).ap()
    y_b = t("y_buf", [SEQ_PER_CORE, S, D], F32, kind=IN).ap()
    xnfm_b = t("xnfm_buf", [SEQ_PER_CORE, NCH, 128, S], BF16, kind=IN).ap()
    x1_b = t("x1_buf", [SEQ_PER_CORE, S, D], F32, kind=IN).ap()
    xn1fm_b = t("xn1fm_buf", [SEQ_PER_CORE, NCH, 128, S], BF16, kind=IN).ap()
    xrfm_b = t("xrfm_buf", [SEQ_PER_CORE, NCH, 128, S], BF16, kind=IN).ap()
    kafm_b = t("kafm_buf", [SEQ_PER_CORE, NCH, 128, S], BF16, kind=IN).ap()
    kv_b = t("kv_buf", [SEQ_PER_CORE, 16, 64, 65], BF16, kind=IN).ap()

    phases = [
        ("a_", emit_l0a, {"x": x_d, "w_gate": wg_d, "w_up": wu_d,
                          "w_down": wd_d, "ident": ident_d,
                          "y_out": y_b, "xnfm_out": xnfm_b}),
        ("b_", emit_l0b, {"xnfm": xnfm_b, "y_in": y_b, "qkv_wT": qkvwT_d,
                          "qkv_b_pc": qkvb_d, "proj_wT": projwT_d,
                          "proj_b_row": projbr_d, "x1_out": x1_b}),
        ("c_", emit_l1a, {"x1": x1_b, "Wk_a": wka_d, "Wv_a": wva_d,
                          "WkT_f": wktf_d, "bk_row": bkr_d, "mu_r_pc": mur_d,
                          "mu_k_pc": muk_d, "ident": ident_d,
                          "xn1fm_out": xn1fm_b, "xrfm_out": xrfm_b,
                          "kafm_out": kafm_b, "kv_out": kv_b}),
        ("d_", emit_l1b, {"x1": x1_b, "xn1fm": xn1fm_b, "xrfm": xrfm_b,
                          "kafm": kafm_b, "kv": kv_b, "Wq_a": wqa_d,
                          "bq_pc": bq_d, "Wo_a": wo_d, "bo_row": bor_d,
                          "bv_pc": bv_d, "WrT_f": wrt_d, "WvT_f": wvt_d,
                          "out_final": out_d}),
    ]
    with contextlib.ExitStack() as ctx:
        sch = Sched(nc, ctx)
        block = ctx.enter_context(nc.Block())
        for pfx, emitf, dr in phases:
            with contextlib.ExitStack() as pctx:
                emitf(nc, sch, pctx.enter_context, dr, pfx=pfx)
                sch.barrier()
                sch.build(block)
        if END_CLEAR:
            sch.end_clear()
            sch.build(block)
    return nc


def prep_in_maps(inputs):
    f32 = np.float32
    x_full = np.ascontiguousarray(inputs["L0_attn_x"], dtype=f32)
    qkv_b = np.asarray(inputs["L0_attn_qkv_b"], dtype=f32)
    sh = {
        "w_gate": np.ascontiguousarray(inputs["L0_ffn_w_gate"], dtype=f32),
        "w_up": np.ascontiguousarray(inputs["L0_ffn_w_up"], dtype=f32),
        "w_down": np.ascontiguousarray(inputs["L0_ffn_w_down"], dtype=f32),
        "ident": np.eye(128, dtype=f32),
        "qkv_wT": np.ascontiguousarray(np.asarray(inputs["L0_attn_qkv_w"], f32).T),
        "qkv_b_pc": np.ascontiguousarray(qkv_b.reshape(24, 128).T),
        "proj_wT": np.ascontiguousarray(np.asarray(inputs["L0_attn_proj_w"], f32).T),
        "proj_b_row": np.ascontiguousarray(
            np.asarray(inputs["L0_attn_proj_b"], f32).reshape(1, D)),
        "Wk_a": np.ascontiguousarray(np.asarray(inputs["L1_attn_Wk"], f32)),
        "Wv_a": np.ascontiguousarray(np.asarray(inputs["L1_attn_Wv"], f32)),
        "WkT_f": np.ascontiguousarray(np.asarray(inputs["L1_ffn_W_k"], f32).T),
        "bk_row": np.ascontiguousarray(
            np.asarray(inputs["L1_attn_bk"], f32).reshape(1, D)),
        "mu_r_pc": np.ascontiguousarray(
            np.asarray(inputs["L1_ffn_mu_r"], f32).reshape(8, 128).T),
        "mu_k_pc": np.ascontiguousarray(
            np.asarray(inputs["L1_ffn_mu_k"], f32).reshape(8, 128).T),
        "Wq_a": np.ascontiguousarray(np.asarray(inputs["L1_attn_Wq"], f32)),
        "bq_pc": np.ascontiguousarray(
            np.asarray(inputs["L1_attn_bq"], f32).reshape(8, 128).T),
        "Wo_a": np.ascontiguousarray(np.asarray(inputs["L1_attn_Wo"], f32)),
        "bo_row": np.ascontiguousarray(
            np.asarray(inputs["L1_attn_bo"], f32).reshape(1, D)),
        "bv_pc": np.ascontiguousarray(
            np.asarray(inputs["L1_attn_bv"], f32).reshape(8, 128).T),
        "WrT_f": np.ascontiguousarray(np.asarray(inputs["L1_ffn_W_r"], f32).T),
        "WvT_f": np.ascontiguousarray(np.asarray(inputs["L1_ffn_W_v"], f32).T),
    }
    return [dict(sh, x=x_full[2 * c:2 * c + 2]) for c in range(NCORES)]



_NCS = {}


def _get_nc(name, builder):
    if name not in _NCS:
        _NCS[name] = builder(bass.Bass("TRN2", target_bir_lowering=False))
    return _NCS[name]


def kernel(**inputs):
    in_maps = prep_in_maps(inputs)
    nc = _get_nc("fused", build_fused)
    res = run_bass_kernel_spmd(nc, in_maps, core_ids=list(range(NCORES)))
    return np.concatenate([np.asarray(r["out_final"]) for r in res.results], axis=0)
